# version 03
# baseline (speedup 1.0000x reference)
import math
from contextlib import ExitStack

import numpy as np
import ml_dtypes
import concourse.bass as bass
import concourse.mybir as mybir
from concourse.bass_utils import run_bass_kernel_spmd

F32 = mybir.dt.float32
BF16 = mybir.dt.bfloat16
FP16 = mybir.dt.float16
FP8 = mybir.dt.float8e4
I32 = mybir.dt.int32
AL = mybir.AluOpType
AF = mybir.ActivationFunctionType
AX = mybir.AxisListType

D = 4096
R = 1024
NH = 32
HD = 128
SEQ = 4096
TOPK = 819
QG = 512
NQG = R // QG
NBIS = 24
NDESC = 4
ASCALE = float(np.float32(1.0 / math.sqrt(HD)))
EPS_LN = 1.0e-5
W0 = 96.0
INV448 = float(np.float32(1.0 / 448.0))
DEK = float(np.float32(2.0 ** 20 + 1.0))


class Cnt:
    def __init__(self, h):
        self.h = h
        self.n = 0

    def inc(self, k=1):
        self.n += k
        return self.n

    def tick(self):
        self.n += 1
        return 1


DEBUG_DUMPS = False


def build_program():
    nc = bass.Bass("TRN2", target_bir_lowering=False, debug=False,
                   detect_race_conditions=False)

    x_tok = nc.dram_tensor("x_tok", [R, D], F32, kind="ExternalInput").ap()
    wqT = nc.dram_tensor("wqT", [D, D], F32, kind="ExternalInput").ap()
    wkT = nc.dram_tensor("wkT", [D, D], F32, kind="ExternalInput").ap()
    wvT = nc.dram_tensor("wvT", [D, D], F32, kind="ExternalInput").ap()
    woT = nc.dram_tensor("woT", [D, D], F32, kind="ExternalInput").ap()
    bq_in = nc.dram_tensor("bq", [D], F32, kind="ExternalInput").ap()
    bk_in = nc.dram_tensor("bk", [D], F32, kind="ExternalInput").ap()
    bv_in = nc.dram_tensor("bv", [D], F32, kind="ExternalInput").ap()
    bo_rep = nc.dram_tensor("bo_rep", [128, D], F32, kind="ExternalInput").ap()
    fw = nc.dram_tensor("fw", [D, D], F32, kind="ExternalInput").ap()
    fb_in = nc.dram_tensor("fb", [D], F32, kind="ExternalInput").ap()
    y_out = nc.dram_tensor("y", [R, D], F32, kind="ExternalOutput").ap()

    qt_sc = nc.dram_tensor("qt_sc", [D, R], F32).ap()
    kin = nc.dram_tensor("kin", [D, R], F32).ap()
    vin = nc.dram_tensor("vin", [D, R], F32).ap()
    kall = nc.dram_tensor("kall", [4, D, R], F32).ap()
    vall = nc.dram_tensor("vall", [4, D, R], F32).ap()
    x2_sc = nc.dram_tensor("x2_sc", [R, D], F32).ap()
    w8t_sc = nc.dram_tensor("w8t_sc", [D, D], BF16).ap()
    ot_sc = nc.dram_tensor("ot_sc", [D, R], BF16).ap()

    ident_f = nc.inline_tensor(np.eye(128, dtype=np.float32), "ident_f").ap()
    ident_b = nc.inline_tensor(np.eye(128, dtype=ml_dtypes.bfloat16), "ident_b").ap()
    ones_f = nc.inline_tensor(np.ones((1, 128), dtype=np.float32), "ones_f").ap()

    es = ExitStack()
    with es:
        def sbuf(name, shape, dt=F32, st=None):
            return (st or es).enter_context(nc.sbuf_tensor(name, shape, dt))

        def sem(name):
            return Cnt(es.enter_context(nc.semaphore(name)))

        block = es.enter_context(nc.Block())

        dl = sem("dl")
        dg = sem("dg")
        sv = sem("sv")
        ss = sem("ss")
        st = sem("st")
        cc = sem("cc")
        vb = sem("vb")

        pMM = es.enter_context(nc.psum_tensor("pMM", [128, 2048], F32))
        pT = es.enter_context(nc.psum_tensor("pT", [128, 1024], F32))
        pCT = es.enter_context(nc.psum_tensor("pCT", [128, 128], F32))
        pTb = es.enter_context(nc.psum_tensor("pTb", [128, 512], BF16))

        sid = sbuf("sid", [128, 128])
        sidb = sbuf("sidb", [128, 128], BF16)
        sones = sbuf("sones", [1, 128])
        sbq = sbuf("sbq", [128, 32])
        sbk = sbuf("sbk", [128, 32])
        sbv = sbuf("sbv", [128, 32])
        sfb = sbuf("sfb", [128, 32])
        sfb16 = sbuf("sfb16", [128, 32], BF16)
        sws = sbuf("sws", [128, 32])
        swsr = sbuf("swsr", [128, 32])
        sws16 = sbuf("sws16", [128, 32], FP16)
        sasr = sbuf("sasr", [128, 8])
        sasrow = sbuf("sasrow", [1, 1024])
        bc_as = sbuf("bc_as", [128, 1024])
        stny = sbuf("stny", [128, 48])
        cst = sbuf("cst", [128, 1])

        T = stny

        def _(sy):
            sy.dma_start(out=sid[:], in_=ident_f).then_inc(dl.h, 16); dl.inc(16)
            sy.dma_start(out=sidb[:], in_=ident_b).then_inc(dl.h, 16); dl.inc(16)
            sy.dma_start(out=sones[:], in_=ones_f).then_inc(dl.h, 16); dl.inc(16)
            for dst, src in ((sbq, bq_in), (sbk, bk_in), (sbv, bv_in), (sfb, fb_in)):
                sy.dma_start(out=dst[:], in_=src.rearrange("(c p) -> p c", p=128),
                             allow_slow_non_contiguous=True).then_inc(dl.h, 16)
                dl.inc(16)
        block.sync(_)

        def _(v):
            v.wait_ge(dl.h, dl.n)
            v.memset(cst[:], ASCALE)
            v.tensor_scalar(sbq[:], sbq[:], ASCALE, None, AL.mult, AL.bypass)
            v.tensor_copy(sfb16[:], sfb[:]).then_inc(vb.h, vb.tick())
            v.wait_ge(vb.h, vb.n)
            v.tensor_copy(sfb[:], sfb16[:]).then_inc(sv.h, sv.tick())
        block.vector(_)

        with ExitStack() as pa:
            sx = sbuf("sx", [128, D], st=pa)
            ssq = sbuf("ssq", [128, D], st=pa)
            sxT = sbuf("sxT", [128, 32 * R], st=pa)
            sev = sbuf("sev", [128, 1024], st=pa)
            wcol = sbuf("wcol", [128, 32 * 128], st=pa)

            _dxh_ap = (nc.dram_tensor("d_xh", [R, D], F32, kind="ExternalOutput").ap()
                       if DEBUG_DUMPS else None)

            for tt in range(8):
                def _(sy, tt=tt):
                    if tt > 0:
                        sy.wait_ge(sv.h, sv.n)
                    sy.dma_start(out=sx[:], in_=x_tok[tt * 128:(tt + 1) * 128, :]).then_inc(dl.h, 16)
                    dl.inc(16)
                block.sync(_)

                def _(v):
                    v.wait_ge(dl.h, dl.n)
                    v.tensor_reduce(T[:, 0:1], sx[:], AX.X, AL.add).then_inc(vb.h, vb.tick())
                    v.wait_ge(vb.h, vb.n)
                    v.tensor_scalar(T[:, 0:1], T[:, 0:1], 1.0 / D, None, AL.mult,
                                    AL.bypass).then_inc(sv.h, sv.tick())
                block.vector(_)

                def _(s):
                    s.wait_ge(sv.h, sv.n)
                    s.wait_ge(st.h, st.n)
                    s.activation(ssq[:], sx[:], AF.Square, bias=0.0, scale=1.0,
                                 accum_out=T[:, 1:2]).then_inc(ss.h, ss.tick())
                block.scalar(_)

                def _(v):
                    v.wait_ge(ss.h, ss.n)
                    v.tensor_scalar(T[:, 1:2], T[:, 1:2], 1.0 / D, None, AL.mult, AL.bypass)
                    v.tensor_tensor(T[:, 2:3], T[:, 0:1], T[:, 0:1], AL.mult).then_inc(vb.h, vb.tick())
                    v.wait_ge(vb.h, vb.n)
                    v.tensor_tensor(T[:, 1:2], T[:, 1:2], T[:, 2:3], AL.subtract).then_inc(vb.h, vb.tick())
                    v.wait_ge(vb.h, vb.n)
                    v.tensor_scalar(T[:, 1:2], T[:, 1:2], EPS_LN, None, AL.add,
                                    AL.bypass).then_inc(sv.h, sv.tick())
                block.vector(_)

                def _(s):
                    s.wait_ge(sv.h, sv.n)
                    s.activation(T[:, 2:3], T[:, 1:2], AF.Sqrt, bias=0.0,
                                 scale=1.0).then_inc(ss.h, ss.tick())
                block.scalar(_)

                def _(v):
                    v.wait_ge(ss.h, ss.n)
                    v.reciprocal(T[:, 3:4], T[:, 2:3]).then_inc(vb.h, vb.tick())
                    v.wait_ge(vb.h, vb.n)
                    v.tensor_scalar(ssq[:], sx[:], T[:, 0:1], T[:, 3:4], AL.subtract,
                                    AL.mult).then_inc(sv.h, sv.tick())
                block.vector(_)

                if DEBUG_DUMPS:
                    def _(g, tt=tt):
                        g.wait_ge(sv.h, sv.n)
                        g.dma_start(out=_dxh_ap[tt * 128:(tt + 1) * 128, :],
                                    in_=ssq[:]).then_inc(dg.h, 16)
                        dg.inc(16)
                        g.wait_ge(dg.h, dg.n)
                    block.gpsimd(_)

                for q4 in range(8):
                    ph = (q4 % 2) * 512
                    def _(te, tt=tt, q4=q4, ph=ph):
                        te.wait_ge(sv.h, sv.n)
                        for j in range(4):
                            c = q4 * 4 + j
                            ins = te.transpose(pT[:, ph + j * 128: ph + (j + 1) * 128],
                                               ssq[:, c * 128:(c + 1) * 128], sid[:])
                        ins.then_inc(st.h, st.tick())
                    block.tensor(_)

                    def _(v, tt=tt, q4=q4, ph=ph):
                        v.wait_ge(st.h, st.n)
                        for j in range(4):
                            c = q4 * 4 + j
                            ins = v.tensor_copy(
                                sxT[:, c * R + tt * 128: c * R + (tt + 1) * 128],
                                pT[:, ph + j * 128: ph + (j + 1) * 128])
                        ins.then_inc(sv.h, sv.tick())
                    block.vector(_)

            if DEBUG_DUMPS:
                d_xT = nc.dram_tensor("d_xT", [128, 32 * R], F32, kind="ExternalOutput").ap()

                def _(g):
                    g.wait_ge(sv.h, sv.n)
                    g.dma_start(out=d_xT[:], in_=sxT[:]).then_inc(dg.h, 16)
                    dg.inc(16)
                    g.wait_ge(dg.h, dg.n)
                block.gpsimd(_)

            projs = [(wqT, sbq, qt_sc, ASCALE), (wkT, sbk, kin, 1.0), (wvT, sbv, vin, 1.0)]
            for pi, (wT, sbias, dst, scale) in enumerate(projs):
                for do in range(32):
                    ph = (do % 2) * 1024
                    def _(sy, wT=wT, do=do):
                        sy.wait_ge(st.h, st.n)
                        src3 = wT[:, do * 128:(do + 1) * 128].rearrange("(c p) n -> p c n", p=128)
                        dst3 = wcol[:].rearrange("p (c n) -> p c n", c=32)
                        for grp in range(4):
                            sy.dma_start(out=dst3[:, grp * 8:(grp + 1) * 8, :],
                                         in_=src3[:, grp * 8:(grp + 1) * 8, :]).then_inc(dl.h, 16)
                            dl.inc(16)
                    block.sync(_)

                    def _(te, do=do, ph=ph):
                        te.wait_ge(dl.h, dl.n)
                        te.wait_ge(sv.h, sv.n)
                        te.wait_ge(ss.h, ss.n)
                        for half in range(2):
                            for di in range(32):
                                ins = te.matmul(
                                    pMM[:, ph + half * 512: ph + half * 512 + 512],
                                    wcol[:, di * 128:(di + 1) * 128],
                                    sxT[:, di * R + half * 512: di * R + half * 512 + 512],
                                    start=(di == 0), stop=(di == 31))
                        ins.then_inc(st.h, st.tick())
                    block.tensor(_)

                    def _(s, do=do, sbias=sbias, scale=scale, ph=ph):
                        s.wait_ge(st.h, st.n)
                        s.wait_ge(dg.h, dg.n)
                        sc = 1.0 if scale == 1.0 else cst[:, 0:1]
                        s.activation(sev[:, 0:512], pMM[:, ph:ph + 512], AF.Identity,
                                     bias=sbias[:, do:do + 1], scale=sc)
                        s.activation(sev[:, 512:1024], pMM[:, ph + 512:ph + 1024], AF.Identity,
                                     bias=sbias[:, do:do + 1], scale=sc).then_inc(ss.h, ss.tick())
                    block.scalar(_)

                    def _(g, do=do, dst=dst):
                        g.wait_ge(ss.h, ss.n)
                        g.dma_start(out=dst[do * 128:(do + 1) * 128, :], in_=sev[:]).then_inc(dg.h, 16)
                        dg.inc(16)
                    block.gpsimd(_)

        def _(g):
            g.wait_ge(dg.h, dg.n)
            g.collective_compute("AllGather", AL.bypass,
                                 replica_groups=[[0, 1, 2, 3], [4, 5, 6, 7]],
                                 ins=[kin.opt()], outs=[kall.opt()]).then_inc(cc.h, cc.tick())
            g.collective_compute("AllGather", AL.bypass,
                                 replica_groups=[[0, 1, 2, 3], [4, 5, 6, 7]],
                                 ins=[vin.opt()], outs=[vall.opt()]).then_inc(cc.h, cc.tick())
        block.gpsimd(_)

        def _(sy):
            sy.wait_ge(cc.h, cc.n)
            sy.sem_inc(dl.h, 16)
            dl.inc(16)
        block.sync(_)

        with ExitStack() as pb:
            ctxT = sbuf("ctxT", [128, 32 * QG], st=pb)

            for qg in range(NQG):
                with ExitStack() as ph_heads:
                    kTh = sbuf(f"kTh_{qg}", [128, SEQ], st=ph_heads)
                    vTh = sbuf(f"vTh_{qg}", [128, SEQ], st=ph_heads)
                    vtok = sbuf(f"vtok_{qg}", [128, 32 * 128], st=ph_heads)
                    qTh = sbuf(f"qTh_{qg}", [128, QG], st=ph_heads)
                    sS = sbuf(f"sS_{qg}", [128, SEQ], st=ph_heads)
                    sE = sbuf(f"sE_{qg}", [128, SEQ], st=ph_heads)
                    jkb = sbuf(f"jkb_{qg}", [128, SEQ], BF16, st=ph_heads)
                    jkf = sbuf(f"jkf_{qg}", [128, SEQ], st=ph_heads)
                    spt = sbuf(f"spt_{qg}", [128, 256], st=ph_heads)

                    for h in range(NH):
                        def _(sy, h=h, qg=qg):
                            sy.wait_ge(st.h, st.n)
                            sy.wait_ge(sv.h, sv.n)
                            for g4 in range(4):
                                sy.dma_start(out=kTh[:, g4 * R:(g4 + 1) * R],
                                             in_=kall[g4, h * 128:(h + 1) * 128, :]).then_inc(dl.h, 16)
                                dl.inc(16)
                                sy.dma_start(out=vTh[:, g4 * R:(g4 + 1) * R],
                                             in_=vall[g4, h * 128:(h + 1) * 128, :]).then_inc(dl.h, 16)
                                dl.inc(16)
                            sy.dma_start(out=qTh[:],
                                         in_=qt_sc[h * 128:(h + 1) * 128,
                                                   qg * QG:(qg + 1) * QG]).then_inc(dl.h, 16)
                            dl.inc(16)
                        block.sync(_)

                        for b8 in range(8):
                            ph = (b8 % 2) * 512
                            def _(te, b8=b8, ph=ph):
                                te.wait_ge(dl.h, dl.n)
                                te.wait_ge(sv.h, sv.n)
                                for j in range(4):
                                    c = b8 * 4 + j
                                    ins = te.transpose(pT[:, ph + j * 128:ph + (j + 1) * 128],
                                                       vTh[:, c * 128:(c + 1) * 128], sid[:])
                                ins.then_inc(st.h, st.tick())
                            block.tensor(_)

                            def _(v, b8=b8, ph=ph):
                                v.wait_ge(st.h, st.n)
                                ins = v.tensor_copy(vtok[:, b8 * 512:(b8 + 1) * 512],
                                                    pT[:, ph:ph + 512])
                                ins.then_inc(sv.h, sv.tick())
                            block.vector(_)

                        for qt in range(QG // 128):
                            for half in range(2):
                                def _(te, qt=qt, half=half):
                                    te.wait_ge(ss.h, ss.n)
                                    te.wait_ge(sv.h, sv.n)
                                    for j in range(4):
                                        ins = te.matmul(
                                            pMM[:, j * 512:(j + 1) * 512],
                                            qTh[:, qt * 128:(qt + 1) * 128],
                                            kTh[:, half * 2048 + j * 512: half * 2048 + (j + 1) * 512],
                                            start=True, stop=True)
                                    ins.then_inc(st.h, st.tick())
                                block.tensor(_)

                                def _(s, qt=qt, half=half):
                                    s.wait_ge(st.h, st.n)
                                    s.copy(sS[:, half * 2048:(half + 1) * 2048],
                                           pMM[:, 0:2048]).then_inc(ss.h, ss.tick())
                                block.scalar(_)

                            def _(v, qt=qt):
                                v.wait_ge(ss.h, ss.n)
                                v.tensor_reduce(T[:, 4:5], sS[:], AX.X, AL.max,
                                                negate=True).then_inc(sv.h, sv.tick())
                            block.vector(_)

                            def _(s, qt=qt):
                                s.wait_ge(sv.h, sv.n)
                                s.activation(sE[:], sS[:], AF.Exp, bias=T[:, 4:5],
                                             scale=1.0).then_inc(ss.h, ss.tick())
                            block.scalar(_)

                            def _(v):
                                v.wait_ge(ss.h, ss.n)
                                v.tensor_scalar(T[:, 5:6], T[:, 4:5], -1.0, -W0, AL.mult,
                                                AL.add).then_inc(vb.h, vb.tick())
                                v.wait_ge(vb.h, vb.n)
                                v.sem_inc(sv.h, sv.tick())
                            block.vector(_)

                            def _(v):
                                v.tensor_scalar(T[:, 6:7], T[:, 5:6], W0 * 0.5, None,
                                                AL.add, AL.bypass).then_inc(vb.h, vb.tick())
                                v.wait_ge(vb.h, vb.n)
                            block.vector(_)
                            for it in range(NBIS):
                                wkw = W0 * (0.5 ** (it + 1))
                                def _(v, wkw=wkw):
                                    v.tensor_scalar(jkb[:], sS[:], T[:, 6:7], None, AL.is_ge,
                                                    AL.add, accum_out=T[:, 7:8]).then_inc(vb.h, vb.tick())
                                    v.wait_ge(vb.h, vb.n)
                                    v.tensor_scalar(T[:, 8:9], T[:, 7:8], TOPK - 0.5, -0.5,
                                                    AL.is_ge, AL.add).then_inc(vb.h, vb.tick())
                                    v.wait_ge(vb.h, vb.n)
                                    v.scalar_tensor_tensor(T[:, 6:7], T[:, 8:9], wkw, T[:, 6:7],
                                                           AL.mult, AL.add).then_inc(vb.h, vb.tick())
                                    v.wait_ge(vb.h, vb.n)
                                block.vector(_)

                            def _(v):
                                v.tensor_scalar(T[:, 6:7], T[:, 6:7], W0 * (0.5 ** NBIS), None,
                                                AL.add, AL.bypass).then_inc(sv.h, sv.tick())
                            block.vector(_)

                            def _(s):
                                s.wait_ge(sv.h, sv.n)
                                s.activation(T[:, 9:10], T[:, 6:7], AF.Exp, bias=T[:, 4:5],
                                             scale=1.0).then_inc(ss.h, ss.tick())
                            block.scalar(_)

                            def _(v):
                                v.wait_ge(ss.h, ss.n)
                                v.memset(T[:, 10:11], 0.0)
                                v.memset(T[:, 11:12], 0.0)
                            block.vector(_)

                            for d in range(NDESC):
                                def _(v):
                                    v.scalar_tensor_tensor(jkf[:], sE[:], T[:, 9:10], sE[:],
                                                           AL.is_lt, AL.mult)
                                    v.tensor_reduce(T[:, 12:13], jkf[:], AX.X,
                                                    AL.max).then_inc(vb.h, vb.tick())
                                    v.wait_ge(vb.h, vb.n)
                                    v.tensor_scalar(jkb[:], sE[:], T[:, 12:13], None, AL.is_ge,
                                                    AL.add, accum_out=T[:, 7:8]).then_inc(vb.h, vb.tick())
                                    v.wait_ge(vb.h, vb.n)
                                    v.tensor_scalar(T[:, 8:9], T[:, 7:8], TOPK - 0.5, None,
                                                    AL.is_ge, AL.bypass)
                                    v.tensor_scalar(T[:, 13:14], T[:, 11:12], 0.5, None,
                                                    AL.is_lt, AL.bypass).then_inc(vb.h, vb.tick())
                                    v.wait_ge(vb.h, vb.n)
                                    v.tensor_tensor(T[:, 13:14], T[:, 13:14], T[:, 8:9],
                                                    AL.mult).then_inc(vb.h, vb.tick())
                                    v.wait_ge(vb.h, vb.n)
                                    v.scalar_tensor_tensor(T[:, 10:11], T[:, 12:13], T[:, 13:14],
                                                           T[:, 10:11], AL.mult, AL.add)
                                    v.tensor_tensor(T[:, 11:12], T[:, 11:12], T[:, 8:9], AL.max)
                                    v.tensor_copy(T[:, 9:10], T[:, 12:13]).then_inc(vb.h, vb.tick())
                                    v.wait_ge(vb.h, vb.n)
                                block.vector(_)

                            def _(v):
                                v.scalar_tensor_tensor(sS[:], sE[:], T[:, 10:11], sE[:],
                                                       AL.is_ge, AL.mult,
                                                       accum_out=T[:, 14:15]).then_inc(vb.h, vb.tick())
                                v.wait_ge(vb.h, vb.n)
                                v.reciprocal(T[:, 15:16], T[:, 14:15]).then_inc(vb.h, vb.tick())
                                v.wait_ge(vb.h, vb.n)
                                ins = v.tensor_scalar(sE[:], sS[:], T[:, 15:16], None,
                                                      AL.mult, AL.bypass)
                                ins.then_inc(sv.h, sv.tick())
                            block.vector(_)

                            for ck in range(32):
                                pp = (ck % 2)
                                def _(te, ck=ck, pp=pp, qt=qt, h=h):
                                    te.wait_ge(sv.h, sv.n)
                                    te.transpose(pT[:, pp * 512: pp * 512 + 128],
                                                 sE[:, ck * 128:(ck + 1) * 128],
                                                 sid[:]).then_inc(st.h, st.tick())
                                block.tensor(_)

                                def _(v, ck=ck, pp=pp):
                                    v.wait_ge(st.h, st.n)
                                    v.tensor_copy(spt[:, pp * 128:(pp + 1) * 128],
                                                  pT[:, pp * 512: pp * 512 + 128]).then_inc(sv.h, sv.tick())
                                block.vector(_)

                                def _(te, ck=ck, pp=pp):
                                    te.wait_ge(sv.h, sv.n)
                                    te.matmul(pCT[:], vtok[:, ck * 128:(ck + 1) * 128],
                                              spt[:, pp * 128:(pp + 1) * 128],
                                              start=(ck == 0), stop=(ck == 31)).then_inc(st.h, st.tick())
                                block.tensor(_)

                            def _(v, qt=qt, h=h):
                                v.wait_ge(st.h, st.n)
                                v.tensor_copy(ctxT[:, h * QG + qt * 128: h * QG + (qt + 1) * 128],
                                              pCT[:]).then_inc(sv.h, sv.tick())
                            block.vector(_)

                with ExitStack() as po:
                    swo = sbuf(f"swo_{qg}", [128, 32 * 512], st=po)
                    sbo = sbuf(f"sbo_{qg}", [128, 512], st=po)
                    sxr = sbuf(f"sxr_{qg}", [128, 512], st=po)
                    sev2 = sbuf(f"sev2_{qg}", [128, 512], st=po)

                    for dk in range(8):
                        def _(sy, dk=dk):
                            sy.wait_ge(st.h, st.n)
                            src3 = woT[:, dk * 512:(dk + 1) * 512].rearrange(
                                "(c p) n -> p c n", p=128)
                            dst3 = swo[:].rearrange("p (c n) -> p c n", c=32)
                            for grp in range(8):
                                sy.dma_start(out=dst3[:, grp * 4:(grp + 1) * 4, :],
                                             in_=src3[:, grp * 4:(grp + 1) * 4, :]).then_inc(dl.h, 16)
                                dl.inc(16)
                            sy.dma_start(out=sbo[:], in_=bo_rep[:, dk * 512:(dk + 1) * 512]).then_inc(dl.h, 16)
                            dl.inc(16)
                        block.sync(_)

                        for qt in range(QG // 128):
                            pp = (qt % 2) * 512
                            def _(sy, dk=dk, qt=qt, qg=qg):
                                sy.wait_ge(sv.h, sv.n)
                                row0 = qg * QG + qt * 128
                                sy.dma_start(out=sxr[:],
                                             in_=x_tok[row0:row0 + 128,
                                                       dk * 512:(dk + 1) * 512]).then_inc(dl.h, 16)
                                dl.inc(16)
                            block.sync(_)

                            def _(te, dk=dk, qt=qt, pp=pp):
                                te.wait_ge(dl.h, dl.n)
                                te.wait_ge(sv.h, sv.n)
                                for dc in range(32):
                                    ins = te.matmul(pMM[:, pp:pp + 512],
                                                    ctxT[:, dc * QG + qt * 128: dc * QG + (qt + 1) * 128],
                                                    swo[:, dc * 512:(dc + 1) * 512],
                                                    start=(dc == 0), stop=(dc == 31))
                                ins.then_inc(st.h, st.tick())
                            block.tensor(_)

                            def _(v, dk=dk, qt=qt, pp=pp):
                                v.wait_ge(st.h, st.n)
                                v.wait_ge(dg.h, dg.n)
                                v.tensor_tensor(sev2[:], sbo[:], sxr[:], AL.add)
                                v.tensor_tensor(sev2[:], sev2[:], pMM[:, pp:pp + 512],
                                                AL.add).then_inc(sv.h, sv.tick())
                            block.vector(_)

                            def _(g, dk=dk, qt=qt, qg=qg):
                                g.wait_ge(sv.h, sv.n)
                                row0 = qg * QG + qt * 128
                                g.dma_start(out=x2_sc[row0:row0 + 128, dk * 512:(dk + 1) * 512],
                                            in_=sev2[:]).then_inc(dg.h, 16)
                                dg.inc(16)
                            block.gpsimd(_)

        with ExitStack() as pc:
            sx = sbuf("sxc", [128, D], st=pc)
            ssq = sbuf("ssqc", [128, D], st=pc)
            sq1 = sbuf("sq1", [128, D], st=pc)
            sq2 = sbuf("sq2", [128, D], st=pc)
            sf8 = sbuf("sf8", [128, D], FP8, st=pc)
            smk = sbuf("smk", [128, D], I32, st=pc)
            sqb = sbuf("sqb", [128, D], BF16, st=pc)
            sqb2 = sbuf("sqb2", [128, 512], BF16, st=pc)
            xqT = sbuf("xqT", [128, 32 * R], BF16, st=pc)

            for tt in range(8):
                def _(sy, tt=tt):
                    sy.wait_ge(st.h, st.n)
                    sy.wait_ge(dg.h, dg.n)
                    sy.dma_start(out=sx[:], in_=x2_sc[tt * 128:(tt + 1) * 128, :]).then_inc(dl.h, 16)
                    dl.inc(16)
                block.sync(_)

                def _(v):
                    v.wait_ge(dl.h, dl.n)
                    v.tensor_reduce(T[:, 0:1], sx[:], AX.X, AL.add).then_inc(vb.h, vb.tick())
                    v.wait_ge(vb.h, vb.n)
                    v.tensor_scalar(T[:, 0:1], T[:, 0:1], 1.0 / D, None, AL.mult,
                                    AL.bypass).then_inc(sv.h, sv.tick())
                block.vector(_)

                def _(s):
                    s.wait_ge(sv.h, sv.n)
                    s.activation(ssq[:], sx[:], AF.Square, bias=0.0, scale=1.0,
                                 accum_out=T[:, 1:2]).then_inc(ss.h, ss.tick())
                block.scalar(_)

                def _(v):
                    v.wait_ge(ss.h, ss.n)
                    v.tensor_scalar(T[:, 1:2], T[:, 1:2], 1.0 / D, None, AL.mult, AL.bypass)
                    v.tensor_tensor(T[:, 2:3], T[:, 0:1], T[:, 0:1], AL.mult).then_inc(vb.h, vb.tick())
                    v.wait_ge(vb.h, vb.n)
                    v.tensor_tensor(T[:, 1:2], T[:, 1:2], T[:, 2:3], AL.subtract).then_inc(vb.h, vb.tick())
                    v.wait_ge(vb.h, vb.n)
                    v.tensor_scalar(T[:, 1:2], T[:, 1:2], EPS_LN, None, AL.add,
                                    AL.bypass).then_inc(sv.h, sv.tick())
                block.vector(_)

                def _(s):
                    s.wait_ge(sv.h, sv.n)
                    s.activation(T[:, 2:3], T[:, 1:2], AF.Sqrt, bias=0.0,
                                 scale=1.0).then_inc(ss.h, ss.tick())
                block.scalar(_)

                def _(v, tt=tt):
                    v.wait_ge(ss.h, ss.n)
                    v.reciprocal(T[:, 3:4], T[:, 2:3]).then_inc(vb.h, vb.tick())
                    v.wait_ge(vb.h, vb.n)
                    v.tensor_scalar(ssq[:], sx[:], T[:, 0:1], T[:, 3:4], AL.subtract, AL.mult)
                    v.tensor_reduce(T[:, 5:6], ssq[:], AX.X, AL.max,
                                    apply_absolute_value=True).then_inc(vb.h, vb.tick())
                    v.wait_ge(vb.h, vb.n)
                    v.tensor_scalar(T[:, 5:6], T[:, 5:6], INV448, 1e-6, AL.mult,
                                    AL.max).then_inc(vb.h, vb.tick())
                    v.wait_ge(vb.h, vb.n)
                    v.tensor_copy(sasr[:, tt:tt + 1], T[:, 5:6])
                    v.reciprocal(T[:, 6:7], T[:, 5:6]).then_inc(vb.h, vb.tick())
                    v.wait_ge(vb.h, vb.n)
                    v.tensor_scalar(sq1[:], ssq[:], T[:, 6:7], -448.0, AL.mult, AL.max)
                    v.tensor_scalar(sq1[:], sq1[:], 448.0, None, AL.min, AL.bypass)
                    v.tensor_scalar(sq2[:], sq1[:], -240.0, 240.0, AL.max, AL.min)
                    v.tensor_copy(sf8[:], sq2[:]).then_inc(vb.h, vb.tick())
                    v.wait_ge(vb.h, vb.n)
                    v.tensor_copy(sq2[:], sf8[:])
                    v.tensor_tensor(ssq[:], sq1[:], sq1[:], AL.mult).then_inc(vb.h, vb.tick())
                    v.wait_ge(vb.h, vb.n)
                    v.tensor_scalar(smk[:], ssq[:], 224.0 * 224.0, None, AL.is_gt, AL.bypass)
                    v.tensor_scalar(ssq[:], sq1[:], DEK, None, AL.mult, AL.bypass)
                    v.tensor_tensor(sx[:], ssq[:], sq1[:], AL.subtract)
                    v.tensor_tensor(ssq[:], ssq[:], sx[:], AL.subtract).then_inc(vb.h, vb.tick())
                    v.wait_ge(vb.h, vb.n)
                    v.select(sq1[:], smk[:], ssq[:], sq2[:]).then_inc(vb.h, vb.tick())
                    v.wait_ge(vb.h, vb.n)
                    v.tensor_copy(sqb[:], sq1[:]).then_inc(sv.h, sv.tick())
                block.vector(_)

                for b8 in range(8):
                    def _(te, tt=tt, b8=b8):
                        te.wait_ge(sv.h, sv.n)
                        for j in range(4):
                            c = b8 * 4 + j
                            ins = te.transpose(pTb[:, j * 128:(j + 1) * 128],
                                               sqb[:, c * 128:(c + 1) * 128], sidb[:])
                        ins.then_inc(st.h, st.tick())
                    block.tensor(_)

                    def _(v, tt=tt, b8=b8):
                        v.wait_ge(st.h, st.n)
                        for j in range(4):
                            c = b8 * 4 + j
                            ins = v.tensor_copy(
                                xqT[:, c * R + tt * 128: c * R + (tt + 1) * 128],
                                pTb[:, j * 128:(j + 1) * 128])
                        ins.then_inc(sv.h, sv.tick())
                    block.vector(_)

                def _(te, tt=tt):
                    te.wait_ge(sv.h, sv.n)
                    te.transpose(pT[0:1, 0:128], sasr[:, tt:tt + 1], sid[:]).then_inc(st.h, st.tick())
                block.tensor(_)

                def _(v, tt=tt):
                    v.wait_ge(st.h, st.n)
                    v.tensor_copy(sasrow[0:1, tt * 128:(tt + 1) * 128],
                                  pT[0:1, 0:128]).then_inc(sv.h, sv.tick())
                block.vector(_)

            def _(te):
                te.wait_ge(sv.h, sv.n)
                for half in range(2):
                    ins = te.matmul(pMM[:, half * 512:(half + 1) * 512], sones[:],
                                    sasrow[0:1, half * 512:(half + 1) * 512],
                                    start=True, stop=True)
                ins.then_inc(st.h, st.tick())
            block.tensor(_)

            def _(v):
                v.wait_ge(st.h, st.n)
                v.tensor_copy(bc_as[:, 0:1024], pMM[:, 0:1024]).then_inc(sv.h, sv.tick())
            block.vector(_)

            for wt in range(32):
                def _(sy, wt=wt):
                    sy.wait_ge(st.h, st.n)
                    sy.dma_start(out=sx[:], in_=fw[wt * 128:(wt + 1) * 128, :]).then_inc(dl.h, 16)
                    dl.inc(16)
                block.sync(_)

                def _(v, wt=wt):
                    v.wait_ge(dl.h, dl.n)
                    v.tensor_reduce(T[:, 0:1], sx[:], AX.X, AL.max,
                                    apply_absolute_value=True).then_inc(vb.h, vb.tick())
                    v.wait_ge(vb.h, vb.n)
                    v.tensor_scalar(T[:, 0:1], T[:, 0:1], INV448, 1e-6, AL.mult,
                                    AL.max).then_inc(vb.h, vb.tick())
                    v.wait_ge(vb.h, vb.n)
                    v.tensor_copy(sws16[:, wt:wt + 1], T[:, 0:1]).then_inc(vb.h, vb.tick())
                    v.wait_ge(vb.h, vb.n)
                    v.tensor_copy(sws[:, wt:wt + 1], sws16[:, wt:wt + 1]).then_inc(vb.h, vb.tick())
                    v.wait_ge(vb.h, vb.n)
                    v.reciprocal(swsr[:, wt:wt + 1], sws[:, wt:wt + 1]).then_inc(vb.h, vb.tick())
                    v.wait_ge(vb.h, vb.n)
                    v.tensor_scalar(sq1[:], sx[:], swsr[:, wt:wt + 1], -448.0, AL.mult, AL.max)
                    v.tensor_scalar(sq1[:], sq1[:], 448.0, None, AL.min, AL.bypass)
                    v.tensor_scalar(sq2[:], sq1[:], -240.0, 240.0, AL.max, AL.min)
                    v.tensor_copy(sf8[:], sq2[:]).then_inc(vb.h, vb.tick())
                    v.wait_ge(vb.h, vb.n)
                    v.tensor_copy(sq2[:], sf8[:])
                    v.tensor_tensor(ssq[:], sq1[:], sq1[:], AL.mult).then_inc(vb.h, vb.tick())
                    v.wait_ge(vb.h, vb.n)
                    v.tensor_scalar(smk[:], ssq[:], 224.0 * 224.0, None, AL.is_gt, AL.bypass)
                    v.tensor_scalar(ssq[:], sq1[:], DEK, None, AL.mult, AL.bypass)
                    v.tensor_tensor(sx[:], ssq[:], sq1[:], AL.subtract)
                    v.tensor_tensor(ssq[:], ssq[:], sx[:], AL.subtract).then_inc(vb.h, vb.tick())
                    v.wait_ge(vb.h, vb.n)
                    v.select(sq1[:], smk[:], ssq[:], sq2[:]).then_inc(vb.h, vb.tick())
                    v.wait_ge(vb.h, vb.n)
                    v.tensor_copy(sqb[:], sq1[:]).then_inc(sv.h, sv.tick())
                block.vector(_)

                for b8 in range(8):
                    def _(te, wt=wt, b8=b8):
                        te.wait_ge(sv.h, sv.n)
                        te.wait_ge(dg.h, dg.n)
                        for j in range(4):
                            c = b8 * 4 + j
                            ins = te.transpose(pTb[:, j * 128:(j + 1) * 128],
                                               sqb[:, c * 128:(c + 1) * 128], sidb[:])
                        ins.then_inc(st.h, st.tick())
                    block.tensor(_)

                    def _(v, wt=wt, b8=b8):
                        v.wait_ge(st.h, st.n)
                        v.tensor_copy(sqb2[:], pTb[:]).then_inc(sv.h, sv.tick())
                    block.vector(_)

                    def _(g, wt=wt, b8=b8):
                        g.wait_ge(sv.h, sv.n)
                        for j in range(4):
                            c = b8 * 4 + j
                            g.dma_start(out=w8t_sc[c * 128:(c + 1) * 128, wt * 128:(wt + 1) * 128],
                                        in_=sqb2[:, j * 128:(j + 1) * 128]).then_inc(dg.h, 16)
                            dg.inc(16)
                    block.gpsimd(_)

            with ExitStack() as pd:
                sw8 = sbuf("sw8", [128, 32 * 128], BF16, st=pd)
                t2f = sbuf("t2f", [128, 1024], st=pd)
                o16 = sbuf("o16", [128, 1024], BF16, st=pd)

                for do in range(32):
                    pp = (do % 2) * 1024
                    def _(sy, do=do):
                        sy.wait_ge(st.h, st.n)
                        src3 = w8t_sc[:, do * 128:(do + 1) * 128].rearrange(
                            "(c p) n -> p c n", p=128)
                        dst3 = sw8[:].rearrange("p (c n) -> p c n", c=32)
                        for grp in range(4):
                            sy.dma_start(out=dst3[:, grp * 8:(grp + 1) * 8, :],
                                         in_=src3[:, grp * 8:(grp + 1) * 8, :]).then_inc(dl.h, 16)
                            dl.inc(16)
                    block.sync(_)

                    def _(te, do=do, pp=pp):
                        te.wait_ge(dl.h, dl.n)
                        te.wait_ge(sv.h, sv.n)
                        for hf in range(2):
                            for di in range(32):
                                ins = te.matmul(pMM[:, pp + hf * 512:pp + hf * 512 + 512],
                                                sw8[:, di * 128:(di + 1) * 128],
                                                xqT[:, di * R + hf * 512:di * R + hf * 512 + 512],
                                                start=(di == 0), stop=(di == 31))
                        ins.then_inc(st.h, st.tick())
                    block.tensor(_)

                    def _(v, do=do, pp=pp):
                        v.wait_ge(st.h, st.n)
                        v.wait_ge(ss.h, ss.n)
                        v.scalar_tensor_tensor(t2f[:], pMM[:, pp:pp + 1024], sws[:, do:do + 1],
                                               bc_as[:, 0:1024], AL.mult,
                                               AL.mult).then_inc(sv.h, sv.tick())
                    block.vector(_)

                    def _(s, do=do):
                        s.wait_ge(sv.h, sv.n)
                        s.wait_ge(dg.h, dg.n)
                        s.activation(o16[:], t2f[:], AF.Identity, bias=sfb[:, do:do + 1],
                                     scale=1.0).then_inc(ss.h, ss.tick())
                    block.scalar(_)

                    def _(g, do=do):
                        g.wait_ge(ss.h, ss.n)
                        g.dma_start(out=ot_sc[do * 128:(do + 1) * 128, :], in_=o16[:]).then_inc(dg.h, 16)
                        dg.inc(16)
                    block.gpsimd(_)

                sot = sbuf("sot", [128, D], BF16, st=pd)
                sy_ = sbuf("sy", [128, D], st=pd)

                for tt in range(8):
                    def _(sy, tt=tt):
                        sy.wait_ge(dg.h, dg.n)
                        sy.wait_ge(sv.h, sv.n)
                        sy.dma_start(out=sx[:], in_=x2_sc[tt * 128:(tt + 1) * 128, :]).then_inc(dl.h, 16)
                        dl.inc(16)
                        sy.dma_start(out=sot[:], in_=ot_sc[:, tt * 128:(tt + 1) * 128],
                                     transpose=True).then_inc(dl.h, 16)
                        dl.inc(16)
                    block.sync(_)

                    def _(v, tt=tt):
                        v.wait_ge(dl.h, dl.n)
                        v.tensor_tensor(sy_[:], sx[:], sot[:], AL.add).then_inc(sv.h, sv.tick())
                    block.vector(_)

                    def _(g, tt=tt):
                        g.wait_ge(sv.h, sv.n)
                        g.dma_start(out=y_out[tt * 128:(tt + 1) * 128, :], in_=sy_[:]).then_inc(dg.h, 16)
                        dg.inc(16)
                        if tt == 7 and not DEBUG_DUMPS:
                            g.wait_ge(dg.h, dg.n)
                    block.gpsimd(_)

                if DEBUG_DUMPS:
                    dumps = [("d_qt", qt_sc, [D, R], F32), ("d_kin", kin, [D, R], F32),
                             ("d_vin", vin, [D, R], F32), ("d_kall", kall, [4, D, R], F32),
                             ("d_x2", x2_sc, [R, D], F32), ("d_ot", ot_sc, [D, R], BF16),
                             ("d_w8t", w8t_sc, [D, D], BF16)]
                    def _(g):
                        for nm, srcap, shp, dt in dumps:
                            dout = nc.dram_tensor(nm, shp, dt, kind="ExternalOutput").ap()
                            flat_in = srcap.rearrange("... -> (...)") if len(shp) > 2 else srcap
                            nchunk = 8
                            if len(shp) == 3:
                                for j in range(shp[0]):
                                    g.dma_start(out=dout[j], in_=srcap[j]).then_inc(dg.h, 16)
                                    dg.inc(16)
                            else:
                                rows = shp[0]
                                for j in range(nchunk):
                                    r0, r1 = rows * j // nchunk, rows * (j + 1) // nchunk
                                    g.dma_start(out=dout[r0:r1], in_=srcap[r0:r1]).then_inc(dg.h, 16)
                                    dg.inc(16)
                        g.wait_ge(dg.h, dg.n)
                    block.gpsimd(_)

    return nc


_NC_CACHE = [None]


def kernel(**inputs):
    x = np.asarray(inputs["L0_attn_hidden_states"], np.float32)
    wq = np.asarray(inputs["L0_attn_wq"], np.float32)
    wk = np.asarray(inputs["L0_attn_wk"], np.float32)
    wv = np.asarray(inputs["L0_attn_wv"], np.float32)
    wo = np.asarray(inputs["L0_attn_wo"], np.float32)
    bq = np.asarray(inputs["L0_attn_bq"], np.float32)
    bk = np.asarray(inputs["L0_attn_bk"], np.float32)
    bv = np.asarray(inputs["L0_attn_bv"], np.float32)
    bo = np.asarray(inputs["L0_attn_bo"], np.float32)
    fw_ = np.asarray(inputs["L0_ffn_weight"], np.float32)
    fb_ = np.asarray(inputs["L0_ffn_bias"], np.float32)

    wqT = np.ascontiguousarray(wq.T)
    wkT = np.ascontiguousarray(wk.T)
    wvT = np.ascontiguousarray(wv.T)
    woT = np.ascontiguousarray(wo.T)
    boR = np.ascontiguousarray(np.broadcast_to(bo, (128, D)))

    if _NC_CACHE[0] is None:
        _NC_CACHE[0] = build_program()
    nc = _NC_CACHE[0]

    in_maps = []
    for c in range(8):
        b = c // 4
        r0 = (c % 4) * R
        in_maps.append({
            "x_tok": np.ascontiguousarray(x[b, r0:r0 + R]),
            "wqT": wqT, "wkT": wkT, "wvT": wvT, "woT": woT,
            "bq": bq, "bk": bk, "bv": bv, "bo_rep": boR,
            "fw": fw_, "fb": fb_,
        })
    res = run_bass_kernel_spmd(nc, in_maps, core_ids=list(range(8)))
    out = np.empty((2, SEQ, D), np.float32)
    for c in range(8):
        b = c // 4
        r0 = (c % 4) * R
        out[b, r0:r0 + R] = res.results[c]["y"]
    return out


# version 04
# speedup vs baseline: 1.3380x; 1.3380x over previous
import math
from contextlib import ExitStack

import numpy as np
import ml_dtypes
import concourse.bass as bass
import concourse.mybir as mybir
from concourse.bass_utils import run_bass_kernel_spmd

F32 = mybir.dt.float32
BF16 = mybir.dt.bfloat16
FP16 = mybir.dt.float16
FP8 = mybir.dt.float8e4
I32 = mybir.dt.int32
AL = mybir.AluOpType
AF = mybir.ActivationFunctionType
AX = mybir.AxisListType

D = 4096
R = 1024
NH = 32
HD = 128
SEQ = 4096
TOPK = 819
QG = 512
NQG = R // QG
NBIS = 17
NDESC = 2
ASCALE = float(np.float32(1.0 / math.sqrt(HD)))
EPS_LN = 1.0e-5
W0 = 64.0
LOFF = 64.5
INV448 = float(np.float32(1.0 / 448.0))
DEK = float(np.float32(2.0 ** 20 + 1.0))


class Cnt:
    def __init__(self, h):
        self.h = h
        self.n = 0

    def inc(self, k=1):
        self.n += k
        return self.n

    def tick(self):
        self.n += 1
        return 1


DEBUG_DUMPS = False


def build_program():
    nc = bass.Bass("TRN2", target_bir_lowering=False, debug=False,
                   detect_race_conditions=False)

    x_tok = nc.dram_tensor("x_tok", [R, D], F32, kind="ExternalInput").ap()
    wqT = nc.dram_tensor("wqT", [D, D], F32, kind="ExternalInput").ap()
    wkT = nc.dram_tensor("wkT", [D, D], F32, kind="ExternalInput").ap()
    wvT = nc.dram_tensor("wvT", [D, D], F32, kind="ExternalInput").ap()
    woT = nc.dram_tensor("woT", [D, D], F32, kind="ExternalInput").ap()
    bq_in = nc.dram_tensor("bq", [D], F32, kind="ExternalInput").ap()
    bk_in = nc.dram_tensor("bk", [D], F32, kind="ExternalInput").ap()
    bv_in = nc.dram_tensor("bv", [D], F32, kind="ExternalInput").ap()
    bo_rep = nc.dram_tensor("bo_rep", [128, D], F32, kind="ExternalInput").ap()
    fw = nc.dram_tensor("fw", [D, D], F32, kind="ExternalInput").ap()
    fb_in = nc.dram_tensor("fb", [D], F32, kind="ExternalInput").ap()
    y_out = nc.dram_tensor("y", [R, D], F32, kind="ExternalOutput").ap()

    qt_sc = nc.dram_tensor("qt_sc", [D, R], F32).ap()
    kin = nc.dram_tensor("kin", [D, R], F32).ap()
    vin = nc.dram_tensor("vin", [D, R], F32).ap()
    kall = nc.dram_tensor("kall", [4, D, R], F32).ap()
    vall = nc.dram_tensor("vall", [4, D, R], F32).ap()
    x2_sc = nc.dram_tensor("x2_sc", [R, D], F32).ap()
    w8t_sc = nc.dram_tensor("w8t_sc", [D, D], BF16).ap()
    ot_sc = nc.dram_tensor("ot_sc", [D, R], BF16).ap()

    ident_f = nc.inline_tensor(np.eye(128, dtype=np.float32), "ident_f").ap()
    ident_b = nc.inline_tensor(np.eye(128, dtype=ml_dtypes.bfloat16), "ident_b").ap()
    ones_f = nc.inline_tensor(np.ones((1, 128), dtype=np.float32), "ones_f").ap()

    es = ExitStack()
    with es:
        def sbuf(name, shape, dt=F32, st=None):
            return (st or es).enter_context(nc.sbuf_tensor(name, shape, dt))

        def sem(name):
            return Cnt(es.enter_context(nc.semaphore(name)))

        block = es.enter_context(nc.Block())

        dl = sem("dl")
        dg = sem("dg")
        sv = sem("sv")
        ss = sem("ss")
        st = sem("st")
        cc = sem("cc")
        vb = sem("vb")

        pMM = es.enter_context(nc.psum_tensor("pMM", [128, 2048], F32))
        pT = es.enter_context(nc.psum_tensor("pT", [128, 1024], F32))
        pCT = es.enter_context(nc.psum_tensor("pCT", [128, 128], F32))
        pTb = es.enter_context(nc.psum_tensor("pTb", [128, 512], BF16))

        sid = sbuf("sid", [128, 128])
        sidb = sbuf("sidb", [128, 128], BF16)
        sones = sbuf("sones", [1, 128])
        sbq = sbuf("sbq", [128, 32])
        sbk = sbuf("sbk", [128, 32])
        sbv = sbuf("sbv", [128, 32])
        sfb = sbuf("sfb", [128, 32])
        sfb16 = sbuf("sfb16", [128, 32], BF16)
        sws = sbuf("sws", [128, 32])
        swsr = sbuf("swsr", [128, 32])
        sws16 = sbuf("sws16", [128, 32], FP16)
        sasr = sbuf("sasr", [128, 8])
        sasrow = sbuf("sasrow", [1, 1024])
        bc_as = sbuf("bc_as", [128, 1024])
        stny = sbuf("stny", [128, 48])
        cst = sbuf("cst", [128, 1])

        T = stny

        def _(sy):
            sy.dma_start(out=sid[:], in_=ident_f).then_inc(dl.h, 16); dl.inc(16)
            sy.dma_start(out=sidb[:], in_=ident_b).then_inc(dl.h, 16); dl.inc(16)
            sy.dma_start(out=sones[:], in_=ones_f).then_inc(dl.h, 16); dl.inc(16)
            for dst, src in ((sbq, bq_in), (sbk, bk_in), (sbv, bv_in), (sfb, fb_in)):
                sy.dma_start(out=dst[:], in_=src.rearrange("(c p) -> p c", p=128),
                             allow_slow_non_contiguous=True).then_inc(dl.h, 16)
                dl.inc(16)
        block.sync(_)

        def _(v):
            v.wait_ge(dl.h, dl.n)
            v.memset(cst[:], ASCALE)
            v.tensor_scalar(sbq[:], sbq[:], ASCALE, None, AL.mult, AL.bypass)
            v.tensor_copy(sfb16[:], sfb[:]).then_inc(vb.h, vb.tick())
            v.wait_ge(vb.h, vb.n)
            v.tensor_copy(sfb[:], sfb16[:]).then_inc(sv.h, sv.tick())
        block.vector(_)

        with ExitStack() as pa:
            sx = sbuf("sx", [128, D], st=pa)
            ssq = sbuf("ssq", [128, D], st=pa)
            sxT = sbuf("sxT", [128, 32 * R], st=pa)
            sev = sbuf("sev", [128, 1024], st=pa)
            wcol = sbuf("wcol", [128, 32 * 128], st=pa)

            _dxh_ap = (nc.dram_tensor("d_xh", [R, D], F32, kind="ExternalOutput").ap()
                       if DEBUG_DUMPS else None)

            for tt in range(8):
                def _(sy, tt=tt):
                    if tt > 0:
                        sy.wait_ge(sv.h, sv.n)
                    sy.dma_start(out=sx[:], in_=x_tok[tt * 128:(tt + 1) * 128, :]).then_inc(dl.h, 16)
                    dl.inc(16)
                block.sync(_)

                def _(v):
                    v.wait_ge(dl.h, dl.n)
                    v.tensor_reduce(T[:, 0:1], sx[:], AX.X, AL.add).then_inc(vb.h, vb.tick())
                    v.wait_ge(vb.h, vb.n)
                    v.tensor_scalar(T[:, 0:1], T[:, 0:1], 1.0 / D, None, AL.mult,
                                    AL.bypass).then_inc(sv.h, sv.tick())
                block.vector(_)

                def _(s):
                    s.wait_ge(sv.h, sv.n)
                    s.wait_ge(st.h, st.n)
                    s.activation(ssq[:], sx[:], AF.Square, bias=0.0, scale=1.0,
                                 accum_out=T[:, 1:2]).then_inc(ss.h, ss.tick())
                block.scalar(_)

                def _(v):
                    v.wait_ge(ss.h, ss.n)
                    v.tensor_scalar(T[:, 1:2], T[:, 1:2], 1.0 / D, None, AL.mult, AL.bypass)
                    v.tensor_tensor(T[:, 2:3], T[:, 0:1], T[:, 0:1], AL.mult).then_inc(vb.h, vb.tick())
                    v.wait_ge(vb.h, vb.n)
                    v.tensor_tensor(T[:, 1:2], T[:, 1:2], T[:, 2:3], AL.subtract).then_inc(vb.h, vb.tick())
                    v.wait_ge(vb.h, vb.n)
                    v.tensor_scalar(T[:, 1:2], T[:, 1:2], EPS_LN, None, AL.add,
                                    AL.bypass).then_inc(sv.h, sv.tick())
                block.vector(_)

                def _(s):
                    s.wait_ge(sv.h, sv.n)
                    s.activation(T[:, 2:3], T[:, 1:2], AF.Sqrt, bias=0.0,
                                 scale=1.0).then_inc(ss.h, ss.tick())
                block.scalar(_)

                def _(v):
                    v.wait_ge(ss.h, ss.n)
                    v.reciprocal(T[:, 3:4], T[:, 2:3]).then_inc(vb.h, vb.tick())
                    v.wait_ge(vb.h, vb.n)
                    v.tensor_scalar(ssq[:], sx[:], T[:, 0:1], T[:, 3:4], AL.subtract,
                                    AL.mult).then_inc(sv.h, sv.tick())
                block.vector(_)

                if DEBUG_DUMPS:
                    def _(g, tt=tt):
                        g.wait_ge(sv.h, sv.n)
                        g.dma_start(out=_dxh_ap[tt * 128:(tt + 1) * 128, :],
                                    in_=ssq[:]).then_inc(dg.h, 16)
                        dg.inc(16)
                        g.wait_ge(dg.h, dg.n)
                    block.gpsimd(_)

                for q4 in range(8):
                    ph = (q4 % 2) * 512
                    def _(te, tt=tt, q4=q4, ph=ph):
                        te.wait_ge(sv.h, sv.n)
                        for j in range(4):
                            c = q4 * 4 + j
                            ins = te.transpose(pT[:, ph + j * 128: ph + (j + 1) * 128],
                                               ssq[:, c * 128:(c + 1) * 128], sid[:])
                        ins.then_inc(st.h, st.tick())
                    block.tensor(_)

                    def _(v, tt=tt, q4=q4, ph=ph):
                        v.wait_ge(st.h, st.n)
                        for j in range(4):
                            c = q4 * 4 + j
                            ins = v.tensor_copy(
                                sxT[:, c * R + tt * 128: c * R + (tt + 1) * 128],
                                pT[:, ph + j * 128: ph + (j + 1) * 128])
                        ins.then_inc(sv.h, sv.tick())
                    block.vector(_)

            if DEBUG_DUMPS:
                d_xT = nc.dram_tensor("d_xT", [128, 32 * R], F32, kind="ExternalOutput").ap()

                def _(g):
                    g.wait_ge(sv.h, sv.n)
                    g.dma_start(out=d_xT[:], in_=sxT[:]).then_inc(dg.h, 16)
                    dg.inc(16)
                    g.wait_ge(dg.h, dg.n)
                block.gpsimd(_)

            projs = [(wqT, sbq, qt_sc, ASCALE), (wkT, sbk, kin, 1.0), (wvT, sbv, vin, 1.0)]
            for pi, (wT, sbias, dst, scale) in enumerate(projs):
                for do in range(32):
                    ph = (do % 2) * 1024
                    def _(sy, wT=wT, do=do):
                        sy.wait_ge(st.h, st.n)
                        src3 = wT[:, do * 128:(do + 1) * 128].rearrange("(c p) n -> p c n", p=128)
                        dst3 = wcol[:].rearrange("p (c n) -> p c n", c=32)
                        for grp in range(4):
                            sy.dma_start(out=dst3[:, grp * 8:(grp + 1) * 8, :],
                                         in_=src3[:, grp * 8:(grp + 1) * 8, :]).then_inc(dl.h, 16)
                            dl.inc(16)
                    block.sync(_)

                    def _(te, do=do, ph=ph):
                        te.wait_ge(dl.h, dl.n)
                        te.wait_ge(sv.h, sv.n)
                        te.wait_ge(ss.h, ss.n)
                        for half in range(2):
                            for di in range(32):
                                ins = te.matmul(
                                    pMM[:, ph + half * 512: ph + half * 512 + 512],
                                    wcol[:, di * 128:(di + 1) * 128],
                                    sxT[:, di * R + half * 512: di * R + half * 512 + 512],
                                    start=(di == 0), stop=(di == 31))
                        ins.then_inc(st.h, st.tick())
                    block.tensor(_)

                    def _(s, do=do, sbias=sbias, scale=scale, ph=ph):
                        s.wait_ge(st.h, st.n)
                        s.wait_ge(dg.h, dg.n)
                        sc = 1.0 if scale == 1.0 else cst[:, 0:1]
                        s.activation(sev[:, 0:512], pMM[:, ph:ph + 512], AF.Identity,
                                     bias=sbias[:, do:do + 1], scale=sc)
                        s.activation(sev[:, 512:1024], pMM[:, ph + 512:ph + 1024], AF.Identity,
                                     bias=sbias[:, do:do + 1], scale=sc).then_inc(ss.h, ss.tick())
                    block.scalar(_)

                    def _(g, do=do, dst=dst):
                        g.wait_ge(ss.h, ss.n)
                        g.dma_start(out=dst[do * 128:(do + 1) * 128, :], in_=sev[:]).then_inc(dg.h, 16)
                        dg.inc(16)
                    block.gpsimd(_)

        def _(g):
            g.wait_ge(dg.h, dg.n)
            g.collective_compute("AllGather", AL.bypass,
                                 replica_groups=[[0, 1, 2, 3], [4, 5, 6, 7]],
                                 ins=[kin.opt()], outs=[kall.opt()]).then_inc(cc.h, cc.tick())
            g.collective_compute("AllGather", AL.bypass,
                                 replica_groups=[[0, 1, 2, 3], [4, 5, 6, 7]],
                                 ins=[vin.opt()], outs=[vall.opt()]).then_inc(cc.h, cc.tick())
        block.gpsimd(_)

        def _(sy):
            sy.wait_ge(cc.h, cc.n)
            sy.sem_inc(dl.h, 16)
            dl.inc(16)
        block.sync(_)

        with ExitStack() as pb:
            ctxT = sbuf("ctxT", [128, 32 * QG], st=pb)

            for qg in range(NQG):
                with ExitStack() as ph_heads:
                    kTh = sbuf(f"kTh_{qg}", [128, SEQ], st=ph_heads)
                    vTh = sbuf(f"vTh_{qg}", [128, SEQ], st=ph_heads)
                    vtok = sbuf(f"vtok_{qg}", [128, 32 * 128], st=ph_heads)
                    qTh = sbuf(f"qTh_{qg}", [128, QG], st=ph_heads)
                    sS = sbuf(f"sS_{qg}", [128, SEQ], st=ph_heads)
                    sE = sbuf(f"sE_{qg}", [128, SEQ], st=ph_heads)
                    jkb = sbuf(f"jkb_{qg}", [128, SEQ], BF16, st=ph_heads)
                    spt = sbuf(f"spt_{qg}", [128, 1024], st=ph_heads)

                    for h in range(NH):
                        def _(sy, h=h, qg=qg):
                            sy.wait_ge(st.h, st.n)
                            sy.wait_ge(sv.h, sv.n)
                            for g4 in range(4):
                                sy.dma_start(out=kTh[:, g4 * R:(g4 + 1) * R],
                                             in_=kall[g4, h * 128:(h + 1) * 128, :]).then_inc(dl.h, 16)
                                dl.inc(16)
                                sy.dma_start(out=vTh[:, g4 * R:(g4 + 1) * R],
                                             in_=vall[g4, h * 128:(h + 1) * 128, :]).then_inc(dl.h, 16)
                                dl.inc(16)
                            sy.dma_start(out=qTh[:],
                                         in_=qt_sc[h * 128:(h + 1) * 128,
                                                   qg * QG:(qg + 1) * QG]).then_inc(dl.h, 16)
                            dl.inc(16)
                        block.sync(_)

                        for b8 in range(8):
                            ph = (b8 % 2) * 512
                            def _(te, b8=b8, ph=ph):
                                te.wait_ge(dl.h, dl.n)
                                te.wait_ge(sv.h, sv.n)
                                for j in range(4):
                                    c = b8 * 4 + j
                                    ins = te.transpose(pT[:, ph + j * 128:ph + (j + 1) * 128],
                                                       vTh[:, c * 128:(c + 1) * 128], sid[:])
                                ins.then_inc(st.h, st.tick())
                            block.tensor(_)

                            def _(v, b8=b8, ph=ph):
                                v.wait_ge(st.h, st.n)
                                ins = v.tensor_copy(vtok[:, b8 * 512:(b8 + 1) * 512],
                                                    pT[:, ph:ph + 512])
                                ins.then_inc(sv.h, sv.tick())
                            block.vector(_)

                        for qt in range(QG // 128):
                            for half in range(2):
                                def _(te, qt=qt, half=half):
                                    te.wait_ge(ss.h, ss.n)
                                    te.wait_ge(sv.h, sv.n)
                                    for j in range(4):
                                        ins = te.matmul(
                                            pMM[:, j * 512:(j + 1) * 512],
                                            qTh[:, qt * 128:(qt + 1) * 128],
                                            kTh[:, half * 2048 + j * 512: half * 2048 + (j + 1) * 512],
                                            start=True, stop=True)
                                    ins.then_inc(st.h, st.tick())
                                block.tensor(_)

                                def _(s, qt=qt, half=half):
                                    s.wait_ge(st.h, st.n)
                                    s.copy(sS[:, half * 2048:(half + 1) * 2048],
                                           pMM[:, 0:2048]).then_inc(ss.h, ss.tick())
                                block.scalar(_)

                            def _(v, qt=qt):
                                v.wait_ge(ss.h, ss.n)
                                v.tensor_reduce(T[:, 4:5], sS[:], AX.X, AL.max,
                                                negate=True).then_inc(sv.h, sv.tick())
                            block.vector(_)

                            def _(s, qt=qt):
                                s.wait_ge(sv.h, sv.n)
                                s.activation(sE[:], sS[:], AF.Exp, bias=T[:, 4:5],
                                             scale=1.0).then_inc(ss.h, ss.tick())
                            block.scalar(_)

                            def _(v):
                                v.wait_ge(ss.h, ss.n)
                                v.tensor_scalar(T[:, 5:6], T[:, 4:5], -1.0, -LOFF, AL.mult,
                                                AL.add).then_inc(vb.h, vb.tick())
                                v.wait_ge(vb.h, vb.n)
                                v.sem_inc(sv.h, sv.tick())
                            block.vector(_)

                            def _(v):
                                v.tensor_scalar(T[:, 6:7], T[:, 5:6], W0 * 0.5, None,
                                                AL.add, AL.bypass).then_inc(vb.h, vb.tick())
                                v.wait_ge(vb.h, vb.n)
                            block.vector(_)
                            for it in range(NBIS):
                                wkw = W0 * (0.5 ** (it + 1))
                                def _(v, wkw=wkw):
                                    v.tensor_scalar(jkb[:], sS[:], T[:, 6:7], None, AL.is_ge,
                                                    AL.add, accum_out=T[:, 7:8]).then_inc(vb.h, vb.tick())
                                    v.wait_ge(vb.h, vb.n)
                                    v.tensor_scalar(T[:, 8:9], T[:, 7:8], TOPK - 0.5, -0.5,
                                                    AL.is_ge, AL.add).then_inc(vb.h, vb.tick())
                                    v.wait_ge(vb.h, vb.n)
                                    v.scalar_tensor_tensor(T[:, 6:7], T[:, 8:9], wkw, T[:, 6:7],
                                                           AL.mult, AL.add).then_inc(vb.h, vb.tick())
                                    v.wait_ge(vb.h, vb.n)
                                block.vector(_)

                            def _(v):
                                v.tensor_scalar(T[:, 6:7], T[:, 6:7], W0 * (0.5 ** NBIS), None,
                                                AL.add, AL.bypass).then_inc(sv.h, sv.tick())
                            block.vector(_)

                            def _(s):
                                s.wait_ge(sv.h, sv.n)
                                s.activation(T[:, 9:10], T[:, 6:7], AF.Exp, bias=T[:, 4:5],
                                             scale=1.0).then_inc(ss.h, ss.tick())
                            block.scalar(_)

                            def _(v):
                                v.wait_ge(ss.h, ss.n)
                                v.memset(T[:, 10:11], 0.0)
                                v.memset(T[:, 11:12], 0.0)
                            block.vector(_)

                            for d in range(NDESC):
                                def _(v):
                                    v.scalar_tensor_tensor(sS[:], sE[:], T[:, 9:10], sE[:],
                                                           AL.is_lt, AL.mult).then_inc(vb.h, vb.tick())
                                    v.wait_ge(vb.h, vb.n)
                                    v.tensor_reduce(T[:, 12:13], sS[:], AX.X,
                                                    AL.max).then_inc(vb.h, vb.tick())
                                    v.wait_ge(vb.h, vb.n)
                                    v.tensor_scalar(jkb[:], sE[:], T[:, 12:13], None, AL.is_ge,
                                                    AL.add, accum_out=T[:, 7:8]).then_inc(vb.h, vb.tick())
                                    v.wait_ge(vb.h, vb.n)
                                    v.tensor_scalar(T[:, 8:9], T[:, 7:8], TOPK - 0.5, None,
                                                    AL.is_ge, AL.bypass)
                                    v.tensor_scalar(T[:, 13:14], T[:, 11:12], 0.5, None,
                                                    AL.is_lt, AL.bypass).then_inc(vb.h, vb.tick())
                                    v.wait_ge(vb.h, vb.n)
                                    v.tensor_tensor(T[:, 13:14], T[:, 13:14], T[:, 8:9],
                                                    AL.mult).then_inc(vb.h, vb.tick())
                                    v.wait_ge(vb.h, vb.n)
                                    v.scalar_tensor_tensor(T[:, 10:11], T[:, 12:13], T[:, 13:14],
                                                           T[:, 10:11], AL.mult, AL.add)
                                    v.tensor_tensor(T[:, 11:12], T[:, 11:12], T[:, 8:9], AL.max)
                                    v.tensor_copy(T[:, 9:10], T[:, 12:13]).then_inc(vb.h, vb.tick())
                                    v.wait_ge(vb.h, vb.n)
                                block.vector(_)

                            def _(v):
                                v.tensor_scalar(T[:, 12:13], T[:, 6:7], -2.0 * W0 * (0.5 ** NBIS),
                                                None, AL.add, AL.bypass).then_inc(sv.h, sv.tick())
                            block.vector(_)

                            def _(s):
                                s.wait_ge(sv.h, sv.n)
                                s.activation(T[:, 13:14], T[:, 12:13], AF.Exp, bias=T[:, 4:5],
                                             scale=1.0).then_inc(ss.h, ss.tick())
                            block.scalar(_)

                            def _(v):
                                v.wait_ge(ss.h, ss.n)
                                v.tensor_scalar(T[:, 12:13], T[:, 11:12], 0.5, None,
                                                AL.is_lt, AL.bypass).then_inc(vb.h, vb.tick())
                                v.wait_ge(vb.h, vb.n)
                                v.scalar_tensor_tensor(T[:, 10:11], T[:, 13:14], T[:, 12:13],
                                                       T[:, 10:11], AL.mult,
                                                       AL.add).then_inc(vb.h, vb.tick())
                                v.wait_ge(vb.h, vb.n)
                            block.vector(_)

                            def _(v):
                                v.scalar_tensor_tensor(sS[:], sE[:], T[:, 10:11], sE[:],
                                                       AL.is_ge, AL.mult,
                                                       accum_out=T[:, 14:15]).then_inc(vb.h, vb.tick())
                                v.wait_ge(vb.h, vb.n)
                                v.reciprocal(T[:, 15:16], T[:, 14:15]).then_inc(vb.h, vb.tick())
                                v.wait_ge(vb.h, vb.n)
                                ins = v.tensor_scalar(sE[:], sS[:], T[:, 15:16], None,
                                                      AL.mult, AL.bypass)
                                ins.then_inc(sv.h, sv.tick())
                            block.vector(_)

                            for grp in range(8):
                                ph2 = (grp % 2) * 512
                                def _(te, grp=grp, ph2=ph2):
                                    te.wait_ge(sv.h, sv.n)
                                    for j in range(4):
                                        ck = grp * 4 + j
                                        ins = te.transpose(pT[:, ph2 + j * 128: ph2 + (j + 1) * 128],
                                                           sE[:, ck * 128:(ck + 1) * 128], sid[:])
                                    ins.then_inc(st.h, st.tick())
                                block.tensor(_)

                                def _(v, grp=grp, ph2=ph2):
                                    v.wait_ge(st.h, st.n)
                                    v.tensor_copy(spt[:, ph2:ph2 + 512],
                                                  pT[:, ph2:ph2 + 512]).then_inc(sv.h, sv.tick())
                                block.vector(_)

                                def _(te, grp=grp, ph2=ph2):
                                    te.wait_ge(sv.h, sv.n)
                                    for j in range(4):
                                        ck = grp * 4 + j
                                        ins = te.matmul(pCT[:], vtok[:, ck * 128:(ck + 1) * 128],
                                                        spt[:, ph2 + j * 128: ph2 + (j + 1) * 128],
                                                        start=(ck == 0), stop=(ck == 31))
                                    ins.then_inc(st.h, st.tick())
                                block.tensor(_)

                            def _(v, qt=qt, h=h):
                                v.wait_ge(st.h, st.n)
                                v.tensor_copy(ctxT[:, h * QG + qt * 128: h * QG + (qt + 1) * 128],
                                              pCT[:]).then_inc(sv.h, sv.tick())
                            block.vector(_)

                with ExitStack() as po:
                    swo = sbuf(f"swo_{qg}", [128, 32 * 512], st=po)
                    sbo = sbuf(f"sbo_{qg}", [128, 512], st=po)
                    sxr = sbuf(f"sxr_{qg}", [128, 512], st=po)
                    sev2 = sbuf(f"sev2_{qg}", [128, 512], st=po)

                    for dk in range(8):
                        def _(sy, dk=dk):
                            sy.wait_ge(st.h, st.n)
                            src3 = woT[:, dk * 512:(dk + 1) * 512].rearrange(
                                "(c p) n -> p c n", p=128)
                            dst3 = swo[:].rearrange("p (c n) -> p c n", c=32)
                            for grp in range(8):
                                sy.dma_start(out=dst3[:, grp * 4:(grp + 1) * 4, :],
                                             in_=src3[:, grp * 4:(grp + 1) * 4, :]).then_inc(dl.h, 16)
                                dl.inc(16)
                            sy.dma_start(out=sbo[:], in_=bo_rep[:, dk * 512:(dk + 1) * 512]).then_inc(dl.h, 16)
                            dl.inc(16)
                        block.sync(_)

                        for qt in range(QG // 128):
                            pp = (qt % 2) * 512
                            def _(sy, dk=dk, qt=qt, qg=qg):
                                sy.wait_ge(sv.h, sv.n)
                                row0 = qg * QG + qt * 128
                                sy.dma_start(out=sxr[:],
                                             in_=x_tok[row0:row0 + 128,
                                                       dk * 512:(dk + 1) * 512]).then_inc(dl.h, 16)
                                dl.inc(16)
                            block.sync(_)

                            def _(te, dk=dk, qt=qt, pp=pp):
                                te.wait_ge(dl.h, dl.n)
                                te.wait_ge(sv.h, sv.n)
                                for dc in range(32):
                                    ins = te.matmul(pMM[:, pp:pp + 512],
                                                    ctxT[:, dc * QG + qt * 128: dc * QG + (qt + 1) * 128],
                                                    swo[:, dc * 512:(dc + 1) * 512],
                                                    start=(dc == 0), stop=(dc == 31))
                                ins.then_inc(st.h, st.tick())
                            block.tensor(_)

                            def _(v, dk=dk, qt=qt, pp=pp):
                                v.wait_ge(st.h, st.n)
                                v.wait_ge(dg.h, dg.n)
                                v.tensor_tensor(sev2[:], sbo[:], sxr[:], AL.add)
                                v.tensor_tensor(sev2[:], sev2[:], pMM[:, pp:pp + 512],
                                                AL.add).then_inc(sv.h, sv.tick())
                            block.vector(_)

                            def _(g, dk=dk, qt=qt, qg=qg):
                                g.wait_ge(sv.h, sv.n)
                                row0 = qg * QG + qt * 128
                                g.dma_start(out=x2_sc[row0:row0 + 128, dk * 512:(dk + 1) * 512],
                                            in_=sev2[:]).then_inc(dg.h, 16)
                                dg.inc(16)
                            block.gpsimd(_)

        with ExitStack() as pc:
            sx = sbuf("sxc", [128, D], st=pc)
            ssq = sbuf("ssqc", [128, D], st=pc)
            sq1 = sbuf("sq1", [128, D], st=pc)
            sq2 = sbuf("sq2", [128, D], st=pc)
            sf8 = sbuf("sf8", [128, D], FP8, st=pc)
            smk = sbuf("smk", [128, D], I32, st=pc)
            sqb = sbuf("sqb", [128, D], BF16, st=pc)
            sqb2 = sbuf("sqb2", [128, 512], BF16, st=pc)
            xqT = sbuf("xqT", [128, 32 * R], BF16, st=pc)

            for tt in range(8):
                def _(sy, tt=tt):
                    sy.wait_ge(st.h, st.n)
                    sy.wait_ge(dg.h, dg.n)
                    sy.dma_start(out=sx[:], in_=x2_sc[tt * 128:(tt + 1) * 128, :]).then_inc(dl.h, 16)
                    dl.inc(16)
                block.sync(_)

                def _(v):
                    v.wait_ge(dl.h, dl.n)
                    v.tensor_reduce(T[:, 0:1], sx[:], AX.X, AL.add).then_inc(vb.h, vb.tick())
                    v.wait_ge(vb.h, vb.n)
                    v.tensor_scalar(T[:, 0:1], T[:, 0:1], 1.0 / D, None, AL.mult,
                                    AL.bypass).then_inc(sv.h, sv.tick())
                block.vector(_)

                def _(s):
                    s.wait_ge(sv.h, sv.n)
                    s.activation(ssq[:], sx[:], AF.Square, bias=0.0, scale=1.0,
                                 accum_out=T[:, 1:2]).then_inc(ss.h, ss.tick())
                block.scalar(_)

                def _(v):
                    v.wait_ge(ss.h, ss.n)
                    v.tensor_scalar(T[:, 1:2], T[:, 1:2], 1.0 / D, None, AL.mult, AL.bypass)
                    v.tensor_tensor(T[:, 2:3], T[:, 0:1], T[:, 0:1], AL.mult).then_inc(vb.h, vb.tick())
                    v.wait_ge(vb.h, vb.n)
                    v.tensor_tensor(T[:, 1:2], T[:, 1:2], T[:, 2:3], AL.subtract).then_inc(vb.h, vb.tick())
                    v.wait_ge(vb.h, vb.n)
                    v.tensor_scalar(T[:, 1:2], T[:, 1:2], EPS_LN, None, AL.add,
                                    AL.bypass).then_inc(sv.h, sv.tick())
                block.vector(_)

                def _(s):
                    s.wait_ge(sv.h, sv.n)
                    s.activation(T[:, 2:3], T[:, 1:2], AF.Sqrt, bias=0.0,
                                 scale=1.0).then_inc(ss.h, ss.tick())
                block.scalar(_)

                def _(v, tt=tt):
                    v.wait_ge(ss.h, ss.n)
                    v.reciprocal(T[:, 3:4], T[:, 2:3]).then_inc(vb.h, vb.tick())
                    v.wait_ge(vb.h, vb.n)
                    v.tensor_scalar(ssq[:], sx[:], T[:, 0:1], T[:, 3:4], AL.subtract, AL.mult)
                    v.tensor_reduce(T[:, 5:6], ssq[:], AX.X, AL.max,
                                    apply_absolute_value=True).then_inc(vb.h, vb.tick())
                    v.wait_ge(vb.h, vb.n)
                    v.tensor_scalar(T[:, 5:6], T[:, 5:6], INV448, 1e-6, AL.mult,
                                    AL.max).then_inc(vb.h, vb.tick())
                    v.wait_ge(vb.h, vb.n)
                    v.tensor_copy(sasr[:, tt:tt + 1], T[:, 5:6])
                    v.reciprocal(T[:, 6:7], T[:, 5:6]).then_inc(vb.h, vb.tick())
                    v.wait_ge(vb.h, vb.n)
                    v.tensor_scalar(sq1[:], ssq[:], T[:, 6:7], -448.0, AL.mult, AL.max)
                    v.tensor_scalar(sq1[:], sq1[:], 448.0, None, AL.min, AL.bypass)
                    v.tensor_scalar(sq2[:], sq1[:], -240.0, 240.0, AL.max, AL.min)
                    v.tensor_copy(sf8[:], sq2[:]).then_inc(vb.h, vb.tick())
                    v.wait_ge(vb.h, vb.n)
                    v.tensor_copy(sq2[:], sf8[:])
                    v.tensor_tensor(ssq[:], sq1[:], sq1[:], AL.mult).then_inc(vb.h, vb.tick())
                    v.wait_ge(vb.h, vb.n)
                    v.tensor_scalar(smk[:], ssq[:], 224.0 * 224.0, None, AL.is_gt, AL.bypass)
                    v.tensor_scalar(ssq[:], sq1[:], DEK, None, AL.mult, AL.bypass)
                    v.tensor_tensor(sx[:], ssq[:], sq1[:], AL.subtract)
                    v.tensor_tensor(ssq[:], ssq[:], sx[:], AL.subtract).then_inc(vb.h, vb.tick())
                    v.wait_ge(vb.h, vb.n)
                    v.select(sq1[:], smk[:], ssq[:], sq2[:]).then_inc(vb.h, vb.tick())
                    v.wait_ge(vb.h, vb.n)
                    v.tensor_copy(sqb[:], sq1[:]).then_inc(sv.h, sv.tick())
                block.vector(_)

                for b8 in range(8):
                    def _(te, tt=tt, b8=b8):
                        te.wait_ge(sv.h, sv.n)
                        for j in range(4):
                            c = b8 * 4 + j
                            ins = te.transpose(pTb[:, j * 128:(j + 1) * 128],
                                               sqb[:, c * 128:(c + 1) * 128], sidb[:])
                        ins.then_inc(st.h, st.tick())
                    block.tensor(_)

                    def _(v, tt=tt, b8=b8):
                        v.wait_ge(st.h, st.n)
                        for j in range(4):
                            c = b8 * 4 + j
                            ins = v.tensor_copy(
                                xqT[:, c * R + tt * 128: c * R + (tt + 1) * 128],
                                pTb[:, j * 128:(j + 1) * 128])
                        ins.then_inc(sv.h, sv.tick())
                    block.vector(_)

                def _(te, tt=tt):
                    te.wait_ge(sv.h, sv.n)
                    te.transpose(pT[0:1, 0:128], sasr[:, tt:tt + 1], sid[:]).then_inc(st.h, st.tick())
                block.tensor(_)

                def _(v, tt=tt):
                    v.wait_ge(st.h, st.n)
                    v.tensor_copy(sasrow[0:1, tt * 128:(tt + 1) * 128],
                                  pT[0:1, 0:128]).then_inc(sv.h, sv.tick())
                block.vector(_)

            def _(te):
                te.wait_ge(sv.h, sv.n)
                for half in range(2):
                    ins = te.matmul(pMM[:, half * 512:(half + 1) * 512], sones[:],
                                    sasrow[0:1, half * 512:(half + 1) * 512],
                                    start=True, stop=True)
                ins.then_inc(st.h, st.tick())
            block.tensor(_)

            def _(v):
                v.wait_ge(st.h, st.n)
                v.tensor_copy(bc_as[:, 0:1024], pMM[:, 0:1024]).then_inc(sv.h, sv.tick())
            block.vector(_)

            for wt in range(32):
                def _(sy, wt=wt):
                    sy.wait_ge(st.h, st.n)
                    sy.dma_start(out=sx[:], in_=fw[wt * 128:(wt + 1) * 128, :]).then_inc(dl.h, 16)
                    dl.inc(16)
                block.sync(_)

                def _(v, wt=wt):
                    v.wait_ge(dl.h, dl.n)
                    v.tensor_reduce(T[:, 0:1], sx[:], AX.X, AL.max,
                                    apply_absolute_value=True).then_inc(vb.h, vb.tick())
                    v.wait_ge(vb.h, vb.n)
                    v.tensor_scalar(T[:, 0:1], T[:, 0:1], INV448, 1e-6, AL.mult,
                                    AL.max).then_inc(vb.h, vb.tick())
                    v.wait_ge(vb.h, vb.n)
                    v.tensor_copy(sws16[:, wt:wt + 1], T[:, 0:1]).then_inc(vb.h, vb.tick())
                    v.wait_ge(vb.h, vb.n)
                    v.tensor_copy(sws[:, wt:wt + 1], sws16[:, wt:wt + 1]).then_inc(vb.h, vb.tick())
                    v.wait_ge(vb.h, vb.n)
                    v.reciprocal(swsr[:, wt:wt + 1], sws[:, wt:wt + 1]).then_inc(vb.h, vb.tick())
                    v.wait_ge(vb.h, vb.n)
                    v.tensor_scalar(sq1[:], sx[:], swsr[:, wt:wt + 1], -448.0, AL.mult, AL.max)
                    v.tensor_scalar(sq1[:], sq1[:], 448.0, None, AL.min, AL.bypass)
                    v.tensor_scalar(sq2[:], sq1[:], -240.0, 240.0, AL.max, AL.min)
                    v.tensor_copy(sf8[:], sq2[:]).then_inc(vb.h, vb.tick())
                    v.wait_ge(vb.h, vb.n)
                    v.tensor_copy(sq2[:], sf8[:])
                    v.tensor_tensor(ssq[:], sq1[:], sq1[:], AL.mult).then_inc(vb.h, vb.tick())
                    v.wait_ge(vb.h, vb.n)
                    v.tensor_scalar(smk[:], ssq[:], 224.0 * 224.0, None, AL.is_gt, AL.bypass)
                    v.tensor_scalar(ssq[:], sq1[:], DEK, None, AL.mult, AL.bypass)
                    v.tensor_tensor(sx[:], ssq[:], sq1[:], AL.subtract)
                    v.tensor_tensor(ssq[:], ssq[:], sx[:], AL.subtract).then_inc(vb.h, vb.tick())
                    v.wait_ge(vb.h, vb.n)
                    v.select(sq1[:], smk[:], ssq[:], sq2[:]).then_inc(vb.h, vb.tick())
                    v.wait_ge(vb.h, vb.n)
                    v.tensor_copy(sqb[:], sq1[:]).then_inc(sv.h, sv.tick())
                block.vector(_)

                for b8 in range(8):
                    def _(te, wt=wt, b8=b8):
                        te.wait_ge(sv.h, sv.n)
                        te.wait_ge(dg.h, dg.n)
                        for j in range(4):
                            c = b8 * 4 + j
                            ins = te.transpose(pTb[:, j * 128:(j + 1) * 128],
                                               sqb[:, c * 128:(c + 1) * 128], sidb[:])
                        ins.then_inc(st.h, st.tick())
                    block.tensor(_)

                    def _(v, wt=wt, b8=b8):
                        v.wait_ge(st.h, st.n)
                        v.tensor_copy(sqb2[:], pTb[:]).then_inc(sv.h, sv.tick())
                    block.vector(_)

                    def _(g, wt=wt, b8=b8):
                        g.wait_ge(sv.h, sv.n)
                        for j in range(4):
                            c = b8 * 4 + j
                            g.dma_start(out=w8t_sc[c * 128:(c + 1) * 128, wt * 128:(wt + 1) * 128],
                                        in_=sqb2[:, j * 128:(j + 1) * 128]).then_inc(dg.h, 16)
                            dg.inc(16)
                    block.gpsimd(_)

            with ExitStack() as pd:
                sw8 = sbuf("sw8", [128, 32 * 128], BF16, st=pd)
                t2f = sbuf("t2f", [128, 1024], st=pd)
                o16 = sbuf("o16", [128, 1024], BF16, st=pd)

                for do in range(32):
                    pp = (do % 2) * 1024
                    def _(sy, do=do):
                        sy.wait_ge(st.h, st.n)
                        src3 = w8t_sc[:, do * 128:(do + 1) * 128].rearrange(
                            "(c p) n -> p c n", p=128)
                        dst3 = sw8[:].rearrange("p (c n) -> p c n", c=32)
                        for grp in range(4):
                            sy.dma_start(out=dst3[:, grp * 8:(grp + 1) * 8, :],
                                         in_=src3[:, grp * 8:(grp + 1) * 8, :]).then_inc(dl.h, 16)
                            dl.inc(16)
                    block.sync(_)

                    def _(te, do=do, pp=pp):
                        te.wait_ge(dl.h, dl.n)
                        te.wait_ge(sv.h, sv.n)
                        for hf in range(2):
                            for di in range(32):
                                ins = te.matmul(pMM[:, pp + hf * 512:pp + hf * 512 + 512],
                                                sw8[:, di * 128:(di + 1) * 128],
                                                xqT[:, di * R + hf * 512:di * R + hf * 512 + 512],
                                                start=(di == 0), stop=(di == 31))
                        ins.then_inc(st.h, st.tick())
                    block.tensor(_)

                    def _(v, do=do, pp=pp):
                        v.wait_ge(st.h, st.n)
                        v.wait_ge(ss.h, ss.n)
                        v.scalar_tensor_tensor(t2f[:], pMM[:, pp:pp + 1024], sws[:, do:do + 1],
                                               bc_as[:, 0:1024], AL.mult,
                                               AL.mult).then_inc(sv.h, sv.tick())
                    block.vector(_)

                    def _(s, do=do):
                        s.wait_ge(sv.h, sv.n)
                        s.wait_ge(dg.h, dg.n)
                        s.activation(o16[:], t2f[:], AF.Identity, bias=sfb[:, do:do + 1],
                                     scale=1.0).then_inc(ss.h, ss.tick())
                    block.scalar(_)

                    def _(g, do=do):
                        g.wait_ge(ss.h, ss.n)
                        g.dma_start(out=ot_sc[do * 128:(do + 1) * 128, :], in_=o16[:]).then_inc(dg.h, 16)
                        dg.inc(16)
                    block.gpsimd(_)

                sot = sbuf("sot", [128, D], BF16, st=pd)
                sy_ = sbuf("sy", [128, D], st=pd)

                for tt in range(8):
                    def _(sy, tt=tt):
                        sy.wait_ge(dg.h, dg.n)
                        sy.wait_ge(sv.h, sv.n)
                        sy.dma_start(out=sx[:], in_=x2_sc[tt * 128:(tt + 1) * 128, :]).then_inc(dl.h, 16)
                        dl.inc(16)
                        sy.dma_start(out=sot[:], in_=ot_sc[:, tt * 128:(tt + 1) * 128],
                                     transpose=True).then_inc(dl.h, 16)
                        dl.inc(16)
                    block.sync(_)

                    def _(v, tt=tt):
                        v.wait_ge(dl.h, dl.n)
                        v.tensor_tensor(sy_[:], sx[:], sot[:], AL.add).then_inc(sv.h, sv.tick())
                    block.vector(_)

                    def _(g, tt=tt):
                        g.wait_ge(sv.h, sv.n)
                        g.dma_start(out=y_out[tt * 128:(tt + 1) * 128, :], in_=sy_[:]).then_inc(dg.h, 16)
                        dg.inc(16)
                        if tt == 7 and not DEBUG_DUMPS:
                            g.wait_ge(dg.h, dg.n)
                    block.gpsimd(_)

                if DEBUG_DUMPS:
                    dumps = [("d_qt", qt_sc, [D, R], F32), ("d_kin", kin, [D, R], F32),
                             ("d_vin", vin, [D, R], F32), ("d_kall", kall, [4, D, R], F32),
                             ("d_x2", x2_sc, [R, D], F32), ("d_ot", ot_sc, [D, R], BF16),
                             ("d_w8t", w8t_sc, [D, D], BF16)]
                    def _(g):
                        for nm, srcap, shp, dt in dumps:
                            dout = nc.dram_tensor(nm, shp, dt, kind="ExternalOutput").ap()
                            flat_in = srcap.rearrange("... -> (...)") if len(shp) > 2 else srcap
                            nchunk = 8
                            if len(shp) == 3:
                                for j in range(shp[0]):
                                    g.dma_start(out=dout[j], in_=srcap[j]).then_inc(dg.h, 16)
                                    dg.inc(16)
                            else:
                                rows = shp[0]
                                for j in range(nchunk):
                                    r0, r1 = rows * j // nchunk, rows * (j + 1) // nchunk
                                    g.dma_start(out=dout[r0:r1], in_=srcap[r0:r1]).then_inc(dg.h, 16)
                                    dg.inc(16)
                        g.wait_ge(dg.h, dg.n)
                    block.gpsimd(_)

    return nc


_NC_CACHE = [None]


def kernel(**inputs):
    x = np.asarray(inputs["L0_attn_hidden_states"], np.float32)
    wq = np.asarray(inputs["L0_attn_wq"], np.float32)
    wk = np.asarray(inputs["L0_attn_wk"], np.float32)
    wv = np.asarray(inputs["L0_attn_wv"], np.float32)
    wo = np.asarray(inputs["L0_attn_wo"], np.float32)
    bq = np.asarray(inputs["L0_attn_bq"], np.float32)
    bk = np.asarray(inputs["L0_attn_bk"], np.float32)
    bv = np.asarray(inputs["L0_attn_bv"], np.float32)
    bo = np.asarray(inputs["L0_attn_bo"], np.float32)
    fw_ = np.asarray(inputs["L0_ffn_weight"], np.float32)
    fb_ = np.asarray(inputs["L0_ffn_bias"], np.float32)

    wqT = np.ascontiguousarray(wq.T)
    wkT = np.ascontiguousarray(wk.T)
    wvT = np.ascontiguousarray(wv.T)
    woT = np.ascontiguousarray(wo.T)
    boR = np.ascontiguousarray(np.broadcast_to(bo, (128, D)))

    if _NC_CACHE[0] is None:
        _NC_CACHE[0] = build_program()
    nc = _NC_CACHE[0]

    in_maps = []
    for c in range(8):
        b = c // 4
        r0 = (c % 4) * R
        in_maps.append({
            "x_tok": np.ascontiguousarray(x[b, r0:r0 + R]),
            "wqT": wqT, "wkT": wkT, "wvT": wvT, "woT": woT,
            "bq": bq, "bk": bk, "bv": bv, "bo_rep": boR,
            "fw": fw_, "fb": fb_,
        })
    res = run_bass_kernel_spmd(nc, in_maps, core_ids=list(range(8)))
    out = np.empty((2, SEQ, D), np.float32)
    for c in range(8):
        b = c // 4
        r0 = (c % 4) * R
        out[b, r0:r0 + R] = res.results[c]["y"]
    return out


# version 05
# speedup vs baseline: 1.4502x; 1.0839x over previous
import math
from contextlib import ExitStack

import numpy as np
import ml_dtypes
import concourse.bass as bass
import concourse.mybir as mybir
from concourse.bass_utils import run_bass_kernel_spmd

F32 = mybir.dt.float32
BF16 = mybir.dt.bfloat16
FP16 = mybir.dt.float16
FP8 = mybir.dt.float8e4
I32 = mybir.dt.int32
AL = mybir.AluOpType
AF = mybir.ActivationFunctionType
AX = mybir.AxisListType

D = 4096
R = 1024
NH = 32
HD = 128
SEQ = 4096
TOPK = 819
QG = 512
NQG = R // QG
NBIS = 17
NDESC = 1
ASCALE = float(np.float32(1.0 / math.sqrt(HD)))
EPS_LN = 1.0e-5
W0 = 64.0
LOFF = 64.5
INV448 = float(np.float32(1.0 / 448.0))
DEK = float(np.float32(2.0 ** 20 + 1.0))


class Cnt:
    def __init__(self, h):
        self.h = h
        self.n = 0

    def inc(self, k=1):
        self.n += k
        return self.n

    def tick(self):
        self.n += 1
        return 1


DEBUG_DUMPS = False


def build_program():
    nc = bass.Bass("TRN2", target_bir_lowering=False, debug=False,
                   detect_race_conditions=False)

    x_tok = nc.dram_tensor("x_tok", [R, D], F32, kind="ExternalInput").ap()
    wqT = nc.dram_tensor("wqT", [D, D], F32, kind="ExternalInput").ap()
    wkT = nc.dram_tensor("wkT", [D, D], F32, kind="ExternalInput").ap()
    wvT = nc.dram_tensor("wvT", [D, D], F32, kind="ExternalInput").ap()
    woT = nc.dram_tensor("woT", [D, D], F32, kind="ExternalInput").ap()
    bq_in = nc.dram_tensor("bq", [D], F32, kind="ExternalInput").ap()
    bk_in = nc.dram_tensor("bk", [D], F32, kind="ExternalInput").ap()
    bv_in = nc.dram_tensor("bv", [D], F32, kind="ExternalInput").ap()
    bo_rep = nc.dram_tensor("bo_rep", [128, D], F32, kind="ExternalInput").ap()
    fw = nc.dram_tensor("fw", [D, D], F32, kind="ExternalInput").ap()
    fb_in = nc.dram_tensor("fb", [D], F32, kind="ExternalInput").ap()
    y_out = nc.dram_tensor("y", [R, D], F32, kind="ExternalOutput").ap()

    qt_sc = nc.dram_tensor("qt_sc", [D, R], F32).ap()
    kin = nc.dram_tensor("kin", [D, R], F32).ap()
    vin = nc.dram_tensor("vin", [D, R], F32).ap()
    kall = nc.dram_tensor("kall", [4, D, R], F32).ap()
    vall = nc.dram_tensor("vall", [4, D, R], F32).ap()
    x2_sc = nc.dram_tensor("x2_sc", [R, D], F32).ap()
    w8t_sc = nc.dram_tensor("w8t_sc", [D, D], BF16).ap()
    ot_sc = nc.dram_tensor("ot_sc", [D, R], BF16).ap()

    ident_f = nc.inline_tensor(np.eye(128, dtype=np.float32), "ident_f").ap()
    ident_b = nc.inline_tensor(np.eye(128, dtype=ml_dtypes.bfloat16), "ident_b").ap()
    ones_f = nc.inline_tensor(np.ones((1, 128), dtype=np.float32), "ones_f").ap()

    es = ExitStack()
    with es:
        def sbuf(name, shape, dt=F32, st=None):
            return (st or es).enter_context(nc.sbuf_tensor(name, shape, dt))

        def sem(name):
            return Cnt(es.enter_context(nc.semaphore(name)))

        block = es.enter_context(nc.Block())

        dl = sem("dl")
        dg = sem("dg")
        sv = sem("sv")
        ss = sem("ss")
        st = sem("st")
        cc = sem("cc")
        vb = sem("vb")

        pMM = es.enter_context(nc.psum_tensor("pMM", [128, 2048], F32))
        pT = es.enter_context(nc.psum_tensor("pT", [128, 1024], F32))
        pCT = es.enter_context(nc.psum_tensor("pCT", [128, 128], F32))
        pTb = es.enter_context(nc.psum_tensor("pTb", [128, 512], BF16))

        sid = sbuf("sid", [128, 128])
        sidb = sbuf("sidb", [128, 128], BF16)
        sones = sbuf("sones", [1, 128])
        sbq = sbuf("sbq", [128, 32])
        sbk = sbuf("sbk", [128, 32])
        sbv = sbuf("sbv", [128, 32])
        sfb = sbuf("sfb", [128, 32])
        sfb16 = sbuf("sfb16", [128, 32], BF16)
        sws = sbuf("sws", [128, 32])
        swsr = sbuf("swsr", [128, 32])
        sws16 = sbuf("sws16", [128, 32], FP16)
        sasr = sbuf("sasr", [128, 8])
        sasrow = sbuf("sasrow", [1, 1024])
        bc_as = sbuf("bc_as", [128, 1024])
        stny = sbuf("stny", [128, 48])
        cst = sbuf("cst", [128, 1])

        T = stny

        def _(sy):
            sy.dma_start(out=sid[:], in_=ident_f).then_inc(dl.h, 16); dl.inc(16)
            sy.dma_start(out=sidb[:], in_=ident_b).then_inc(dl.h, 16); dl.inc(16)
            sy.dma_start(out=sones[:], in_=ones_f).then_inc(dl.h, 16); dl.inc(16)
            for dst, src in ((sbq, bq_in), (sbk, bk_in), (sbv, bv_in), (sfb, fb_in)):
                sy.dma_start(out=dst[:], in_=src.rearrange("(c p) -> p c", p=128),
                             allow_slow_non_contiguous=True).then_inc(dl.h, 16)
                dl.inc(16)
        block.sync(_)

        def _(v):
            v.wait_ge(dl.h, dl.n)
            v.memset(cst[:], ASCALE)
            v.tensor_scalar(sbq[:], sbq[:], ASCALE, None, AL.mult, AL.bypass)
            v.tensor_copy(sfb16[:], sfb[:]).then_inc(vb.h, vb.tick())
            v.wait_ge(vb.h, vb.n)
            v.tensor_copy(sfb[:], sfb16[:]).then_inc(sv.h, sv.tick())
        block.vector(_)

        with ExitStack() as pa:
            sx = sbuf("sx", [128, D], st=pa)
            ssq = sbuf("ssq", [128, D], st=pa)
            sxT = sbuf("sxT", [128, 32 * R], st=pa)
            sev = sbuf("sev", [128, 1024], st=pa)
            wcol = sbuf("wcol", [128, 32 * 128], st=pa)

            _dxh_ap = (nc.dram_tensor("d_xh", [R, D], F32, kind="ExternalOutput").ap()
                       if DEBUG_DUMPS else None)

            for tt in range(8):
                def _(sy, tt=tt):
                    if tt > 0:
                        sy.wait_ge(sv.h, sv.n)
                    sy.dma_start(out=sx[:], in_=x_tok[tt * 128:(tt + 1) * 128, :]).then_inc(dl.h, 16)
                    dl.inc(16)
                block.sync(_)

                def _(v):
                    v.wait_ge(dl.h, dl.n)
                    v.tensor_reduce(T[:, 0:1], sx[:], AX.X, AL.add).then_inc(vb.h, vb.tick())
                    v.wait_ge(vb.h, vb.n)
                    v.tensor_scalar(T[:, 0:1], T[:, 0:1], 1.0 / D, None, AL.mult,
                                    AL.bypass).then_inc(sv.h, sv.tick())
                block.vector(_)

                def _(s):
                    s.wait_ge(sv.h, sv.n)
                    s.wait_ge(st.h, st.n)
                    s.activation(ssq[:], sx[:], AF.Square, bias=0.0, scale=1.0,
                                 accum_out=T[:, 1:2]).then_inc(ss.h, ss.tick())
                block.scalar(_)

                def _(v):
                    v.wait_ge(ss.h, ss.n)
                    v.tensor_scalar(T[:, 1:2], T[:, 1:2], 1.0 / D, None, AL.mult, AL.bypass)
                    v.tensor_tensor(T[:, 2:3], T[:, 0:1], T[:, 0:1], AL.mult).then_inc(vb.h, vb.tick())
                    v.wait_ge(vb.h, vb.n)
                    v.tensor_tensor(T[:, 1:2], T[:, 1:2], T[:, 2:3], AL.subtract).then_inc(vb.h, vb.tick())
                    v.wait_ge(vb.h, vb.n)
                    v.tensor_scalar(T[:, 1:2], T[:, 1:2], EPS_LN, None, AL.add,
                                    AL.bypass).then_inc(sv.h, sv.tick())
                block.vector(_)

                def _(s):
                    s.wait_ge(sv.h, sv.n)
                    s.activation(T[:, 2:3], T[:, 1:2], AF.Sqrt, bias=0.0,
                                 scale=1.0).then_inc(ss.h, ss.tick())
                block.scalar(_)

                def _(v):
                    v.wait_ge(ss.h, ss.n)
                    v.reciprocal(T[:, 3:4], T[:, 2:3]).then_inc(vb.h, vb.tick())
                    v.wait_ge(vb.h, vb.n)
                    v.tensor_scalar(ssq[:], sx[:], T[:, 0:1], T[:, 3:4], AL.subtract,
                                    AL.mult).then_inc(sv.h, sv.tick())
                block.vector(_)

                if DEBUG_DUMPS:
                    def _(g, tt=tt):
                        g.wait_ge(sv.h, sv.n)
                        g.dma_start(out=_dxh_ap[tt * 128:(tt + 1) * 128, :],
                                    in_=ssq[:]).then_inc(dg.h, 16)
                        dg.inc(16)
                        g.wait_ge(dg.h, dg.n)
                    block.gpsimd(_)

                for q4 in range(8):
                    ph = (q4 % 2) * 512
                    def _(te, tt=tt, q4=q4, ph=ph):
                        te.wait_ge(sv.h, sv.n)
                        for j in range(4):
                            c = q4 * 4 + j
                            ins = te.transpose(pT[:, ph + j * 128: ph + (j + 1) * 128],
                                               ssq[:, c * 128:(c + 1) * 128], sid[:])
                        ins.then_inc(st.h, st.tick())
                    block.tensor(_)

                    def _(v, tt=tt, q4=q4, ph=ph):
                        v.wait_ge(st.h, st.n)
                        for j in range(4):
                            c = q4 * 4 + j
                            ins = v.tensor_copy(
                                sxT[:, c * R + tt * 128: c * R + (tt + 1) * 128],
                                pT[:, ph + j * 128: ph + (j + 1) * 128])
                        ins.then_inc(sv.h, sv.tick())
                    block.vector(_)

            if DEBUG_DUMPS:
                d_xT = nc.dram_tensor("d_xT", [128, 32 * R], F32, kind="ExternalOutput").ap()

                def _(g):
                    g.wait_ge(sv.h, sv.n)
                    g.dma_start(out=d_xT[:], in_=sxT[:]).then_inc(dg.h, 16)
                    dg.inc(16)
                    g.wait_ge(dg.h, dg.n)
                block.gpsimd(_)

            projs = [(wqT, sbq, qt_sc, ASCALE), (wkT, sbk, kin, 1.0), (wvT, sbv, vin, 1.0)]
            for pi, (wT, sbias, dst, scale) in enumerate(projs):
                for do in range(32):
                    ph = (do % 2) * 1024
                    def _(sy, wT=wT, do=do):
                        sy.wait_ge(st.h, st.n)
                        src3 = wT[:, do * 128:(do + 1) * 128].rearrange("(c p) n -> p c n", p=128)
                        dst3 = wcol[:].rearrange("p (c n) -> p c n", c=32)
                        for grp in range(4):
                            sy.dma_start(out=dst3[:, grp * 8:(grp + 1) * 8, :],
                                         in_=src3[:, grp * 8:(grp + 1) * 8, :]).then_inc(dl.h, 16)
                            dl.inc(16)
                    block.sync(_)

                    def _(te, do=do, ph=ph):
                        te.wait_ge(dl.h, dl.n)
                        te.wait_ge(sv.h, sv.n)
                        te.wait_ge(ss.h, ss.n)
                        for half in range(2):
                            for di in range(32):
                                ins = te.matmul(
                                    pMM[:, ph + half * 512: ph + half * 512 + 512],
                                    wcol[:, di * 128:(di + 1) * 128],
                                    sxT[:, di * R + half * 512: di * R + half * 512 + 512],
                                    start=(di == 0), stop=(di == 31))
                        ins.then_inc(st.h, st.tick())
                    block.tensor(_)

                    def _(s, do=do, sbias=sbias, scale=scale, ph=ph):
                        s.wait_ge(st.h, st.n)
                        s.wait_ge(dg.h, dg.n)
                        sc = 1.0 if scale == 1.0 else cst[:, 0:1]
                        s.activation(sev[:, 0:512], pMM[:, ph:ph + 512], AF.Identity,
                                     bias=sbias[:, do:do + 1], scale=sc)
                        s.activation(sev[:, 512:1024], pMM[:, ph + 512:ph + 1024], AF.Identity,
                                     bias=sbias[:, do:do + 1], scale=sc).then_inc(ss.h, ss.tick())
                    block.scalar(_)

                    def _(g, do=do, dst=dst):
                        g.wait_ge(ss.h, ss.n)
                        g.dma_start(out=dst[do * 128:(do + 1) * 128, :], in_=sev[:]).then_inc(dg.h, 16)
                        dg.inc(16)
                    block.gpsimd(_)

        def _(g):
            g.wait_ge(dg.h, dg.n)
            g.collective_compute("AllGather", AL.bypass,
                                 replica_groups=[[0, 1, 2, 3], [4, 5, 6, 7]],
                                 ins=[kin.opt()], outs=[kall.opt()]).then_inc(cc.h, cc.tick())
            g.collective_compute("AllGather", AL.bypass,
                                 replica_groups=[[0, 1, 2, 3], [4, 5, 6, 7]],
                                 ins=[vin.opt()], outs=[vall.opt()]).then_inc(cc.h, cc.tick())
        block.gpsimd(_)

        def _(sy):
            sy.wait_ge(cc.h, cc.n)
            sy.sem_inc(dl.h, 16)
            dl.inc(16)
        block.sync(_)

        with ExitStack() as pb:
            ctxT = sbuf("ctxT", [128, 32 * QG], st=pb)

            for qg in range(NQG):
                with ExitStack() as ph_heads:
                    kTh = sbuf(f"kTh_{qg}", [128, SEQ], st=ph_heads)
                    vTh = sbuf(f"vTh_{qg}", [128, SEQ], st=ph_heads)
                    vtok = sbuf(f"vtok_{qg}", [128, 32 * 128], st=ph_heads)
                    qTh = sbuf(f"qTh_{qg}", [128, QG], st=ph_heads)
                    sS = sbuf(f"sS_{qg}", [128, SEQ], st=ph_heads)
                    sE = sbuf(f"sE_{qg}", [128, SEQ], st=ph_heads)
                    jkb = sbuf(f"jkb_{qg}", [128, SEQ], BF16, st=ph_heads)
                    spt = sbuf(f"spt_{qg}", [128, 1024], st=ph_heads)

                    for h in range(NH):
                        def _(sy, h=h, qg=qg):
                            sy.wait_ge(st.h, st.n)
                            sy.wait_ge(sv.h, sv.n)
                            for g4 in range(4):
                                sy.dma_start(out=kTh[:, g4 * R:(g4 + 1) * R],
                                             in_=kall[g4, h * 128:(h + 1) * 128, :]).then_inc(dl.h, 16)
                                dl.inc(16)
                                sy.dma_start(out=vTh[:, g4 * R:(g4 + 1) * R],
                                             in_=vall[g4, h * 128:(h + 1) * 128, :]).then_inc(dl.h, 16)
                                dl.inc(16)
                            sy.dma_start(out=qTh[:],
                                         in_=qt_sc[h * 128:(h + 1) * 128,
                                                   qg * QG:(qg + 1) * QG]).then_inc(dl.h, 16)
                            dl.inc(16)
                        block.sync(_)

                        for b8 in range(8):
                            ph = (b8 % 2) * 512
                            def _(te, b8=b8, ph=ph):
                                te.wait_ge(dl.h, dl.n)
                                te.wait_ge(sv.h, sv.n)
                                for j in range(4):
                                    c = b8 * 4 + j
                                    ins = te.transpose(pT[:, ph + j * 128:ph + (j + 1) * 128],
                                                       vTh[:, c * 128:(c + 1) * 128], sid[:])
                                ins.then_inc(st.h, st.tick())
                            block.tensor(_)

                            def _(v, b8=b8, ph=ph):
                                v.wait_ge(st.h, st.n)
                                ins = v.tensor_copy(vtok[:, b8 * 512:(b8 + 1) * 512],
                                                    pT[:, ph:ph + 512])
                                ins.then_inc(sv.h, sv.tick())
                            block.vector(_)

                        for qt in range(QG // 128):
                            for half in range(2):
                                def _(te, qt=qt, half=half):
                                    te.wait_ge(sv.h, sv.n)
                                    for j in range(4):
                                        ins = te.matmul(
                                            pMM[:, j * 512:(j + 1) * 512],
                                            qTh[:, qt * 128:(qt + 1) * 128],
                                            kTh[:, half * 2048 + j * 512: half * 2048 + (j + 1) * 512],
                                            start=True, stop=True)
                                    ins.then_inc(st.h, st.tick())
                                block.tensor(_)

                                def _(v, qt=qt, half=half):
                                    v.wait_ge(st.h, st.n)
                                    v.tensor_scalar(sS[:, half * 2048:(half + 1) * 2048],
                                                    pMM[:, 0:2048], 0.0, None, AL.add, AL.max,
                                                    accum_out=T[:, 16 + half:17 + half]).then_inc(
                                        sv.h, sv.tick())
                                block.vector(_)

                            def _(v, qt=qt):
                                v.tensor_tensor(T[:, 4:5], T[:, 16:17], T[:, 17:18],
                                                AL.max).then_inc(vb.h, vb.tick())
                                v.wait_ge(vb.h, vb.n)
                                v.tensor_scalar(T[:, 4:5], T[:, 4:5], -1.0, None, AL.mult,
                                                AL.bypass).then_inc(sv.h, sv.tick())
                            block.vector(_)

                            def _(s, qt=qt):
                                s.wait_ge(sv.h, sv.n)
                                s.activation(sE[:], sS[:], AF.Exp, bias=T[:, 4:5],
                                             scale=1.0).then_inc(ss.h, ss.tick())
                            block.scalar(_)

                            def _(v):
                                v.wait_ge(ss.h, ss.n)
                                v.tensor_scalar(T[:, 5:6], T[:, 4:5], -1.0, -LOFF, AL.mult,
                                                AL.add).then_inc(vb.h, vb.tick())
                                v.wait_ge(vb.h, vb.n)
                                v.sem_inc(sv.h, sv.tick())
                            block.vector(_)

                            def _(v):
                                v.tensor_scalar(T[:, 6:7], T[:, 5:6], W0 * 0.5, None,
                                                AL.add, AL.bypass).then_inc(vb.h, vb.tick())
                                v.wait_ge(vb.h, vb.n)
                            block.vector(_)
                            for it in range(NBIS):
                                wkw = W0 * (0.5 ** (it + 1))
                                def _(v, wkw=wkw):
                                    v.tensor_scalar(jkb[:], sS[:], T[:, 6:7], None, AL.is_ge,
                                                    AL.add, accum_out=T[:, 7:8]).then_inc(vb.h, vb.tick())
                                    v.wait_ge(vb.h, vb.n)
                                    v.tensor_scalar(T[:, 8:9], T[:, 7:8], TOPK - 0.5, -0.5,
                                                    AL.is_ge, AL.add).then_inc(vb.h, vb.tick())
                                    v.wait_ge(vb.h, vb.n)
                                    v.scalar_tensor_tensor(T[:, 6:7], T[:, 8:9], wkw, T[:, 6:7],
                                                           AL.mult, AL.add).then_inc(vb.h, vb.tick())
                                    v.wait_ge(vb.h, vb.n)
                                block.vector(_)

                            def _(v):
                                v.tensor_scalar(T[:, 6:7], T[:, 6:7], W0 * (0.5 ** NBIS), None,
                                                AL.add, AL.bypass).then_inc(sv.h, sv.tick())
                            block.vector(_)

                            def _(s):
                                s.wait_ge(sv.h, sv.n)
                                s.activation(T[:, 9:10], T[:, 6:7], AF.Exp, bias=T[:, 4:5],
                                             scale=1.0).then_inc(ss.h, ss.tick())
                            block.scalar(_)

                            def _(v):
                                v.wait_ge(ss.h, ss.n)
                                v.memset(T[:, 10:11], 0.0)
                                v.memset(T[:, 11:12], 0.0)
                            block.vector(_)

                            for d in range(NDESC):
                                def _(v):
                                    v.scalar_tensor_tensor(sS[:], sE[:], T[:, 9:10], sE[:],
                                                           AL.is_lt, AL.mult).then_inc(vb.h, vb.tick())
                                    v.wait_ge(vb.h, vb.n)
                                    v.tensor_reduce(T[:, 12:13], sS[:], AX.X,
                                                    AL.max).then_inc(vb.h, vb.tick())
                                    v.wait_ge(vb.h, vb.n)
                                    v.tensor_scalar(jkb[:], sE[:], T[:, 12:13], None, AL.is_ge,
                                                    AL.add, accum_out=T[:, 7:8]).then_inc(vb.h, vb.tick())
                                    v.wait_ge(vb.h, vb.n)
                                    v.tensor_scalar(T[:, 8:9], T[:, 7:8], TOPK - 0.5, None,
                                                    AL.is_ge, AL.bypass)
                                    v.tensor_scalar(T[:, 13:14], T[:, 11:12], 0.5, None,
                                                    AL.is_lt, AL.bypass).then_inc(vb.h, vb.tick())
                                    v.wait_ge(vb.h, vb.n)
                                    v.tensor_tensor(T[:, 13:14], T[:, 13:14], T[:, 8:9],
                                                    AL.mult).then_inc(vb.h, vb.tick())
                                    v.wait_ge(vb.h, vb.n)
                                    v.scalar_tensor_tensor(T[:, 10:11], T[:, 12:13], T[:, 13:14],
                                                           T[:, 10:11], AL.mult, AL.add)
                                    v.tensor_tensor(T[:, 11:12], T[:, 11:12], T[:, 8:9], AL.max)
                                    v.tensor_copy(T[:, 9:10], T[:, 12:13]).then_inc(vb.h, vb.tick())
                                    v.wait_ge(vb.h, vb.n)
                                block.vector(_)

                            def _(v):
                                v.tensor_scalar(T[:, 12:13], T[:, 6:7], -2.0 * W0 * (0.5 ** NBIS),
                                                None, AL.add, AL.bypass).then_inc(sv.h, sv.tick())
                            block.vector(_)

                            def _(s):
                                s.wait_ge(sv.h, sv.n)
                                s.activation(T[:, 13:14], T[:, 12:13], AF.Exp, bias=T[:, 4:5],
                                             scale=1.0).then_inc(ss.h, ss.tick())
                            block.scalar(_)

                            def _(v):
                                v.wait_ge(ss.h, ss.n)
                                v.tensor_scalar(T[:, 12:13], T[:, 11:12], 0.5, None,
                                                AL.is_lt, AL.bypass).then_inc(vb.h, vb.tick())
                                v.wait_ge(vb.h, vb.n)
                                v.scalar_tensor_tensor(T[:, 10:11], T[:, 13:14], T[:, 12:13],
                                                       T[:, 10:11], AL.mult,
                                                       AL.add).then_inc(vb.h, vb.tick())
                                v.wait_ge(vb.h, vb.n)
                            block.vector(_)

                            def _(v):
                                v.scalar_tensor_tensor(sS[:], sE[:], T[:, 10:11], sE[:],
                                                       AL.is_ge, AL.mult,
                                                       accum_out=T[:, 14:15]).then_inc(vb.h, vb.tick())
                                v.wait_ge(vb.h, vb.n)
                                v.reciprocal(T[:, 15:16], T[:, 14:15]).then_inc(vb.h, vb.tick())
                                v.wait_ge(vb.h, vb.n)
                                ins = v.tensor_scalar(sE[:], sS[:], T[:, 15:16], None,
                                                      AL.mult, AL.bypass)
                                ins.then_inc(sv.h, sv.tick())
                            block.vector(_)

                            for grp in range(8):
                                ph2 = (grp % 2) * 512
                                def _(te, grp=grp, ph2=ph2):
                                    te.wait_ge(sv.h, sv.n)
                                    for j in range(4):
                                        ck = grp * 4 + j
                                        ins = te.transpose(pT[:, ph2 + j * 128: ph2 + (j + 1) * 128],
                                                           sE[:, ck * 128:(ck + 1) * 128], sid[:])
                                    ins.then_inc(st.h, st.tick())
                                block.tensor(_)

                                def _(v, grp=grp, ph2=ph2):
                                    v.wait_ge(st.h, st.n)
                                    v.tensor_copy(spt[:, ph2:ph2 + 512],
                                                  pT[:, ph2:ph2 + 512]).then_inc(sv.h, sv.tick())
                                block.vector(_)

                                def _(te, grp=grp, ph2=ph2):
                                    te.wait_ge(sv.h, sv.n)
                                    for j in range(4):
                                        ck = grp * 4 + j
                                        ins = te.matmul(pCT[:], vtok[:, ck * 128:(ck + 1) * 128],
                                                        spt[:, ph2 + j * 128: ph2 + (j + 1) * 128],
                                                        start=(ck == 0), stop=(ck == 31))
                                    ins.then_inc(st.h, st.tick())
                                block.tensor(_)

                            def _(v, qt=qt, h=h):
                                v.wait_ge(st.h, st.n)
                                v.tensor_copy(ctxT[:, h * QG + qt * 128: h * QG + (qt + 1) * 128],
                                              pCT[:]).then_inc(sv.h, sv.tick())
                            block.vector(_)

                with ExitStack() as po:
                    swo = sbuf(f"swo_{qg}", [128, 32 * 512], st=po)
                    sbo = sbuf(f"sbo_{qg}", [128, 512], st=po)
                    sxr = sbuf(f"sxr_{qg}", [128, 512], st=po)
                    sev2 = sbuf(f"sev2_{qg}", [128, 512], st=po)

                    for dk in range(8):
                        def _(sy, dk=dk):
                            sy.wait_ge(st.h, st.n)
                            src3 = woT[:, dk * 512:(dk + 1) * 512].rearrange(
                                "(c p) n -> p c n", p=128)
                            dst3 = swo[:].rearrange("p (c n) -> p c n", c=32)
                            for grp in range(8):
                                sy.dma_start(out=dst3[:, grp * 4:(grp + 1) * 4, :],
                                             in_=src3[:, grp * 4:(grp + 1) * 4, :]).then_inc(dl.h, 16)
                                dl.inc(16)
                            sy.dma_start(out=sbo[:], in_=bo_rep[:, dk * 512:(dk + 1) * 512]).then_inc(dl.h, 16)
                            dl.inc(16)
                        block.sync(_)

                        for qt in range(QG // 128):
                            pp = (qt % 2) * 512
                            def _(sy, dk=dk, qt=qt, qg=qg):
                                sy.wait_ge(sv.h, sv.n)
                                row0 = qg * QG + qt * 128
                                sy.dma_start(out=sxr[:],
                                             in_=x_tok[row0:row0 + 128,
                                                       dk * 512:(dk + 1) * 512]).then_inc(dl.h, 16)
                                dl.inc(16)
                            block.sync(_)

                            def _(te, dk=dk, qt=qt, pp=pp):
                                te.wait_ge(dl.h, dl.n)
                                te.wait_ge(sv.h, sv.n)
                                for dc in range(32):
                                    ins = te.matmul(pMM[:, pp:pp + 512],
                                                    ctxT[:, dc * QG + qt * 128: dc * QG + (qt + 1) * 128],
                                                    swo[:, dc * 512:(dc + 1) * 512],
                                                    start=(dc == 0), stop=(dc == 31))
                                ins.then_inc(st.h, st.tick())
                            block.tensor(_)

                            def _(v, dk=dk, qt=qt, pp=pp):
                                v.wait_ge(st.h, st.n)
                                v.wait_ge(dg.h, dg.n)
                                v.tensor_tensor(sev2[:], sbo[:], sxr[:], AL.add)
                                v.tensor_tensor(sev2[:], sev2[:], pMM[:, pp:pp + 512],
                                                AL.add).then_inc(sv.h, sv.tick())
                            block.vector(_)

                            def _(g, dk=dk, qt=qt, qg=qg):
                                g.wait_ge(sv.h, sv.n)
                                row0 = qg * QG + qt * 128
                                g.dma_start(out=x2_sc[row0:row0 + 128, dk * 512:(dk + 1) * 512],
                                            in_=sev2[:]).then_inc(dg.h, 16)
                                dg.inc(16)
                            block.gpsimd(_)

        with ExitStack() as pc:
            sx = sbuf("sxc", [128, D], st=pc)
            ssq = sbuf("ssqc", [128, D], st=pc)
            sq1 = sbuf("sq1", [128, D], st=pc)
            sq2 = sbuf("sq2", [128, D], st=pc)
            sf8 = sbuf("sf8", [128, D], FP8, st=pc)
            smk = sbuf("smk", [128, D], I32, st=pc)
            sqb = sbuf("sqb", [128, D], BF16, st=pc)
            sqb2 = sbuf("sqb2", [128, 512], BF16, st=pc)
            xqT = sbuf("xqT", [128, 32 * R], BF16, st=pc)

            for tt in range(8):
                def _(sy, tt=tt):
                    sy.wait_ge(st.h, st.n)
                    sy.wait_ge(dg.h, dg.n)
                    sy.dma_start(out=sx[:], in_=x2_sc[tt * 128:(tt + 1) * 128, :]).then_inc(dl.h, 16)
                    dl.inc(16)
                block.sync(_)

                def _(v):
                    v.wait_ge(dl.h, dl.n)
                    v.tensor_reduce(T[:, 0:1], sx[:], AX.X, AL.add).then_inc(vb.h, vb.tick())
                    v.wait_ge(vb.h, vb.n)
                    v.tensor_scalar(T[:, 0:1], T[:, 0:1], 1.0 / D, None, AL.mult,
                                    AL.bypass).then_inc(sv.h, sv.tick())
                block.vector(_)

                def _(s):
                    s.wait_ge(sv.h, sv.n)
                    s.activation(ssq[:], sx[:], AF.Square, bias=0.0, scale=1.0,
                                 accum_out=T[:, 1:2]).then_inc(ss.h, ss.tick())
                block.scalar(_)

                def _(v):
                    v.wait_ge(ss.h, ss.n)
                    v.tensor_scalar(T[:, 1:2], T[:, 1:2], 1.0 / D, None, AL.mult, AL.bypass)
                    v.tensor_tensor(T[:, 2:3], T[:, 0:1], T[:, 0:1], AL.mult).then_inc(vb.h, vb.tick())
                    v.wait_ge(vb.h, vb.n)
                    v.tensor_tensor(T[:, 1:2], T[:, 1:2], T[:, 2:3], AL.subtract).then_inc(vb.h, vb.tick())
                    v.wait_ge(vb.h, vb.n)
                    v.tensor_scalar(T[:, 1:2], T[:, 1:2], EPS_LN, None, AL.add,
                                    AL.bypass).then_inc(sv.h, sv.tick())
                block.vector(_)

                def _(s):
                    s.wait_ge(sv.h, sv.n)
                    s.activation(T[:, 2:3], T[:, 1:2], AF.Sqrt, bias=0.0,
                                 scale=1.0).then_inc(ss.h, ss.tick())
                block.scalar(_)

                def _(v, tt=tt):
                    v.wait_ge(ss.h, ss.n)
                    v.reciprocal(T[:, 3:4], T[:, 2:3]).then_inc(vb.h, vb.tick())
                    v.wait_ge(vb.h, vb.n)
                    v.tensor_scalar(ssq[:], sx[:], T[:, 0:1], T[:, 3:4], AL.subtract, AL.mult)
                    v.tensor_reduce(T[:, 5:6], ssq[:], AX.X, AL.max,
                                    apply_absolute_value=True).then_inc(vb.h, vb.tick())
                    v.wait_ge(vb.h, vb.n)
                    v.tensor_scalar(T[:, 5:6], T[:, 5:6], INV448, 1e-6, AL.mult,
                                    AL.max).then_inc(vb.h, vb.tick())
                    v.wait_ge(vb.h, vb.n)
                    v.tensor_copy(sasr[:, tt:tt + 1], T[:, 5:6])
                    v.reciprocal(T[:, 6:7], T[:, 5:6]).then_inc(vb.h, vb.tick())
                    v.wait_ge(vb.h, vb.n)
                    v.tensor_scalar(sq1[:], ssq[:], T[:, 6:7], -448.0, AL.mult, AL.max)
                    v.tensor_scalar(sq1[:], sq1[:], 448.0, None, AL.min, AL.bypass)
                    v.tensor_scalar(sq2[:], sq1[:], -240.0, 240.0, AL.max, AL.min)
                    v.tensor_copy(sf8[:], sq2[:]).then_inc(vb.h, vb.tick())
                    v.wait_ge(vb.h, vb.n)
                    v.tensor_copy(sq2[:], sf8[:])
                    v.tensor_tensor(ssq[:], sq1[:], sq1[:], AL.mult).then_inc(vb.h, vb.tick())
                    v.wait_ge(vb.h, vb.n)
                    v.tensor_scalar(smk[:], ssq[:], 224.0 * 224.0, None, AL.is_gt, AL.bypass)
                    v.tensor_scalar(ssq[:], sq1[:], DEK, None, AL.mult, AL.bypass)
                    v.tensor_tensor(sx[:], ssq[:], sq1[:], AL.subtract)
                    v.tensor_tensor(ssq[:], ssq[:], sx[:], AL.subtract).then_inc(vb.h, vb.tick())
                    v.wait_ge(vb.h, vb.n)
                    v.select(sq1[:], smk[:], ssq[:], sq2[:]).then_inc(vb.h, vb.tick())
                    v.wait_ge(vb.h, vb.n)
                    v.tensor_copy(sqb[:], sq1[:]).then_inc(sv.h, sv.tick())
                block.vector(_)

                for b8 in range(8):
                    def _(te, tt=tt, b8=b8):
                        te.wait_ge(sv.h, sv.n)
                        for j in range(4):
                            c = b8 * 4 + j
                            ins = te.transpose(pTb[:, j * 128:(j + 1) * 128],
                                               sqb[:, c * 128:(c + 1) * 128], sidb[:])
                        ins.then_inc(st.h, st.tick())
                    block.tensor(_)

                    def _(v, tt=tt, b8=b8):
                        v.wait_ge(st.h, st.n)
                        for j in range(4):
                            c = b8 * 4 + j
                            ins = v.tensor_copy(
                                xqT[:, c * R + tt * 128: c * R + (tt + 1) * 128],
                                pTb[:, j * 128:(j + 1) * 128])
                        ins.then_inc(sv.h, sv.tick())
                    block.vector(_)

                def _(te, tt=tt):
                    te.wait_ge(sv.h, sv.n)
                    te.transpose(pT[0:1, 0:128], sasr[:, tt:tt + 1], sid[:]).then_inc(st.h, st.tick())
                block.tensor(_)

                def _(v, tt=tt):
                    v.wait_ge(st.h, st.n)
                    v.tensor_copy(sasrow[0:1, tt * 128:(tt + 1) * 128],
                                  pT[0:1, 0:128]).then_inc(sv.h, sv.tick())
                block.vector(_)

            def _(te):
                te.wait_ge(sv.h, sv.n)
                for half in range(2):
                    ins = te.matmul(pMM[:, half * 512:(half + 1) * 512], sones[:],
                                    sasrow[0:1, half * 512:(half + 1) * 512],
                                    start=True, stop=True)
                ins.then_inc(st.h, st.tick())
            block.tensor(_)

            def _(v):
                v.wait_ge(st.h, st.n)
                v.tensor_copy(bc_as[:, 0:1024], pMM[:, 0:1024]).then_inc(sv.h, sv.tick())
            block.vector(_)

            for wt in range(32):
                def _(sy, wt=wt):
                    sy.wait_ge(st.h, st.n)
                    sy.dma_start(out=sx[:], in_=fw[wt * 128:(wt + 1) * 128, :]).then_inc(dl.h, 16)
                    dl.inc(16)
                block.sync(_)

                def _(v, wt=wt):
                    v.wait_ge(dl.h, dl.n)
                    v.tensor_reduce(T[:, 0:1], sx[:], AX.X, AL.max,
                                    apply_absolute_value=True).then_inc(vb.h, vb.tick())
                    v.wait_ge(vb.h, vb.n)
                    v.tensor_scalar(T[:, 0:1], T[:, 0:1], INV448, 1e-6, AL.mult,
                                    AL.max).then_inc(vb.h, vb.tick())
                    v.wait_ge(vb.h, vb.n)
                    v.tensor_copy(sws16[:, wt:wt + 1], T[:, 0:1]).then_inc(vb.h, vb.tick())
                    v.wait_ge(vb.h, vb.n)
                    v.tensor_copy(sws[:, wt:wt + 1], sws16[:, wt:wt + 1]).then_inc(vb.h, vb.tick())
                    v.wait_ge(vb.h, vb.n)
                    v.reciprocal(swsr[:, wt:wt + 1], sws[:, wt:wt + 1]).then_inc(vb.h, vb.tick())
                    v.wait_ge(vb.h, vb.n)
                    v.tensor_scalar(sq1[:], sx[:], swsr[:, wt:wt + 1], -448.0, AL.mult, AL.max)
                    v.tensor_scalar(sq1[:], sq1[:], 448.0, None, AL.min, AL.bypass)
                    v.tensor_scalar(sq2[:], sq1[:], -240.0, 240.0, AL.max, AL.min)
                    v.tensor_copy(sf8[:], sq2[:]).then_inc(vb.h, vb.tick())
                    v.wait_ge(vb.h, vb.n)
                    v.tensor_copy(sq2[:], sf8[:])
                    v.tensor_tensor(ssq[:], sq1[:], sq1[:], AL.mult).then_inc(vb.h, vb.tick())
                    v.wait_ge(vb.h, vb.n)
                    v.tensor_scalar(smk[:], ssq[:], 224.0 * 224.0, None, AL.is_gt, AL.bypass)
                    v.tensor_scalar(ssq[:], sq1[:], DEK, None, AL.mult, AL.bypass)
                    v.tensor_tensor(sx[:], ssq[:], sq1[:], AL.subtract)
                    v.tensor_tensor(ssq[:], ssq[:], sx[:], AL.subtract).then_inc(vb.h, vb.tick())
                    v.wait_ge(vb.h, vb.n)
                    v.select(sq1[:], smk[:], ssq[:], sq2[:]).then_inc(vb.h, vb.tick())
                    v.wait_ge(vb.h, vb.n)
                    v.tensor_copy(sqb[:], sq1[:]).then_inc(sv.h, sv.tick())
                block.vector(_)

                for b8 in range(8):
                    def _(te, wt=wt, b8=b8):
                        te.wait_ge(sv.h, sv.n)
                        te.wait_ge(dg.h, dg.n)
                        for j in range(4):
                            c = b8 * 4 + j
                            ins = te.transpose(pTb[:, j * 128:(j + 1) * 128],
                                               sqb[:, c * 128:(c + 1) * 128], sidb[:])
                        ins.then_inc(st.h, st.tick())
                    block.tensor(_)

                    def _(v, wt=wt, b8=b8):
                        v.wait_ge(st.h, st.n)
                        v.tensor_copy(sqb2[:], pTb[:]).then_inc(sv.h, sv.tick())
                    block.vector(_)

                    def _(g, wt=wt, b8=b8):
                        g.wait_ge(sv.h, sv.n)
                        for j in range(4):
                            c = b8 * 4 + j
                            g.dma_start(out=w8t_sc[c * 128:(c + 1) * 128, wt * 128:(wt + 1) * 128],
                                        in_=sqb2[:, j * 128:(j + 1) * 128]).then_inc(dg.h, 16)
                            dg.inc(16)
                    block.gpsimd(_)

            with ExitStack() as pd:
                sw8 = sbuf("sw8", [128, 32 * 128], BF16, st=pd)
                t2f = sbuf("t2f", [128, 1024], st=pd)
                o16 = sbuf("o16", [128, 1024], BF16, st=pd)

                for do in range(32):
                    pp = (do % 2) * 1024
                    def _(sy, do=do):
                        sy.wait_ge(st.h, st.n)
                        src3 = w8t_sc[:, do * 128:(do + 1) * 128].rearrange(
                            "(c p) n -> p c n", p=128)
                        dst3 = sw8[:].rearrange("p (c n) -> p c n", c=32)
                        for grp in range(4):
                            sy.dma_start(out=dst3[:, grp * 8:(grp + 1) * 8, :],
                                         in_=src3[:, grp * 8:(grp + 1) * 8, :]).then_inc(dl.h, 16)
                            dl.inc(16)
                    block.sync(_)

                    def _(te, do=do, pp=pp):
                        te.wait_ge(dl.h, dl.n)
                        te.wait_ge(sv.h, sv.n)
                        for hf in range(2):
                            for di in range(32):
                                ins = te.matmul(pMM[:, pp + hf * 512:pp + hf * 512 + 512],
                                                sw8[:, di * 128:(di + 1) * 128],
                                                xqT[:, di * R + hf * 512:di * R + hf * 512 + 512],
                                                start=(di == 0), stop=(di == 31))
                        ins.then_inc(st.h, st.tick())
                    block.tensor(_)

                    def _(v, do=do, pp=pp):
                        v.wait_ge(st.h, st.n)
                        v.wait_ge(ss.h, ss.n)
                        v.scalar_tensor_tensor(t2f[:], pMM[:, pp:pp + 1024], sws[:, do:do + 1],
                                               bc_as[:, 0:1024], AL.mult,
                                               AL.mult).then_inc(sv.h, sv.tick())
                    block.vector(_)

                    def _(s, do=do):
                        s.wait_ge(sv.h, sv.n)
                        s.wait_ge(dg.h, dg.n)
                        s.activation(o16[:], t2f[:], AF.Identity, bias=sfb[:, do:do + 1],
                                     scale=1.0).then_inc(ss.h, ss.tick())
                    block.scalar(_)

                    def _(g, do=do):
                        g.wait_ge(ss.h, ss.n)
                        g.dma_start(out=ot_sc[do * 128:(do + 1) * 128, :], in_=o16[:]).then_inc(dg.h, 16)
                        dg.inc(16)
                    block.gpsimd(_)

                sot = sbuf("sot", [128, D], BF16, st=pd)
                sy_ = sbuf("sy", [128, D], st=pd)

                for tt in range(8):
                    def _(sy, tt=tt):
                        sy.wait_ge(dg.h, dg.n)
                        sy.wait_ge(sv.h, sv.n)
                        sy.dma_start(out=sx[:], in_=x2_sc[tt * 128:(tt + 1) * 128, :]).then_inc(dl.h, 16)
                        dl.inc(16)
                        sy.dma_start(out=sot[:], in_=ot_sc[:, tt * 128:(tt + 1) * 128],
                                     transpose=True).then_inc(dl.h, 16)
                        dl.inc(16)
                    block.sync(_)

                    def _(v, tt=tt):
                        v.wait_ge(dl.h, dl.n)
                        v.tensor_tensor(sy_[:], sx[:], sot[:], AL.add).then_inc(sv.h, sv.tick())
                    block.vector(_)

                    def _(g, tt=tt):
                        g.wait_ge(sv.h, sv.n)
                        g.dma_start(out=y_out[tt * 128:(tt + 1) * 128, :], in_=sy_[:]).then_inc(dg.h, 16)
                        dg.inc(16)
                        if tt == 7 and not DEBUG_DUMPS:
                            g.wait_ge(dg.h, dg.n)
                    block.gpsimd(_)

                if DEBUG_DUMPS:
                    dumps = [("d_qt", qt_sc, [D, R], F32), ("d_kin", kin, [D, R], F32),
                             ("d_vin", vin, [D, R], F32), ("d_kall", kall, [4, D, R], F32),
                             ("d_x2", x2_sc, [R, D], F32), ("d_ot", ot_sc, [D, R], BF16),
                             ("d_w8t", w8t_sc, [D, D], BF16)]
                    def _(g):
                        for nm, srcap, shp, dt in dumps:
                            dout = nc.dram_tensor(nm, shp, dt, kind="ExternalOutput").ap()
                            flat_in = srcap.rearrange("... -> (...)") if len(shp) > 2 else srcap
                            nchunk = 8
                            if len(shp) == 3:
                                for j in range(shp[0]):
                                    g.dma_start(out=dout[j], in_=srcap[j]).then_inc(dg.h, 16)
                                    dg.inc(16)
                            else:
                                rows = shp[0]
                                for j in range(nchunk):
                                    r0, r1 = rows * j // nchunk, rows * (j + 1) // nchunk
                                    g.dma_start(out=dout[r0:r1], in_=srcap[r0:r1]).then_inc(dg.h, 16)
                                    dg.inc(16)
                        g.wait_ge(dg.h, dg.n)
                    block.gpsimd(_)

    return nc


_NC_CACHE = [None]


def kernel(**inputs):
    x = np.asarray(inputs["L0_attn_hidden_states"], np.float32)
    wq = np.asarray(inputs["L0_attn_wq"], np.float32)
    wk = np.asarray(inputs["L0_attn_wk"], np.float32)
    wv = np.asarray(inputs["L0_attn_wv"], np.float32)
    wo = np.asarray(inputs["L0_attn_wo"], np.float32)
    bq = np.asarray(inputs["L0_attn_bq"], np.float32)
    bk = np.asarray(inputs["L0_attn_bk"], np.float32)
    bv = np.asarray(inputs["L0_attn_bv"], np.float32)
    bo = np.asarray(inputs["L0_attn_bo"], np.float32)
    fw_ = np.asarray(inputs["L0_ffn_weight"], np.float32)
    fb_ = np.asarray(inputs["L0_ffn_bias"], np.float32)

    wqT = np.ascontiguousarray(wq.T)
    wkT = np.ascontiguousarray(wk.T)
    wvT = np.ascontiguousarray(wv.T)
    woT = np.ascontiguousarray(wo.T)
    boR = np.ascontiguousarray(np.broadcast_to(bo, (128, D)))

    if _NC_CACHE[0] is None:
        _NC_CACHE[0] = build_program()
    nc = _NC_CACHE[0]

    in_maps = []
    for c in range(8):
        b = c // 4
        r0 = (c % 4) * R
        in_maps.append({
            "x_tok": np.ascontiguousarray(x[b, r0:r0 + R]),
            "wqT": wqT, "wkT": wkT, "wvT": wvT, "woT": woT,
            "bq": bq, "bk": bk, "bv": bv, "bo_rep": boR,
            "fw": fw_, "fb": fb_,
        })
    res = run_bass_kernel_spmd(nc, in_maps, core_ids=list(range(8)))
    out = np.empty((2, SEQ, D), np.float32)
    for c in range(8):
        b = c // 4
        r0 = (c % 4) * R
        out[b, r0:r0 + R] = res.results[c]["y"]
    return out


# version 06
# speedup vs baseline: 1.6516x; 1.1389x over previous
import math
from contextlib import ExitStack

import numpy as np
import ml_dtypes
import concourse.bass as bass
import concourse.mybir as mybir
from concourse.bass_utils import run_bass_kernel_spmd

F32 = mybir.dt.float32
BF16 = mybir.dt.bfloat16
FP16 = mybir.dt.float16
FP8 = mybir.dt.float8e4
I32 = mybir.dt.int32
AL = mybir.AluOpType
AF = mybir.ActivationFunctionType
AX = mybir.AxisListType

D = 4096
R = 1024
NH = 32
HD = 128
SEQ = 4096
TOPK = 819
QG = 512
NQG = R // QG
NRND = 11
NDESC = 1
ASCALE = float(np.float32(1.0 / math.sqrt(HD)))
EPS_LN = 1.0e-5
W0 = 64.0
LOFF = 64.5
INV448 = float(np.float32(1.0 / 448.0))
DEK = float(np.float32(2.0 ** 20 + 1.0))


class Cnt:
    def __init__(self, h):
        self.h = h
        self.n = 0

    def inc(self, k=1):
        self.n += k
        return self.n

    def tick(self):
        self.n += 1
        return 1


DEBUG_DUMPS = False


def build_program():
    nc = bass.Bass("TRN2", target_bir_lowering=False, debug=False,
                   detect_race_conditions=False)

    x_tok = nc.dram_tensor("x_tok", [R, D], F32, kind="ExternalInput").ap()
    wqT = nc.dram_tensor("wqT", [D, D], F32, kind="ExternalInput").ap()
    wkT = nc.dram_tensor("wkT", [D, D], F32, kind="ExternalInput").ap()
    wvT = nc.dram_tensor("wvT", [D, D], F32, kind="ExternalInput").ap()
    woT = nc.dram_tensor("woT", [D, D], F32, kind="ExternalInput").ap()
    bq_in = nc.dram_tensor("bq", [D], F32, kind="ExternalInput").ap()
    bk_in = nc.dram_tensor("bk", [D], F32, kind="ExternalInput").ap()
    bv_in = nc.dram_tensor("bv", [D], F32, kind="ExternalInput").ap()
    bo_rep = nc.dram_tensor("bo_rep", [128, D], F32, kind="ExternalInput").ap()
    fw = nc.dram_tensor("fw", [D, D], F32, kind="ExternalInput").ap()
    fb_in = nc.dram_tensor("fb", [D], F32, kind="ExternalInput").ap()
    y_out = nc.dram_tensor("y", [R, D], F32, kind="ExternalOutput").ap()

    qt_sc = nc.dram_tensor("qt_sc", [D, R], F32).ap()
    kin = nc.dram_tensor("kin", [D, R], F32).ap()
    vin = nc.dram_tensor("vin", [D, R], F32).ap()
    kall = nc.dram_tensor("kall", [4, D, R], F32).ap()
    vall = nc.dram_tensor("vall", [4, D, R], F32).ap()
    x2_sc = nc.dram_tensor("x2_sc", [R, D], F32).ap()
    w8t_sc = nc.dram_tensor("w8t_sc", [D, D], BF16).ap()
    ot_sc = nc.dram_tensor("ot_sc", [D, R], BF16).ap()

    ident_f = nc.inline_tensor(np.eye(128, dtype=np.float32), "ident_f").ap()
    ident_b = nc.inline_tensor(np.eye(128, dtype=ml_dtypes.bfloat16), "ident_b").ap()
    ones_f = nc.inline_tensor(np.ones((1, 128), dtype=np.float32), "ones_f").ap()

    es = ExitStack()
    with es:
        def sbuf(name, shape, dt=F32, st=None):
            return (st or es).enter_context(nc.sbuf_tensor(name, shape, dt))

        def sem(name):
            return Cnt(es.enter_context(nc.semaphore(name)))

        block = es.enter_context(nc.Block())

        dl = sem("dl")
        dg = sem("dg")
        sv = sem("sv")
        ss = sem("ss")
        st = sem("st")
        cc = sem("cc")
        vb = sem("vb")

        pMM = es.enter_context(nc.psum_tensor("pMM", [128, 2048], F32))
        pT = es.enter_context(nc.psum_tensor("pT", [128, 1024], F32))
        pCT = es.enter_context(nc.psum_tensor("pCT", [128, 128], F32))
        pTb = es.enter_context(nc.psum_tensor("pTb", [128, 512], BF16))

        sid = sbuf("sid", [128, 128])
        sidb = sbuf("sidb", [128, 128], BF16)
        sones = sbuf("sones", [1, 128])
        sbq = sbuf("sbq", [128, 32])
        sbk = sbuf("sbk", [128, 32])
        sbv = sbuf("sbv", [128, 32])
        sfb = sbuf("sfb", [128, 32])
        sfb16 = sbuf("sfb16", [128, 32], BF16)
        sws = sbuf("sws", [128, 32])
        swsr = sbuf("swsr", [128, 32])
        sws16 = sbuf("sws16", [128, 32], FP16)
        sasr = sbuf("sasr", [128, 8])
        sasrow = sbuf("sasrow", [1, 1024])
        bc_as = sbuf("bc_as", [128, 1024])
        stny = sbuf("stny", [128, 48])
        cst = sbuf("cst", [128, 1])

        T = stny

        def _(sy):
            sy.dma_start(out=sid[:], in_=ident_f).then_inc(dl.h, 16); dl.inc(16)
            sy.dma_start(out=sidb[:], in_=ident_b).then_inc(dl.h, 16); dl.inc(16)
            sy.dma_start(out=sones[:], in_=ones_f).then_inc(dl.h, 16); dl.inc(16)
            for dst, src in ((sbq, bq_in), (sbk, bk_in), (sbv, bv_in), (sfb, fb_in)):
                sy.dma_start(out=dst[:], in_=src.rearrange("(c p) -> p c", p=128),
                             allow_slow_non_contiguous=True).then_inc(dl.h, 16)
                dl.inc(16)
        block.sync(_)

        def _(v):
            v.wait_ge(dl.h, dl.n)
            v.memset(cst[:], ASCALE)
            v.tensor_scalar(sbq[:], sbq[:], ASCALE, None, AL.mult, AL.bypass)
            v.tensor_copy(sfb16[:], sfb[:]).then_inc(vb.h, vb.tick())
            v.wait_ge(vb.h, vb.n)
            v.tensor_copy(sfb[:], sfb16[:]).then_inc(sv.h, sv.tick())
        block.vector(_)

        with ExitStack() as pa:
            sx = sbuf("sx", [128, D], st=pa)
            ssq = sbuf("ssq", [128, D], st=pa)
            sxT = sbuf("sxT", [128, 32 * R], st=pa)
            sev = sbuf("sev", [128, 1024], st=pa)
            wcol = sbuf("wcol", [128, 32 * 128], st=pa)

            _dxh_ap = (nc.dram_tensor("d_xh", [R, D], F32, kind="ExternalOutput").ap()
                       if DEBUG_DUMPS else None)

            for tt in range(8):
                def _(sy, tt=tt):
                    if tt > 0:
                        sy.wait_ge(sv.h, sv.n)
                    sy.dma_start(out=sx[:], in_=x_tok[tt * 128:(tt + 1) * 128, :]).then_inc(dl.h, 16)
                    dl.inc(16)
                block.sync(_)

                def _(v):
                    v.wait_ge(dl.h, dl.n)
                    v.tensor_reduce(T[:, 0:1], sx[:], AX.X, AL.add).then_inc(vb.h, vb.tick())
                    v.wait_ge(vb.h, vb.n)
                    v.tensor_scalar(T[:, 0:1], T[:, 0:1], 1.0 / D, None, AL.mult,
                                    AL.bypass).then_inc(sv.h, sv.tick())
                block.vector(_)

                def _(s):
                    s.wait_ge(sv.h, sv.n)
                    s.wait_ge(st.h, st.n)
                    s.activation(ssq[:], sx[:], AF.Square, bias=0.0, scale=1.0,
                                 accum_out=T[:, 1:2]).then_inc(ss.h, ss.tick())
                block.scalar(_)

                def _(v):
                    v.wait_ge(ss.h, ss.n)
                    v.tensor_scalar(T[:, 1:2], T[:, 1:2], 1.0 / D, None, AL.mult, AL.bypass)
                    v.tensor_tensor(T[:, 2:3], T[:, 0:1], T[:, 0:1], AL.mult).then_inc(vb.h, vb.tick())
                    v.wait_ge(vb.h, vb.n)
                    v.tensor_tensor(T[:, 1:2], T[:, 1:2], T[:, 2:3], AL.subtract).then_inc(vb.h, vb.tick())
                    v.wait_ge(vb.h, vb.n)
                    v.tensor_scalar(T[:, 1:2], T[:, 1:2], EPS_LN, None, AL.add,
                                    AL.bypass).then_inc(sv.h, sv.tick())
                block.vector(_)

                def _(s):
                    s.wait_ge(sv.h, sv.n)
                    s.activation(T[:, 2:3], T[:, 1:2], AF.Sqrt, bias=0.0,
                                 scale=1.0).then_inc(ss.h, ss.tick())
                block.scalar(_)

                def _(v):
                    v.wait_ge(ss.h, ss.n)
                    v.reciprocal(T[:, 3:4], T[:, 2:3]).then_inc(vb.h, vb.tick())
                    v.wait_ge(vb.h, vb.n)
                    v.tensor_scalar(ssq[:], sx[:], T[:, 0:1], T[:, 3:4], AL.subtract,
                                    AL.mult).then_inc(sv.h, sv.tick())
                block.vector(_)

                if DEBUG_DUMPS:
                    def _(g, tt=tt):
                        g.wait_ge(sv.h, sv.n)
                        g.dma_start(out=_dxh_ap[tt * 128:(tt + 1) * 128, :],
                                    in_=ssq[:]).then_inc(dg.h, 16)
                        dg.inc(16)
                        g.wait_ge(dg.h, dg.n)
                    block.gpsimd(_)

                for q4 in range(8):
                    ph = (q4 % 2) * 512
                    def _(te, tt=tt, q4=q4, ph=ph):
                        te.wait_ge(sv.h, sv.n)
                        for j in range(4):
                            c = q4 * 4 + j
                            ins = te.transpose(pT[:, ph + j * 128: ph + (j + 1) * 128],
                                               ssq[:, c * 128:(c + 1) * 128], sid[:])
                        ins.then_inc(st.h, st.tick())
                    block.tensor(_)

                    def _(v, tt=tt, q4=q4, ph=ph):
                        v.wait_ge(st.h, st.n)
                        for j in range(4):
                            c = q4 * 4 + j
                            ins = v.tensor_copy(
                                sxT[:, c * R + tt * 128: c * R + (tt + 1) * 128],
                                pT[:, ph + j * 128: ph + (j + 1) * 128])
                        ins.then_inc(sv.h, sv.tick())
                    block.vector(_)

            if DEBUG_DUMPS:
                d_xT = nc.dram_tensor("d_xT", [128, 32 * R], F32, kind="ExternalOutput").ap()

                def _(g):
                    g.wait_ge(sv.h, sv.n)
                    g.dma_start(out=d_xT[:], in_=sxT[:]).then_inc(dg.h, 16)
                    dg.inc(16)
                    g.wait_ge(dg.h, dg.n)
                block.gpsimd(_)

            projs = [(wqT, sbq, qt_sc, ASCALE), (wkT, sbk, kin, 1.0), (wvT, sbv, vin, 1.0)]
            for pi, (wT, sbias, dst, scale) in enumerate(projs):
                for do in range(32):
                    ph = (do % 2) * 1024
                    def _(sy, wT=wT, do=do):
                        sy.wait_ge(st.h, st.n)
                        src3 = wT[:, do * 128:(do + 1) * 128].rearrange("(c p) n -> p c n", p=128)
                        dst3 = wcol[:].rearrange("p (c n) -> p c n", c=32)
                        for grp in range(4):
                            sy.dma_start(out=dst3[:, grp * 8:(grp + 1) * 8, :],
                                         in_=src3[:, grp * 8:(grp + 1) * 8, :]).then_inc(dl.h, 16)
                            dl.inc(16)
                    block.sync(_)

                    def _(te, do=do, ph=ph):
                        te.wait_ge(dl.h, dl.n)
                        te.wait_ge(sv.h, sv.n)
                        te.wait_ge(ss.h, ss.n)
                        for half in range(2):
                            for di in range(32):
                                ins = te.matmul(
                                    pMM[:, ph + half * 512: ph + half * 512 + 512],
                                    wcol[:, di * 128:(di + 1) * 128],
                                    sxT[:, di * R + half * 512: di * R + half * 512 + 512],
                                    start=(di == 0), stop=(di == 31))
                        ins.then_inc(st.h, st.tick())
                    block.tensor(_)

                    def _(s, do=do, sbias=sbias, scale=scale, ph=ph):
                        s.wait_ge(st.h, st.n)
                        s.wait_ge(dg.h, dg.n)
                        sc = 1.0 if scale == 1.0 else cst[:, 0:1]
                        s.activation(sev[:, 0:512], pMM[:, ph:ph + 512], AF.Identity,
                                     bias=sbias[:, do:do + 1], scale=sc)
                        s.activation(sev[:, 512:1024], pMM[:, ph + 512:ph + 1024], AF.Identity,
                                     bias=sbias[:, do:do + 1], scale=sc).then_inc(ss.h, ss.tick())
                    block.scalar(_)

                    def _(g, do=do, dst=dst):
                        g.wait_ge(ss.h, ss.n)
                        g.dma_start(out=dst[do * 128:(do + 1) * 128, :], in_=sev[:]).then_inc(dg.h, 16)
                        dg.inc(16)
                    block.gpsimd(_)

        def _(g):
            g.wait_ge(dg.h, dg.n)
            g.collective_compute("AllGather", AL.bypass,
                                 replica_groups=[[0, 1, 2, 3], [4, 5, 6, 7]],
                                 ins=[kin.opt()], outs=[kall.opt()]).then_inc(cc.h, cc.tick())
            g.collective_compute("AllGather", AL.bypass,
                                 replica_groups=[[0, 1, 2, 3], [4, 5, 6, 7]],
                                 ins=[vin.opt()], outs=[vall.opt()]).then_inc(cc.h, cc.tick())
        block.gpsimd(_)

        def _(sy):
            sy.wait_ge(cc.h, cc.n)
            sy.sem_inc(dl.h, 16)
            dl.inc(16)
        block.sync(_)

        with ExitStack() as pb:
            ctxT = sbuf("ctxT", [128, 32 * QG], st=pb)

            for qg in range(NQG):
                with ExitStack() as ph_heads:
                    kTh = sbuf(f"kTh_{qg}", [128, SEQ], st=ph_heads)
                    vTh = sbuf(f"vTh_{qg}", [128, SEQ], st=ph_heads)
                    vtok = sbuf(f"vtok_{qg}", [128, 32 * 128], st=ph_heads)
                    qTh = sbuf(f"qTh_{qg}", [128, QG], st=ph_heads)
                    sS = sbuf(f"sS_{qg}", [128, SEQ], st=ph_heads)
                    sEs = [sbuf(f"sE{p}_{qg}", [128, SEQ], st=ph_heads) for p in range(2)]
                    jkb = sbuf(f"jkb_{qg}", [128, SEQ], BF16, st=ph_heads)
                    jks = sbuf(f"jks_{qg}", [128, SEQ], BF16, st=ph_heads)
                    spt = sbuf(f"spt_{qg}", [128, 1024], st=ph_heads)

                    NSUB = NH * 4
                    snap_pmm_free = 0
                    snap_P = {}
                    snap_pv_mm = {}
                    snap_ctx_free = 0

                    def emit_head_loads(h, qg=qg):
                        def _(sy, h=h, qg=qg):
                            sy.wait_ge(st.h, st.n)
                            for g4 in range(4):
                                sy.dma_start(out=kTh[:, g4 * R:(g4 + 1) * R],
                                             in_=kall[g4, h * 128:(h + 1) * 128, :]).then_inc(dl.h, 16)
                                dl.inc(16)
                                sy.dma_start(out=vTh[:, g4 * R:(g4 + 1) * R],
                                             in_=vall[g4, h * 128:(h + 1) * 128, :]).then_inc(dl.h, 16)
                                dl.inc(16)
                            sy.dma_start(out=qTh[:],
                                         in_=qt_sc[h * 128:(h + 1) * 128,
                                                   qg * QG:(qg + 1) * QG]).then_inc(dl.h, 16)
                            dl.inc(16)
                        block.sync(_)

                    def emit_vtrans(h):
                        for b8 in range(8):
                            ph = (b8 % 2) * 512
                            def _(te, b8=b8, ph=ph):
                                te.wait_ge(dl.h, dl.n)
                                te.wait_ge(sv.h, sv.n)
                                for j in range(4):
                                    c = b8 * 4 + j
                                    ins = te.transpose(pT[:, ph + j * 128:ph + (j + 1) * 128],
                                                       vTh[:, c * 128:(c + 1) * 128], sid[:])
                                ins.then_inc(st.h, st.tick())
                            block.tensor(_)

                            def _(v, b8=b8, ph=ph):
                                v.wait_ge(st.h, st.n)
                                v.tensor_copy(vtok[:, b8 * 512:(b8 + 1) * 512],
                                              pT[:, ph:ph + 512]).then_inc(sv.h, sv.tick())
                            block.vector(_)

                    def emit_scores_te(i):
                        h, qt = divmod(i, 4)
                        def _h0(te, h=h, qt=qt, w=snap_pmm_free):
                            te.wait_ge(dl.h, dl.n)
                            te.wait_ge(sv.h, w)
                            for j in range(4):
                                ins = te.matmul(pMM[:, j * 512:(j + 1) * 512],
                                                qTh[:, qt * 128:(qt + 1) * 128],
                                                kTh[:, j * 512:(j + 1) * 512],
                                                start=True, stop=True)
                            ins.then_inc(st.h, st.tick())
                        block.tensor(_h0)

                    def emit_evac_half(i, half):
                        def _(v, i=i, half=half):
                            v.wait_ge(st.h, st.n)
                            v.tensor_scalar(sS[:, half * 2048:(half + 1) * 2048],
                                            pMM[:, 0:2048], 0.0, None, AL.add, AL.max,
                                            accum_out=T[:, 16 + half:17 + half]).then_inc(
                                sv.h, sv.tick())
                        block.vector(_)

                    def emit_scores_half1(i):
                        h, qt = divmod(i, 4)
                        def _(te, h=h, qt=qt):
                            te.wait_ge(sv.h, sv.n)
                            for j in range(4):
                                ins = te.matmul(pMM[:, j * 512:(j + 1) * 512],
                                                qTh[:, qt * 128:(qt + 1) * 128],
                                                kTh[:, 2048 + j * 512: 2048 + (j + 1) * 512],
                                                start=True, stop=True)
                            ins.then_inc(st.h, st.tick())
                        block.tensor(_)

                    def emit_pv(i):
                        p = i % 2
                        sE = sEs[p]
                        for grp in range(8):
                            ph = (grp % 2) * 512
                            def _t(te, grp=grp, ph=ph, sE=sE):
                                te.wait_ge(sv.h, sv.n)
                                for j in range(4):
                                    ck = grp * 4 + j
                                    ins = te.transpose(pT[:, ph + j * 128: ph + (j + 1) * 128],
                                                       sE[:, ck * 128:(ck + 1) * 128], sid[:])
                                ins.then_inc(st.h, st.tick())
                            block.tensor(_t)

                            def _c(v, grp=grp, ph=ph):
                                v.wait_ge(st.h, st.n)
                                v.tensor_copy(spt[:, ph:ph + 512],
                                              pT[:, ph:ph + 512]).then_inc(sv.h, sv.tick())
                            block.vector(_c)

                            def _m(te, grp=grp, ph=ph):
                                te.wait_ge(sv.h, sv.n)
                                for j in range(4):
                                    ck = grp * 4 + j
                                    ins = te.matmul(pCT[:], vtok[:, ck * 128:(ck + 1) * 128],
                                                    spt[:, ph + j * 128: ph + (j + 1) * 128],
                                                    start=(ck == 0), stop=(ck == 31))
                                ins.then_inc(st.h, st.tick())
                            block.tensor(_m)
                        snap_pv_mm[i] = st.n

                    def emit_ctx_evac(i):
                        h, qt = divmod(i, 4)
                        def _(v, h=h, qt=qt, w=snap_pv_mm[i]):
                            v.wait_ge(st.h, w)
                            v.tensor_copy(ctxT[:, h * QG + qt * 128: h * QG + (qt + 1) * 128],
                                          pCT[:]).then_inc(sv.h, sv.tick())
                        block.vector(_)

                    def emit_negmax_exp(i):
                        p = i % 2
                        def _(v):
                            v.wait_ge(sv.h, sv.n)
                            v.tensor_tensor(T[:, 4:5], T[:, 16:17], T[:, 17:18],
                                            AL.max).then_inc(vb.h, vb.tick())
                            v.wait_ge(vb.h, vb.n)
                            v.tensor_scalar(T[:, 4:5], T[:, 4:5], -1.0, None, AL.mult,
                                            AL.bypass).then_inc(sv.h, sv.tick())
                        block.vector(_)

                        def _(s, p=p):
                            s.wait_ge(sv.h, sv.n)
                            s.wait_ge(st.h, st.n)
                            s.activation(sEs[p][:], sS[:], AF.Exp, bias=T[:, 4:5],
                                         scale=1.0).then_inc(ss.h, ss.tick())
                        block.scalar(_)

                    def emit_bisect(i):
                        def _(v):
                            v.wait_ge(ss.h, ss.n)
                            v.tensor_scalar(T[:, 5:6], T[:, 4:5], -1.0, -LOFF, AL.mult,
                                            AL.add).then_inc(vb.h, vb.tick())
                            v.wait_ge(vb.h, vb.n)
                        block.vector(_)

                        for r in range(NRND):
                            w3 = W0 / (3.0 ** (r + 1))
                            def _(v, w3=w3):
                                v.tensor_scalar(T[:, 6:7], T[:, 5:6], w3, None, AL.add, AL.bypass)
                                v.tensor_scalar(T[:, 12:13], T[:, 5:6], -1.0, -2.0 * w3,
                                                AL.mult, AL.add).then_inc(sv.h, sv.tick())
                                v.tensor_scalar(jkb[:], sS[:], T[:, 6:7], None, AL.is_ge,
                                                AL.add, accum_out=T[:, 7:8]).then_inc(vb.h, vb.tick())
                            block.vector(_)

                            def _(s, w3=w3):
                                s.wait_ge(sv.h, sv.n)
                                s.activation(jks[:], sS[:], AF.Sign, bias=T[:, 12:13],
                                             scale=1.0, accum_out=T[:, 18:19]).then_inc(ss.h, ss.tick())
                            block.scalar(_)

                            def _(v, w3=w3):
                                v.wait_ge(vb.h, vb.n)
                                v.tensor_scalar(T[:, 8:9], T[:, 7:8], TOPK - 0.5, None,
                                                AL.is_ge, AL.bypass)
                                v.wait_ge(ss.h, ss.n)
                                v.tensor_scalar(T[:, 19:20], T[:, 18:19],
                                                2.0 * TOPK - SEQ - 0.5, None,
                                                AL.is_ge, AL.bypass).then_inc(vb.h, vb.tick())
                                v.wait_ge(vb.h, vb.n)
                                v.tensor_tensor(T[:, 8:9], T[:, 8:9], T[:, 19:20],
                                                AL.add).then_inc(vb.h, vb.tick())
                                v.wait_ge(vb.h, vb.n)
                                v.scalar_tensor_tensor(T[:, 5:6], T[:, 8:9], w3, T[:, 5:6],
                                                       AL.mult, AL.add).then_inc(vb.h, vb.tick())
                                v.wait_ge(vb.h, vb.n)
                            block.vector(_)

                    def emit_desc_mask(i):
                        p = i % 2
                        sE = sEs[p]
                        wfin = W0 / (3.0 ** NRND)
                        def _(v, wfin=wfin):
                            v.tensor_scalar(T[:, 6:7], T[:, 5:6], wfin, None, AL.add,
                                            AL.bypass).then_inc(sv.h, sv.tick())
                        block.vector(_)

                        def _(s):
                            s.wait_ge(sv.h, sv.n)
                            s.activation(T[:, 9:11], T[:, 5:7], AF.Exp, bias=T[:, 4:5],
                                         scale=1.0).then_inc(ss.h, ss.tick())
                        block.scalar(_)

                        def _(v):
                            v.wait_ge(ss.h, ss.n)
                            v.memset(T[:, 13:14], 0.0)
                            v.memset(T[:, 11:12], 0.0)
                        block.vector(_)

                        for d in range(NDESC):
                            def _(v, sE=sE):
                                v.scalar_tensor_tensor(sS[:], sE[:], T[:, 10:11], sE[:],
                                                       AL.is_lt, AL.mult).then_inc(vb.h, vb.tick())
                                v.wait_ge(vb.h, vb.n)
                                v.tensor_reduce(T[:, 12:13], sS[:], AX.X,
                                                AL.max).then_inc(vb.h, vb.tick())
                                v.wait_ge(vb.h, vb.n)
                                v.tensor_scalar(jkb[:], sE[:], T[:, 12:13], None, AL.is_ge,
                                                AL.add, accum_out=T[:, 7:8]).then_inc(vb.h, vb.tick())
                                v.wait_ge(vb.h, vb.n)
                                v.tensor_scalar(T[:, 8:9], T[:, 7:8], TOPK - 0.5, None,
                                                AL.is_ge, AL.bypass)
                                v.tensor_scalar(T[:, 14:15], T[:, 11:12], 0.5, None,
                                                AL.is_lt, AL.bypass).then_inc(vb.h, vb.tick())
                                v.wait_ge(vb.h, vb.n)
                                v.tensor_tensor(T[:, 14:15], T[:, 14:15], T[:, 8:9],
                                                AL.mult).then_inc(vb.h, vb.tick())
                                v.wait_ge(vb.h, vb.n)
                                v.scalar_tensor_tensor(T[:, 13:14], T[:, 12:13], T[:, 14:15],
                                                       T[:, 13:14], AL.mult, AL.add)
                                v.tensor_tensor(T[:, 11:12], T[:, 11:12], T[:, 8:9], AL.max)
                                v.tensor_copy(T[:, 10:11], T[:, 12:13]).then_inc(vb.h, vb.tick())
                                v.wait_ge(vb.h, vb.n)
                            block.vector(_)

                        def _(v):
                            v.tensor_scalar(T[:, 14:15], T[:, 11:12], 0.5, None,
                                            AL.is_lt, AL.bypass).then_inc(vb.h, vb.tick())
                            v.wait_ge(vb.h, vb.n)
                            v.scalar_tensor_tensor(T[:, 13:14], T[:, 9:10], T[:, 14:15],
                                                   T[:, 13:14], AL.mult, AL.add).then_inc(vb.h, vb.tick())
                            v.wait_ge(vb.h, vb.n)
                        block.vector(_)

                        def _(v, sE=sE):
                            v.scalar_tensor_tensor(sS[:], sE[:], T[:, 13:14], sE[:],
                                                   AL.is_ge, AL.mult,
                                                   accum_out=T[:, 14:15]).then_inc(vb.h, vb.tick())
                            v.wait_ge(vb.h, vb.n)
                            v.reciprocal(T[:, 15:16], T[:, 14:15]).then_inc(vb.h, vb.tick())
                            v.wait_ge(vb.h, vb.n)
                            v.tensor_scalar(sE[:], sS[:], T[:, 15:16], None,
                                            AL.mult, AL.bypass).then_inc(sv.h, sv.tick())
                        block.vector(_)

                    for i in range(NSUB):
                        h, qt = divmod(i, 4)
                        if qt == 0:
                            emit_head_loads(h)
                            if i > 0:
                                emit_pv(i - 1)
                            emit_vtrans(h)
                            emit_scores_te(i)
                        else:
                            emit_scores_te(i)
                            emit_pv(i - 1)
                        emit_evac_half(i, 0)
                        emit_scores_half1(i)
                        emit_evac_half(i, 1)
                        snap_pmm_free = sv.n
                        emit_negmax_exp(i)
                        emit_bisect(i)
                        if i > 0:
                            emit_ctx_evac(i - 1)
                            snap_ctx_free = sv.n
                        emit_desc_mask(i)
                        snap_P[i] = sv.n
                    emit_pv(NSUB - 1)
                    emit_ctx_evac(NSUB - 1)

                with ExitStack() as po:
                    swo = sbuf(f"swo_{qg}", [128, 32 * 512], st=po)
                    sbo = sbuf(f"sbo_{qg}", [128, 512], st=po)
                    sxr = sbuf(f"sxr_{qg}", [128, 512], st=po)
                    sev2 = sbuf(f"sev2_{qg}", [128, 512], st=po)

                    for dk in range(8):
                        def _(sy, dk=dk):
                            sy.wait_ge(st.h, st.n)
                            src3 = woT[:, dk * 512:(dk + 1) * 512].rearrange(
                                "(c p) n -> p c n", p=128)
                            dst3 = swo[:].rearrange("p (c n) -> p c n", c=32)
                            for grp in range(8):
                                sy.dma_start(out=dst3[:, grp * 4:(grp + 1) * 4, :],
                                             in_=src3[:, grp * 4:(grp + 1) * 4, :]).then_inc(dl.h, 16)
                                dl.inc(16)
                            sy.dma_start(out=sbo[:], in_=bo_rep[:, dk * 512:(dk + 1) * 512]).then_inc(dl.h, 16)
                            dl.inc(16)
                        block.sync(_)

                        for qt in range(QG // 128):
                            pp = (qt % 2) * 512
                            def _(sy, dk=dk, qt=qt, qg=qg):
                                sy.wait_ge(sv.h, sv.n)
                                row0 = qg * QG + qt * 128
                                sy.dma_start(out=sxr[:],
                                             in_=x_tok[row0:row0 + 128,
                                                       dk * 512:(dk + 1) * 512]).then_inc(dl.h, 16)
                                dl.inc(16)
                            block.sync(_)

                            def _(te, dk=dk, qt=qt, pp=pp):
                                te.wait_ge(dl.h, dl.n)
                                te.wait_ge(sv.h, sv.n)
                                for dc in range(32):
                                    ins = te.matmul(pMM[:, pp:pp + 512],
                                                    ctxT[:, dc * QG + qt * 128: dc * QG + (qt + 1) * 128],
                                                    swo[:, dc * 512:(dc + 1) * 512],
                                                    start=(dc == 0), stop=(dc == 31))
                                ins.then_inc(st.h, st.tick())
                            block.tensor(_)

                            def _(v, dk=dk, qt=qt, pp=pp):
                                v.wait_ge(st.h, st.n)
                                v.wait_ge(dg.h, dg.n)
                                v.tensor_tensor(sev2[:], sbo[:], sxr[:], AL.add)
                                v.tensor_tensor(sev2[:], sev2[:], pMM[:, pp:pp + 512],
                                                AL.add).then_inc(sv.h, sv.tick())
                            block.vector(_)

                            def _(g, dk=dk, qt=qt, qg=qg):
                                g.wait_ge(sv.h, sv.n)
                                row0 = qg * QG + qt * 128
                                g.dma_start(out=x2_sc[row0:row0 + 128, dk * 512:(dk + 1) * 512],
                                            in_=sev2[:]).then_inc(dg.h, 16)
                                dg.inc(16)
                            block.gpsimd(_)

        with ExitStack() as pc:
            sx = sbuf("sxc", [128, D], st=pc)
            ssq = sbuf("ssqc", [128, D], st=pc)
            sq1 = sbuf("sq1", [128, D], st=pc)
            sq2 = sbuf("sq2", [128, D], st=pc)
            sf8 = sbuf("sf8", [128, D], FP8, st=pc)
            smk = sbuf("smk", [128, D], I32, st=pc)
            sqb = sbuf("sqb", [128, D], BF16, st=pc)
            sqb2 = sbuf("sqb2", [128, 512], BF16, st=pc)
            xqT = sbuf("xqT", [128, 32 * R], BF16, st=pc)

            for tt in range(8):
                def _(sy, tt=tt):
                    sy.wait_ge(st.h, st.n)
                    sy.wait_ge(dg.h, dg.n)
                    sy.dma_start(out=sx[:], in_=x2_sc[tt * 128:(tt + 1) * 128, :]).then_inc(dl.h, 16)
                    dl.inc(16)
                block.sync(_)

                def _(v):
                    v.wait_ge(dl.h, dl.n)
                    v.tensor_reduce(T[:, 0:1], sx[:], AX.X, AL.add).then_inc(vb.h, vb.tick())
                    v.wait_ge(vb.h, vb.n)
                    v.tensor_scalar(T[:, 0:1], T[:, 0:1], 1.0 / D, None, AL.mult,
                                    AL.bypass).then_inc(sv.h, sv.tick())
                block.vector(_)

                def _(s):
                    s.wait_ge(sv.h, sv.n)
                    s.activation(ssq[:], sx[:], AF.Square, bias=0.0, scale=1.0,
                                 accum_out=T[:, 1:2]).then_inc(ss.h, ss.tick())
                block.scalar(_)

                def _(v):
                    v.wait_ge(ss.h, ss.n)
                    v.tensor_scalar(T[:, 1:2], T[:, 1:2], 1.0 / D, None, AL.mult, AL.bypass)
                    v.tensor_tensor(T[:, 2:3], T[:, 0:1], T[:, 0:1], AL.mult).then_inc(vb.h, vb.tick())
                    v.wait_ge(vb.h, vb.n)
                    v.tensor_tensor(T[:, 1:2], T[:, 1:2], T[:, 2:3], AL.subtract).then_inc(vb.h, vb.tick())
                    v.wait_ge(vb.h, vb.n)
                    v.tensor_scalar(T[:, 1:2], T[:, 1:2], EPS_LN, None, AL.add,
                                    AL.bypass).then_inc(sv.h, sv.tick())
                block.vector(_)

                def _(s):
                    s.wait_ge(sv.h, sv.n)
                    s.activation(T[:, 2:3], T[:, 1:2], AF.Sqrt, bias=0.0,
                                 scale=1.0).then_inc(ss.h, ss.tick())
                block.scalar(_)

                def _(v, tt=tt):
                    v.wait_ge(ss.h, ss.n)
                    v.reciprocal(T[:, 3:4], T[:, 2:3]).then_inc(vb.h, vb.tick())
                    v.wait_ge(vb.h, vb.n)
                    v.tensor_scalar(ssq[:], sx[:], T[:, 0:1], T[:, 3:4], AL.subtract, AL.mult)
                    v.tensor_reduce(T[:, 5:6], ssq[:], AX.X, AL.max,
                                    apply_absolute_value=True).then_inc(vb.h, vb.tick())
                    v.wait_ge(vb.h, vb.n)
                    v.tensor_scalar(T[:, 5:6], T[:, 5:6], INV448, 1e-6, AL.mult,
                                    AL.max).then_inc(vb.h, vb.tick())
                    v.wait_ge(vb.h, vb.n)
                    v.tensor_copy(sasr[:, tt:tt + 1], T[:, 5:6])
                    v.reciprocal(T[:, 6:7], T[:, 5:6]).then_inc(vb.h, vb.tick())
                    v.wait_ge(vb.h, vb.n)
                    v.tensor_scalar(sq1[:], ssq[:], T[:, 6:7], -448.0, AL.mult, AL.max)
                    v.tensor_scalar(sq1[:], sq1[:], 448.0, None, AL.min, AL.bypass)
                    v.tensor_scalar(sq2[:], sq1[:], -240.0, 240.0, AL.max, AL.min)
                    v.tensor_copy(sf8[:], sq2[:]).then_inc(vb.h, vb.tick())
                    v.wait_ge(vb.h, vb.n)
                    v.tensor_copy(sq2[:], sf8[:])
                    v.tensor_tensor(ssq[:], sq1[:], sq1[:], AL.mult).then_inc(vb.h, vb.tick())
                    v.wait_ge(vb.h, vb.n)
                    v.tensor_scalar(smk[:], ssq[:], 224.0 * 224.0, None, AL.is_gt, AL.bypass)
                    v.tensor_scalar(ssq[:], sq1[:], DEK, None, AL.mult, AL.bypass)
                    v.tensor_tensor(sx[:], ssq[:], sq1[:], AL.subtract)
                    v.tensor_tensor(ssq[:], ssq[:], sx[:], AL.subtract).then_inc(vb.h, vb.tick())
                    v.wait_ge(vb.h, vb.n)
                    v.select(sq1[:], smk[:], ssq[:], sq2[:]).then_inc(vb.h, vb.tick())
                    v.wait_ge(vb.h, vb.n)
                    v.tensor_copy(sqb[:], sq1[:]).then_inc(sv.h, sv.tick())
                block.vector(_)

                for b8 in range(8):
                    def _(te, tt=tt, b8=b8):
                        te.wait_ge(sv.h, sv.n)
                        for j in range(4):
                            c = b8 * 4 + j
                            ins = te.transpose(pTb[:, j * 128:(j + 1) * 128],
                                               sqb[:, c * 128:(c + 1) * 128], sidb[:])
                        ins.then_inc(st.h, st.tick())
                    block.tensor(_)

                    def _(v, tt=tt, b8=b8):
                        v.wait_ge(st.h, st.n)
                        for j in range(4):
                            c = b8 * 4 + j
                            ins = v.tensor_copy(
                                xqT[:, c * R + tt * 128: c * R + (tt + 1) * 128],
                                pTb[:, j * 128:(j + 1) * 128])
                        ins.then_inc(sv.h, sv.tick())
                    block.vector(_)

                def _(te, tt=tt):
                    te.wait_ge(sv.h, sv.n)
                    te.transpose(pT[0:1, 0:128], sasr[:, tt:tt + 1], sid[:]).then_inc(st.h, st.tick())
                block.tensor(_)

                def _(v, tt=tt):
                    v.wait_ge(st.h, st.n)
                    v.tensor_copy(sasrow[0:1, tt * 128:(tt + 1) * 128],
                                  pT[0:1, 0:128]).then_inc(sv.h, sv.tick())
                block.vector(_)

            def _(te):
                te.wait_ge(sv.h, sv.n)
                for half in range(2):
                    ins = te.matmul(pMM[:, half * 512:(half + 1) * 512], sones[:],
                                    sasrow[0:1, half * 512:(half + 1) * 512],
                                    start=True, stop=True)
                ins.then_inc(st.h, st.tick())
            block.tensor(_)

            def _(v):
                v.wait_ge(st.h, st.n)
                v.tensor_copy(bc_as[:, 0:1024], pMM[:, 0:1024]).then_inc(sv.h, sv.tick())
            block.vector(_)

            for wt in range(32):
                def _(sy, wt=wt):
                    sy.wait_ge(st.h, st.n)
                    sy.dma_start(out=sx[:], in_=fw[wt * 128:(wt + 1) * 128, :]).then_inc(dl.h, 16)
                    dl.inc(16)
                block.sync(_)

                def _(v, wt=wt):
                    v.wait_ge(dl.h, dl.n)
                    v.tensor_reduce(T[:, 0:1], sx[:], AX.X, AL.max,
                                    apply_absolute_value=True).then_inc(vb.h, vb.tick())
                    v.wait_ge(vb.h, vb.n)
                    v.tensor_scalar(T[:, 0:1], T[:, 0:1], INV448, 1e-6, AL.mult,
                                    AL.max).then_inc(vb.h, vb.tick())
                    v.wait_ge(vb.h, vb.n)
                    v.tensor_copy(sws16[:, wt:wt + 1], T[:, 0:1]).then_inc(vb.h, vb.tick())
                    v.wait_ge(vb.h, vb.n)
                    v.tensor_copy(sws[:, wt:wt + 1], sws16[:, wt:wt + 1]).then_inc(vb.h, vb.tick())
                    v.wait_ge(vb.h, vb.n)
                    v.reciprocal(swsr[:, wt:wt + 1], sws[:, wt:wt + 1]).then_inc(vb.h, vb.tick())
                    v.wait_ge(vb.h, vb.n)
                    v.tensor_scalar(sq1[:], sx[:], swsr[:, wt:wt + 1], -448.0, AL.mult, AL.max)
                    v.tensor_scalar(sq1[:], sq1[:], 448.0, None, AL.min, AL.bypass)
                    v.tensor_scalar(sq2[:], sq1[:], -240.0, 240.0, AL.max, AL.min)
                    v.tensor_copy(sf8[:], sq2[:]).then_inc(vb.h, vb.tick())
                    v.wait_ge(vb.h, vb.n)
                    v.tensor_copy(sq2[:], sf8[:])
                    v.tensor_tensor(ssq[:], sq1[:], sq1[:], AL.mult).then_inc(vb.h, vb.tick())
                    v.wait_ge(vb.h, vb.n)
                    v.tensor_scalar(smk[:], ssq[:], 224.0 * 224.0, None, AL.is_gt, AL.bypass)
                    v.tensor_scalar(ssq[:], sq1[:], DEK, None, AL.mult, AL.bypass)
                    v.tensor_tensor(sx[:], ssq[:], sq1[:], AL.subtract)
                    v.tensor_tensor(ssq[:], ssq[:], sx[:], AL.subtract).then_inc(vb.h, vb.tick())
                    v.wait_ge(vb.h, vb.n)
                    v.select(sq1[:], smk[:], ssq[:], sq2[:]).then_inc(vb.h, vb.tick())
                    v.wait_ge(vb.h, vb.n)
                    v.tensor_copy(sqb[:], sq1[:]).then_inc(sv.h, sv.tick())
                block.vector(_)

                for b8 in range(8):
                    def _(te, wt=wt, b8=b8):
                        te.wait_ge(sv.h, sv.n)
                        te.wait_ge(dg.h, dg.n)
                        for j in range(4):
                            c = b8 * 4 + j
                            ins = te.transpose(pTb[:, j * 128:(j + 1) * 128],
                                               sqb[:, c * 128:(c + 1) * 128], sidb[:])
                        ins.then_inc(st.h, st.tick())
                    block.tensor(_)

                    def _(v, wt=wt, b8=b8):
                        v.wait_ge(st.h, st.n)
                        v.tensor_copy(sqb2[:], pTb[:]).then_inc(sv.h, sv.tick())
                    block.vector(_)

                    def _(g, wt=wt, b8=b8):
                        g.wait_ge(sv.h, sv.n)
                        for j in range(4):
                            c = b8 * 4 + j
                            g.dma_start(out=w8t_sc[c * 128:(c + 1) * 128, wt * 128:(wt + 1) * 128],
                                        in_=sqb2[:, j * 128:(j + 1) * 128]).then_inc(dg.h, 16)
                            dg.inc(16)
                    block.gpsimd(_)

            with ExitStack() as pd:
                sw8 = sbuf("sw8", [128, 32 * 128], BF16, st=pd)
                t2f = sbuf("t2f", [128, 1024], st=pd)
                o16 = sbuf("o16", [128, 1024], BF16, st=pd)

                for do in range(32):
                    pp = (do % 2) * 1024
                    def _(sy, do=do):
                        sy.wait_ge(st.h, st.n)
                        src3 = w8t_sc[:, do * 128:(do + 1) * 128].rearrange(
                            "(c p) n -> p c n", p=128)
                        dst3 = sw8[:].rearrange("p (c n) -> p c n", c=32)
                        for grp in range(4):
                            sy.dma_start(out=dst3[:, grp * 8:(grp + 1) * 8, :],
                                         in_=src3[:, grp * 8:(grp + 1) * 8, :]).then_inc(dl.h, 16)
                            dl.inc(16)
                    block.sync(_)

                    def _(te, do=do, pp=pp):
                        te.wait_ge(dl.h, dl.n)
                        te.wait_ge(sv.h, sv.n)
                        for hf in range(2):
                            for di in range(32):
                                ins = te.matmul(pMM[:, pp + hf * 512:pp + hf * 512 + 512],
                                                sw8[:, di * 128:(di + 1) * 128],
                                                xqT[:, di * R + hf * 512:di * R + hf * 512 + 512],
                                                start=(di == 0), stop=(di == 31))
                        ins.then_inc(st.h, st.tick())
                    block.tensor(_)

                    def _(v, do=do, pp=pp):
                        v.wait_ge(st.h, st.n)
                        v.wait_ge(ss.h, ss.n)
                        v.scalar_tensor_tensor(t2f[:], pMM[:, pp:pp + 1024], sws[:, do:do + 1],
                                               bc_as[:, 0:1024], AL.mult,
                                               AL.mult).then_inc(sv.h, sv.tick())
                    block.vector(_)

                    def _(s, do=do):
                        s.wait_ge(sv.h, sv.n)
                        s.wait_ge(dg.h, dg.n)
                        s.activation(o16[:], t2f[:], AF.Identity, bias=sfb[:, do:do + 1],
                                     scale=1.0).then_inc(ss.h, ss.tick())
                    block.scalar(_)

                    def _(g, do=do):
                        g.wait_ge(ss.h, ss.n)
                        g.dma_start(out=ot_sc[do * 128:(do + 1) * 128, :], in_=o16[:]).then_inc(dg.h, 16)
                        dg.inc(16)
                    block.gpsimd(_)

                sot = sbuf("sot", [128, D], BF16, st=pd)
                sy_ = sbuf("sy", [128, D], st=pd)

                for tt in range(8):
                    def _(sy, tt=tt):
                        sy.wait_ge(dg.h, dg.n)
                        sy.wait_ge(sv.h, sv.n)
                        sy.dma_start(out=sx[:], in_=x2_sc[tt * 128:(tt + 1) * 128, :]).then_inc(dl.h, 16)
                        dl.inc(16)
                        sy.dma_start(out=sot[:], in_=ot_sc[:, tt * 128:(tt + 1) * 128],
                                     transpose=True).then_inc(dl.h, 16)
                        dl.inc(16)
                    block.sync(_)

                    def _(v, tt=tt):
                        v.wait_ge(dl.h, dl.n)
                        v.tensor_tensor(sy_[:], sx[:], sot[:], AL.add).then_inc(sv.h, sv.tick())
                    block.vector(_)

                    def _(g, tt=tt):
                        g.wait_ge(sv.h, sv.n)
                        g.dma_start(out=y_out[tt * 128:(tt + 1) * 128, :], in_=sy_[:]).then_inc(dg.h, 16)
                        dg.inc(16)
                        if tt == 7 and not DEBUG_DUMPS:
                            g.wait_ge(dg.h, dg.n)
                    block.gpsimd(_)

                if DEBUG_DUMPS:
                    dumps = [("d_qt", qt_sc, [D, R], F32), ("d_kin", kin, [D, R], F32),
                             ("d_vin", vin, [D, R], F32), ("d_kall", kall, [4, D, R], F32),
                             ("d_x2", x2_sc, [R, D], F32), ("d_ot", ot_sc, [D, R], BF16),
                             ("d_w8t", w8t_sc, [D, D], BF16)]
                    def _(g):
                        for nm, srcap, shp, dt in dumps:
                            dout = nc.dram_tensor(nm, shp, dt, kind="ExternalOutput").ap()
                            flat_in = srcap.rearrange("... -> (...)") if len(shp) > 2 else srcap
                            nchunk = 8
                            if len(shp) == 3:
                                for j in range(shp[0]):
                                    g.dma_start(out=dout[j], in_=srcap[j]).then_inc(dg.h, 16)
                                    dg.inc(16)
                            else:
                                rows = shp[0]
                                for j in range(nchunk):
                                    r0, r1 = rows * j // nchunk, rows * (j + 1) // nchunk
                                    g.dma_start(out=dout[r0:r1], in_=srcap[r0:r1]).then_inc(dg.h, 16)
                                    dg.inc(16)
                        g.wait_ge(dg.h, dg.n)
                    block.gpsimd(_)

    return nc


_NC_CACHE = [None]


def kernel(**inputs):
    x = np.asarray(inputs["L0_attn_hidden_states"], np.float32)
    wq = np.asarray(inputs["L0_attn_wq"], np.float32)
    wk = np.asarray(inputs["L0_attn_wk"], np.float32)
    wv = np.asarray(inputs["L0_attn_wv"], np.float32)
    wo = np.asarray(inputs["L0_attn_wo"], np.float32)
    bq = np.asarray(inputs["L0_attn_bq"], np.float32)
    bk = np.asarray(inputs["L0_attn_bk"], np.float32)
    bv = np.asarray(inputs["L0_attn_bv"], np.float32)
    bo = np.asarray(inputs["L0_attn_bo"], np.float32)
    fw_ = np.asarray(inputs["L0_ffn_weight"], np.float32)
    fb_ = np.asarray(inputs["L0_ffn_bias"], np.float32)

    wqT = np.ascontiguousarray(wq.T)
    wkT = np.ascontiguousarray(wk.T)
    wvT = np.ascontiguousarray(wv.T)
    woT = np.ascontiguousarray(wo.T)
    boR = np.ascontiguousarray(np.broadcast_to(bo, (128, D)))

    if _NC_CACHE[0] is None:
        _NC_CACHE[0] = build_program()
    nc = _NC_CACHE[0]

    in_maps = []
    for c in range(8):
        b = c // 4
        r0 = (c % 4) * R
        in_maps.append({
            "x_tok": np.ascontiguousarray(x[b, r0:r0 + R]),
            "wqT": wqT, "wkT": wkT, "wvT": wvT, "woT": woT,
            "bq": bq, "bk": bk, "bv": bv, "bo_rep": boR,
            "fw": fw_, "fb": fb_,
        })
    res = run_bass_kernel_spmd(nc, in_maps, core_ids=list(range(8)))
    out = np.empty((2, SEQ, D), np.float32)
    for c in range(8):
        b = c // 4
        r0 = (c % 4) * R
        out[b, r0:r0 + R] = res.results[c]["y"]
    return out


# version 07
# speedup vs baseline: 1.6773x; 1.0155x over previous
import math
from contextlib import ExitStack

import numpy as np
import ml_dtypes
import concourse.bass as bass
import concourse.mybir as mybir
from concourse.bass_utils import run_bass_kernel_spmd

F32 = mybir.dt.float32
BF16 = mybir.dt.bfloat16
FP16 = mybir.dt.float16
FP8 = mybir.dt.float8e4
I32 = mybir.dt.int32
AL = mybir.AluOpType
AF = mybir.ActivationFunctionType
AX = mybir.AxisListType

D = 4096
R = 1024
NH = 32
HD = 128
SEQ = 4096
TOPK = 819
QG = 512
NQG = R // QG
NRND = 11
TSAT = 1.0e30
NDESC = 1
ASCALE = float(np.float32(1.0 / math.sqrt(HD)))
EPS_LN = 1.0e-5
W0 = 64.0
LOFF = 64.5
INV448 = float(np.float32(1.0 / 448.0))
DEK = float(np.float32(2.0 ** 20 + 1.0))


class Cnt:
    def __init__(self, h):
        self.h = h
        self.n = 0

    def inc(self, k=1):
        self.n += k
        return self.n

    def tick(self):
        self.n += 1
        return 1


DEBUG_DUMPS = False


def build_program():
    nc = bass.Bass("TRN2", target_bir_lowering=False, debug=False,
                   detect_race_conditions=False)

    x_tok = nc.dram_tensor("x_tok", [R, D], F32, kind="ExternalInput").ap()
    wqT = nc.dram_tensor("wqT", [D, D], F32, kind="ExternalInput").ap()
    wkT = nc.dram_tensor("wkT", [D, D], F32, kind="ExternalInput").ap()
    wvT = nc.dram_tensor("wvT", [D, D], F32, kind="ExternalInput").ap()
    woT = nc.dram_tensor("woT", [D, D], F32, kind="ExternalInput").ap()
    bq_in = nc.dram_tensor("bq", [D], F32, kind="ExternalInput").ap()
    bk_in = nc.dram_tensor("bk", [D], F32, kind="ExternalInput").ap()
    bv_in = nc.dram_tensor("bv", [D], F32, kind="ExternalInput").ap()
    bo_rep = nc.dram_tensor("bo_rep", [128, D], F32, kind="ExternalInput").ap()
    fw = nc.dram_tensor("fw", [D, D], F32, kind="ExternalInput").ap()
    fb_in = nc.dram_tensor("fb", [D], F32, kind="ExternalInput").ap()
    y_out = nc.dram_tensor("y", [R, D], F32, kind="ExternalOutput").ap()

    qt_sc = nc.dram_tensor("qt_sc", [D, R], F32).ap()
    kin = nc.dram_tensor("kin", [D, R], F32).ap()
    vin = nc.dram_tensor("vin", [D, R], F32).ap()
    kall = nc.dram_tensor("kall", [4, D, R], F32).ap()
    vall = nc.dram_tensor("vall", [4, D, R], F32).ap()
    x2_sc = nc.dram_tensor("x2_sc", [R, D], F32).ap()
    w8t_sc = nc.dram_tensor("w8t_sc", [D, D], BF16).ap()
    ot_sc = nc.dram_tensor("ot_sc", [D, R], BF16).ap()

    ident_f = nc.inline_tensor(np.eye(128, dtype=np.float32), "ident_f").ap()
    ident_b = nc.inline_tensor(np.eye(128, dtype=ml_dtypes.bfloat16), "ident_b").ap()
    ones_f = nc.inline_tensor(np.ones((1, 128), dtype=np.float32), "ones_f").ap()

    es = ExitStack()
    with es:
        def sbuf(name, shape, dt=F32, st=None):
            return (st or es).enter_context(nc.sbuf_tensor(name, shape, dt))

        def sem(name):
            return Cnt(es.enter_context(nc.semaphore(name)))

        block = es.enter_context(nc.Block())

        dl = sem("dl")
        dg = sem("dg")
        sv = sem("sv")
        ss = sem("ss")
        st = sem("st")
        cc = sem("cc")
        vb = sem("vb")

        pMM = es.enter_context(nc.psum_tensor("pMM", [128, 2048], F32))
        pT = es.enter_context(nc.psum_tensor("pT", [128, 1024], F32))
        pCT = es.enter_context(nc.psum_tensor("pCT", [128, 128], F32))
        pTb = es.enter_context(nc.psum_tensor("pTb", [128, 512], BF16))

        sid = sbuf("sid", [128, 128])
        sidb = sbuf("sidb", [128, 128], BF16)
        sones = sbuf("sones", [1, 128])
        sbq = sbuf("sbq", [128, 32])
        sbk = sbuf("sbk", [128, 32])
        sbv = sbuf("sbv", [128, 32])
        sfb = sbuf("sfb", [128, 32])
        sfb16 = sbuf("sfb16", [128, 32], BF16)
        sws = sbuf("sws", [128, 32])
        swsr = sbuf("swsr", [128, 32])
        sws16 = sbuf("sws16", [128, 32], FP16)
        sasr = sbuf("sasr", [128, 8])
        sasrow = sbuf("sasrow", [1, 1024])
        bc_as = sbuf("bc_as", [128, 1024])
        stny = sbuf("stny", [128, 48])
        cst = sbuf("cst", [128, 1])

        T = stny

        def _(sy):
            sy.dma_start(out=sid[:], in_=ident_f).then_inc(dl.h, 16); dl.inc(16)
            sy.dma_start(out=sidb[:], in_=ident_b).then_inc(dl.h, 16); dl.inc(16)
            sy.dma_start(out=sones[:], in_=ones_f).then_inc(dl.h, 16); dl.inc(16)
            for dst, src in ((sbq, bq_in), (sbk, bk_in), (sbv, bv_in), (sfb, fb_in)):
                sy.dma_start(out=dst[:], in_=src.rearrange("(c p) -> p c", p=128),
                             allow_slow_non_contiguous=True).then_inc(dl.h, 16)
                dl.inc(16)
        block.sync(_)

        def _(v):
            v.wait_ge(dl.h, dl.n)
            v.memset(cst[:], ASCALE)
            v.tensor_scalar(sbq[:], sbq[:], ASCALE, None, AL.mult, AL.bypass)
            v.tensor_copy(sfb16[:], sfb[:]).then_inc(vb.h, vb.tick())
            v.wait_ge(vb.h, vb.n)
            v.tensor_copy(sfb[:], sfb16[:]).then_inc(sv.h, sv.tick())
        block.vector(_)

        with ExitStack() as pa:
            sx = sbuf("sx", [128, D], st=pa)
            ssq = sbuf("ssq", [128, D], st=pa)
            sxT = sbuf("sxT", [128, 32 * R], st=pa)
            sev = sbuf("sev", [128, 1024], st=pa)
            wcols = [sbuf(f"wcol{j}", [128, 32 * 128], st=pa) for j in range(2)]

            _dxh_ap = (nc.dram_tensor("d_xh", [R, D], F32, kind="ExternalOutput").ap()
                       if DEBUG_DUMPS else None)

            for tt in range(8):
                def _(sy, tt=tt):
                    if tt > 0:
                        sy.wait_ge(sv.h, sv.n)
                    sy.dma_start(out=sx[:], in_=x_tok[tt * 128:(tt + 1) * 128, :]).then_inc(dl.h, 16)
                    dl.inc(16)
                block.sync(_)

                def _(v):
                    v.wait_ge(dl.h, dl.n)
                    v.tensor_reduce(T[:, 0:1], sx[:], AX.X, AL.add).then_inc(vb.h, vb.tick())
                    v.wait_ge(vb.h, vb.n)
                    v.tensor_scalar(T[:, 0:1], T[:, 0:1], 1.0 / D, None, AL.mult,
                                    AL.bypass).then_inc(sv.h, sv.tick())
                block.vector(_)

                def _(s):
                    s.wait_ge(sv.h, sv.n)
                    s.wait_ge(st.h, st.n)
                    s.activation(ssq[:], sx[:], AF.Square, bias=0.0, scale=1.0,
                                 accum_out=T[:, 1:2]).then_inc(ss.h, ss.tick())
                block.scalar(_)

                def _(v):
                    v.wait_ge(ss.h, ss.n)
                    v.tensor_scalar(T[:, 1:2], T[:, 1:2], 1.0 / D, None, AL.mult, AL.bypass)
                    v.tensor_tensor(T[:, 2:3], T[:, 0:1], T[:, 0:1], AL.mult).then_inc(vb.h, vb.tick())
                    v.wait_ge(vb.h, vb.n)
                    v.tensor_tensor(T[:, 1:2], T[:, 1:2], T[:, 2:3], AL.subtract).then_inc(vb.h, vb.tick())
                    v.wait_ge(vb.h, vb.n)
                    v.tensor_scalar(T[:, 1:2], T[:, 1:2], EPS_LN, None, AL.add,
                                    AL.bypass).then_inc(sv.h, sv.tick())
                block.vector(_)

                def _(s):
                    s.wait_ge(sv.h, sv.n)
                    s.activation(T[:, 2:3], T[:, 1:2], AF.Sqrt, bias=0.0,
                                 scale=1.0).then_inc(ss.h, ss.tick())
                block.scalar(_)

                def _(v):
                    v.wait_ge(ss.h, ss.n)
                    v.reciprocal(T[:, 3:4], T[:, 2:3]).then_inc(vb.h, vb.tick())
                    v.wait_ge(vb.h, vb.n)
                    v.tensor_scalar(ssq[:], sx[:], T[:, 0:1], T[:, 3:4], AL.subtract,
                                    AL.mult).then_inc(sv.h, sv.tick())
                block.vector(_)

                if DEBUG_DUMPS:
                    def _(g, tt=tt):
                        g.wait_ge(sv.h, sv.n)
                        g.dma_start(out=_dxh_ap[tt * 128:(tt + 1) * 128, :],
                                    in_=ssq[:]).then_inc(dg.h, 16)
                        dg.inc(16)
                        g.wait_ge(dg.h, dg.n)
                    block.gpsimd(_)

                for q4 in range(8):
                    ph = (q4 % 2) * 512
                    def _(te, tt=tt, q4=q4, ph=ph):
                        te.wait_ge(sv.h, sv.n)
                        for j in range(4):
                            c = q4 * 4 + j
                            ins = te.transpose(pT[:, ph + j * 128: ph + (j + 1) * 128],
                                               ssq[:, c * 128:(c + 1) * 128], sid[:])
                        ins.then_inc(st.h, st.tick())
                    block.tensor(_)

                    def _(v, tt=tt, q4=q4, ph=ph):
                        v.wait_ge(st.h, st.n)
                        for j in range(4):
                            c = q4 * 4 + j
                            ins = v.tensor_copy(
                                sxT[:, c * R + tt * 128: c * R + (tt + 1) * 128],
                                pT[:, ph + j * 128: ph + (j + 1) * 128])
                        ins.then_inc(sv.h, sv.tick())
                    block.vector(_)

            if DEBUG_DUMPS:
                d_xT = nc.dram_tensor("d_xT", [128, 32 * R], F32, kind="ExternalOutput").ap()

                def _(g):
                    g.wait_ge(sv.h, sv.n)
                    g.dma_start(out=d_xT[:], in_=sxT[:]).then_inc(dg.h, 16)
                    dg.inc(16)
                    g.wait_ge(dg.h, dg.n)
                block.gpsimd(_)

            projs = [(wqT, sbq, qt_sc, ASCALE), (wkT, sbk, kin, 1.0), (wvT, sbv, vin, 1.0)]
            for pi, (wT, sbias, dst, scale) in enumerate(projs):
                snap_wfree = [st.n, st.n]
                for do in range(32):
                    ph = (do % 2) * 1024
                    wcol = wcols[do % 2]
                    def _(sy, wT=wT, do=do, wcol=wcol, w=snap_wfree[do % 2]):
                        sy.wait_ge(st.h, w)
                        src3 = wT[:, do * 128:(do + 1) * 128].rearrange("(c p) n -> p c n", p=128)
                        dst3 = wcol[:].rearrange("p (c n) -> p c n", c=32)
                        for grp in range(4):
                            sy.dma_start(out=dst3[:, grp * 8:(grp + 1) * 8, :],
                                         in_=src3[:, grp * 8:(grp + 1) * 8, :]).then_inc(dl.h, 16)
                            dl.inc(16)
                    block.sync(_)

                    def _(te, do=do, ph=ph, wcol=wcol):
                        te.wait_ge(dl.h, dl.n)
                        te.wait_ge(sv.h, sv.n)
                        te.wait_ge(ss.h, ss.n)
                        for half in range(2):
                            for di in range(32):
                                ins = te.matmul(
                                    pMM[:, ph + half * 512: ph + half * 512 + 512],
                                    wcol[:, di * 128:(di + 1) * 128],
                                    sxT[:, di * R + half * 512: di * R + half * 512 + 512],
                                    start=(di == 0), stop=(di == 31))
                        ins.then_inc(st.h, st.tick())
                        snap_wfree[do % 2] = st.n
                    block.tensor(_)

                    def _(s, do=do, sbias=sbias, scale=scale, ph=ph):
                        s.wait_ge(st.h, st.n)
                        s.wait_ge(dg.h, dg.n)
                        sc = 1.0 if scale == 1.0 else cst[:, 0:1]
                        s.activation(sev[:, 0:512], pMM[:, ph:ph + 512], AF.Identity,
                                     bias=sbias[:, do:do + 1], scale=sc)
                        s.activation(sev[:, 512:1024], pMM[:, ph + 512:ph + 1024], AF.Identity,
                                     bias=sbias[:, do:do + 1], scale=sc).then_inc(ss.h, ss.tick())
                    block.scalar(_)

                    def _(g, do=do, dst=dst):
                        g.wait_ge(ss.h, ss.n)
                        g.dma_start(out=dst[do * 128:(do + 1) * 128, :], in_=sev[:]).then_inc(dg.h, 16)
                        dg.inc(16)
                    block.gpsimd(_)

        def _(g):
            g.wait_ge(dg.h, dg.n)
            g.collective_compute("AllGather", AL.bypass,
                                 replica_groups=[[0, 1, 2, 3], [4, 5, 6, 7]],
                                 ins=[kin.opt()], outs=[kall.opt()]).then_inc(cc.h, cc.tick())
            g.collective_compute("AllGather", AL.bypass,
                                 replica_groups=[[0, 1, 2, 3], [4, 5, 6, 7]],
                                 ins=[vin.opt()], outs=[vall.opt()]).then_inc(cc.h, cc.tick())
        block.gpsimd(_)

        def _(sy):
            sy.wait_ge(cc.h, cc.n)
            sy.sem_inc(dl.h, 16)
            dl.inc(16)
        block.sync(_)

        with ExitStack() as pb:
            ctxT = sbuf("ctxT", [128, 32 * QG], st=pb)

            for qg in range(NQG):
                with ExitStack() as ph_heads:
                    kTh = sbuf(f"kTh_{qg}", [128, SEQ], st=ph_heads)
                    vTh = sbuf(f"vTh_{qg}", [128, SEQ], st=ph_heads)
                    vtok = sbuf(f"vtok_{qg}", [128, 32 * 128], st=ph_heads)
                    qTh = sbuf(f"qTh_{qg}", [128, QG], st=ph_heads)
                    sS = sbuf(f"sS_{qg}", [128, SEQ], st=ph_heads)
                    sEs = [sbuf(f"sE{p}_{qg}", [128, SEQ], st=ph_heads) for p in range(2)]
                    jkb = sbuf(f"jkb_{qg}", [128, SEQ], BF16, st=ph_heads)
                    jks = sbuf(f"jks_{qg}", [128, SEQ], BF16, st=ph_heads)
                    spt = sbuf(f"spt_{qg}", [128, 1024], st=ph_heads)

                    NSUB = NH * 4
                    snap_pmm_free = 0
                    snap_P = {}
                    snap_pv_mm = {}
                    snap_ctx_free = 0

                    def emit_head_loads(h, qg=qg):
                        def _(sy, h=h, qg=qg):
                            sy.wait_ge(st.h, st.n)
                            for g4 in range(4):
                                sy.dma_start(out=kTh[:, g4 * R:(g4 + 1) * R],
                                             in_=kall[g4, h * 128:(h + 1) * 128, :]).then_inc(dl.h, 16)
                                dl.inc(16)
                                sy.dma_start(out=vTh[:, g4 * R:(g4 + 1) * R],
                                             in_=vall[g4, h * 128:(h + 1) * 128, :]).then_inc(dl.h, 16)
                                dl.inc(16)
                            sy.dma_start(out=qTh[:],
                                         in_=qt_sc[h * 128:(h + 1) * 128,
                                                   qg * QG:(qg + 1) * QG]).then_inc(dl.h, 16)
                            dl.inc(16)
                        block.sync(_)

                    def emit_vtrans(h):
                        for b8 in range(8):
                            ph = (b8 % 2) * 512
                            def _(te, b8=b8, ph=ph):
                                te.wait_ge(dl.h, dl.n)
                                te.wait_ge(sv.h, sv.n)
                                for j in range(4):
                                    c = b8 * 4 + j
                                    ins = te.transpose(pT[:, ph + j * 128:ph + (j + 1) * 128],
                                                       vTh[:, c * 128:(c + 1) * 128], sid[:])
                                ins.then_inc(st.h, st.tick())
                            block.tensor(_)

                            def _(v, b8=b8, ph=ph):
                                v.wait_ge(st.h, st.n)
                                v.tensor_copy(vtok[:, b8 * 512:(b8 + 1) * 512],
                                              pT[:, ph:ph + 512]).then_inc(sv.h, sv.tick())
                            block.vector(_)

                    def emit_scores_te(i):
                        h, qt = divmod(i, 4)
                        def _h0(te, h=h, qt=qt, w=snap_pmm_free):
                            te.wait_ge(dl.h, dl.n)
                            te.wait_ge(sv.h, w)
                            for j in range(4):
                                ins = te.matmul(pMM[:, j * 512:(j + 1) * 512],
                                                qTh[:, qt * 128:(qt + 1) * 128],
                                                kTh[:, j * 512:(j + 1) * 512],
                                                start=True, stop=True)
                            ins.then_inc(st.h, st.tick())
                        block.tensor(_h0)

                    def emit_evac_half(i, half):
                        def _(v, i=i, half=half):
                            v.wait_ge(st.h, st.n)
                            v.tensor_scalar(sS[:, half * 2048:(half + 1) * 2048],
                                            pMM[:, 0:2048], 0.0, None, AL.add, AL.max,
                                            accum_out=T[:, 16 + half:17 + half]).then_inc(
                                sv.h, sv.tick())
                        block.vector(_)

                    def emit_scores_half1(i):
                        h, qt = divmod(i, 4)
                        def _(te, h=h, qt=qt):
                            te.wait_ge(sv.h, sv.n)
                            for j in range(4):
                                ins = te.matmul(pMM[:, j * 512:(j + 1) * 512],
                                                qTh[:, qt * 128:(qt + 1) * 128],
                                                kTh[:, 2048 + j * 512: 2048 + (j + 1) * 512],
                                                start=True, stop=True)
                            ins.then_inc(st.h, st.tick())
                        block.tensor(_)

                    def emit_pv(i):
                        p = i % 2
                        sE = sEs[p]
                        for grp in range(8):
                            ph = (grp % 2) * 512
                            def _t(te, grp=grp, ph=ph, sE=sE):
                                te.wait_ge(sv.h, sv.n)
                                for j in range(4):
                                    ck = grp * 4 + j
                                    ins = te.transpose(pT[:, ph + j * 128: ph + (j + 1) * 128],
                                                       sE[:, ck * 128:(ck + 1) * 128], sid[:])
                                ins.then_inc(st.h, st.tick())
                            block.tensor(_t)

                            def _c(v, grp=grp, ph=ph):
                                v.wait_ge(st.h, st.n)
                                v.tensor_copy(spt[:, ph:ph + 512],
                                              pT[:, ph:ph + 512]).then_inc(sv.h, sv.tick())
                            block.vector(_c)

                            def _m(te, grp=grp, ph=ph):
                                te.wait_ge(sv.h, sv.n)
                                for j in range(4):
                                    ck = grp * 4 + j
                                    ins = te.matmul(pCT[:], vtok[:, ck * 128:(ck + 1) * 128],
                                                    spt[:, ph + j * 128: ph + (j + 1) * 128],
                                                    start=(ck == 0), stop=(ck == 31))
                                ins.then_inc(st.h, st.tick())
                            block.tensor(_m)
                        snap_pv_mm[i] = st.n

                    def emit_ctx_evac(i):
                        h, qt = divmod(i, 4)
                        def _(v, h=h, qt=qt, w=snap_pv_mm[i]):
                            v.wait_ge(st.h, w)
                            v.tensor_copy(ctxT[:, h * QG + qt * 128: h * QG + (qt + 1) * 128],
                                          pCT[:]).then_inc(sv.h, sv.tick())
                        block.vector(_)

                    def emit_negmax_exp(i):
                        p = i % 2
                        def _(v):
                            v.wait_ge(sv.h, sv.n)
                            v.tensor_tensor(T[:, 4:5], T[:, 16:17], T[:, 17:18],
                                            AL.max).then_inc(vb.h, vb.tick())
                            v.wait_ge(vb.h, vb.n)
                            v.tensor_scalar(T[:, 4:5], T[:, 4:5], -1.0, None, AL.mult,
                                            AL.bypass).then_inc(sv.h, sv.tick())
                        block.vector(_)

                        def _(s, p=p):
                            s.wait_ge(sv.h, sv.n)
                            s.wait_ge(st.h, st.n)
                            s.activation(sEs[p][:], sS[:], AF.Exp, bias=T[:, 4:5],
                                         scale=1.0).then_inc(ss.h, ss.tick())
                        block.scalar(_)

                    def emit_bisect(i):
                        def _(v):
                            v.wait_ge(ss.h, ss.n)
                            v.tensor_scalar(T[:, 5:6], T[:, 4:5], -1.0, -LOFF, AL.mult,
                                            AL.add).then_inc(vb.h, vb.tick())
                            v.wait_ge(vb.h, vb.n)
                        block.vector(_)

                        for r in range(NRND):
                            w3 = W0 / (3.0 ** (r + 1))
                            def _(v, w3=w3):
                                v.tensor_scalar(T[:, 6:7], T[:, 5:6], w3, None, AL.add, AL.bypass)
                                v.tensor_scalar(T[:, 12:13], T[:, 5:6], -TSAT, -2.0 * w3 * TSAT,
                                                AL.mult, AL.add).then_inc(sv.h, sv.tick())
                                v.tensor_scalar(jkb[:], sS[:], T[:, 6:7], None, AL.is_ge,
                                                AL.add, accum_out=T[:, 7:8]).then_inc(vb.h, vb.tick())
                            block.vector(_)

                            def _(s, w3=w3):
                                s.wait_ge(sv.h, sv.n)
                                s.activation(jks[:], sS[:], AF.Tanh, bias=T[:, 12:13],
                                             scale=TSAT, accum_out=T[:, 18:19]).then_inc(ss.h, ss.tick())
                            block.scalar(_)

                            def _(v, w3=w3):
                                v.wait_ge(vb.h, vb.n)
                                v.tensor_scalar(T[:, 8:9], T[:, 7:8], TOPK - 0.5, None,
                                                AL.is_ge, AL.bypass)
                                v.wait_ge(ss.h, ss.n)
                                v.tensor_scalar(T[:, 19:20], T[:, 18:19],
                                                2.0 * TOPK - SEQ - 0.5, None,
                                                AL.is_ge, AL.bypass).then_inc(vb.h, vb.tick())
                                v.wait_ge(vb.h, vb.n)
                                v.tensor_tensor(T[:, 8:9], T[:, 8:9], T[:, 19:20],
                                                AL.add).then_inc(vb.h, vb.tick())
                                v.wait_ge(vb.h, vb.n)
                                v.scalar_tensor_tensor(T[:, 5:6], T[:, 8:9], w3, T[:, 5:6],
                                                       AL.mult, AL.add).then_inc(vb.h, vb.tick())
                                v.wait_ge(vb.h, vb.n)
                            block.vector(_)

                    def emit_desc_mask(i):
                        p = i % 2
                        sE = sEs[p]
                        wfin = W0 / (3.0 ** NRND)
                        def _(v, wfin=wfin):
                            v.tensor_scalar(T[:, 6:7], T[:, 5:6], wfin, None, AL.add,
                                            AL.bypass).then_inc(sv.h, sv.tick())
                        block.vector(_)

                        def _(s):
                            s.wait_ge(sv.h, sv.n)
                            s.activation(T[:, 9:11], T[:, 5:7], AF.Exp, bias=T[:, 4:5],
                                         scale=1.0).then_inc(ss.h, ss.tick())
                        block.scalar(_)

                        def _(v):
                            v.wait_ge(ss.h, ss.n)
                            v.memset(T[:, 13:14], 0.0)
                            v.memset(T[:, 11:12], 0.0)
                        block.vector(_)

                        for d in range(NDESC):
                            def _(v, sE=sE):
                                v.scalar_tensor_tensor(sS[:], sE[:], T[:, 10:11], sE[:],
                                                       AL.is_lt, AL.mult).then_inc(vb.h, vb.tick())
                                v.wait_ge(vb.h, vb.n)
                                v.tensor_reduce(T[:, 12:13], sS[:], AX.X,
                                                AL.max).then_inc(vb.h, vb.tick())
                                v.wait_ge(vb.h, vb.n)
                                v.tensor_scalar(jkb[:], sE[:], T[:, 12:13], None, AL.is_ge,
                                                AL.add, accum_out=T[:, 7:8]).then_inc(vb.h, vb.tick())
                                v.wait_ge(vb.h, vb.n)
                                v.tensor_scalar(T[:, 8:9], T[:, 7:8], TOPK - 0.5, None,
                                                AL.is_ge, AL.bypass)
                                v.tensor_scalar(T[:, 14:15], T[:, 11:12], 0.5, None,
                                                AL.is_lt, AL.bypass).then_inc(vb.h, vb.tick())
                                v.wait_ge(vb.h, vb.n)
                                v.tensor_tensor(T[:, 14:15], T[:, 14:15], T[:, 8:9],
                                                AL.mult).then_inc(vb.h, vb.tick())
                                v.wait_ge(vb.h, vb.n)
                                v.scalar_tensor_tensor(T[:, 13:14], T[:, 12:13], T[:, 14:15],
                                                       T[:, 13:14], AL.mult, AL.add)
                                v.tensor_tensor(T[:, 11:12], T[:, 11:12], T[:, 8:9], AL.max)
                                v.tensor_copy(T[:, 10:11], T[:, 12:13]).then_inc(vb.h, vb.tick())
                                v.wait_ge(vb.h, vb.n)
                            block.vector(_)

                        def _(v):
                            v.tensor_scalar(T[:, 14:15], T[:, 11:12], 0.5, None,
                                            AL.is_lt, AL.bypass).then_inc(vb.h, vb.tick())
                            v.wait_ge(vb.h, vb.n)
                            v.scalar_tensor_tensor(T[:, 13:14], T[:, 9:10], T[:, 14:15],
                                                   T[:, 13:14], AL.mult, AL.add).then_inc(vb.h, vb.tick())
                            v.wait_ge(vb.h, vb.n)
                        block.vector(_)

                        def _(v, sE=sE):
                            v.scalar_tensor_tensor(sS[:], sE[:], T[:, 13:14], sE[:],
                                                   AL.is_ge, AL.mult,
                                                   accum_out=T[:, 14:15]).then_inc(vb.h, vb.tick())
                            v.wait_ge(vb.h, vb.n)
                            v.reciprocal(T[:, 15:16], T[:, 14:15]).then_inc(vb.h, vb.tick())
                            v.wait_ge(vb.h, vb.n)
                            v.tensor_scalar(sE[:], sS[:], T[:, 15:16], None,
                                            AL.mult, AL.bypass).then_inc(sv.h, sv.tick())
                        block.vector(_)

                    for i in range(NSUB):
                        h, qt = divmod(i, 4)
                        if qt == 0:
                            emit_head_loads(h)
                            if i > 0:
                                emit_pv(i - 1)
                            emit_vtrans(h)
                            emit_scores_te(i)
                        else:
                            emit_scores_te(i)
                            emit_pv(i - 1)
                        emit_evac_half(i, 0)
                        emit_scores_half1(i)
                        emit_evac_half(i, 1)
                        snap_pmm_free = sv.n
                        emit_negmax_exp(i)
                        emit_bisect(i)
                        if i > 0:
                            emit_ctx_evac(i - 1)
                            snap_ctx_free = sv.n
                        emit_desc_mask(i)
                        snap_P[i] = sv.n
                    emit_pv(NSUB - 1)
                    emit_ctx_evac(NSUB - 1)

                with ExitStack() as po:
                    swo = sbuf(f"swo_{qg}", [128, 32 * 512], st=po)
                    sbo = sbuf(f"sbo_{qg}", [128, 512], st=po)
                    sxr = sbuf(f"sxr_{qg}", [128, 512], st=po)
                    sev2 = sbuf(f"sev2_{qg}", [128, 512], st=po)

                    for dk in range(8):
                        def _(sy, dk=dk):
                            sy.wait_ge(st.h, st.n)
                            src3 = woT[:, dk * 512:(dk + 1) * 512].rearrange(
                                "(c p) n -> p c n", p=128)
                            dst3 = swo[:].rearrange("p (c n) -> p c n", c=32)
                            for grp in range(8):
                                sy.dma_start(out=dst3[:, grp * 4:(grp + 1) * 4, :],
                                             in_=src3[:, grp * 4:(grp + 1) * 4, :]).then_inc(dl.h, 16)
                                dl.inc(16)
                            sy.dma_start(out=sbo[:], in_=bo_rep[:, dk * 512:(dk + 1) * 512]).then_inc(dl.h, 16)
                            dl.inc(16)
                        block.sync(_)

                        for qt in range(QG // 128):
                            pp = (qt % 2) * 512
                            def _(sy, dk=dk, qt=qt, qg=qg):
                                sy.wait_ge(sv.h, sv.n)
                                row0 = qg * QG + qt * 128
                                sy.dma_start(out=sxr[:],
                                             in_=x_tok[row0:row0 + 128,
                                                       dk * 512:(dk + 1) * 512]).then_inc(dl.h, 16)
                                dl.inc(16)
                            block.sync(_)

                            def _(te, dk=dk, qt=qt, pp=pp):
                                te.wait_ge(dl.h, dl.n)
                                te.wait_ge(sv.h, sv.n)
                                for dc in range(32):
                                    ins = te.matmul(pMM[:, pp:pp + 512],
                                                    ctxT[:, dc * QG + qt * 128: dc * QG + (qt + 1) * 128],
                                                    swo[:, dc * 512:(dc + 1) * 512],
                                                    start=(dc == 0), stop=(dc == 31))
                                ins.then_inc(st.h, st.tick())
                            block.tensor(_)

                            def _(v, dk=dk, qt=qt, pp=pp):
                                v.wait_ge(st.h, st.n)
                                v.wait_ge(dg.h, dg.n)
                                v.tensor_tensor(sev2[:], sbo[:], sxr[:], AL.add)
                                v.tensor_tensor(sev2[:], sev2[:], pMM[:, pp:pp + 512],
                                                AL.add).then_inc(sv.h, sv.tick())
                            block.vector(_)

                            def _(g, dk=dk, qt=qt, qg=qg):
                                g.wait_ge(sv.h, sv.n)
                                row0 = qg * QG + qt * 128
                                g.dma_start(out=x2_sc[row0:row0 + 128, dk * 512:(dk + 1) * 512],
                                            in_=sev2[:]).then_inc(dg.h, 16)
                                dg.inc(16)
                            block.gpsimd(_)

        with ExitStack() as pc:
            sx = sbuf("sxc", [128, D], st=pc)
            ssq = sbuf("ssqc", [128, D], st=pc)
            sq1 = sbuf("sq1", [128, D], st=pc)
            sq2 = sbuf("sq2", [128, D], st=pc)
            sf8 = sbuf("sf8", [128, D], FP8, st=pc)
            smk = sbuf("smk", [128, D], I32, st=pc)
            sqb = sbuf("sqb", [128, D], BF16, st=pc)
            sqb2 = sbuf("sqb2", [128, 512], BF16, st=pc)
            xqT = sbuf("xqT", [128, 32 * R], BF16, st=pc)

            for tt in range(8):
                def _(sy, tt=tt):
                    sy.wait_ge(st.h, st.n)
                    sy.wait_ge(dg.h, dg.n)
                    sy.dma_start(out=sx[:], in_=x2_sc[tt * 128:(tt + 1) * 128, :]).then_inc(dl.h, 16)
                    dl.inc(16)
                block.sync(_)

                def _(v):
                    v.wait_ge(dl.h, dl.n)
                    v.tensor_reduce(T[:, 0:1], sx[:], AX.X, AL.add).then_inc(vb.h, vb.tick())
                    v.wait_ge(vb.h, vb.n)
                    v.tensor_scalar(T[:, 0:1], T[:, 0:1], 1.0 / D, None, AL.mult,
                                    AL.bypass).then_inc(sv.h, sv.tick())
                block.vector(_)

                def _(s):
                    s.wait_ge(sv.h, sv.n)
                    s.activation(ssq[:], sx[:], AF.Square, bias=0.0, scale=1.0,
                                 accum_out=T[:, 1:2]).then_inc(ss.h, ss.tick())
                block.scalar(_)

                def _(v):
                    v.wait_ge(ss.h, ss.n)
                    v.tensor_scalar(T[:, 1:2], T[:, 1:2], 1.0 / D, None, AL.mult, AL.bypass)
                    v.tensor_tensor(T[:, 2:3], T[:, 0:1], T[:, 0:1], AL.mult).then_inc(vb.h, vb.tick())
                    v.wait_ge(vb.h, vb.n)
                    v.tensor_tensor(T[:, 1:2], T[:, 1:2], T[:, 2:3], AL.subtract).then_inc(vb.h, vb.tick())
                    v.wait_ge(vb.h, vb.n)
                    v.tensor_scalar(T[:, 1:2], T[:, 1:2], EPS_LN, None, AL.add,
                                    AL.bypass).then_inc(sv.h, sv.tick())
                block.vector(_)

                def _(s):
                    s.wait_ge(sv.h, sv.n)
                    s.activation(T[:, 2:3], T[:, 1:2], AF.Sqrt, bias=0.0,
                                 scale=1.0).then_inc(ss.h, ss.tick())
                block.scalar(_)

                def _(v, tt=tt):
                    v.wait_ge(ss.h, ss.n)
                    v.reciprocal(T[:, 3:4], T[:, 2:3]).then_inc(vb.h, vb.tick())
                    v.wait_ge(vb.h, vb.n)
                    v.tensor_scalar(ssq[:], sx[:], T[:, 0:1], T[:, 3:4], AL.subtract, AL.mult)
                    v.tensor_reduce(T[:, 5:6], ssq[:], AX.X, AL.max,
                                    apply_absolute_value=True).then_inc(vb.h, vb.tick())
                    v.wait_ge(vb.h, vb.n)
                    v.tensor_scalar(T[:, 5:6], T[:, 5:6], INV448, 1e-6, AL.mult,
                                    AL.max).then_inc(vb.h, vb.tick())
                    v.wait_ge(vb.h, vb.n)
                    v.tensor_copy(sasr[:, tt:tt + 1], T[:, 5:6])
                    v.reciprocal(T[:, 6:7], T[:, 5:6]).then_inc(vb.h, vb.tick())
                    v.wait_ge(vb.h, vb.n)
                    v.tensor_scalar(sq1[:], ssq[:], T[:, 6:7], -448.0, AL.mult, AL.max)
                    v.tensor_scalar(sq1[:], sq1[:], 448.0, None, AL.min, AL.bypass)
                    v.tensor_scalar(sq2[:], sq1[:], -240.0, 240.0, AL.max, AL.min)
                    v.tensor_copy(sf8[:], sq2[:]).then_inc(vb.h, vb.tick())
                    v.wait_ge(vb.h, vb.n)
                    v.tensor_copy(sq2[:], sf8[:])
                    v.tensor_tensor(ssq[:], sq1[:], sq1[:], AL.mult).then_inc(vb.h, vb.tick())
                    v.wait_ge(vb.h, vb.n)
                    v.tensor_scalar(smk[:], ssq[:], 224.0 * 224.0, None, AL.is_gt, AL.bypass)
                    v.tensor_scalar(ssq[:], sq1[:], DEK, None, AL.mult, AL.bypass)
                    v.tensor_tensor(sx[:], ssq[:], sq1[:], AL.subtract)
                    v.tensor_tensor(ssq[:], ssq[:], sx[:], AL.subtract).then_inc(vb.h, vb.tick())
                    v.wait_ge(vb.h, vb.n)
                    v.select(sq1[:], smk[:], ssq[:], sq2[:]).then_inc(vb.h, vb.tick())
                    v.wait_ge(vb.h, vb.n)
                    v.tensor_copy(sqb[:], sq1[:]).then_inc(sv.h, sv.tick())
                block.vector(_)

                for b8 in range(8):
                    def _(te, tt=tt, b8=b8):
                        te.wait_ge(sv.h, sv.n)
                        for j in range(4):
                            c = b8 * 4 + j
                            ins = te.transpose(pTb[:, j * 128:(j + 1) * 128],
                                               sqb[:, c * 128:(c + 1) * 128], sidb[:])
                        ins.then_inc(st.h, st.tick())
                    block.tensor(_)

                    def _(v, tt=tt, b8=b8):
                        v.wait_ge(st.h, st.n)
                        for j in range(4):
                            c = b8 * 4 + j
                            ins = v.tensor_copy(
                                xqT[:, c * R + tt * 128: c * R + (tt + 1) * 128],
                                pTb[:, j * 128:(j + 1) * 128])
                        ins.then_inc(sv.h, sv.tick())
                    block.vector(_)

                def _(te, tt=tt):
                    te.wait_ge(sv.h, sv.n)
                    te.transpose(pT[0:1, 0:128], sasr[:, tt:tt + 1], sid[:]).then_inc(st.h, st.tick())
                block.tensor(_)

                def _(v, tt=tt):
                    v.wait_ge(st.h, st.n)
                    v.tensor_copy(sasrow[0:1, tt * 128:(tt + 1) * 128],
                                  pT[0:1, 0:128]).then_inc(sv.h, sv.tick())
                block.vector(_)

            def _(te):
                te.wait_ge(sv.h, sv.n)
                for half in range(2):
                    ins = te.matmul(pMM[:, half * 512:(half + 1) * 512], sones[:],
                                    sasrow[0:1, half * 512:(half + 1) * 512],
                                    start=True, stop=True)
                ins.then_inc(st.h, st.tick())
            block.tensor(_)

            def _(v):
                v.wait_ge(st.h, st.n)
                v.tensor_copy(bc_as[:, 0:1024], pMM[:, 0:1024]).then_inc(sv.h, sv.tick())
            block.vector(_)

            for wt in range(32):
                def _(sy, wt=wt):
                    sy.wait_ge(st.h, st.n)
                    sy.dma_start(out=sx[:], in_=fw[wt * 128:(wt + 1) * 128, :]).then_inc(dl.h, 16)
                    dl.inc(16)
                block.sync(_)

                def _(v, wt=wt):
                    v.wait_ge(dl.h, dl.n)
                    v.tensor_reduce(T[:, 0:1], sx[:], AX.X, AL.max,
                                    apply_absolute_value=True).then_inc(vb.h, vb.tick())
                    v.wait_ge(vb.h, vb.n)
                    v.tensor_scalar(T[:, 0:1], T[:, 0:1], INV448, 1e-6, AL.mult,
                                    AL.max).then_inc(vb.h, vb.tick())
                    v.wait_ge(vb.h, vb.n)
                    v.tensor_copy(sws16[:, wt:wt + 1], T[:, 0:1]).then_inc(vb.h, vb.tick())
                    v.wait_ge(vb.h, vb.n)
                    v.tensor_copy(sws[:, wt:wt + 1], sws16[:, wt:wt + 1]).then_inc(vb.h, vb.tick())
                    v.wait_ge(vb.h, vb.n)
                    v.reciprocal(swsr[:, wt:wt + 1], sws[:, wt:wt + 1]).then_inc(vb.h, vb.tick())
                    v.wait_ge(vb.h, vb.n)
                    v.tensor_scalar(sq1[:], sx[:], swsr[:, wt:wt + 1], -448.0, AL.mult, AL.max)
                    v.tensor_scalar(sq1[:], sq1[:], 448.0, None, AL.min, AL.bypass)
                    v.tensor_scalar(sq2[:], sq1[:], -240.0, 240.0, AL.max, AL.min)
                    v.tensor_copy(sf8[:], sq2[:]).then_inc(vb.h, vb.tick())
                    v.wait_ge(vb.h, vb.n)
                    v.tensor_copy(sq2[:], sf8[:])
                    v.tensor_tensor(ssq[:], sq1[:], sq1[:], AL.mult).then_inc(vb.h, vb.tick())
                    v.wait_ge(vb.h, vb.n)
                    v.tensor_scalar(smk[:], ssq[:], 224.0 * 224.0, None, AL.is_gt, AL.bypass)
                    v.tensor_scalar(ssq[:], sq1[:], DEK, None, AL.mult, AL.bypass)
                    v.tensor_tensor(sx[:], ssq[:], sq1[:], AL.subtract)
                    v.tensor_tensor(ssq[:], ssq[:], sx[:], AL.subtract).then_inc(vb.h, vb.tick())
                    v.wait_ge(vb.h, vb.n)
                    v.select(sq1[:], smk[:], ssq[:], sq2[:]).then_inc(vb.h, vb.tick())
                    v.wait_ge(vb.h, vb.n)
                    v.tensor_copy(sqb[:], sq1[:]).then_inc(sv.h, sv.tick())
                block.vector(_)

                for b8 in range(8):
                    def _(te, wt=wt, b8=b8):
                        te.wait_ge(sv.h, sv.n)
                        te.wait_ge(dg.h, dg.n)
                        for j in range(4):
                            c = b8 * 4 + j
                            ins = te.transpose(pTb[:, j * 128:(j + 1) * 128],
                                               sqb[:, c * 128:(c + 1) * 128], sidb[:])
                        ins.then_inc(st.h, st.tick())
                    block.tensor(_)

                    def _(v, wt=wt, b8=b8):
                        v.wait_ge(st.h, st.n)
                        v.tensor_copy(sqb2[:], pTb[:]).then_inc(sv.h, sv.tick())
                    block.vector(_)

                    def _(g, wt=wt, b8=b8):
                        g.wait_ge(sv.h, sv.n)
                        for j in range(4):
                            c = b8 * 4 + j
                            g.dma_start(out=w8t_sc[c * 128:(c + 1) * 128, wt * 128:(wt + 1) * 128],
                                        in_=sqb2[:, j * 128:(j + 1) * 128]).then_inc(dg.h, 16)
                            dg.inc(16)
                    block.gpsimd(_)

            with ExitStack() as pd:
                sw8 = sbuf("sw8", [128, 32 * 128], BF16, st=pd)
                t2f = sbuf("t2f", [128, 1024], st=pd)
                o16 = sbuf("o16", [128, 1024], BF16, st=pd)

                for do in range(32):
                    pp = (do % 2) * 1024
                    def _(sy, do=do):
                        sy.wait_ge(st.h, st.n)
                        src3 = w8t_sc[:, do * 128:(do + 1) * 128].rearrange(
                            "(c p) n -> p c n", p=128)
                        dst3 = sw8[:].rearrange("p (c n) -> p c n", c=32)
                        for grp in range(4):
                            sy.dma_start(out=dst3[:, grp * 8:(grp + 1) * 8, :],
                                         in_=src3[:, grp * 8:(grp + 1) * 8, :]).then_inc(dl.h, 16)
                            dl.inc(16)
                    block.sync(_)

                    def _(te, do=do, pp=pp):
                        te.wait_ge(dl.h, dl.n)
                        te.wait_ge(sv.h, sv.n)
                        for hf in range(2):
                            for di in range(32):
                                ins = te.matmul(pMM[:, pp + hf * 512:pp + hf * 512 + 512],
                                                sw8[:, di * 128:(di + 1) * 128],
                                                xqT[:, di * R + hf * 512:di * R + hf * 512 + 512],
                                                start=(di == 0), stop=(di == 31))
                        ins.then_inc(st.h, st.tick())
                    block.tensor(_)

                    def _(v, do=do, pp=pp):
                        v.wait_ge(st.h, st.n)
                        v.wait_ge(ss.h, ss.n)
                        v.scalar_tensor_tensor(t2f[:], pMM[:, pp:pp + 1024], sws[:, do:do + 1],
                                               bc_as[:, 0:1024], AL.mult,
                                               AL.mult).then_inc(sv.h, sv.tick())
                    block.vector(_)

                    def _(s, do=do):
                        s.wait_ge(sv.h, sv.n)
                        s.wait_ge(dg.h, dg.n)
                        s.activation(o16[:], t2f[:], AF.Identity, bias=sfb[:, do:do + 1],
                                     scale=1.0).then_inc(ss.h, ss.tick())
                    block.scalar(_)

                    def _(g, do=do):
                        g.wait_ge(ss.h, ss.n)
                        g.dma_start(out=ot_sc[do * 128:(do + 1) * 128, :], in_=o16[:]).then_inc(dg.h, 16)
                        dg.inc(16)
                    block.gpsimd(_)

                sot = sbuf("sot", [128, D], BF16, st=pd)
                sy_ = sbuf("sy", [128, D], st=pd)

                for tt in range(8):
                    def _(sy, tt=tt):
                        sy.wait_ge(dg.h, dg.n)
                        sy.wait_ge(sv.h, sv.n)
                        sy.dma_start(out=sx[:], in_=x2_sc[tt * 128:(tt + 1) * 128, :]).then_inc(dl.h, 16)
                        dl.inc(16)
                        sy.dma_start(out=sot[:], in_=ot_sc[:, tt * 128:(tt + 1) * 128],
                                     transpose=True).then_inc(dl.h, 16)
                        dl.inc(16)
                    block.sync(_)

                    def _(v, tt=tt):
                        v.wait_ge(dl.h, dl.n)
                        v.tensor_tensor(sy_[:], sx[:], sot[:], AL.add).then_inc(sv.h, sv.tick())
                    block.vector(_)

                    def _(g, tt=tt):
                        g.wait_ge(sv.h, sv.n)
                        g.dma_start(out=y_out[tt * 128:(tt + 1) * 128, :], in_=sy_[:]).then_inc(dg.h, 16)
                        dg.inc(16)
                        if tt == 7 and not DEBUG_DUMPS:
                            g.wait_ge(dg.h, dg.n)
                    block.gpsimd(_)

                if DEBUG_DUMPS:
                    dumps = [("d_qt", qt_sc, [D, R], F32), ("d_kin", kin, [D, R], F32),
                             ("d_vin", vin, [D, R], F32), ("d_kall", kall, [4, D, R], F32),
                             ("d_x2", x2_sc, [R, D], F32), ("d_ot", ot_sc, [D, R], BF16),
                             ("d_w8t", w8t_sc, [D, D], BF16)]
                    def _(g):
                        for nm, srcap, shp, dt in dumps:
                            dout = nc.dram_tensor(nm, shp, dt, kind="ExternalOutput").ap()
                            flat_in = srcap.rearrange("... -> (...)") if len(shp) > 2 else srcap
                            nchunk = 8
                            if len(shp) == 3:
                                for j in range(shp[0]):
                                    g.dma_start(out=dout[j], in_=srcap[j]).then_inc(dg.h, 16)
                                    dg.inc(16)
                            else:
                                rows = shp[0]
                                for j in range(nchunk):
                                    r0, r1 = rows * j // nchunk, rows * (j + 1) // nchunk
                                    g.dma_start(out=dout[r0:r1], in_=srcap[r0:r1]).then_inc(dg.h, 16)
                                    dg.inc(16)
                        g.wait_ge(dg.h, dg.n)
                    block.gpsimd(_)

    return nc


_NC_CACHE = [None]


def kernel(**inputs):
    x = np.asarray(inputs["L0_attn_hidden_states"], np.float32)
    wq = np.asarray(inputs["L0_attn_wq"], np.float32)
    wk = np.asarray(inputs["L0_attn_wk"], np.float32)
    wv = np.asarray(inputs["L0_attn_wv"], np.float32)
    wo = np.asarray(inputs["L0_attn_wo"], np.float32)
    bq = np.asarray(inputs["L0_attn_bq"], np.float32)
    bk = np.asarray(inputs["L0_attn_bk"], np.float32)
    bv = np.asarray(inputs["L0_attn_bv"], np.float32)
    bo = np.asarray(inputs["L0_attn_bo"], np.float32)
    fw_ = np.asarray(inputs["L0_ffn_weight"], np.float32)
    fb_ = np.asarray(inputs["L0_ffn_bias"], np.float32)

    wqT = np.ascontiguousarray(wq.T)
    wkT = np.ascontiguousarray(wk.T)
    wvT = np.ascontiguousarray(wv.T)
    woT = np.ascontiguousarray(wo.T)
    boR = np.ascontiguousarray(np.broadcast_to(bo, (128, D)))

    if _NC_CACHE[0] is None:
        _NC_CACHE[0] = build_program()
    nc = _NC_CACHE[0]

    in_maps = []
    for c in range(8):
        b = c // 4
        r0 = (c % 4) * R
        in_maps.append({
            "x_tok": np.ascontiguousarray(x[b, r0:r0 + R]),
            "wqT": wqT, "wkT": wkT, "wvT": wvT, "woT": woT,
            "bq": bq, "bk": bk, "bv": bv, "bo_rep": boR,
            "fw": fw_, "fb": fb_,
        })
    res = run_bass_kernel_spmd(nc, in_maps, core_ids=list(range(8)))
    out = np.empty((2, SEQ, D), np.float32)
    for c in range(8):
        b = c // 4
        r0 = (c % 4) * R
        out[b, r0:r0 + R] = res.results[c]["y"]
    return out


# version 08
# speedup vs baseline: 1.9079x; 1.1375x over previous
import math
from contextlib import ExitStack

import numpy as np
import ml_dtypes
import concourse.bass as bass
import concourse.mybir as mybir
from concourse.bass_utils import run_bass_kernel_spmd

F32 = mybir.dt.float32
BF16 = mybir.dt.bfloat16
FP16 = mybir.dt.float16
FP8 = mybir.dt.float8e4
I32 = mybir.dt.int32
AL = mybir.AluOpType
AF = mybir.ActivationFunctionType
AX = mybir.AxisListType

D = 4096
R = 1024
NH = 32
HD = 128
SEQ = 4096
TOPK = 819
QG = 512
NQG = R // QG
NRND = 10
TSAT = 1.0e30
NDESC = 1
ASCALE = float(np.float32(1.0 / math.sqrt(HD)))
EPS_LN = 1.0e-5
W0 = 64.0
LOFF = 64.5
INV448 = float(np.float32(1.0 / 448.0))
DEK = float(np.float32(2.0 ** 20 + 1.0))


class Cnt:
    def __init__(self, h):
        self.h = h
        self.n = 0

    def inc(self, k=1):
        self.n += k
        return self.n

    def tick(self):
        self.n += 1
        return 1


DEBUG_DUMPS = False


def build_program():
    nc = bass.Bass("TRN2", target_bir_lowering=False, debug=False,
                   detect_race_conditions=False)

    x_tok = nc.dram_tensor("x_tok", [R, D], F32, kind="ExternalInput").ap()
    wqT = nc.dram_tensor("wqT", [D, D], F32, kind="ExternalInput").ap()
    wkT = nc.dram_tensor("wkT", [D, D], F32, kind="ExternalInput").ap()
    wvT = nc.dram_tensor("wvT", [D, D], F32, kind="ExternalInput").ap()
    woT = nc.dram_tensor("woT", [D, D], F32, kind="ExternalInput").ap()
    bq_in = nc.dram_tensor("bq", [D], F32, kind="ExternalInput").ap()
    bk_in = nc.dram_tensor("bk", [D], F32, kind="ExternalInput").ap()
    bv_in = nc.dram_tensor("bv", [D], F32, kind="ExternalInput").ap()
    bo_rep = nc.dram_tensor("bo_rep", [128, D], F32, kind="ExternalInput").ap()
    fw = nc.dram_tensor("fw", [D, D], F32, kind="ExternalInput").ap()
    fb_in = nc.dram_tensor("fb", [D], F32, kind="ExternalInput").ap()
    y_out = nc.dram_tensor("y", [R, D], F32, kind="ExternalOutput").ap()

    qt_sc = nc.dram_tensor("qt_sc", [D, R], F32).ap()
    kin = nc.dram_tensor("kin", [D, R], F32).ap()
    vin = nc.dram_tensor("vin", [D, R], F32).ap()
    kall = nc.dram_tensor("kall", [4, D, R], F32).ap()
    vall = nc.dram_tensor("vall", [4, D, R], F32).ap()
    x2_sc = nc.dram_tensor("x2_sc", [R, D], F32).ap()
    w8t_sc = nc.dram_tensor("w8t_sc", [D, D], BF16).ap()
    ot_sc = nc.dram_tensor("ot_sc", [D, R], BF16).ap()

    ident_f = nc.inline_tensor(np.eye(128, dtype=np.float32), "ident_f").ap()
    ident_b = nc.inline_tensor(np.eye(128, dtype=ml_dtypes.bfloat16), "ident_b").ap()
    ones_f = nc.inline_tensor(np.ones((1, 128), dtype=np.float32), "ones_f").ap()

    es = ExitStack()
    with es:
        def sbuf(name, shape, dt=F32, st=None):
            return (st or es).enter_context(nc.sbuf_tensor(name, shape, dt))

        def sem(name):
            return Cnt(es.enter_context(nc.semaphore(name)))

        block = es.enter_context(nc.Block())

        dl = sem("dl")
        dg = sem("dg")
        sv = sem("sv")
        ss = sem("ss")
        st = sem("st")
        cc = sem("cc")
        vb = sem("vb")

        pMM = es.enter_context(nc.psum_tensor("pMM", [128, 2048], F32))
        pT = es.enter_context(nc.psum_tensor("pT", [128, 1024], F32))
        pCT = es.enter_context(nc.psum_tensor("pCT", [128, 128], F32))
        pTb = es.enter_context(nc.psum_tensor("pTb", [128, 512], BF16))

        sid = sbuf("sid", [128, 128])
        sidb = sbuf("sidb", [128, 128], BF16)
        sones = sbuf("sones", [1, 128])
        sbq = sbuf("sbq", [128, 32])
        sbk = sbuf("sbk", [128, 32])
        sbv = sbuf("sbv", [128, 32])
        sfb = sbuf("sfb", [128, 32])
        sfb16 = sbuf("sfb16", [128, 32], BF16)
        sws = sbuf("sws", [128, 32])
        swsr = sbuf("swsr", [128, 32])
        sws16 = sbuf("sws16", [128, 32], FP16)
        sasr = sbuf("sasr", [128, 8])
        stny = sbuf("stny", [128, 48])
        cst = sbuf("cst", [128, 1])

        T = stny

        def _(sy):
            sy.dma_start(out=sid[:], in_=ident_f).then_inc(dl.h, 16); dl.inc(16)
            sy.dma_start(out=sidb[:], in_=ident_b).then_inc(dl.h, 16); dl.inc(16)
            sy.dma_start(out=sones[:], in_=ones_f).then_inc(dl.h, 16); dl.inc(16)
            for dst, src in ((sbq, bq_in), (sbk, bk_in), (sbv, bv_in), (sfb, fb_in)):
                sy.dma_start(out=dst[:], in_=src.rearrange("(c p) -> p c", p=128),
                             allow_slow_non_contiguous=True).then_inc(dl.h, 16)
                dl.inc(16)
        block.sync(_)

        def _(v):
            v.wait_ge(dl.h, dl.n)
            v.memset(cst[:], ASCALE)
            v.tensor_scalar(sbq[:], sbq[:], ASCALE, None, AL.mult, AL.bypass)
            v.tensor_copy(sfb16[:], sfb[:]).then_inc(vb.h, vb.tick())
            v.wait_ge(vb.h, vb.n)
            v.tensor_copy(sfb[:], sfb16[:]).then_inc(sv.h, sv.tick())
        block.vector(_)

        with ExitStack() as pa:
            sx = sbuf("sx", [128, D], st=pa)
            ssq = sbuf("ssq", [128, D], st=pa)
            sxT = sbuf("sxT", [128, 32 * R], st=pa)
            sev = sbuf("sev", [128, 1024], st=pa)
            wcols = [sbuf(f"wcol{j}", [128, 32 * 128], st=pa) for j in range(2)]

            _dxh_ap = (nc.dram_tensor("d_xh", [R, D], F32, kind="ExternalOutput").ap()
                       if DEBUG_DUMPS else None)

            for tt in range(8):
                def _(sy, tt=tt):
                    if tt > 0:
                        sy.wait_ge(sv.h, sv.n)
                    sy.dma_start(out=sx[:], in_=x_tok[tt * 128:(tt + 1) * 128, :]).then_inc(dl.h, 16)
                    dl.inc(16)
                block.sync(_)

                def _(v):
                    v.wait_ge(dl.h, dl.n)
                    v.tensor_reduce(T[:, 0:1], sx[:], AX.X, AL.add).then_inc(vb.h, vb.tick())
                    v.wait_ge(vb.h, vb.n)
                    v.tensor_scalar(T[:, 0:1], T[:, 0:1], 1.0 / D, None, AL.mult,
                                    AL.bypass).then_inc(sv.h, sv.tick())
                block.vector(_)

                def _(s):
                    s.wait_ge(sv.h, sv.n)
                    s.wait_ge(st.h, st.n)
                    s.activation(ssq[:], sx[:], AF.Square, bias=0.0, scale=1.0,
                                 accum_out=T[:, 1:2]).then_inc(ss.h, ss.tick())
                block.scalar(_)

                def _(v):
                    v.wait_ge(ss.h, ss.n)
                    v.tensor_scalar(T[:, 1:2], T[:, 1:2], 1.0 / D, None, AL.mult, AL.bypass)
                    v.tensor_tensor(T[:, 2:3], T[:, 0:1], T[:, 0:1], AL.mult).then_inc(vb.h, vb.tick())
                    v.wait_ge(vb.h, vb.n)
                    v.tensor_tensor(T[:, 1:2], T[:, 1:2], T[:, 2:3], AL.subtract).then_inc(vb.h, vb.tick())
                    v.wait_ge(vb.h, vb.n)
                    v.tensor_scalar(T[:, 1:2], T[:, 1:2], EPS_LN, None, AL.add,
                                    AL.bypass).then_inc(sv.h, sv.tick())
                block.vector(_)

                def _(s):
                    s.wait_ge(sv.h, sv.n)
                    s.activation(T[:, 2:3], T[:, 1:2], AF.Sqrt, bias=0.0,
                                 scale=1.0).then_inc(ss.h, ss.tick())
                block.scalar(_)

                def _(v):
                    v.wait_ge(ss.h, ss.n)
                    v.reciprocal(T[:, 3:4], T[:, 2:3]).then_inc(vb.h, vb.tick())
                    v.wait_ge(vb.h, vb.n)
                    v.tensor_scalar(ssq[:], sx[:], T[:, 0:1], T[:, 3:4], AL.subtract,
                                    AL.mult).then_inc(sv.h, sv.tick())
                block.vector(_)

                if DEBUG_DUMPS:
                    def _(g, tt=tt):
                        g.wait_ge(sv.h, sv.n)
                        g.dma_start(out=_dxh_ap[tt * 128:(tt + 1) * 128, :],
                                    in_=ssq[:]).then_inc(dg.h, 16)
                        dg.inc(16)
                        g.wait_ge(dg.h, dg.n)
                    block.gpsimd(_)

                for q4 in range(8):
                    ph = (q4 % 2) * 512
                    def _(te, tt=tt, q4=q4, ph=ph):
                        te.wait_ge(sv.h, sv.n)
                        for j in range(4):
                            c = q4 * 4 + j
                            ins = te.transpose(pT[:, ph + j * 128: ph + (j + 1) * 128],
                                               ssq[:, c * 128:(c + 1) * 128], sid[:])
                        ins.then_inc(st.h, st.tick())
                    block.tensor(_)

                    def _(v, tt=tt, q4=q4, ph=ph):
                        v.wait_ge(st.h, st.n)
                        for j in range(4):
                            c = q4 * 4 + j
                            ins = v.tensor_copy(
                                sxT[:, c * R + tt * 128: c * R + (tt + 1) * 128],
                                pT[:, ph + j * 128: ph + (j + 1) * 128])
                        ins.then_inc(sv.h, sv.tick())
                    block.vector(_)

            if DEBUG_DUMPS:
                d_xT = nc.dram_tensor("d_xT", [128, 32 * R], F32, kind="ExternalOutput").ap()

                def _(g):
                    g.wait_ge(sv.h, sv.n)
                    g.dma_start(out=d_xT[:], in_=sxT[:]).then_inc(dg.h, 16)
                    dg.inc(16)
                    g.wait_ge(dg.h, dg.n)
                block.gpsimd(_)

            projs = [(wkT, sbk, kin, 1.0), (wvT, sbv, vin, 1.0), (wqT, sbq, qt_sc, ASCALE)]
            for pi, (wT, sbias, dst, scale) in enumerate(projs):
                snap_wfree = [st.n, st.n]
                for do in range(32):
                    ph = (do % 2) * 1024
                    wcol = wcols[do % 2]
                    def _(sy, wT=wT, do=do, wcol=wcol, w=snap_wfree[do % 2]):
                        sy.wait_ge(st.h, w)
                        src3 = wT[:, do * 128:(do + 1) * 128].rearrange("(c p) n -> p c n", p=128)
                        dst3 = wcol[:].rearrange("p (c n) -> p c n", c=32)
                        for grp in range(4):
                            sy.dma_start(out=dst3[:, grp * 8:(grp + 1) * 8, :],
                                         in_=src3[:, grp * 8:(grp + 1) * 8, :]).then_inc(dl.h, 16)
                            dl.inc(16)
                    block.sync(_)

                    def _(te, do=do, ph=ph, wcol=wcol):
                        te.wait_ge(dl.h, dl.n)
                        te.wait_ge(sv.h, sv.n)
                        te.wait_ge(ss.h, ss.n)
                        for half in range(2):
                            for di in range(32):
                                ins = te.matmul(
                                    pMM[:, ph + half * 512: ph + half * 512 + 512],
                                    wcol[:, di * 128:(di + 1) * 128],
                                    sxT[:, di * R + half * 512: di * R + half * 512 + 512],
                                    start=(di == 0), stop=(di == 31))
                        ins.then_inc(st.h, st.tick())
                        snap_wfree[do % 2] = st.n
                    block.tensor(_)

                    def _(s, do=do, sbias=sbias, scale=scale, ph=ph):
                        s.wait_ge(st.h, st.n)
                        s.wait_ge(dg.h, dg.n)
                        sc = 1.0 if scale == 1.0 else cst[:, 0:1]
                        s.activation(sev[:, 0:512], pMM[:, ph:ph + 512], AF.Identity,
                                     bias=sbias[:, do:do + 1], scale=sc)
                        s.activation(sev[:, 512:1024], pMM[:, ph + 512:ph + 1024], AF.Identity,
                                     bias=sbias[:, do:do + 1], scale=sc).then_inc(ss.h, ss.tick())
                    block.scalar(_)

                    def _(g, do=do, dst=dst):
                        g.wait_ge(ss.h, ss.n)
                        g.dma_start(out=dst[do * 128:(do + 1) * 128, :], in_=sev[:]).then_inc(dg.h, 16)
                        dg.inc(16)
                    block.gpsimd(_)

                if pi in (0, 1):
                    def _(g, pi=pi):
                        g.wait_ge(dg.h, dg.n)
                        src_ap, dst_ap = (kin, kall) if pi == 0 else (vin, vall)
                        g.collective_compute("AllGather", AL.bypass,
                                             replica_groups=[[0, 1, 2, 3], [4, 5, 6, 7]],
                                             ins=[src_ap.opt()],
                                             outs=[dst_ap.opt()]).then_inc(cc.h, cc.tick())
                    block.gpsimd(_)

        def _(sy):
            sy.wait_ge(cc.h, cc.n)
            sy.sem_inc(dl.h, 16)
            dl.inc(16)
        block.sync(_)

        with ExitStack() as pb:
            ctxT = sbuf("ctxT", [128, 32 * QG], st=pb)

            for qg in range(NQG):
                with ExitStack() as ph_heads:
                    kTh = sbuf(f"kTh_{qg}", [128, SEQ], st=ph_heads)
                    vTh = sbuf(f"vTh_{qg}", [128, SEQ], st=ph_heads)
                    vtok = sbuf(f"vtok_{qg}", [128, 32 * 128], st=ph_heads)
                    qTh = sbuf(f"qTh_{qg}", [128, QG], st=ph_heads)
                    sSs = [sbuf(f"sS{p}_{qg}", [128, SEQ], st=ph_heads) for p in range(2)]
                    sEs = [sbuf(f"sE{p}_{qg}", [128, SEQ], st=ph_heads) for p in range(2)]
                    jkb = sbuf(f"jkb_{qg}", [128, SEQ], BF16, st=ph_heads)
                    jks = sbuf(f"jks_{qg}", [128, SEQ], BF16, st=ph_heads)
                    spt = sbuf(f"spt_{qg}", [128, 1024], st=ph_heads)

                    NSUB = NH * 4
                    snap_pmm_free = 0
                    snap_P = {}
                    snap_pv_mm = {}

                    def TB(i, c):
                        return (i % 2) * 16 + c

                    def emit_head_loads(h, qg=qg):
                        def _(sy, h=h, qg=qg):
                            sy.wait_ge(st.h, st.n)
                            for g4 in range(4):
                                sy.dma_start(out=kTh[:, g4 * R:(g4 + 1) * R],
                                             in_=kall[g4, h * 128:(h + 1) * 128, :]).then_inc(dl.h, 16)
                                dl.inc(16)
                                sy.dma_start(out=vTh[:, g4 * R:(g4 + 1) * R],
                                             in_=vall[g4, h * 128:(h + 1) * 128, :]).then_inc(dl.h, 16)
                                dl.inc(16)
                            sy.dma_start(out=qTh[:],
                                         in_=qt_sc[h * 128:(h + 1) * 128,
                                                   qg * QG:(qg + 1) * QG]).then_inc(dl.h, 16)
                            dl.inc(16)
                        block.sync(_)

                    def emit_vtrans(h):
                        for b8 in range(8):
                            ph = (b8 % 2) * 512
                            def _(te, b8=b8, ph=ph):
                                te.wait_ge(dl.h, dl.n)
                                te.wait_ge(sv.h, sv.n)
                                for j in range(4):
                                    c = b8 * 4 + j
                                    ins = te.transpose(pT[:, ph + j * 128:ph + (j + 1) * 128],
                                                       vTh[:, c * 128:(c + 1) * 128], sid[:])
                                ins.then_inc(st.h, st.tick())
                            block.tensor(_)

                            def _(v, b8=b8, ph=ph):
                                v.wait_ge(st.h, st.n)
                                v.tensor_copy(vtok[:, b8 * 512:(b8 + 1) * 512],
                                              pT[:, ph:ph + 512]).then_inc(sv.h, sv.tick())
                            block.vector(_)

                    def emit_scores(i):
                        h, qt = divmod(i, 4)
                        p = i % 2
                        sS = sSs[p]

                        def _h0(te, h=h, qt=qt, w=snap_pmm_free):
                            te.wait_ge(dl.h, dl.n)
                            te.wait_ge(sv.h, w)
                            for j in range(4):
                                ins = te.matmul(pMM[:, j * 512:(j + 1) * 512],
                                                qTh[:, qt * 128:(qt + 1) * 128],
                                                kTh[:, j * 512:(j + 1) * 512],
                                                start=True, stop=True)
                            ins.then_inc(st.h, st.tick())
                        block.tensor(_h0)

                        def _e0(v, i=i, sS=sS):
                            v.wait_ge(st.h, st.n)
                            v.tensor_scalar(sS[:, 0:2048], pMM[:, 0:2048], 0.0, None,
                                            AL.add, AL.max,
                                            accum_out=T[:, TB(i, 16):TB(i, 17)]).then_inc(
                                sv.h, sv.tick())
                        block.vector(_e0)

                        def _h1(te, h=h, qt=qt):
                            te.wait_ge(sv.h, sv.n)
                            for j in range(4):
                                ins = te.matmul(pMM[:, j * 512:(j + 1) * 512],
                                                qTh[:, qt * 128:(qt + 1) * 128],
                                                kTh[:, 2048 + j * 512: 2048 + (j + 1) * 512],
                                                start=True, stop=True)
                            ins.then_inc(st.h, st.tick())
                        block.tensor(_h1)

                        def _e1(v, i=i, sS=sS):
                            v.wait_ge(st.h, st.n)
                            v.tensor_scalar(sS[:, 2048:4096], pMM[:, 0:2048], 0.0, None,
                                            AL.add, AL.max,
                                            accum_out=T[:, TB(i, 17):TB(i, 18)]).then_inc(
                                sv.h, sv.tick())
                        block.vector(_e1)

                        def _nm(v, i=i):
                            v.wait_ge(sv.h, sv.n)
                            v.tensor_tensor(T[:, TB(i, 4):TB(i, 5)], T[:, TB(i, 16):TB(i, 17)],
                                            T[:, TB(i, 17):TB(i, 18)], AL.max).then_inc(vb.h, vb.tick())
                            v.wait_ge(vb.h, vb.n)
                            v.tensor_scalar(T[:, TB(i, 4):TB(i, 5)], T[:, TB(i, 4):TB(i, 5)],
                                            -1.0, None, AL.mult, AL.bypass).then_inc(sv.h, sv.tick())
                        block.vector(_nm)

                        def _ex(s, i=i, p=p, sS=sS):
                            s.wait_ge(sv.h, sv.n)
                            s.wait_ge(st.h, st.n)
                            s.activation(sEs[p][:], sS[:], AF.Exp, bias=T[:, TB(i, 4):TB(i, 5)],
                                         scale=1.0).then_inc(ss.h, ss.tick())
                        block.scalar(_ex)

                    def emit_linit(i):
                        def _(v, i=i):
                            v.wait_ge(ss.h, ss.n)
                            v.tensor_scalar(T[:, TB(i, 5):TB(i, 6)], T[:, TB(i, 4):TB(i, 5)],
                                            -1.0, -LOFF, AL.mult, AL.add).then_inc(vb.h, vb.tick())
                            v.wait_ge(vb.h, vb.n)
                        block.vector(_)

                    def emit_bisect_pair(a, b):
                        emit_linit(a)
                        emit_linit(b)
                        for r in range(NRND):
                            w3 = W0 / (3.0 ** (r + 1))
                            snaps = {}
                            for i in (a, b):
                                sS = sSs[i % 2]
                                def _p(v, i=i, w3=w3, sS=sS):
                                    v.tensor_scalar(T[:, TB(i, 6):TB(i, 7)], T[:, TB(i, 5):TB(i, 6)],
                                                    w3, None, AL.add, AL.bypass)
                                    v.tensor_scalar(T[:, TB(i, 12):TB(i, 13)], T[:, TB(i, 5):TB(i, 6)],
                                                    -TSAT, -2.0 * w3 * TSAT,
                                                    AL.mult, AL.add).then_inc(sv.h, sv.tick())
                                    v.tensor_scalar(jkb[:], sS[:], T[:, TB(i, 6):TB(i, 7)], None,
                                                    AL.is_ge, AL.add,
                                                    accum_out=T[:, TB(i, 7):TB(i, 8)]).then_inc(
                                        vb.h, vb.tick())
                                block.vector(_p)
                                snaps[i] = (sv.n, vb.n)

                                def _t(s, i=i, sS=sS, w=snaps[i][0]):
                                    s.wait_ge(sv.h, w)
                                    s.activation(jks[:], sS[:], AF.Tanh, bias=T[:, TB(i, 12):TB(i, 13)],
                                                 scale=TSAT,
                                                 accum_out=T[:, TB(i, 18):TB(i, 19)]).then_inc(ss.h, ss.tick())
                                block.scalar(_t)
                                snaps[i] = (snaps[i][1], ss.n)

                            for i in (a, b):
                                wv_, ws_ = snaps[i]
                                def _c(v, i=i, w3=w3, wv=wv_, ws=ws_):
                                    v.wait_ge(vb.h, wv)
                                    v.tensor_scalar(T[:, TB(i, 8):TB(i, 9)], T[:, TB(i, 7):TB(i, 8)],
                                                    TOPK - 0.5, None, AL.is_ge, AL.bypass)
                                    v.wait_ge(ss.h, ws)
                                    v.tensor_scalar(T[:, TB(i, 19):TB(i, 20)], T[:, TB(i, 18):TB(i, 19)],
                                                    2.0 * TOPK - SEQ - 0.5, None,
                                                    AL.is_ge, AL.bypass).then_inc(vb.h, vb.tick())
                                    v.wait_ge(vb.h, vb.n)
                                    v.tensor_tensor(T[:, TB(i, 8):TB(i, 9)], T[:, TB(i, 8):TB(i, 9)],
                                                    T[:, TB(i, 19):TB(i, 20)], AL.add).then_inc(vb.h, vb.tick())
                                    v.wait_ge(vb.h, vb.n)
                                    v.scalar_tensor_tensor(T[:, TB(i, 5):TB(i, 6)], T[:, TB(i, 8):TB(i, 9)],
                                                           w3, T[:, TB(i, 5):TB(i, 6)],
                                                           AL.mult, AL.add).then_inc(vb.h, vb.tick())
                                    v.wait_ge(vb.h, vb.n)
                                block.vector(_c)

                    def emit_desc_mask(i):
                        p = i % 2
                        sE = sEs[p]
                        sS = sSs[p]
                        wfin = W0 / (3.0 ** NRND)

                        def _(v, i=i, wfin=wfin):
                            v.tensor_scalar(T[:, TB(i, 6):TB(i, 7)], T[:, TB(i, 5):TB(i, 6)],
                                            wfin, None, AL.add, AL.bypass).then_inc(sv.h, sv.tick())
                        block.vector(_)

                        def _(s, i=i):
                            s.wait_ge(sv.h, sv.n)
                            s.activation(T[:, TB(i, 9):TB(i, 11)], T[:, TB(i, 5):TB(i, 7)],
                                         AF.Exp, bias=T[:, TB(i, 4):TB(i, 5)],
                                         scale=1.0).then_inc(ss.h, ss.tick())
                        block.scalar(_)

                        def _(v, i=i):
                            v.wait_ge(ss.h, ss.n)
                            v.memset(T[:, TB(i, 13):TB(i, 14)], 0.0)
                            v.memset(T[:, TB(i, 11):TB(i, 12)], 0.0)
                        block.vector(_)

                        for d in range(NDESC):
                            def _(v, i=i, sE=sE, sS=sS):
                                v.scalar_tensor_tensor(sS[:], sE[:], T[:, TB(i, 10):TB(i, 11)], sE[:],
                                                       AL.is_lt, AL.mult).then_inc(vb.h, vb.tick())
                                v.wait_ge(vb.h, vb.n)
                                v.tensor_reduce(T[:, TB(i, 12):TB(i, 13)], sS[:], AX.X,
                                                AL.max).then_inc(vb.h, vb.tick())
                                v.wait_ge(vb.h, vb.n)
                                v.tensor_scalar(jkb[:], sE[:], T[:, TB(i, 12):TB(i, 13)], None,
                                                AL.is_ge, AL.add,
                                                accum_out=T[:, TB(i, 7):TB(i, 8)]).then_inc(vb.h, vb.tick())
                                v.wait_ge(vb.h, vb.n)
                                v.tensor_scalar(T[:, TB(i, 8):TB(i, 9)], T[:, TB(i, 7):TB(i, 8)],
                                                TOPK - 0.5, None, AL.is_ge, AL.bypass)
                                v.tensor_scalar(T[:, TB(i, 14):TB(i, 15)], T[:, TB(i, 11):TB(i, 12)],
                                                0.5, None, AL.is_lt, AL.bypass).then_inc(vb.h, vb.tick())
                                v.wait_ge(vb.h, vb.n)
                                v.tensor_tensor(T[:, TB(i, 14):TB(i, 15)], T[:, TB(i, 14):TB(i, 15)],
                                                T[:, TB(i, 8):TB(i, 9)], AL.mult).then_inc(vb.h, vb.tick())
                                v.wait_ge(vb.h, vb.n)
                                v.scalar_tensor_tensor(T[:, TB(i, 13):TB(i, 14)], T[:, TB(i, 12):TB(i, 13)],
                                                       T[:, TB(i, 14):TB(i, 15)],
                                                       T[:, TB(i, 13):TB(i, 14)], AL.mult, AL.add)
                                v.tensor_tensor(T[:, TB(i, 11):TB(i, 12)], T[:, TB(i, 11):TB(i, 12)],
                                                T[:, TB(i, 8):TB(i, 9)], AL.max)
                                v.tensor_copy(T[:, TB(i, 10):TB(i, 11)],
                                              T[:, TB(i, 12):TB(i, 13)]).then_inc(vb.h, vb.tick())
                                v.wait_ge(vb.h, vb.n)
                            block.vector(_)

                        def _(v, i=i, sE=sE, sS=sS):
                            v.tensor_scalar(T[:, TB(i, 14):TB(i, 15)], T[:, TB(i, 11):TB(i, 12)],
                                            0.5, None, AL.is_lt, AL.bypass).then_inc(vb.h, vb.tick())
                            v.wait_ge(vb.h, vb.n)
                            v.scalar_tensor_tensor(T[:, TB(i, 13):TB(i, 14)], T[:, TB(i, 9):TB(i, 10)],
                                                   T[:, TB(i, 14):TB(i, 15)],
                                                   T[:, TB(i, 13):TB(i, 14)], AL.mult,
                                                   AL.add).then_inc(vb.h, vb.tick())
                            v.wait_ge(vb.h, vb.n)
                            v.scalar_tensor_tensor(sS[:], sE[:], T[:, TB(i, 13):TB(i, 14)], sE[:],
                                                   AL.is_ge, AL.mult,
                                                   accum_out=T[:, TB(i, 14):TB(i, 15)]).then_inc(vb.h, vb.tick())
                            v.wait_ge(vb.h, vb.n)
                            v.reciprocal(T[:, TB(i, 15):TB(i, 16)],
                                         T[:, TB(i, 14):TB(i, 15)]).then_inc(vb.h, vb.tick())
                            v.wait_ge(vb.h, vb.n)
                            v.tensor_scalar(sE[:], sS[:], T[:, TB(i, 15):TB(i, 16)], None,
                                            AL.mult, AL.bypass).then_inc(sv.h, sv.tick())
                        block.vector(_)

                    def emit_pv(i):
                        p = i % 2
                        sE = sEs[p]
                        for grp in range(8):
                            ph = (grp % 2) * 512
                            def _t(te, grp=grp, ph=ph, sE=sE):
                                te.wait_ge(sv.h, sv.n)
                                for j in range(4):
                                    ck = grp * 4 + j
                                    ins = te.transpose(pT[:, ph + j * 128: ph + (j + 1) * 128],
                                                       sE[:, ck * 128:(ck + 1) * 128], sid[:])
                                ins.then_inc(st.h, st.tick())
                            block.tensor(_t)

                            def _c(v, grp=grp, ph=ph):
                                v.wait_ge(st.h, st.n)
                                v.tensor_copy(spt[:, ph:ph + 512],
                                              pT[:, ph:ph + 512]).then_inc(sv.h, sv.tick())
                            block.vector(_c)

                            def _m(te, grp=grp, ph=ph):
                                te.wait_ge(sv.h, sv.n)
                                for j in range(4):
                                    ck = grp * 4 + j
                                    ins = te.matmul(pCT[:], vtok[:, ck * 128:(ck + 1) * 128],
                                                    spt[:, ph + j * 128: ph + (j + 1) * 128],
                                                    start=(ck == 0), stop=(ck == 31))
                                ins.then_inc(st.h, st.tick())
                            block.tensor(_m)
                        snap_pv_mm[i] = st.n

                    def emit_ctx_evac(i):
                        h, qt = divmod(i, 4)
                        def _(v, h=h, qt=qt, w=snap_pv_mm[i]):
                            v.wait_ge(st.h, w)
                            v.tensor_copy(ctxT[:, h * QG + qt * 128: h * QG + (qt + 1) * 128],
                                          pCT[:]).then_inc(sv.h, sv.tick())
                        block.vector(_)

                    for pr in range(NSUB // 2):
                        a, b = 2 * pr, 2 * pr + 1
                        h, qta = divmod(a, 4)
                        if qta == 0:
                            emit_head_loads(h)
                            if a > 0:
                                emit_pv(a - 2)
                                emit_ctx_evac(a - 2)
                                emit_pv(a - 1)
                                emit_ctx_evac(a - 1)
                            emit_vtrans(h)
                        else:
                            emit_pv(a - 2)
                            emit_ctx_evac(a - 2)
                            emit_pv(a - 1)
                            emit_ctx_evac(a - 1)
                        emit_scores(a)
                        snap_pmm_free = sv.n
                        emit_scores(b)
                        snap_pmm_free = sv.n
                        emit_bisect_pair(a, b)
                        emit_desc_mask(a)
                        snap_P[a] = sv.n
                        emit_desc_mask(b)
                        snap_P[b] = sv.n
                    emit_pv(NSUB - 2)
                    emit_ctx_evac(NSUB - 2)
                    emit_pv(NSUB - 1)
                    emit_ctx_evac(NSUB - 1)

                with ExitStack() as po:
                    swo = sbuf(f"swo_{qg}", [128, 32 * 512], st=po)
                    sbo = sbuf(f"sbo_{qg}", [128, 512], st=po)
                    sxr = sbuf(f"sxr_{qg}", [128, 512], st=po)
                    sev2 = sbuf(f"sev2_{qg}", [128, 512], st=po)

                    for dk in range(8):
                        def _(sy, dk=dk):
                            sy.wait_ge(st.h, st.n)
                            src3 = woT[:, dk * 512:(dk + 1) * 512].rearrange(
                                "(c p) n -> p c n", p=128)
                            dst3 = swo[:].rearrange("p (c n) -> p c n", c=32)
                            for grp in range(8):
                                sy.dma_start(out=dst3[:, grp * 4:(grp + 1) * 4, :],
                                             in_=src3[:, grp * 4:(grp + 1) * 4, :]).then_inc(dl.h, 16)
                                dl.inc(16)
                            sy.dma_start(out=sbo[:], in_=bo_rep[:, dk * 512:(dk + 1) * 512]).then_inc(dl.h, 16)
                            dl.inc(16)
                        block.sync(_)

                        for qt in range(QG // 128):
                            pp = (qt % 2) * 512
                            def _(sy, dk=dk, qt=qt, qg=qg):
                                sy.wait_ge(sv.h, sv.n)
                                row0 = qg * QG + qt * 128
                                sy.dma_start(out=sxr[:],
                                             in_=x_tok[row0:row0 + 128,
                                                       dk * 512:(dk + 1) * 512]).then_inc(dl.h, 16)
                                dl.inc(16)
                            block.sync(_)

                            def _(te, dk=dk, qt=qt, pp=pp):
                                te.wait_ge(dl.h, dl.n)
                                te.wait_ge(sv.h, sv.n)
                                for dc in range(32):
                                    ins = te.matmul(pMM[:, pp:pp + 512],
                                                    ctxT[:, dc * QG + qt * 128: dc * QG + (qt + 1) * 128],
                                                    swo[:, dc * 512:(dc + 1) * 512],
                                                    start=(dc == 0), stop=(dc == 31))
                                ins.then_inc(st.h, st.tick())
                            block.tensor(_)

                            def _(v, dk=dk, qt=qt, pp=pp):
                                v.wait_ge(st.h, st.n)
                                v.wait_ge(dg.h, dg.n)
                                v.tensor_tensor(sev2[:], sbo[:], sxr[:], AL.add)
                                v.tensor_tensor(sev2[:], sev2[:], pMM[:, pp:pp + 512],
                                                AL.add).then_inc(sv.h, sv.tick())
                            block.vector(_)

                            def _(g, dk=dk, qt=qt, qg=qg):
                                g.wait_ge(sv.h, sv.n)
                                row0 = qg * QG + qt * 128
                                g.dma_start(out=x2_sc[row0:row0 + 128, dk * 512:(dk + 1) * 512],
                                            in_=sev2[:]).then_inc(dg.h, 16)
                                dg.inc(16)
                            block.gpsimd(_)

        with ExitStack() as pc:
            sx = sbuf("sxc", [128, D], st=pc)
            ssq = sbuf("ssqc", [128, D], st=pc)
            sq1 = sbuf("sq1", [128, D], st=pc)
            sq2 = sbuf("sq2", [128, D], st=pc)
            sf8 = sbuf("sf8", [128, D], FP8, st=pc)
            smk = sbuf("smk", [128, D], I32, st=pc)
            sqb = sbuf("sqb", [128, D], BF16, st=pc)
            sqb2 = sbuf("sqb2", [128, 512], BF16, st=pc)
            sasrow = sbuf("sasrow", [1, 1024], st=pc)
            bc_as = sbuf("bc_as", [128, 1024], st=pc)
            xqT = sbuf("xqT", [128, 32 * R], BF16, st=pc)

            for tt in range(8):
                def _(sy, tt=tt):
                    sy.wait_ge(st.h, st.n)
                    sy.wait_ge(dg.h, dg.n)
                    sy.dma_start(out=sx[:], in_=x2_sc[tt * 128:(tt + 1) * 128, :]).then_inc(dl.h, 16)
                    dl.inc(16)
                block.sync(_)

                def _(v):
                    v.wait_ge(dl.h, dl.n)
                    v.tensor_reduce(T[:, 0:1], sx[:], AX.X, AL.add).then_inc(vb.h, vb.tick())
                    v.wait_ge(vb.h, vb.n)
                    v.tensor_scalar(T[:, 0:1], T[:, 0:1], 1.0 / D, None, AL.mult,
                                    AL.bypass).then_inc(sv.h, sv.tick())
                block.vector(_)

                def _(s):
                    s.wait_ge(sv.h, sv.n)
                    s.activation(ssq[:], sx[:], AF.Square, bias=0.0, scale=1.0,
                                 accum_out=T[:, 1:2]).then_inc(ss.h, ss.tick())
                block.scalar(_)

                def _(v):
                    v.wait_ge(ss.h, ss.n)
                    v.tensor_scalar(T[:, 1:2], T[:, 1:2], 1.0 / D, None, AL.mult, AL.bypass)
                    v.tensor_tensor(T[:, 2:3], T[:, 0:1], T[:, 0:1], AL.mult).then_inc(vb.h, vb.tick())
                    v.wait_ge(vb.h, vb.n)
                    v.tensor_tensor(T[:, 1:2], T[:, 1:2], T[:, 2:3], AL.subtract).then_inc(vb.h, vb.tick())
                    v.wait_ge(vb.h, vb.n)
                    v.tensor_scalar(T[:, 1:2], T[:, 1:2], EPS_LN, None, AL.add,
                                    AL.bypass).then_inc(sv.h, sv.tick())
                block.vector(_)

                def _(s):
                    s.wait_ge(sv.h, sv.n)
                    s.activation(T[:, 2:3], T[:, 1:2], AF.Sqrt, bias=0.0,
                                 scale=1.0).then_inc(ss.h, ss.tick())
                block.scalar(_)

                def _(v, tt=tt):
                    v.wait_ge(ss.h, ss.n)
                    v.reciprocal(T[:, 3:4], T[:, 2:3]).then_inc(vb.h, vb.tick())
                    v.wait_ge(vb.h, vb.n)
                    v.tensor_scalar(ssq[:], sx[:], T[:, 0:1], T[:, 3:4], AL.subtract, AL.mult)
                    v.tensor_reduce(T[:, 5:6], ssq[:], AX.X, AL.max,
                                    apply_absolute_value=True).then_inc(vb.h, vb.tick())
                    v.wait_ge(vb.h, vb.n)
                    v.tensor_scalar(T[:, 5:6], T[:, 5:6], INV448, 1e-6, AL.mult,
                                    AL.max).then_inc(vb.h, vb.tick())
                    v.wait_ge(vb.h, vb.n)
                    v.tensor_copy(sasr[:, tt:tt + 1], T[:, 5:6])
                    v.reciprocal(T[:, 6:7], T[:, 5:6]).then_inc(vb.h, vb.tick())
                    v.wait_ge(vb.h, vb.n)
                    v.tensor_scalar(sq1[:], ssq[:], T[:, 6:7], -448.0, AL.mult, AL.max)
                    v.tensor_scalar(sq1[:], sq1[:], 448.0, None, AL.min, AL.bypass)
                    v.tensor_scalar(sq2[:], sq1[:], -240.0, 240.0, AL.max, AL.min)
                    v.tensor_copy(sf8[:], sq2[:]).then_inc(vb.h, vb.tick())
                    v.wait_ge(vb.h, vb.n)
                    v.tensor_copy(sq2[:], sf8[:])
                    v.tensor_tensor(ssq[:], sq1[:], sq1[:], AL.mult).then_inc(vb.h, vb.tick())
                    v.wait_ge(vb.h, vb.n)
                    v.tensor_scalar(smk[:], ssq[:], 224.0 * 224.0, None, AL.is_gt, AL.bypass)
                    v.tensor_scalar(ssq[:], sq1[:], DEK, None, AL.mult, AL.bypass)
                    v.tensor_tensor(sx[:], ssq[:], sq1[:], AL.subtract)
                    v.tensor_tensor(ssq[:], ssq[:], sx[:], AL.subtract).then_inc(vb.h, vb.tick())
                    v.wait_ge(vb.h, vb.n)
                    v.select(sq1[:], smk[:], ssq[:], sq2[:]).then_inc(vb.h, vb.tick())
                    v.wait_ge(vb.h, vb.n)
                    v.tensor_copy(sqb[:], sq1[:]).then_inc(sv.h, sv.tick())
                block.vector(_)

                for b8 in range(8):
                    def _(te, tt=tt, b8=b8):
                        te.wait_ge(sv.h, sv.n)
                        for j in range(4):
                            c = b8 * 4 + j
                            ins = te.transpose(pTb[:, j * 128:(j + 1) * 128],
                                               sqb[:, c * 128:(c + 1) * 128], sidb[:])
                        ins.then_inc(st.h, st.tick())
                    block.tensor(_)

                    def _(v, tt=tt, b8=b8):
                        v.wait_ge(st.h, st.n)
                        for j in range(4):
                            c = b8 * 4 + j
                            ins = v.tensor_copy(
                                xqT[:, c * R + tt * 128: c * R + (tt + 1) * 128],
                                pTb[:, j * 128:(j + 1) * 128])
                        ins.then_inc(sv.h, sv.tick())
                    block.vector(_)

                def _(te, tt=tt):
                    te.wait_ge(sv.h, sv.n)
                    te.transpose(pT[0:1, 0:128], sasr[:, tt:tt + 1], sid[:]).then_inc(st.h, st.tick())
                block.tensor(_)

                def _(v, tt=tt):
                    v.wait_ge(st.h, st.n)
                    v.tensor_copy(sasrow[0:1, tt * 128:(tt + 1) * 128],
                                  pT[0:1, 0:128]).then_inc(sv.h, sv.tick())
                block.vector(_)

            def _(te):
                te.wait_ge(sv.h, sv.n)
                for half in range(2):
                    ins = te.matmul(pMM[:, half * 512:(half + 1) * 512], sones[:],
                                    sasrow[0:1, half * 512:(half + 1) * 512],
                                    start=True, stop=True)
                ins.then_inc(st.h, st.tick())
            block.tensor(_)

            def _(v):
                v.wait_ge(st.h, st.n)
                v.tensor_copy(bc_as[:, 0:1024], pMM[:, 0:1024]).then_inc(sv.h, sv.tick())
            block.vector(_)

            for wt in range(32):
                def _(sy, wt=wt):
                    sy.wait_ge(st.h, st.n)
                    sy.dma_start(out=sx[:], in_=fw[wt * 128:(wt + 1) * 128, :]).then_inc(dl.h, 16)
                    dl.inc(16)
                block.sync(_)

                def _(v, wt=wt):
                    v.wait_ge(dl.h, dl.n)
                    v.tensor_reduce(T[:, 0:1], sx[:], AX.X, AL.max,
                                    apply_absolute_value=True).then_inc(vb.h, vb.tick())
                    v.wait_ge(vb.h, vb.n)
                    v.tensor_scalar(T[:, 0:1], T[:, 0:1], INV448, 1e-6, AL.mult,
                                    AL.max).then_inc(vb.h, vb.tick())
                    v.wait_ge(vb.h, vb.n)
                    v.tensor_copy(sws16[:, wt:wt + 1], T[:, 0:1]).then_inc(vb.h, vb.tick())
                    v.wait_ge(vb.h, vb.n)
                    v.tensor_copy(sws[:, wt:wt + 1], sws16[:, wt:wt + 1]).then_inc(vb.h, vb.tick())
                    v.wait_ge(vb.h, vb.n)
                    v.reciprocal(swsr[:, wt:wt + 1], sws[:, wt:wt + 1]).then_inc(vb.h, vb.tick())
                    v.wait_ge(vb.h, vb.n)
                    v.tensor_scalar(sq1[:], sx[:], swsr[:, wt:wt + 1], -448.0, AL.mult, AL.max)
                    v.tensor_scalar(sq1[:], sq1[:], 448.0, None, AL.min, AL.bypass)
                    v.tensor_scalar(sq2[:], sq1[:], -240.0, 240.0, AL.max, AL.min)
                    v.tensor_copy(sf8[:], sq2[:]).then_inc(vb.h, vb.tick())
                    v.wait_ge(vb.h, vb.n)
                    v.tensor_copy(sq2[:], sf8[:])
                    v.tensor_tensor(ssq[:], sq1[:], sq1[:], AL.mult).then_inc(vb.h, vb.tick())
                    v.wait_ge(vb.h, vb.n)
                    v.tensor_scalar(smk[:], ssq[:], 224.0 * 224.0, None, AL.is_gt, AL.bypass)
                    v.tensor_scalar(ssq[:], sq1[:], DEK, None, AL.mult, AL.bypass)
                    v.tensor_tensor(sx[:], ssq[:], sq1[:], AL.subtract)
                    v.tensor_tensor(ssq[:], ssq[:], sx[:], AL.subtract).then_inc(vb.h, vb.tick())
                    v.wait_ge(vb.h, vb.n)
                    v.select(sq1[:], smk[:], ssq[:], sq2[:]).then_inc(vb.h, vb.tick())
                    v.wait_ge(vb.h, vb.n)
                    v.tensor_copy(sqb[:], sq1[:]).then_inc(sv.h, sv.tick())
                block.vector(_)

                for b8 in range(8):
                    def _(te, wt=wt, b8=b8):
                        te.wait_ge(sv.h, sv.n)
                        te.wait_ge(dg.h, dg.n)
                        for j in range(4):
                            c = b8 * 4 + j
                            ins = te.transpose(pTb[:, j * 128:(j + 1) * 128],
                                               sqb[:, c * 128:(c + 1) * 128], sidb[:])
                        ins.then_inc(st.h, st.tick())
                    block.tensor(_)

                    def _(v, wt=wt, b8=b8):
                        v.wait_ge(st.h, st.n)
                        v.tensor_copy(sqb2[:], pTb[:]).then_inc(sv.h, sv.tick())
                    block.vector(_)

                    def _(g, wt=wt, b8=b8):
                        g.wait_ge(sv.h, sv.n)
                        for j in range(4):
                            c = b8 * 4 + j
                            g.dma_start(out=w8t_sc[c * 128:(c + 1) * 128, wt * 128:(wt + 1) * 128],
                                        in_=sqb2[:, j * 128:(j + 1) * 128]).then_inc(dg.h, 16)
                            dg.inc(16)
                    block.gpsimd(_)

            with ExitStack() as pd:
                sw8 = sbuf("sw8", [128, 32 * 128], BF16, st=pd)
                t2f = sbuf("t2f", [128, 1024], st=pd)
                o16 = sbuf("o16", [128, 1024], BF16, st=pd)

                for do in range(32):
                    pp = (do % 2) * 1024
                    def _(sy, do=do):
                        sy.wait_ge(st.h, st.n)
                        src3 = w8t_sc[:, do * 128:(do + 1) * 128].rearrange(
                            "(c p) n -> p c n", p=128)
                        dst3 = sw8[:].rearrange("p (c n) -> p c n", c=32)
                        for grp in range(4):
                            sy.dma_start(out=dst3[:, grp * 8:(grp + 1) * 8, :],
                                         in_=src3[:, grp * 8:(grp + 1) * 8, :]).then_inc(dl.h, 16)
                            dl.inc(16)
                    block.sync(_)

                    def _(te, do=do, pp=pp):
                        te.wait_ge(dl.h, dl.n)
                        te.wait_ge(sv.h, sv.n)
                        for hf in range(2):
                            for di in range(32):
                                ins = te.matmul(pMM[:, pp + hf * 512:pp + hf * 512 + 512],
                                                sw8[:, di * 128:(di + 1) * 128],
                                                xqT[:, di * R + hf * 512:di * R + hf * 512 + 512],
                                                start=(di == 0), stop=(di == 31))
                        ins.then_inc(st.h, st.tick())
                    block.tensor(_)

                    def _(v, do=do, pp=pp):
                        v.wait_ge(st.h, st.n)
                        v.wait_ge(ss.h, ss.n)
                        v.scalar_tensor_tensor(t2f[:], pMM[:, pp:pp + 1024], sws[:, do:do + 1],
                                               bc_as[:, 0:1024], AL.mult,
                                               AL.mult).then_inc(sv.h, sv.tick())
                    block.vector(_)

                    def _(s, do=do):
                        s.wait_ge(sv.h, sv.n)
                        s.wait_ge(dg.h, dg.n)
                        s.activation(o16[:], t2f[:], AF.Identity, bias=sfb[:, do:do + 1],
                                     scale=1.0).then_inc(ss.h, ss.tick())
                    block.scalar(_)

                    def _(g, do=do):
                        g.wait_ge(ss.h, ss.n)
                        g.dma_start(out=ot_sc[do * 128:(do + 1) * 128, :], in_=o16[:]).then_inc(dg.h, 16)
                        dg.inc(16)
                    block.gpsimd(_)

                sot = sbuf("sot", [128, D], BF16, st=pd)
                sy_ = sbuf("sy", [128, D], st=pd)

                for tt in range(8):
                    def _(sy, tt=tt):
                        sy.wait_ge(dg.h, dg.n)
                        sy.wait_ge(sv.h, sv.n)
                        sy.dma_start(out=sx[:], in_=x2_sc[tt * 128:(tt + 1) * 128, :]).then_inc(dl.h, 16)
                        dl.inc(16)
                        sy.dma_start(out=sot[:], in_=ot_sc[:, tt * 128:(tt + 1) * 128],
                                     transpose=True).then_inc(dl.h, 16)
                        dl.inc(16)
                    block.sync(_)

                    def _(v, tt=tt):
                        v.wait_ge(dl.h, dl.n)
                        v.tensor_tensor(sy_[:], sx[:], sot[:], AL.add).then_inc(sv.h, sv.tick())
                    block.vector(_)

                    def _(g, tt=tt):
                        g.wait_ge(sv.h, sv.n)
                        g.dma_start(out=y_out[tt * 128:(tt + 1) * 128, :], in_=sy_[:]).then_inc(dg.h, 16)
                        dg.inc(16)
                        if tt == 7 and not DEBUG_DUMPS:
                            g.wait_ge(dg.h, dg.n)
                    block.gpsimd(_)

                if DEBUG_DUMPS:
                    dumps = [("d_qt", qt_sc, [D, R], F32), ("d_kin", kin, [D, R], F32),
                             ("d_vin", vin, [D, R], F32), ("d_kall", kall, [4, D, R], F32),
                             ("d_x2", x2_sc, [R, D], F32), ("d_ot", ot_sc, [D, R], BF16),
                             ("d_w8t", w8t_sc, [D, D], BF16)]
                    def _(g):
                        for nm, srcap, shp, dt in dumps:
                            dout = nc.dram_tensor(nm, shp, dt, kind="ExternalOutput").ap()
                            flat_in = srcap.rearrange("... -> (...)") if len(shp) > 2 else srcap
                            nchunk = 8
                            if len(shp) == 3:
                                for j in range(shp[0]):
                                    g.dma_start(out=dout[j], in_=srcap[j]).then_inc(dg.h, 16)
                                    dg.inc(16)
                            else:
                                rows = shp[0]
                                for j in range(nchunk):
                                    r0, r1 = rows * j // nchunk, rows * (j + 1) // nchunk
                                    g.dma_start(out=dout[r0:r1], in_=srcap[r0:r1]).then_inc(dg.h, 16)
                                    dg.inc(16)
                        g.wait_ge(dg.h, dg.n)
                    block.gpsimd(_)

    return nc


_NC_CACHE = [None]


def kernel(**inputs):
    x = np.asarray(inputs["L0_attn_hidden_states"], np.float32)
    wq = np.asarray(inputs["L0_attn_wq"], np.float32)
    wk = np.asarray(inputs["L0_attn_wk"], np.float32)
    wv = np.asarray(inputs["L0_attn_wv"], np.float32)
    wo = np.asarray(inputs["L0_attn_wo"], np.float32)
    bq = np.asarray(inputs["L0_attn_bq"], np.float32)
    bk = np.asarray(inputs["L0_attn_bk"], np.float32)
    bv = np.asarray(inputs["L0_attn_bv"], np.float32)
    bo = np.asarray(inputs["L0_attn_bo"], np.float32)
    fw_ = np.asarray(inputs["L0_ffn_weight"], np.float32)
    fb_ = np.asarray(inputs["L0_ffn_bias"], np.float32)

    wqT = np.ascontiguousarray(wq.T)
    wkT = np.ascontiguousarray(wk.T)
    wvT = np.ascontiguousarray(wv.T)
    woT = np.ascontiguousarray(wo.T)
    boR = np.ascontiguousarray(np.broadcast_to(bo, (128, D)))

    if _NC_CACHE[0] is None:
        _NC_CACHE[0] = build_program()
    nc = _NC_CACHE[0]

    in_maps = []
    for c in range(8):
        b = c // 4
        r0 = (c % 4) * R
        in_maps.append({
            "x_tok": np.ascontiguousarray(x[b, r0:r0 + R]),
            "wqT": wqT, "wkT": wkT, "wvT": wvT, "woT": woT,
            "bq": bq, "bk": bk, "bv": bv, "bo_rep": boR,
            "fw": fw_, "fb": fb_,
        })
    res = run_bass_kernel_spmd(nc, in_maps, core_ids=list(range(8)))
    out = np.empty((2, SEQ, D), np.float32)
    for c in range(8):
        b = c // 4
        r0 = (c % 4) * R
        out[b, r0:r0 + R] = res.results[c]["y"]
    return out


# version 09
# speedup vs baseline: 1.9793x; 1.0374x over previous
import math
from contextlib import ExitStack

import numpy as np
import ml_dtypes
import concourse.bass as bass
import concourse.mybir as mybir
from concourse.bass_utils import run_bass_kernel_spmd

F32 = mybir.dt.float32
BF16 = mybir.dt.bfloat16
FP16 = mybir.dt.float16
FP8 = mybir.dt.float8e4
I32 = mybir.dt.int32
AL = mybir.AluOpType
AF = mybir.ActivationFunctionType
AX = mybir.AxisListType

D = 4096
R = 1024
NH = 32
HD = 128
SEQ = 4096
TOPK = 819
QG = 512
NQG = R // QG
NRND = 9
TSAT = 1.0e30
NDESC = 1
ASCALE = float(np.float32(1.0 / math.sqrt(HD)))
EPS_LN = 1.0e-5
W0 = 64.0
LOFF = 64.5
INV448 = float(np.float32(1.0 / 448.0))
DEK = float(np.float32(2.0 ** 20 + 1.0))


class Cnt:
    def __init__(self, h):
        self.h = h
        self.n = 0

    def inc(self, k=1):
        self.n += k
        return self.n

    def tick(self):
        self.n += 1
        return 1


DEBUG_DUMPS = False


def build_program():
    nc = bass.Bass("TRN2", target_bir_lowering=False, debug=False,
                   detect_race_conditions=False)

    x_tok = nc.dram_tensor("x_tok", [R, D], F32, kind="ExternalInput").ap()
    wqT = nc.dram_tensor("wqT", [D, D], F32, kind="ExternalInput").ap()
    wkT = nc.dram_tensor("wkT", [D, D], F32, kind="ExternalInput").ap()
    wvT = nc.dram_tensor("wvT", [D, D], F32, kind="ExternalInput").ap()
    woT = nc.dram_tensor("woT", [D, D], F32, kind="ExternalInput").ap()
    bq_in = nc.dram_tensor("bq", [D], F32, kind="ExternalInput").ap()
    bk_in = nc.dram_tensor("bk", [D], F32, kind="ExternalInput").ap()
    bv_in = nc.dram_tensor("bv", [D], F32, kind="ExternalInput").ap()
    bo_rep = nc.dram_tensor("bo_rep", [128, D], F32, kind="ExternalInput").ap()
    fw = nc.dram_tensor("fw", [D, D], F32, kind="ExternalInput").ap()
    fb_in = nc.dram_tensor("fb", [D], F32, kind="ExternalInput").ap()
    y_out = nc.dram_tensor("y", [R, D], F32, kind="ExternalOutput").ap()

    qt_sc = nc.dram_tensor("qt_sc", [D, R], F32).ap()
    kin = nc.dram_tensor("kin", [D, R], F32).ap()
    vin = nc.dram_tensor("vin", [D, R], F32).ap()
    kall = nc.dram_tensor("kall", [4, D, R], F32).ap()
    vall = nc.dram_tensor("vall", [4, D, R], F32).ap()
    x2_sc = nc.dram_tensor("x2_sc", [R, D], F32).ap()
    w8t_sc = nc.dram_tensor("w8t_sc", [D, D], BF16).ap()
    ot_sc = nc.dram_tensor("ot_sc", [D, R], BF16).ap()

    ident_f = nc.inline_tensor(np.eye(128, dtype=np.float32), "ident_f").ap()
    ident_b = nc.inline_tensor(np.eye(128, dtype=ml_dtypes.bfloat16), "ident_b").ap()
    ones_f = nc.inline_tensor(np.ones((1, 128), dtype=np.float32), "ones_f").ap()

    es = ExitStack()
    with es:
        def sbuf(name, shape, dt=F32, st=None):
            return (st or es).enter_context(nc.sbuf_tensor(name, shape, dt))

        def sem(name):
            return Cnt(es.enter_context(nc.semaphore(name)))

        block = es.enter_context(nc.Block())

        dl = sem("dl")
        dg = sem("dg")
        sv = sem("sv")
        ss = sem("ss")
        st = sem("st")
        cc = sem("cc")
        vb = sem("vb")

        pMM = es.enter_context(nc.psum_tensor("pMM", [128, 2048], F32))
        pT = es.enter_context(nc.psum_tensor("pT", [128, 1024], F32))
        pCT = es.enter_context(nc.psum_tensor("pCT", [128, 128], F32))
        pTb = es.enter_context(nc.psum_tensor("pTb", [128, 512], BF16))

        sid = sbuf("sid", [128, 128])
        sidb = sbuf("sidb", [128, 128], BF16)
        sones = sbuf("sones", [1, 128])
        sbq = sbuf("sbq", [128, 32])
        sbk = sbuf("sbk", [128, 32])
        sbv = sbuf("sbv", [128, 32])
        sfb = sbuf("sfb", [128, 32])
        sfb16 = sbuf("sfb16", [128, 32], BF16)
        sws = sbuf("sws", [128, 32])
        swsr = sbuf("swsr", [128, 32])
        sws16 = sbuf("sws16", [128, 32], FP16)
        sasr = sbuf("sasr", [128, 8])
        stny = sbuf("stny", [128, 48])
        cst = sbuf("cst", [128, 1])

        T = stny

        def _(sy):
            sy.dma_start(out=sid[:], in_=ident_f).then_inc(dl.h, 16); dl.inc(16)
            sy.dma_start(out=sidb[:], in_=ident_b).then_inc(dl.h, 16); dl.inc(16)
            sy.dma_start(out=sones[:], in_=ones_f).then_inc(dl.h, 16); dl.inc(16)
            for dst, src in ((sbq, bq_in), (sbk, bk_in), (sbv, bv_in), (sfb, fb_in)):
                sy.dma_start(out=dst[:], in_=src.rearrange("(c p) -> p c", p=128),
                             allow_slow_non_contiguous=True).then_inc(dl.h, 16)
                dl.inc(16)
        block.sync(_)

        def _(v):
            v.wait_ge(dl.h, dl.n)
            v.memset(cst[:], ASCALE)
            v.tensor_scalar(sbq[:], sbq[:], ASCALE, None, AL.mult, AL.bypass)
            v.tensor_copy(sfb16[:], sfb[:]).then_inc(vb.h, vb.tick())
            v.wait_ge(vb.h, vb.n)
            v.tensor_copy(sfb[:], sfb16[:]).then_inc(sv.h, sv.tick())
        block.vector(_)

        with ExitStack() as pa:
            sx = sbuf("sx", [128, D], st=pa)
            ssq = sbuf("ssq", [128, D], st=pa)
            sxT = sbuf("sxT", [128, 32 * R], st=pa)
            sev = sbuf("sev", [128, 1024], st=pa)
            wcols = [sbuf(f"wcol{j}", [128, 32 * 128], st=pa) for j in range(2)]

            _dxh_ap = (nc.dram_tensor("d_xh", [R, D], F32, kind="ExternalOutput").ap()
                       if DEBUG_DUMPS else None)

            for tt in range(8):
                def _(sy, tt=tt):
                    if tt > 0:
                        sy.wait_ge(sv.h, sv.n)
                    sy.dma_start(out=sx[:], in_=x_tok[tt * 128:(tt + 1) * 128, :]).then_inc(dl.h, 16)
                    dl.inc(16)
                block.sync(_)

                def _(v):
                    v.wait_ge(dl.h, dl.n)
                    v.tensor_reduce(T[:, 0:1], sx[:], AX.X, AL.add).then_inc(vb.h, vb.tick())
                    v.wait_ge(vb.h, vb.n)
                    v.tensor_scalar(T[:, 0:1], T[:, 0:1], 1.0 / D, None, AL.mult,
                                    AL.bypass).then_inc(sv.h, sv.tick())
                block.vector(_)

                def _(s):
                    s.wait_ge(sv.h, sv.n)
                    s.wait_ge(st.h, st.n)
                    s.activation(ssq[:], sx[:], AF.Square, bias=0.0, scale=1.0,
                                 accum_out=T[:, 1:2]).then_inc(ss.h, ss.tick())
                block.scalar(_)

                def _(v):
                    v.wait_ge(ss.h, ss.n)
                    v.tensor_scalar(T[:, 1:2], T[:, 1:2], 1.0 / D, None, AL.mult, AL.bypass)
                    v.tensor_tensor(T[:, 2:3], T[:, 0:1], T[:, 0:1], AL.mult).then_inc(vb.h, vb.tick())
                    v.wait_ge(vb.h, vb.n)
                    v.tensor_tensor(T[:, 1:2], T[:, 1:2], T[:, 2:3], AL.subtract).then_inc(vb.h, vb.tick())
                    v.wait_ge(vb.h, vb.n)
                    v.tensor_scalar(T[:, 1:2], T[:, 1:2], EPS_LN, None, AL.add,
                                    AL.bypass).then_inc(sv.h, sv.tick())
                block.vector(_)

                def _(s):
                    s.wait_ge(sv.h, sv.n)
                    s.activation(T[:, 2:3], T[:, 1:2], AF.Sqrt, bias=0.0,
                                 scale=1.0).then_inc(ss.h, ss.tick())
                block.scalar(_)

                def _(v):
                    v.wait_ge(ss.h, ss.n)
                    v.reciprocal(T[:, 3:4], T[:, 2:3]).then_inc(vb.h, vb.tick())
                    v.wait_ge(vb.h, vb.n)
                    v.tensor_scalar(ssq[:], sx[:], T[:, 0:1], T[:, 3:4], AL.subtract,
                                    AL.mult).then_inc(sv.h, sv.tick())
                block.vector(_)

                if DEBUG_DUMPS:
                    def _(g, tt=tt):
                        g.wait_ge(sv.h, sv.n)
                        g.dma_start(out=_dxh_ap[tt * 128:(tt + 1) * 128, :],
                                    in_=ssq[:]).then_inc(dg.h, 16)
                        dg.inc(16)
                        g.wait_ge(dg.h, dg.n)
                    block.gpsimd(_)

                for q4 in range(8):
                    ph = (q4 % 2) * 512
                    def _(te, tt=tt, q4=q4, ph=ph):
                        te.wait_ge(sv.h, sv.n)
                        for j in range(4):
                            c = q4 * 4 + j
                            ins = te.transpose(pT[:, ph + j * 128: ph + (j + 1) * 128],
                                               ssq[:, c * 128:(c + 1) * 128], sid[:])
                        ins.then_inc(st.h, st.tick())
                    block.tensor(_)

                    def _(v, tt=tt, q4=q4, ph=ph):
                        v.wait_ge(st.h, st.n)
                        for j in range(4):
                            c = q4 * 4 + j
                            ins = v.tensor_copy(
                                sxT[:, c * R + tt * 128: c * R + (tt + 1) * 128],
                                pT[:, ph + j * 128: ph + (j + 1) * 128])
                        ins.then_inc(sv.h, sv.tick())
                    block.vector(_)

            if DEBUG_DUMPS:
                d_xT = nc.dram_tensor("d_xT", [128, 32 * R], F32, kind="ExternalOutput").ap()

                def _(g):
                    g.wait_ge(sv.h, sv.n)
                    g.dma_start(out=d_xT[:], in_=sxT[:]).then_inc(dg.h, 16)
                    dg.inc(16)
                    g.wait_ge(dg.h, dg.n)
                block.gpsimd(_)

            projs = [(wkT, sbk, kin, 1.0), (wvT, sbv, vin, 1.0), (wqT, sbq, qt_sc, ASCALE)]
            for pi, (wT, sbias, dst, scale) in enumerate(projs):
                snap_wfree = [st.n, st.n]
                for do in range(32):
                    ph = (do % 2) * 1024
                    wcol = wcols[do % 2]
                    def _(sy, wT=wT, do=do, wcol=wcol, w=snap_wfree[do % 2]):
                        sy.wait_ge(st.h, w)
                        src3 = wT[:, do * 128:(do + 1) * 128].rearrange("(c p) n -> p c n", p=128)
                        dst3 = wcol[:].rearrange("p (c n) -> p c n", c=32)
                        for grp in range(4):
                            sy.dma_start(out=dst3[:, grp * 8:(grp + 1) * 8, :],
                                         in_=src3[:, grp * 8:(grp + 1) * 8, :]).then_inc(dl.h, 16)
                            dl.inc(16)
                    block.sync(_)

                    def _(te, do=do, ph=ph, wcol=wcol):
                        te.wait_ge(dl.h, dl.n)
                        te.wait_ge(sv.h, sv.n)
                        te.wait_ge(ss.h, ss.n)
                        for half in range(2):
                            for di in range(32):
                                ins = te.matmul(
                                    pMM[:, ph + half * 512: ph + half * 512 + 512],
                                    wcol[:, di * 128:(di + 1) * 128],
                                    sxT[:, di * R + half * 512: di * R + half * 512 + 512],
                                    start=(di == 0), stop=(di == 31))
                        ins.then_inc(st.h, st.tick())
                        snap_wfree[do % 2] = st.n
                    block.tensor(_)

                    def _(s, do=do, sbias=sbias, scale=scale, ph=ph):
                        s.wait_ge(st.h, st.n)
                        s.wait_ge(dg.h, dg.n)
                        sc = 1.0 if scale == 1.0 else cst[:, 0:1]
                        s.activation(sev[:, 0:512], pMM[:, ph:ph + 512], AF.Identity,
                                     bias=sbias[:, do:do + 1], scale=sc)
                        s.activation(sev[:, 512:1024], pMM[:, ph + 512:ph + 1024], AF.Identity,
                                     bias=sbias[:, do:do + 1], scale=sc).then_inc(ss.h, ss.tick())
                    block.scalar(_)

                    def _(g, do=do, dst=dst):
                        g.wait_ge(ss.h, ss.n)
                        g.dma_start(out=dst[do * 128:(do + 1) * 128, :], in_=sev[:]).then_inc(dg.h, 16)
                        dg.inc(16)
                    block.gpsimd(_)

                if pi in (0, 1):
                    def _(g, pi=pi):
                        g.wait_ge(dg.h, dg.n)
                        src_ap, dst_ap = (kin, kall) if pi == 0 else (vin, vall)
                        g.collective_compute("AllGather", AL.bypass,
                                             replica_groups=[[0, 1, 2, 3], [4, 5, 6, 7]],
                                             ins=[src_ap.opt()],
                                             outs=[dst_ap.opt()]).then_inc(cc.h, cc.tick())
                    block.gpsimd(_)

        def _(sy):
            sy.wait_ge(cc.h, cc.n)
            sy.sem_inc(dl.h, 16)
            dl.inc(16)
        block.sync(_)

        with ExitStack() as pb:
            ctxT = sbuf("ctxT", [128, 32 * QG], st=pb)

            for qg in range(NQG):
                with ExitStack() as ph_heads:
                    kTh = sbuf(f"kTh_{qg}", [128, SEQ], st=ph_heads)
                    vTh = sbuf(f"vTh_{qg}", [128, SEQ], st=ph_heads)
                    vtok = sbuf(f"vtok_{qg}", [128, 32 * 128], st=ph_heads)
                    qTh = sbuf(f"qTh_{qg}", [128, QG], st=ph_heads)
                    sSs = [sbuf(f"sS{p}_{qg}", [128, SEQ], st=ph_heads) for p in range(2)]
                    sEs = [sbuf(f"sE{p}_{qg}", [128, SEQ], st=ph_heads) for p in range(2)]
                    jkb = sbuf(f"jkb_{qg}", [128, SEQ], BF16, st=ph_heads)
                    jks = sbuf(f"jks_{qg}", [128, SEQ], BF16, st=ph_heads)
                    spt = sbuf(f"spt_{qg}", [128, 1024], st=ph_heads)

                    NSUB = NH * 4
                    snap_pmm_free = 0
                    snap_P = {}
                    snap_pv_mm = {}

                    def TB(i, c):
                        return (i % 2) * 16 + c

                    def emit_head_loads(h, qg=qg):
                        def _(sy, h=h, qg=qg):
                            sy.wait_ge(st.h, st.n)
                            for g4 in range(4):
                                sy.dma_start(out=kTh[:, g4 * R:(g4 + 1) * R],
                                             in_=kall[g4, h * 128:(h + 1) * 128, :]).then_inc(dl.h, 16)
                                dl.inc(16)
                                sy.dma_start(out=vTh[:, g4 * R:(g4 + 1) * R],
                                             in_=vall[g4, h * 128:(h + 1) * 128, :]).then_inc(dl.h, 16)
                                dl.inc(16)
                            sy.dma_start(out=qTh[:],
                                         in_=qt_sc[h * 128:(h + 1) * 128,
                                                   qg * QG:(qg + 1) * QG]).then_inc(dl.h, 16)
                            dl.inc(16)
                        block.sync(_)

                    def emit_vtrans(h):
                        for b8 in range(8):
                            ph = (b8 % 2) * 512
                            def _(te, b8=b8, ph=ph):
                                te.wait_ge(dl.h, dl.n)
                                te.wait_ge(sv.h, sv.n)
                                for j in range(4):
                                    c = b8 * 4 + j
                                    ins = te.transpose(pT[:, ph + j * 128:ph + (j + 1) * 128],
                                                       vTh[:, c * 128:(c + 1) * 128], sid[:])
                                ins.then_inc(st.h, st.tick())
                            block.tensor(_)

                            def _(v, b8=b8, ph=ph):
                                v.wait_ge(st.h, st.n)
                                v.tensor_copy(vtok[:, b8 * 512:(b8 + 1) * 512],
                                              pT[:, ph:ph + 512]).then_inc(sv.h, sv.tick())
                            block.vector(_)

                    def emit_scores(i):
                        h, qt = divmod(i, 4)
                        p = i % 2
                        sS = sSs[p]

                        def _h0(te, h=h, qt=qt, w=snap_pmm_free):
                            te.wait_ge(dl.h, dl.n)
                            te.wait_ge(sv.h, w)
                            for j in range(4):
                                ins = te.matmul(pMM[:, j * 512:(j + 1) * 512],
                                                qTh[:, qt * 128:(qt + 1) * 128],
                                                kTh[:, j * 512:(j + 1) * 512],
                                                start=True, stop=True)
                            ins.then_inc(st.h, st.tick())
                        block.tensor(_h0)

                        def _e0(v, i=i, sS=sS):
                            v.wait_ge(st.h, st.n)
                            v.tensor_scalar(sS[:, 0:2048], pMM[:, 0:2048], 0.0, None,
                                            AL.add, AL.max,
                                            accum_out=T[:, TB(i, 16):TB(i, 17)]).then_inc(
                                sv.h, sv.tick())
                        block.vector(_e0)

                        def _h1(te, h=h, qt=qt):
                            te.wait_ge(sv.h, sv.n)
                            for j in range(4):
                                ins = te.matmul(pMM[:, j * 512:(j + 1) * 512],
                                                qTh[:, qt * 128:(qt + 1) * 128],
                                                kTh[:, 2048 + j * 512: 2048 + (j + 1) * 512],
                                                start=True, stop=True)
                            ins.then_inc(st.h, st.tick())
                        block.tensor(_h1)

                        def _e1(v, i=i, sS=sS):
                            v.wait_ge(st.h, st.n)
                            v.tensor_scalar(sS[:, 2048:4096], pMM[:, 0:2048], 0.0, None,
                                            AL.add, AL.max,
                                            accum_out=T[:, TB(i, 17):TB(i, 18)]).then_inc(
                                sv.h, sv.tick())
                        block.vector(_e1)

                        def _nm(v, i=i):
                            v.wait_ge(sv.h, sv.n)
                            v.tensor_tensor(T[:, TB(i, 4):TB(i, 5)], T[:, TB(i, 16):TB(i, 17)],
                                            T[:, TB(i, 17):TB(i, 18)], AL.max).then_inc(vb.h, vb.tick())
                            v.wait_ge(vb.h, vb.n)
                            v.tensor_scalar(T[:, TB(i, 4):TB(i, 5)], T[:, TB(i, 4):TB(i, 5)],
                                            -1.0, None, AL.mult, AL.bypass).then_inc(sv.h, sv.tick())
                        block.vector(_nm)

                        def _ex(s, i=i, p=p, sS=sS):
                            s.wait_ge(sv.h, sv.n)
                            s.wait_ge(st.h, st.n)
                            s.activation(sEs[p][:], sS[:], AF.Exp, bias=T[:, TB(i, 4):TB(i, 5)],
                                         scale=1.0).then_inc(ss.h, ss.tick())
                        block.scalar(_ex)

                    def emit_linit(i):
                        def _(v, i=i):
                            v.wait_ge(ss.h, ss.n)
                            v.tensor_scalar(T[:, TB(i, 5):TB(i, 6)], T[:, TB(i, 4):TB(i, 5)],
                                            -1.0, -LOFF, AL.mult, AL.add).then_inc(vb.h, vb.tick())
                            v.wait_ge(vb.h, vb.n)
                        block.vector(_)

                    def emit_bisect_pair(a, b):
                        emit_linit(a)
                        emit_linit(b)
                        for r in range(NRND):
                            w3 = W0 / (3.0 ** (r + 1))
                            snaps = {}
                            for i in (a, b):
                                sS = sSs[i % 2]
                                def _p(v, i=i, w3=w3, sS=sS):
                                    v.tensor_scalar(T[:, TB(i, 6):TB(i, 7)], T[:, TB(i, 5):TB(i, 6)],
                                                    w3, None, AL.add, AL.bypass)
                                    v.tensor_scalar(T[:, TB(i, 12):TB(i, 13)], T[:, TB(i, 5):TB(i, 6)],
                                                    -TSAT, -2.0 * w3 * TSAT,
                                                    AL.mult, AL.add).then_inc(sv.h, sv.tick())
                                    v.tensor_scalar(jkb[:], sS[:], T[:, TB(i, 6):TB(i, 7)], None,
                                                    AL.is_ge, AL.add,
                                                    accum_out=T[:, TB(i, 7):TB(i, 8)]).then_inc(
                                        vb.h, vb.tick())
                                block.vector(_p)
                                snaps[i] = (sv.n, vb.n)

                                def _t(s, i=i, sS=sS, w=snaps[i][0]):
                                    s.wait_ge(sv.h, w)
                                    s.activation(jks[:], sS[:], AF.Tanh, bias=T[:, TB(i, 12):TB(i, 13)],
                                                 scale=TSAT,
                                                 accum_out=T[:, TB(i, 18):TB(i, 19)]).then_inc(ss.h, ss.tick())
                                block.scalar(_t)
                                snaps[i] = (snaps[i][1], ss.n)

                            for i in (a, b):
                                wv_, ws_ = snaps[i]
                                def _c(v, i=i, w3=w3, wv=wv_, ws=ws_):
                                    v.wait_ge(vb.h, wv)
                                    v.tensor_scalar(T[:, TB(i, 8):TB(i, 9)], T[:, TB(i, 7):TB(i, 8)],
                                                    TOPK - 0.5, None, AL.is_ge, AL.bypass)
                                    v.wait_ge(ss.h, ws)
                                    v.tensor_scalar(T[:, TB(i, 19):TB(i, 20)], T[:, TB(i, 18):TB(i, 19)],
                                                    2.0 * TOPK - SEQ - 0.5, None,
                                                    AL.is_ge, AL.bypass).then_inc(vb.h, vb.tick())
                                    v.wait_ge(vb.h, vb.n)
                                    v.tensor_tensor(T[:, TB(i, 8):TB(i, 9)], T[:, TB(i, 8):TB(i, 9)],
                                                    T[:, TB(i, 19):TB(i, 20)], AL.add).then_inc(vb.h, vb.tick())
                                    v.wait_ge(vb.h, vb.n)
                                    v.scalar_tensor_tensor(T[:, TB(i, 5):TB(i, 6)], T[:, TB(i, 8):TB(i, 9)],
                                                           w3, T[:, TB(i, 5):TB(i, 6)],
                                                           AL.mult, AL.add).then_inc(vb.h, vb.tick())
                                    v.wait_ge(vb.h, vb.n)
                                block.vector(_c)

                    def emit_desc_mask(i):
                        p = i % 2
                        sE = sEs[p]
                        sS = sSs[p]
                        wfin = W0 / (3.0 ** NRND)

                        def _(v, i=i, wfin=wfin):
                            v.tensor_scalar(T[:, TB(i, 6):TB(i, 7)], T[:, TB(i, 5):TB(i, 6)],
                                            wfin, None, AL.add, AL.bypass).then_inc(sv.h, sv.tick())
                        block.vector(_)

                        def _(s, i=i):
                            s.wait_ge(sv.h, sv.n)
                            s.activation(T[:, TB(i, 9):TB(i, 11)], T[:, TB(i, 5):TB(i, 7)],
                                         AF.Exp, bias=T[:, TB(i, 4):TB(i, 5)],
                                         scale=1.0).then_inc(ss.h, ss.tick())
                        block.scalar(_)

                        def _(v, i=i):
                            v.wait_ge(ss.h, ss.n)
                            v.memset(T[:, TB(i, 13):TB(i, 14)], 0.0)
                            v.memset(T[:, TB(i, 11):TB(i, 12)], 0.0)
                        block.vector(_)

                        for d in range(NDESC):
                            def _(v, i=i, sE=sE, sS=sS):
                                v.scalar_tensor_tensor(sS[:], sE[:], T[:, TB(i, 10):TB(i, 11)], sE[:],
                                                       AL.is_lt, AL.mult).then_inc(vb.h, vb.tick())
                                v.wait_ge(vb.h, vb.n)
                                v.tensor_reduce(T[:, TB(i, 12):TB(i, 13)], sS[:], AX.X,
                                                AL.max).then_inc(vb.h, vb.tick())
                                v.wait_ge(vb.h, vb.n)
                                v.tensor_scalar(jkb[:], sE[:], T[:, TB(i, 12):TB(i, 13)], None,
                                                AL.is_ge, AL.add,
                                                accum_out=T[:, TB(i, 7):TB(i, 8)]).then_inc(vb.h, vb.tick())
                                v.wait_ge(vb.h, vb.n)
                                v.tensor_scalar(T[:, TB(i, 8):TB(i, 9)], T[:, TB(i, 7):TB(i, 8)],
                                                TOPK - 0.5, None, AL.is_ge, AL.bypass)
                                v.tensor_scalar(T[:, TB(i, 14):TB(i, 15)], T[:, TB(i, 11):TB(i, 12)],
                                                0.5, None, AL.is_lt, AL.bypass).then_inc(vb.h, vb.tick())
                                v.wait_ge(vb.h, vb.n)
                                v.tensor_tensor(T[:, TB(i, 14):TB(i, 15)], T[:, TB(i, 14):TB(i, 15)],
                                                T[:, TB(i, 8):TB(i, 9)], AL.mult).then_inc(vb.h, vb.tick())
                                v.wait_ge(vb.h, vb.n)
                                v.scalar_tensor_tensor(T[:, TB(i, 13):TB(i, 14)], T[:, TB(i, 12):TB(i, 13)],
                                                       T[:, TB(i, 14):TB(i, 15)],
                                                       T[:, TB(i, 13):TB(i, 14)], AL.mult, AL.add)
                                v.tensor_tensor(T[:, TB(i, 11):TB(i, 12)], T[:, TB(i, 11):TB(i, 12)],
                                                T[:, TB(i, 8):TB(i, 9)], AL.max)
                                v.tensor_copy(T[:, TB(i, 10):TB(i, 11)],
                                              T[:, TB(i, 12):TB(i, 13)]).then_inc(vb.h, vb.tick())
                                v.wait_ge(vb.h, vb.n)
                            block.vector(_)

                        def _(v, i=i, sE=sE, sS=sS):
                            v.tensor_scalar(T[:, TB(i, 14):TB(i, 15)], T[:, TB(i, 11):TB(i, 12)],
                                            0.5, None, AL.is_lt, AL.bypass).then_inc(vb.h, vb.tick())
                            v.wait_ge(vb.h, vb.n)
                            v.scalar_tensor_tensor(T[:, TB(i, 13):TB(i, 14)], T[:, TB(i, 9):TB(i, 10)],
                                                   T[:, TB(i, 14):TB(i, 15)],
                                                   T[:, TB(i, 13):TB(i, 14)], AL.mult,
                                                   AL.add).then_inc(vb.h, vb.tick())
                            v.wait_ge(vb.h, vb.n)
                            v.scalar_tensor_tensor(sS[:], sE[:], T[:, TB(i, 13):TB(i, 14)], sE[:],
                                                   AL.is_ge, AL.mult,
                                                   accum_out=T[:, TB(i, 14):TB(i, 15)]).then_inc(vb.h, vb.tick())
                            v.wait_ge(vb.h, vb.n)
                            v.reciprocal(T[:, TB(i, 15):TB(i, 16)],
                                         T[:, TB(i, 14):TB(i, 15)]).then_inc(vb.h, vb.tick())
                            v.wait_ge(vb.h, vb.n)
                            v.tensor_scalar(sE[:], sS[:], T[:, TB(i, 15):TB(i, 16)], None,
                                            AL.mult, AL.bypass).then_inc(sv.h, sv.tick())
                        block.vector(_)

                    def emit_desc_mask_pair(a, b):
                        for i in (a, b):
                            wfin = W0 / (3.0 ** NRND)
                            def _(v, i=i, wfin=wfin):
                                v.tensor_scalar(T[:, TB(i, 6):TB(i, 7)], T[:, TB(i, 5):TB(i, 6)],
                                                wfin, None, AL.add, AL.bypass).then_inc(sv.h, sv.tick())
                            block.vector(_)

                            def _(s, i=i):
                                s.wait_ge(sv.h, sv.n)
                                s.activation(T[:, TB(i, 9):TB(i, 11)], T[:, TB(i, 5):TB(i, 7)],
                                             AF.Exp, bias=T[:, TB(i, 4):TB(i, 5)],
                                             scale=1.0).then_inc(ss.h, ss.tick())
                            block.scalar(_)

                        def _(v):
                            v.wait_ge(ss.h, ss.n)
                            for i in (a, b):
                                v.memset(T[:, TB(i, 13):TB(i, 14)], 0.0)
                                v.memset(T[:, TB(i, 11):TB(i, 12)], 0.0)
                            v.sem_inc(vb.h, vb.tick())
                        block.vector(_)

                        def seg_list(i):
                            sE, sS = sEs[i % 2], sSs[i % 2]
                            return [
                                lambda v: v.scalar_tensor_tensor(sS[:], sE[:], T[:, TB(i, 10):TB(i, 11)],
                                                                 sE[:], AL.is_lt, AL.mult),
                                lambda v: v.tensor_reduce(T[:, TB(i, 12):TB(i, 13)], sS[:], AX.X, AL.max),
                                lambda v: v.tensor_scalar(jkb[:], sE[:], T[:, TB(i, 12):TB(i, 13)], None,
                                                          AL.is_ge, AL.add,
                                                          accum_out=T[:, TB(i, 7):TB(i, 8)]),
                                lambda v: (v.tensor_scalar(T[:, TB(i, 8):TB(i, 9)], T[:, TB(i, 7):TB(i, 8)],
                                                           TOPK - 0.5, None, AL.is_ge, AL.bypass),
                                           v.tensor_scalar(T[:, TB(i, 14):TB(i, 15)],
                                                           T[:, TB(i, 11):TB(i, 12)],
                                                           0.5, None, AL.is_lt, AL.bypass))[-1],
                                lambda v: v.tensor_tensor(T[:, TB(i, 14):TB(i, 15)],
                                                          T[:, TB(i, 14):TB(i, 15)],
                                                          T[:, TB(i, 8):TB(i, 9)], AL.mult),
                                lambda v: (v.scalar_tensor_tensor(T[:, TB(i, 13):TB(i, 14)],
                                                                  T[:, TB(i, 12):TB(i, 13)],
                                                                  T[:, TB(i, 14):TB(i, 15)],
                                                                  T[:, TB(i, 13):TB(i, 14)], AL.mult, AL.add),
                                           v.tensor_tensor(T[:, TB(i, 11):TB(i, 12)],
                                                           T[:, TB(i, 11):TB(i, 12)],
                                                           T[:, TB(i, 8):TB(i, 9)], AL.max),
                                           v.tensor_copy(T[:, TB(i, 10):TB(i, 11)],
                                                         T[:, TB(i, 12):TB(i, 13)]))[-1],
                                lambda v: v.tensor_scalar(T[:, TB(i, 14):TB(i, 15)],
                                                          T[:, TB(i, 11):TB(i, 12)],
                                                          0.5, None, AL.is_lt, AL.bypass),
                                lambda v: v.scalar_tensor_tensor(T[:, TB(i, 13):TB(i, 14)],
                                                                 T[:, TB(i, 9):TB(i, 10)],
                                                                 T[:, TB(i, 14):TB(i, 15)],
                                                                 T[:, TB(i, 13):TB(i, 14)], AL.mult, AL.add),
                                lambda v: v.scalar_tensor_tensor(sS[:], sE[:], T[:, TB(i, 13):TB(i, 14)],
                                                                 sE[:], AL.is_ge, AL.mult,
                                                                 accum_out=T[:, TB(i, 14):TB(i, 15)]),
                                lambda v: v.reciprocal(T[:, TB(i, 15):TB(i, 16)],
                                                       T[:, TB(i, 14):TB(i, 15)]),
                                lambda v: v.tensor_scalar(sE[:], sS[:], T[:, TB(i, 15):TB(i, 16)], None,
                                                          AL.mult, AL.bypass),
                            ]
                        assert NDESC == 1
                        segs = {a: seg_list(a), b: seg_list(b)}
                        lastw = {a: vb.n, b: vb.n}
                        nseg = len(segs[a])
                        for k in range(nseg):
                            for i in (a, b):
                                def _(v, i=i, k=k, w=lastw[i]):
                                    v.wait_ge(vb.h, w)
                                    segs[i][k](v).then_inc(vb.h, vb.tick())
                                block.vector(_)
                                lastw[i] = vb.n
                        def _(v, w=lastw[b]):
                            v.wait_ge(vb.h, w)
                            v.sem_inc(sv.h, sv.tick())
                            v.sem_inc(sv.h, sv.tick())
                        block.vector(_)

                    def emit_pv(i):
                        p = i % 2
                        sE = sEs[p]
                        for grp in range(8):
                            ph = (grp % 2) * 512
                            def _t(te, grp=grp, ph=ph, sE=sE):
                                te.wait_ge(sv.h, sv.n)
                                for j in range(4):
                                    ck = grp * 4 + j
                                    ins = te.transpose(pT[:, ph + j * 128: ph + (j + 1) * 128],
                                                       sE[:, ck * 128:(ck + 1) * 128], sid[:])
                                ins.then_inc(st.h, st.tick())
                            block.tensor(_t)

                            def _c(v, grp=grp, ph=ph):
                                v.wait_ge(st.h, st.n)
                                v.tensor_copy(spt[:, ph:ph + 512],
                                              pT[:, ph:ph + 512]).then_inc(sv.h, sv.tick())
                            block.vector(_c)

                            def _m(te, grp=grp, ph=ph):
                                te.wait_ge(sv.h, sv.n)
                                for j in range(4):
                                    ck = grp * 4 + j
                                    ins = te.matmul(pCT[:], vtok[:, ck * 128:(ck + 1) * 128],
                                                    spt[:, ph + j * 128: ph + (j + 1) * 128],
                                                    start=(ck == 0), stop=(ck == 31))
                                ins.then_inc(st.h, st.tick())
                            block.tensor(_m)
                        snap_pv_mm[i] = st.n

                    def emit_ctx_evac(i):
                        h, qt = divmod(i, 4)
                        def _(v, h=h, qt=qt, w=snap_pv_mm[i]):
                            v.wait_ge(st.h, w)
                            v.tensor_copy(ctxT[:, h * QG + qt * 128: h * QG + (qt + 1) * 128],
                                          pCT[:]).then_inc(sv.h, sv.tick())
                        block.vector(_)

                    for pr in range(NSUB // 2):
                        a, b = 2 * pr, 2 * pr + 1
                        h, qta = divmod(a, 4)
                        if qta == 0:
                            emit_head_loads(h)
                            if a > 0:
                                emit_pv(a - 2)
                                emit_ctx_evac(a - 2)
                                emit_pv(a - 1)
                                emit_ctx_evac(a - 1)
                            emit_vtrans(h)
                        else:
                            emit_pv(a - 2)
                            emit_ctx_evac(a - 2)
                            emit_pv(a - 1)
                            emit_ctx_evac(a - 1)
                        emit_scores(a)
                        snap_pmm_free = sv.n
                        emit_scores(b)
                        snap_pmm_free = sv.n
                        emit_bisect_pair(a, b)
                        emit_desc_mask_pair(a, b)
                        snap_P[a] = sv.n
                        snap_P[b] = sv.n
                    emit_pv(NSUB - 2)
                    emit_ctx_evac(NSUB - 2)
                    emit_pv(NSUB - 1)
                    emit_ctx_evac(NSUB - 1)

                with ExitStack() as po:
                    swo = sbuf(f"swo_{qg}", [128, 32 * 512], st=po)
                    sbo = sbuf(f"sbo_{qg}", [128, 512], st=po)
                    sxr = sbuf(f"sxr_{qg}", [128, 512], st=po)
                    sev2 = sbuf(f"sev2_{qg}", [128, 512], st=po)

                    for dk in range(8):
                        def _(sy, dk=dk):
                            sy.wait_ge(st.h, st.n)
                            src3 = woT[:, dk * 512:(dk + 1) * 512].rearrange(
                                "(c p) n -> p c n", p=128)
                            dst3 = swo[:].rearrange("p (c n) -> p c n", c=32)
                            for grp in range(8):
                                sy.dma_start(out=dst3[:, grp * 4:(grp + 1) * 4, :],
                                             in_=src3[:, grp * 4:(grp + 1) * 4, :]).then_inc(dl.h, 16)
                                dl.inc(16)
                            sy.dma_start(out=sbo[:], in_=bo_rep[:, dk * 512:(dk + 1) * 512]).then_inc(dl.h, 16)
                            dl.inc(16)
                        block.sync(_)

                        for qt in range(QG // 128):
                            pp = (qt % 2) * 512
                            def _(sy, dk=dk, qt=qt, qg=qg):
                                sy.wait_ge(sv.h, sv.n)
                                row0 = qg * QG + qt * 128
                                sy.dma_start(out=sxr[:],
                                             in_=x_tok[row0:row0 + 128,
                                                       dk * 512:(dk + 1) * 512]).then_inc(dl.h, 16)
                                dl.inc(16)
                            block.sync(_)

                            def _(te, dk=dk, qt=qt, pp=pp):
                                te.wait_ge(dl.h, dl.n)
                                te.wait_ge(sv.h, sv.n)
                                for dc in range(32):
                                    ins = te.matmul(pMM[:, pp:pp + 512],
                                                    ctxT[:, dc * QG + qt * 128: dc * QG + (qt + 1) * 128],
                                                    swo[:, dc * 512:(dc + 1) * 512],
                                                    start=(dc == 0), stop=(dc == 31))
                                ins.then_inc(st.h, st.tick())
                            block.tensor(_)

                            def _(v, dk=dk, qt=qt, pp=pp):
                                v.wait_ge(st.h, st.n)
                                v.wait_ge(dg.h, dg.n)
                                v.tensor_tensor(sev2[:], sbo[:], sxr[:], AL.add)
                                v.tensor_tensor(sev2[:], sev2[:], pMM[:, pp:pp + 512],
                                                AL.add).then_inc(sv.h, sv.tick())
                            block.vector(_)

                            def _(g, dk=dk, qt=qt, qg=qg):
                                g.wait_ge(sv.h, sv.n)
                                row0 = qg * QG + qt * 128
                                g.dma_start(out=x2_sc[row0:row0 + 128, dk * 512:(dk + 1) * 512],
                                            in_=sev2[:]).then_inc(dg.h, 16)
                                dg.inc(16)
                            block.gpsimd(_)

        with ExitStack() as pc:
            sx = sbuf("sxc", [128, D], st=pc)
            ssq = sbuf("ssqc", [128, D], st=pc)
            sq1 = sbuf("sq1", [128, D], st=pc)
            sq2 = sbuf("sq2", [128, D], st=pc)
            sf8 = sbuf("sf8", [128, D], FP8, st=pc)
            smk = sbuf("smk", [128, D], I32, st=pc)
            sqb = sbuf("sqb", [128, D], BF16, st=pc)
            sqb2 = sbuf("sqb2", [128, 512], BF16, st=pc)
            sasrow = sbuf("sasrow", [1, 1024], st=pc)
            bc_as = sbuf("bc_as", [128, 1024], st=pc)
            xqT = sbuf("xqT", [128, 32 * R], BF16, st=pc)

            for tt in range(8):
                def _(sy, tt=tt):
                    sy.wait_ge(st.h, st.n)
                    sy.wait_ge(dg.h, dg.n)
                    sy.dma_start(out=sx[:], in_=x2_sc[tt * 128:(tt + 1) * 128, :]).then_inc(dl.h, 16)
                    dl.inc(16)
                block.sync(_)

                def _(v):
                    v.wait_ge(dl.h, dl.n)
                    v.tensor_reduce(T[:, 0:1], sx[:], AX.X, AL.add).then_inc(vb.h, vb.tick())
                    v.wait_ge(vb.h, vb.n)
                    v.tensor_scalar(T[:, 0:1], T[:, 0:1], 1.0 / D, None, AL.mult,
                                    AL.bypass).then_inc(sv.h, sv.tick())
                block.vector(_)

                def _(s):
                    s.wait_ge(sv.h, sv.n)
                    s.activation(ssq[:], sx[:], AF.Square, bias=0.0, scale=1.0,
                                 accum_out=T[:, 1:2]).then_inc(ss.h, ss.tick())
                block.scalar(_)

                def _(v):
                    v.wait_ge(ss.h, ss.n)
                    v.tensor_scalar(T[:, 1:2], T[:, 1:2], 1.0 / D, None, AL.mult, AL.bypass)
                    v.tensor_tensor(T[:, 2:3], T[:, 0:1], T[:, 0:1], AL.mult).then_inc(vb.h, vb.tick())
                    v.wait_ge(vb.h, vb.n)
                    v.tensor_tensor(T[:, 1:2], T[:, 1:2], T[:, 2:3], AL.subtract).then_inc(vb.h, vb.tick())
                    v.wait_ge(vb.h, vb.n)
                    v.tensor_scalar(T[:, 1:2], T[:, 1:2], EPS_LN, None, AL.add,
                                    AL.bypass).then_inc(sv.h, sv.tick())
                block.vector(_)

                def _(s):
                    s.wait_ge(sv.h, sv.n)
                    s.activation(T[:, 2:3], T[:, 1:2], AF.Sqrt, bias=0.0,
                                 scale=1.0).then_inc(ss.h, ss.tick())
                block.scalar(_)

                def _(v, tt=tt):
                    v.wait_ge(ss.h, ss.n)
                    v.reciprocal(T[:, 3:4], T[:, 2:3]).then_inc(vb.h, vb.tick())
                    v.wait_ge(vb.h, vb.n)
                    v.tensor_scalar(ssq[:], sx[:], T[:, 0:1], T[:, 3:4], AL.subtract, AL.mult)
                    v.tensor_reduce(T[:, 5:6], ssq[:], AX.X, AL.max,
                                    apply_absolute_value=True).then_inc(vb.h, vb.tick())
                    v.wait_ge(vb.h, vb.n)
                    v.tensor_scalar(T[:, 5:6], T[:, 5:6], INV448, 1e-6, AL.mult,
                                    AL.max).then_inc(vb.h, vb.tick())
                    v.wait_ge(vb.h, vb.n)
                    v.tensor_copy(sasr[:, tt:tt + 1], T[:, 5:6])
                    v.reciprocal(T[:, 6:7], T[:, 5:6]).then_inc(vb.h, vb.tick())
                    v.wait_ge(vb.h, vb.n)
                    v.tensor_scalar(sq1[:], ssq[:], T[:, 6:7], -448.0, AL.mult, AL.max)
                    v.tensor_scalar(sq1[:], sq1[:], 448.0, None, AL.min, AL.bypass)
                    v.tensor_scalar(sq2[:], sq1[:], -240.0, 240.0, AL.max, AL.min)
                    v.tensor_copy(sf8[:], sq2[:]).then_inc(vb.h, vb.tick())
                    v.wait_ge(vb.h, vb.n)
                    v.tensor_copy(sq2[:], sf8[:])
                    v.tensor_tensor(ssq[:], sq1[:], sq1[:], AL.mult).then_inc(vb.h, vb.tick())
                    v.wait_ge(vb.h, vb.n)
                    v.tensor_scalar(smk[:], ssq[:], 224.0 * 224.0, None, AL.is_gt, AL.bypass)
                    v.tensor_scalar(ssq[:], sq1[:], DEK, None, AL.mult, AL.bypass)
                    v.tensor_tensor(sx[:], ssq[:], sq1[:], AL.subtract)
                    v.tensor_tensor(ssq[:], ssq[:], sx[:], AL.subtract).then_inc(vb.h, vb.tick())
                    v.wait_ge(vb.h, vb.n)
                    v.select(sq1[:], smk[:], ssq[:], sq2[:]).then_inc(vb.h, vb.tick())
                    v.wait_ge(vb.h, vb.n)
                    v.tensor_copy(sqb[:], sq1[:]).then_inc(sv.h, sv.tick())
                block.vector(_)

                for b8 in range(8):
                    def _(te, tt=tt, b8=b8):
                        te.wait_ge(sv.h, sv.n)
                        for j in range(4):
                            c = b8 * 4 + j
                            ins = te.transpose(pTb[:, j * 128:(j + 1) * 128],
                                               sqb[:, c * 128:(c + 1) * 128], sidb[:])
                        ins.then_inc(st.h, st.tick())
                    block.tensor(_)

                    def _(v, tt=tt, b8=b8):
                        v.wait_ge(st.h, st.n)
                        for j in range(4):
                            c = b8 * 4 + j
                            ins = v.tensor_copy(
                                xqT[:, c * R + tt * 128: c * R + (tt + 1) * 128],
                                pTb[:, j * 128:(j + 1) * 128])
                        ins.then_inc(sv.h, sv.tick())
                    block.vector(_)

                def _(te, tt=tt):
                    te.wait_ge(sv.h, sv.n)
                    te.transpose(pT[0:1, 0:128], sasr[:, tt:tt + 1], sid[:]).then_inc(st.h, st.tick())
                block.tensor(_)

                def _(v, tt=tt):
                    v.wait_ge(st.h, st.n)
                    v.tensor_copy(sasrow[0:1, tt * 128:(tt + 1) * 128],
                                  pT[0:1, 0:128]).then_inc(sv.h, sv.tick())
                block.vector(_)

            def _(te):
                te.wait_ge(sv.h, sv.n)
                for half in range(2):
                    ins = te.matmul(pMM[:, half * 512:(half + 1) * 512], sones[:],
                                    sasrow[0:1, half * 512:(half + 1) * 512],
                                    start=True, stop=True)
                ins.then_inc(st.h, st.tick())
            block.tensor(_)

            def _(v):
                v.wait_ge(st.h, st.n)
                v.tensor_copy(bc_as[:, 0:1024], pMM[:, 0:1024]).then_inc(sv.h, sv.tick())
            block.vector(_)

            for wt in range(32):
                def _(sy, wt=wt):
                    sy.wait_ge(st.h, st.n)
                    sy.dma_start(out=sx[:], in_=fw[wt * 128:(wt + 1) * 128, :]).then_inc(dl.h, 16)
                    dl.inc(16)
                block.sync(_)

                def _(v, wt=wt):
                    v.wait_ge(dl.h, dl.n)
                    v.tensor_reduce(T[:, 0:1], sx[:], AX.X, AL.max,
                                    apply_absolute_value=True).then_inc(vb.h, vb.tick())
                    v.wait_ge(vb.h, vb.n)
                    v.tensor_scalar(T[:, 0:1], T[:, 0:1], INV448, 1e-6, AL.mult,
                                    AL.max).then_inc(vb.h, vb.tick())
                    v.wait_ge(vb.h, vb.n)
                    v.tensor_copy(sws16[:, wt:wt + 1], T[:, 0:1]).then_inc(vb.h, vb.tick())
                    v.wait_ge(vb.h, vb.n)
                    v.tensor_copy(sws[:, wt:wt + 1], sws16[:, wt:wt + 1]).then_inc(vb.h, vb.tick())
                    v.wait_ge(vb.h, vb.n)
                    v.reciprocal(swsr[:, wt:wt + 1], sws[:, wt:wt + 1]).then_inc(vb.h, vb.tick())
                    v.wait_ge(vb.h, vb.n)
                    v.tensor_scalar(sq1[:], sx[:], swsr[:, wt:wt + 1], -448.0, AL.mult, AL.max)
                    v.tensor_scalar(sq1[:], sq1[:], 448.0, None, AL.min, AL.bypass)
                    v.tensor_scalar(sq2[:], sq1[:], -240.0, 240.0, AL.max, AL.min)
                    v.tensor_copy(sf8[:], sq2[:]).then_inc(vb.h, vb.tick())
                    v.wait_ge(vb.h, vb.n)
                    v.tensor_copy(sq2[:], sf8[:])
                    v.tensor_tensor(ssq[:], sq1[:], sq1[:], AL.mult).then_inc(vb.h, vb.tick())
                    v.wait_ge(vb.h, vb.n)
                    v.tensor_scalar(smk[:], ssq[:], 224.0 * 224.0, None, AL.is_gt, AL.bypass)
                    v.tensor_scalar(ssq[:], sq1[:], DEK, None, AL.mult, AL.bypass)
                    v.tensor_tensor(sx[:], ssq[:], sq1[:], AL.subtract)
                    v.tensor_tensor(ssq[:], ssq[:], sx[:], AL.subtract).then_inc(vb.h, vb.tick())
                    v.wait_ge(vb.h, vb.n)
                    v.select(sq1[:], smk[:], ssq[:], sq2[:]).then_inc(vb.h, vb.tick())
                    v.wait_ge(vb.h, vb.n)
                    v.tensor_copy(sqb[:], sq1[:]).then_inc(sv.h, sv.tick())
                block.vector(_)

                for b8 in range(8):
                    def _(te, wt=wt, b8=b8):
                        te.wait_ge(sv.h, sv.n)
                        te.wait_ge(dg.h, dg.n)
                        for j in range(4):
                            c = b8 * 4 + j
                            ins = te.transpose(pTb[:, j * 128:(j + 1) * 128],
                                               sqb[:, c * 128:(c + 1) * 128], sidb[:])
                        ins.then_inc(st.h, st.tick())
                    block.tensor(_)

                    def _(v, wt=wt, b8=b8):
                        v.wait_ge(st.h, st.n)
                        v.tensor_copy(sqb2[:], pTb[:]).then_inc(sv.h, sv.tick())
                    block.vector(_)

                    def _(g, wt=wt, b8=b8):
                        g.wait_ge(sv.h, sv.n)
                        for j in range(4):
                            c = b8 * 4 + j
                            g.dma_start(out=w8t_sc[c * 128:(c + 1) * 128, wt * 128:(wt + 1) * 128],
                                        in_=sqb2[:, j * 128:(j + 1) * 128]).then_inc(dg.h, 16)
                            dg.inc(16)
                    block.gpsimd(_)

            with ExitStack() as pd:
                sw8 = sbuf("sw8", [128, 32 * 128], BF16, st=pd)
                t2f = sbuf("t2f", [128, 1024], st=pd)
                o16 = sbuf("o16", [128, 1024], BF16, st=pd)

                for do in range(32):
                    pp = (do % 2) * 1024
                    def _(sy, do=do):
                        sy.wait_ge(st.h, st.n)
                        src3 = w8t_sc[:, do * 128:(do + 1) * 128].rearrange(
                            "(c p) n -> p c n", p=128)
                        dst3 = sw8[:].rearrange("p (c n) -> p c n", c=32)
                        for grp in range(4):
                            sy.dma_start(out=dst3[:, grp * 8:(grp + 1) * 8, :],
                                         in_=src3[:, grp * 8:(grp + 1) * 8, :]).then_inc(dl.h, 16)
                            dl.inc(16)
                    block.sync(_)

                    def _(te, do=do, pp=pp):
                        te.wait_ge(dl.h, dl.n)
                        te.wait_ge(sv.h, sv.n)
                        for hf in range(2):
                            for di in range(32):
                                ins = te.matmul(pMM[:, pp + hf * 512:pp + hf * 512 + 512],
                                                sw8[:, di * 128:(di + 1) * 128],
                                                xqT[:, di * R + hf * 512:di * R + hf * 512 + 512],
                                                start=(di == 0), stop=(di == 31))
                        ins.then_inc(st.h, st.tick())
                    block.tensor(_)

                    def _(v, do=do, pp=pp):
                        v.wait_ge(st.h, st.n)
                        v.wait_ge(ss.h, ss.n)
                        v.scalar_tensor_tensor(t2f[:], pMM[:, pp:pp + 1024], sws[:, do:do + 1],
                                               bc_as[:, 0:1024], AL.mult,
                                               AL.mult).then_inc(sv.h, sv.tick())
                    block.vector(_)

                    def _(s, do=do):
                        s.wait_ge(sv.h, sv.n)
                        s.wait_ge(dg.h, dg.n)
                        s.activation(o16[:], t2f[:], AF.Identity, bias=sfb[:, do:do + 1],
                                     scale=1.0).then_inc(ss.h, ss.tick())
                    block.scalar(_)

                    def _(g, do=do):
                        g.wait_ge(ss.h, ss.n)
                        g.dma_start(out=ot_sc[do * 128:(do + 1) * 128, :], in_=o16[:]).then_inc(dg.h, 16)
                        dg.inc(16)
                    block.gpsimd(_)

                sot = sbuf("sot", [128, D], BF16, st=pd)
                sy_ = sbuf("sy", [128, D], st=pd)

                for tt in range(8):
                    def _(sy, tt=tt):
                        sy.wait_ge(dg.h, dg.n)
                        sy.wait_ge(sv.h, sv.n)
                        sy.dma_start(out=sx[:], in_=x2_sc[tt * 128:(tt + 1) * 128, :]).then_inc(dl.h, 16)
                        dl.inc(16)
                        sy.dma_start(out=sot[:], in_=ot_sc[:, tt * 128:(tt + 1) * 128],
                                     transpose=True).then_inc(dl.h, 16)
                        dl.inc(16)
                    block.sync(_)

                    def _(v, tt=tt):
                        v.wait_ge(dl.h, dl.n)
                        v.tensor_tensor(sy_[:], sx[:], sot[:], AL.add).then_inc(sv.h, sv.tick())
                    block.vector(_)

                    def _(g, tt=tt):
                        g.wait_ge(sv.h, sv.n)
                        g.dma_start(out=y_out[tt * 128:(tt + 1) * 128, :], in_=sy_[:]).then_inc(dg.h, 16)
                        dg.inc(16)
                        if tt == 7 and not DEBUG_DUMPS:
                            g.wait_ge(dg.h, dg.n)
                    block.gpsimd(_)

                if DEBUG_DUMPS:
                    dumps = [("d_qt", qt_sc, [D, R], F32), ("d_kin", kin, [D, R], F32),
                             ("d_vin", vin, [D, R], F32), ("d_kall", kall, [4, D, R], F32),
                             ("d_x2", x2_sc, [R, D], F32), ("d_ot", ot_sc, [D, R], BF16),
                             ("d_w8t", w8t_sc, [D, D], BF16)]
                    def _(g):
                        for nm, srcap, shp, dt in dumps:
                            dout = nc.dram_tensor(nm, shp, dt, kind="ExternalOutput").ap()
                            flat_in = srcap.rearrange("... -> (...)") if len(shp) > 2 else srcap
                            nchunk = 8
                            if len(shp) == 3:
                                for j in range(shp[0]):
                                    g.dma_start(out=dout[j], in_=srcap[j]).then_inc(dg.h, 16)
                                    dg.inc(16)
                            else:
                                rows = shp[0]
                                for j in range(nchunk):
                                    r0, r1 = rows * j // nchunk, rows * (j + 1) // nchunk
                                    g.dma_start(out=dout[r0:r1], in_=srcap[r0:r1]).then_inc(dg.h, 16)
                                    dg.inc(16)
                        g.wait_ge(dg.h, dg.n)
                    block.gpsimd(_)

    return nc


_NC_CACHE = [None]


def kernel(**inputs):
    x = np.asarray(inputs["L0_attn_hidden_states"], np.float32)
    wq = np.asarray(inputs["L0_attn_wq"], np.float32)
    wk = np.asarray(inputs["L0_attn_wk"], np.float32)
    wv = np.asarray(inputs["L0_attn_wv"], np.float32)
    wo = np.asarray(inputs["L0_attn_wo"], np.float32)
    bq = np.asarray(inputs["L0_attn_bq"], np.float32)
    bk = np.asarray(inputs["L0_attn_bk"], np.float32)
    bv = np.asarray(inputs["L0_attn_bv"], np.float32)
    bo = np.asarray(inputs["L0_attn_bo"], np.float32)
    fw_ = np.asarray(inputs["L0_ffn_weight"], np.float32)
    fb_ = np.asarray(inputs["L0_ffn_bias"], np.float32)

    wqT = np.ascontiguousarray(wq.T)
    wkT = np.ascontiguousarray(wk.T)
    wvT = np.ascontiguousarray(wv.T)
    woT = np.ascontiguousarray(wo.T)
    boR = np.ascontiguousarray(np.broadcast_to(bo, (128, D)))

    if _NC_CACHE[0] is None:
        _NC_CACHE[0] = build_program()
    nc = _NC_CACHE[0]

    in_maps = []
    for c in range(8):
        b = c // 4
        r0 = (c % 4) * R
        in_maps.append({
            "x_tok": np.ascontiguousarray(x[b, r0:r0 + R]),
            "wqT": wqT, "wkT": wkT, "wvT": wvT, "woT": woT,
            "bq": bq, "bk": bk, "bv": bv, "bo_rep": boR,
            "fw": fw_, "fb": fb_,
        })
    res = run_bass_kernel_spmd(nc, in_maps, core_ids=list(range(8)))
    out = np.empty((2, SEQ, D), np.float32)
    for c in range(8):
        b = c // 4
        r0 = (c % 4) * R
        out[b, r0:r0 + R] = res.results[c]["y"]
    return out


# version 11 (final)
# speedup vs baseline: 1.9884x; 1.0046x over previous
import math
from contextlib import ExitStack

import numpy as np
import ml_dtypes
import concourse.bass as bass
import concourse.mybir as mybir
from concourse.bass_utils import run_bass_kernel_spmd

F32 = mybir.dt.float32
BF16 = mybir.dt.bfloat16
FP16 = mybir.dt.float16
FP8 = mybir.dt.float8e4
I32 = mybir.dt.int32
AL = mybir.AluOpType
AF = mybir.ActivationFunctionType
AX = mybir.AxisListType

D = 4096
R = 1024
NH = 32
HD = 128
SEQ = 4096
TOPK = 819
QG = 512
NQG = R // QG
NRND = 9
TSAT = 1.0e30
NDESC = 1
ASCALE = float(np.float32(1.0 / math.sqrt(HD)))
EPS_LN = 1.0e-5
W0 = 64.0
LOFF = 64.5
INV448 = float(np.float32(1.0 / 448.0))
DEK = float(np.float32(2.0 ** 20 + 1.0))


class Cnt:
    def __init__(self, h):
        self.h = h
        self.n = 0

    def inc(self, k=1):
        self.n += k
        return self.n

    def tick(self):
        self.n += 1
        return 1


DEBUG_DUMPS = False


def build_program():
    nc = bass.Bass("TRN2", target_bir_lowering=False, debug=False,
                   detect_race_conditions=False)

    x_tok = nc.dram_tensor("x_tok", [R, D], F32, kind="ExternalInput").ap()
    wqT = nc.dram_tensor("wqT", [D, D], F32, kind="ExternalInput").ap()
    wkT = nc.dram_tensor("wkT", [D, D], F32, kind="ExternalInput").ap()
    wvT = nc.dram_tensor("wvT", [D, D], F32, kind="ExternalInput").ap()
    woT = nc.dram_tensor("woT", [D, D], F32, kind="ExternalInput").ap()
    bq_in = nc.dram_tensor("bq", [D], F32, kind="ExternalInput").ap()
    bk_in = nc.dram_tensor("bk", [D], F32, kind="ExternalInput").ap()
    bv_in = nc.dram_tensor("bv", [D], F32, kind="ExternalInput").ap()
    bo_rep = nc.dram_tensor("bo_rep", [128, D], F32, kind="ExternalInput").ap()
    fw = nc.dram_tensor("fw", [D, D], F32, kind="ExternalInput").ap()
    fb_in = nc.dram_tensor("fb", [D], F32, kind="ExternalInput").ap()
    y_out = nc.dram_tensor("y", [R, D], F32, kind="ExternalOutput").ap()

    qt_sc = nc.dram_tensor("qt_sc", [D, R], F32).ap()
    kin = nc.dram_tensor("kin", [D, R], F32).ap()
    vin = nc.dram_tensor("vin", [D, R], F32).ap()
    kall = nc.dram_tensor("kall", [4, D, R], F32).ap()
    vall = nc.dram_tensor("vall", [4, D, R], F32).ap()
    x2_sc = nc.dram_tensor("x2_sc", [R, D], F32).ap()
    w8t_sc = nc.dram_tensor("w8t_sc", [D, D], BF16).ap()
    ot_sc = nc.dram_tensor("ot_sc", [D, R], BF16).ap()

    ident_f = nc.inline_tensor(np.eye(128, dtype=np.float32), "ident_f").ap()
    ident_b = nc.inline_tensor(np.eye(128, dtype=ml_dtypes.bfloat16), "ident_b").ap()
    ones_f = nc.inline_tensor(np.ones((1, 128), dtype=np.float32), "ones_f").ap()

    es = ExitStack()
    with es:
        def sbuf(name, shape, dt=F32, st=None):
            return (st or es).enter_context(nc.sbuf_tensor(name, shape, dt))

        def sem(name):
            return Cnt(es.enter_context(nc.semaphore(name)))

        block = es.enter_context(nc.Block())

        dl = sem("dl")
        dg = sem("dg")
        sv = sem("sv")
        ss = sem("ss")
        st = sem("st")
        cc = sem("cc")
        vb = sem("vb")

        pMM = es.enter_context(nc.psum_tensor("pMM", [128, 2048], F32))
        pT = es.enter_context(nc.psum_tensor("pT", [128, 1024], F32))
        pCT = es.enter_context(nc.psum_tensor("pCT", [128, 128], F32))
        pTb = es.enter_context(nc.psum_tensor("pTb", [128, 512], BF16))

        sid = sbuf("sid", [128, 128])
        sidb = sbuf("sidb", [128, 128], BF16)
        sones = sbuf("sones", [1, 128])
        sbq = sbuf("sbq", [128, 32])
        sbk = sbuf("sbk", [128, 32])
        sbv = sbuf("sbv", [128, 32])
        sfb = sbuf("sfb", [128, 32])
        sfb16 = sbuf("sfb16", [128, 32], BF16)
        sws = sbuf("sws", [128, 32])
        swsr = sbuf("swsr", [128, 32])
        sws16 = sbuf("sws16", [128, 32], FP16)
        sasr = sbuf("sasr", [128, 8])
        stny = sbuf("stny", [128, 48])
        cst = sbuf("cst", [128, 1])

        T = stny

        def _(sy):
            sy.dma_start(out=sid[:], in_=ident_f).then_inc(dl.h, 16); dl.inc(16)
            sy.dma_start(out=sidb[:], in_=ident_b).then_inc(dl.h, 16); dl.inc(16)
            sy.dma_start(out=sones[:], in_=ones_f).then_inc(dl.h, 16); dl.inc(16)
            for dst, src in ((sbq, bq_in), (sbk, bk_in), (sbv, bv_in), (sfb, fb_in)):
                sy.dma_start(out=dst[:], in_=src.rearrange("(c p) -> p c", p=128),
                             allow_slow_non_contiguous=True).then_inc(dl.h, 16)
                dl.inc(16)
        block.sync(_)

        def _(v):
            v.wait_ge(dl.h, dl.n)
            v.memset(cst[:], ASCALE)
            v.tensor_scalar(sbq[:], sbq[:], ASCALE, None, AL.mult, AL.bypass)
            v.tensor_copy(sfb16[:], sfb[:]).then_inc(vb.h, vb.tick())
            v.wait_ge(vb.h, vb.n)
            v.tensor_copy(sfb[:], sfb16[:]).then_inc(sv.h, sv.tick())
        block.vector(_)

        with ExitStack() as pa:
            sx = sbuf("sx", [128, D], st=pa)
            ssq = sbuf("ssq", [128, D], st=pa)
            sxT = sbuf("sxT", [128, 32 * R], st=pa)
            sev = sbuf("sev", [128, 1024], st=pa)
            wcols = [sbuf(f"wcol{j}", [128, 32 * 128], st=pa) for j in range(2)]

            _dxh_ap = (nc.dram_tensor("d_xh", [R, D], F32, kind="ExternalOutput").ap()
                       if DEBUG_DUMPS else None)

            for tt in range(8):
                def _(sy, tt=tt):
                    if tt > 0:
                        sy.wait_ge(sv.h, sv.n)
                    sy.dma_start(out=sx[:], in_=x_tok[tt * 128:(tt + 1) * 128, :]).then_inc(dl.h, 16)
                    dl.inc(16)
                block.sync(_)

                def _(v):
                    v.wait_ge(dl.h, dl.n)
                    v.tensor_reduce(T[:, 0:1], sx[:], AX.X, AL.add).then_inc(vb.h, vb.tick())
                    v.wait_ge(vb.h, vb.n)
                    v.tensor_scalar(T[:, 0:1], T[:, 0:1], 1.0 / D, None, AL.mult,
                                    AL.bypass).then_inc(sv.h, sv.tick())
                block.vector(_)

                def _(s):
                    s.wait_ge(sv.h, sv.n)
                    s.wait_ge(st.h, st.n)
                    s.activation(ssq[:], sx[:], AF.Square, bias=0.0, scale=1.0,
                                 accum_out=T[:, 1:2]).then_inc(ss.h, ss.tick())
                block.scalar(_)

                def _(v):
                    v.wait_ge(ss.h, ss.n)
                    v.tensor_scalar(T[:, 1:2], T[:, 1:2], 1.0 / D, None, AL.mult, AL.bypass)
                    v.tensor_tensor(T[:, 2:3], T[:, 0:1], T[:, 0:1], AL.mult).then_inc(vb.h, vb.tick())
                    v.wait_ge(vb.h, vb.n)
                    v.tensor_tensor(T[:, 1:2], T[:, 1:2], T[:, 2:3], AL.subtract).then_inc(vb.h, vb.tick())
                    v.wait_ge(vb.h, vb.n)
                    v.tensor_scalar(T[:, 1:2], T[:, 1:2], EPS_LN, None, AL.add,
                                    AL.bypass).then_inc(sv.h, sv.tick())
                block.vector(_)

                def _(s):
                    s.wait_ge(sv.h, sv.n)
                    s.activation(T[:, 2:3], T[:, 1:2], AF.Sqrt, bias=0.0,
                                 scale=1.0).then_inc(ss.h, ss.tick())
                block.scalar(_)

                def _(v):
                    v.wait_ge(ss.h, ss.n)
                    v.reciprocal(T[:, 3:4], T[:, 2:3]).then_inc(vb.h, vb.tick())
                    v.wait_ge(vb.h, vb.n)
                    v.tensor_scalar(ssq[:], sx[:], T[:, 0:1], T[:, 3:4], AL.subtract,
                                    AL.mult).then_inc(sv.h, sv.tick())
                block.vector(_)

                if DEBUG_DUMPS:
                    def _(g, tt=tt):
                        g.wait_ge(sv.h, sv.n)
                        g.dma_start(out=_dxh_ap[tt * 128:(tt + 1) * 128, :],
                                    in_=ssq[:]).then_inc(dg.h, 16)
                        dg.inc(16)
                        g.wait_ge(dg.h, dg.n)
                    block.gpsimd(_)

                for q4 in range(8):
                    ph = (q4 % 2) * 512
                    def _(te, tt=tt, q4=q4, ph=ph):
                        te.wait_ge(sv.h, sv.n)
                        for j in range(4):
                            c = q4 * 4 + j
                            ins = te.transpose(pT[:, ph + j * 128: ph + (j + 1) * 128],
                                               ssq[:, c * 128:(c + 1) * 128], sid[:])
                        ins.then_inc(st.h, st.tick())
                    block.tensor(_)

                    def _(v, tt=tt, q4=q4, ph=ph):
                        v.wait_ge(st.h, st.n)
                        for j in range(4):
                            c = q4 * 4 + j
                            ins = v.tensor_copy(
                                sxT[:, c * R + tt * 128: c * R + (tt + 1) * 128],
                                pT[:, ph + j * 128: ph + (j + 1) * 128])
                        ins.then_inc(sv.h, sv.tick())
                    block.vector(_)

            if DEBUG_DUMPS:
                d_xT = nc.dram_tensor("d_xT", [128, 32 * R], F32, kind="ExternalOutput").ap()

                def _(g):
                    g.wait_ge(sv.h, sv.n)
                    g.dma_start(out=d_xT[:], in_=sxT[:]).then_inc(dg.h, 16)
                    dg.inc(16)
                    g.wait_ge(dg.h, dg.n)
                block.gpsimd(_)

            projs = [(wkT, sbk, kin, 1.0), (wvT, sbv, vin, 1.0), (wqT, sbq, qt_sc, ASCALE)]
            for pi, (wT, sbias, dst, scale) in enumerate(projs):
                snap_wfree = [st.n, st.n]
                for do in range(32):
                    ph = (do % 2) * 1024
                    wcol = wcols[do % 2]
                    def _(sy, wT=wT, do=do, wcol=wcol, w=snap_wfree[do % 2]):
                        sy.wait_ge(st.h, w)
                        src3 = wT[:, do * 128:(do + 1) * 128].rearrange("(c p) n -> p c n", p=128)
                        dst3 = wcol[:].rearrange("p (c n) -> p c n", c=32)
                        for grp in range(4):
                            sy.dma_start(out=dst3[:, grp * 8:(grp + 1) * 8, :],
                                         in_=src3[:, grp * 8:(grp + 1) * 8, :]).then_inc(dl.h, 16)
                            dl.inc(16)
                    block.sync(_)

                    def _(te, do=do, ph=ph, wcol=wcol):
                        te.wait_ge(dl.h, dl.n)
                        te.wait_ge(sv.h, sv.n)
                        te.wait_ge(ss.h, ss.n)
                        for half in range(2):
                            for di in range(32):
                                ins = te.matmul(
                                    pMM[:, ph + half * 512: ph + half * 512 + 512],
                                    wcol[:, di * 128:(di + 1) * 128],
                                    sxT[:, di * R + half * 512: di * R + half * 512 + 512],
                                    start=(di == 0), stop=(di == 31))
                        ins.then_inc(st.h, st.tick())
                        snap_wfree[do % 2] = st.n
                    block.tensor(_)

                    def _(s, do=do, sbias=sbias, scale=scale, ph=ph):
                        s.wait_ge(st.h, st.n)
                        s.wait_ge(dg.h, dg.n)
                        sc = 1.0 if scale == 1.0 else cst[:, 0:1]
                        s.activation(sev[:, 0:512], pMM[:, ph:ph + 512], AF.Identity,
                                     bias=sbias[:, do:do + 1], scale=sc)
                        s.activation(sev[:, 512:1024], pMM[:, ph + 512:ph + 1024], AF.Identity,
                                     bias=sbias[:, do:do + 1], scale=sc).then_inc(ss.h, ss.tick())
                    block.scalar(_)

                    def _(g, do=do, dst=dst):
                        g.wait_ge(ss.h, ss.n)
                        g.dma_start(out=dst[do * 128:(do + 1) * 128, :], in_=sev[:]).then_inc(dg.h, 16)
                        dg.inc(16)
                    block.gpsimd(_)

                if pi in (0, 1):
                    def _(g, pi=pi):
                        g.wait_ge(dg.h, dg.n)
                        src_ap, dst_ap = (kin, kall) if pi == 0 else (vin, vall)
                        g.collective_compute("AllGather", AL.bypass,
                                             replica_groups=[[0, 1, 2, 3], [4, 5, 6, 7]],
                                             ins=[src_ap.opt()],
                                             outs=[dst_ap.opt()]).then_inc(cc.h, cc.tick())
                    block.gpsimd(_)

        def _(sy):
            sy.wait_ge(cc.h, cc.n)
            sy.sem_inc(dl.h, 16)
            dl.inc(16)
        block.sync(_)

        with ExitStack() as pb:
            ctxT = sbuf("ctxT", [128, 32 * QG], st=pb)

            for qg in range(NQG):
                with ExitStack() as ph_heads:
                    kTh = sbuf(f"kTh_{qg}", [128, SEQ], st=ph_heads)
                    vTh = sbuf(f"vTh_{qg}", [128, SEQ], st=ph_heads)
                    vtok = sbuf(f"vtok_{qg}", [128, 32 * 128], st=ph_heads)
                    qTh = sbuf(f"qTh_{qg}", [128, QG], st=ph_heads)
                    sSs = [sbuf(f"sS{p}_{qg}", [128, SEQ], st=ph_heads) for p in range(2)]
                    sEs = [sbuf(f"sE{p}_{qg}", [128, SEQ], st=ph_heads) for p in range(2)]
                    jkb = sbuf(f"jkb_{qg}", [128, SEQ], BF16, st=ph_heads)
                    jks = sbuf(f"jks_{qg}", [128, SEQ], BF16, st=ph_heads)
                    spt = sbuf(f"spt_{qg}", [128, 1024], st=ph_heads)

                    NSUB = NH * 4
                    snap_pmm_free = 0
                    snap_P = {}
                    snap_pv_mm = {}

                    def TB(i, c):
                        return (i % 2) * 16 + c

                    def emit_head_loads(h, qg=qg):
                        def _(sy, h=h, qg=qg):
                            sy.wait_ge(st.h, st.n)
                            for g4 in range(4):
                                sy.dma_start(out=kTh[:, g4 * R:(g4 + 1) * R],
                                             in_=kall[g4, h * 128:(h + 1) * 128, :]).then_inc(dl.h, 16)
                                dl.inc(16)
                                sy.dma_start(out=vTh[:, g4 * R:(g4 + 1) * R],
                                             in_=vall[g4, h * 128:(h + 1) * 128, :]).then_inc(dl.h, 16)
                                dl.inc(16)
                            sy.dma_start(out=qTh[:],
                                         in_=qt_sc[h * 128:(h + 1) * 128,
                                                   qg * QG:(qg + 1) * QG]).then_inc(dl.h, 16)
                            dl.inc(16)
                        block.sync(_)

                    def emit_vtrans(h):
                        for b8 in range(8):
                            ph = (b8 % 2) * 512
                            def _(te, b8=b8, ph=ph):
                                te.wait_ge(dl.h, dl.n)
                                te.wait_ge(sv.h, sv.n)
                                for j in range(4):
                                    c = b8 * 4 + j
                                    ins = te.transpose(pT[:, ph + j * 128:ph + (j + 1) * 128],
                                                       vTh[:, c * 128:(c + 1) * 128], sid[:])
                                ins.then_inc(st.h, st.tick())
                            block.tensor(_)

                            def _(v, b8=b8, ph=ph):
                                v.wait_ge(st.h, st.n)
                                v.tensor_copy(vtok[:, b8 * 512:(b8 + 1) * 512],
                                              pT[:, ph:ph + 512]).then_inc(sv.h, sv.tick())
                            block.vector(_)

                    def emit_scores(i):
                        h, qt = divmod(i, 4)
                        p = i % 2
                        sS = sSs[p]

                        def _h0(te, h=h, qt=qt, w=snap_pmm_free):
                            te.wait_ge(dl.h, dl.n)
                            te.wait_ge(sv.h, w)
                            for j in range(4):
                                ins = te.matmul(pMM[:, j * 512:(j + 1) * 512],
                                                qTh[:, qt * 128:(qt + 1) * 128],
                                                kTh[:, j * 512:(j + 1) * 512],
                                                start=True, stop=True)
                            ins.then_inc(st.h, st.tick())
                        block.tensor(_h0)

                        def _e0(v, i=i, sS=sS):
                            v.wait_ge(st.h, st.n)
                            v.tensor_scalar(sS[:, 0:2048], pMM[:, 0:2048], 0.0, None,
                                            AL.add, AL.max,
                                            accum_out=T[:, TB(i, 16):TB(i, 17)]).then_inc(
                                sv.h, sv.tick())
                        block.vector(_e0)

                        def _h1(te, h=h, qt=qt):
                            te.wait_ge(sv.h, sv.n)
                            for j in range(4):
                                ins = te.matmul(pMM[:, j * 512:(j + 1) * 512],
                                                qTh[:, qt * 128:(qt + 1) * 128],
                                                kTh[:, 2048 + j * 512: 2048 + (j + 1) * 512],
                                                start=True, stop=True)
                            ins.then_inc(st.h, st.tick())
                        block.tensor(_h1)

                        def _e1(v, i=i, sS=sS):
                            v.wait_ge(st.h, st.n)
                            v.tensor_scalar(sS[:, 2048:4096], pMM[:, 0:2048], 0.0, None,
                                            AL.add, AL.max,
                                            accum_out=T[:, TB(i, 17):TB(i, 18)]).then_inc(
                                sv.h, sv.tick())
                        block.vector(_e1)

                        def _nm(v, i=i):
                            v.wait_ge(sv.h, sv.n)
                            v.tensor_tensor(T[:, TB(i, 4):TB(i, 5)], T[:, TB(i, 16):TB(i, 17)],
                                            T[:, TB(i, 17):TB(i, 18)], AL.max).then_inc(vb.h, vb.tick())
                            v.wait_ge(vb.h, vb.n)
                            v.tensor_scalar(T[:, TB(i, 4):TB(i, 5)], T[:, TB(i, 4):TB(i, 5)],
                                            -1.0, None, AL.mult, AL.bypass).then_inc(sv.h, sv.tick())
                        block.vector(_nm)

                        def _ex(s, i=i, p=p, sS=sS):
                            s.wait_ge(sv.h, sv.n)
                            s.wait_ge(st.h, st.n)
                            s.activation(sEs[p][:], sS[:], AF.Exp, bias=T[:, TB(i, 4):TB(i, 5)],
                                         scale=1.0).then_inc(ss.h, ss.tick())
                        block.scalar(_ex)

                    def emit_linit(i):
                        def _(v, i=i):
                            v.wait_ge(ss.h, ss.n)
                            v.tensor_scalar(T[:, TB(i, 5):TB(i, 6)], T[:, TB(i, 4):TB(i, 5)],
                                            -1.0, -LOFF, AL.mult, AL.add).then_inc(vb.h, vb.tick())
                            v.wait_ge(vb.h, vb.n)
                        block.vector(_)

                    def emit_bisect_pair(a, b):
                        emit_linit(a)
                        emit_linit(b)
                        for r in range(NRND):
                            w3 = W0 / (3.0 ** (r + 1))
                            snaps = {}
                            for i in (a, b):
                                sS = sSs[i % 2]
                                def _p(v, i=i, w3=w3, sS=sS):
                                    v.tensor_scalar(T[:, TB(i, 6):TB(i, 7)], T[:, TB(i, 5):TB(i, 6)],
                                                    w3, None, AL.add, AL.bypass)
                                    v.tensor_scalar(T[:, TB(i, 12):TB(i, 13)], T[:, TB(i, 5):TB(i, 6)],
                                                    -TSAT, -2.0 * w3 * TSAT,
                                                    AL.mult, AL.add).then_inc(sv.h, sv.tick())
                                    v.tensor_scalar(jkb[:], sS[:], T[:, TB(i, 6):TB(i, 7)], None,
                                                    AL.is_ge, AL.add,
                                                    accum_out=T[:, TB(i, 7):TB(i, 8)]).then_inc(
                                        vb.h, vb.tick())
                                block.vector(_p)
                                snaps[i] = (sv.n, vb.n)

                                def _t(s, i=i, sS=sS, w=snaps[i][0]):
                                    s.wait_ge(sv.h, w)
                                    s.activation(jks[:], sS[:], AF.Tanh, bias=T[:, TB(i, 12):TB(i, 13)],
                                                 scale=TSAT,
                                                 accum_out=T[:, TB(i, 18):TB(i, 19)]).then_inc(ss.h, ss.tick())
                                block.scalar(_t)
                                snaps[i] = (snaps[i][1], ss.n)

                            for i in (a, b):
                                wv_, ws_ = snaps[i]
                                def _c(v, i=i, w3=w3, wv=wv_, ws=ws_):
                                    v.wait_ge(vb.h, wv)
                                    v.tensor_scalar(T[:, TB(i, 8):TB(i, 9)], T[:, TB(i, 7):TB(i, 8)],
                                                    TOPK - 0.5, None, AL.is_ge, AL.bypass)
                                    v.wait_ge(ss.h, ws)
                                    v.tensor_scalar(T[:, TB(i, 19):TB(i, 20)], T[:, TB(i, 18):TB(i, 19)],
                                                    2.0 * TOPK - SEQ - 0.5, None,
                                                    AL.is_ge, AL.bypass).then_inc(vb.h, vb.tick())
                                    v.wait_ge(vb.h, vb.n)
                                    v.tensor_tensor(T[:, TB(i, 8):TB(i, 9)], T[:, TB(i, 8):TB(i, 9)],
                                                    T[:, TB(i, 19):TB(i, 20)], AL.add).then_inc(vb.h, vb.tick())
                                    v.wait_ge(vb.h, vb.n)
                                    v.scalar_tensor_tensor(T[:, TB(i, 5):TB(i, 6)], T[:, TB(i, 8):TB(i, 9)],
                                                           w3, T[:, TB(i, 5):TB(i, 6)],
                                                           AL.mult, AL.add).then_inc(vb.h, vb.tick())
                                    v.wait_ge(vb.h, vb.n)
                                block.vector(_c)

                    def emit_desc_mask(i):
                        p = i % 2
                        sE = sEs[p]
                        sS = sSs[p]
                        wfin = W0 / (3.0 ** NRND)

                        def _(v, i=i, wfin=wfin):
                            v.tensor_scalar(T[:, TB(i, 6):TB(i, 7)], T[:, TB(i, 5):TB(i, 6)],
                                            wfin, None, AL.add, AL.bypass).then_inc(sv.h, sv.tick())
                        block.vector(_)

                        def _(s, i=i):
                            s.wait_ge(sv.h, sv.n)
                            s.activation(T[:, TB(i, 9):TB(i, 11)], T[:, TB(i, 5):TB(i, 7)],
                                         AF.Exp, bias=T[:, TB(i, 4):TB(i, 5)],
                                         scale=1.0).then_inc(ss.h, ss.tick())
                        block.scalar(_)

                        def _(v, i=i):
                            v.wait_ge(ss.h, ss.n)
                            v.memset(T[:, TB(i, 13):TB(i, 14)], 0.0)
                            v.memset(T[:, TB(i, 11):TB(i, 12)], 0.0)
                        block.vector(_)

                        for d in range(NDESC):
                            def _(v, i=i, sE=sE, sS=sS):
                                v.scalar_tensor_tensor(sS[:], sE[:], T[:, TB(i, 10):TB(i, 11)], sE[:],
                                                       AL.is_lt, AL.mult).then_inc(vb.h, vb.tick())
                                v.wait_ge(vb.h, vb.n)
                                v.tensor_reduce(T[:, TB(i, 12):TB(i, 13)], sS[:], AX.X,
                                                AL.max).then_inc(vb.h, vb.tick())
                                v.wait_ge(vb.h, vb.n)
                                v.tensor_scalar(jkb[:], sE[:], T[:, TB(i, 12):TB(i, 13)], None,
                                                AL.is_ge, AL.add,
                                                accum_out=T[:, TB(i, 7):TB(i, 8)]).then_inc(vb.h, vb.tick())
                                v.wait_ge(vb.h, vb.n)
                                v.tensor_scalar(T[:, TB(i, 8):TB(i, 9)], T[:, TB(i, 7):TB(i, 8)],
                                                TOPK - 0.5, None, AL.is_ge, AL.bypass)
                                v.tensor_scalar(T[:, TB(i, 14):TB(i, 15)], T[:, TB(i, 11):TB(i, 12)],
                                                0.5, None, AL.is_lt, AL.bypass).then_inc(vb.h, vb.tick())
                                v.wait_ge(vb.h, vb.n)
                                v.tensor_tensor(T[:, TB(i, 14):TB(i, 15)], T[:, TB(i, 14):TB(i, 15)],
                                                T[:, TB(i, 8):TB(i, 9)], AL.mult).then_inc(vb.h, vb.tick())
                                v.wait_ge(vb.h, vb.n)
                                v.scalar_tensor_tensor(T[:, TB(i, 13):TB(i, 14)], T[:, TB(i, 12):TB(i, 13)],
                                                       T[:, TB(i, 14):TB(i, 15)],
                                                       T[:, TB(i, 13):TB(i, 14)], AL.mult, AL.add)
                                v.tensor_tensor(T[:, TB(i, 11):TB(i, 12)], T[:, TB(i, 11):TB(i, 12)],
                                                T[:, TB(i, 8):TB(i, 9)], AL.max)
                                v.tensor_copy(T[:, TB(i, 10):TB(i, 11)],
                                              T[:, TB(i, 12):TB(i, 13)]).then_inc(vb.h, vb.tick())
                                v.wait_ge(vb.h, vb.n)
                            block.vector(_)

                        def _(v, i=i, sE=sE, sS=sS):
                            v.tensor_scalar(T[:, TB(i, 14):TB(i, 15)], T[:, TB(i, 11):TB(i, 12)],
                                            0.5, None, AL.is_lt, AL.bypass).then_inc(vb.h, vb.tick())
                            v.wait_ge(vb.h, vb.n)
                            v.scalar_tensor_tensor(T[:, TB(i, 13):TB(i, 14)], T[:, TB(i, 9):TB(i, 10)],
                                                   T[:, TB(i, 14):TB(i, 15)],
                                                   T[:, TB(i, 13):TB(i, 14)], AL.mult,
                                                   AL.add).then_inc(vb.h, vb.tick())
                            v.wait_ge(vb.h, vb.n)
                            v.scalar_tensor_tensor(sS[:], sE[:], T[:, TB(i, 13):TB(i, 14)], sE[:],
                                                   AL.is_ge, AL.mult,
                                                   accum_out=T[:, TB(i, 14):TB(i, 15)]).then_inc(vb.h, vb.tick())
                            v.wait_ge(vb.h, vb.n)
                            v.reciprocal(T[:, TB(i, 15):TB(i, 16)],
                                         T[:, TB(i, 14):TB(i, 15)]).then_inc(vb.h, vb.tick())
                            v.wait_ge(vb.h, vb.n)
                            v.tensor_scalar(sE[:], sS[:], T[:, TB(i, 15):TB(i, 16)], None,
                                            AL.mult, AL.bypass).then_inc(sv.h, sv.tick())
                        block.vector(_)

                    def emit_desc_mask_pair(a, b):
                        for i in (a, b):
                            wfin = W0 / (3.0 ** NRND)
                            def _(v, i=i, wfin=wfin):
                                v.tensor_scalar(T[:, TB(i, 6):TB(i, 7)], T[:, TB(i, 5):TB(i, 6)],
                                                wfin, None, AL.add, AL.bypass).then_inc(sv.h, sv.tick())
                            block.vector(_)

                            def _(s, i=i):
                                s.wait_ge(sv.h, sv.n)
                                s.activation(T[:, TB(i, 9):TB(i, 11)], T[:, TB(i, 5):TB(i, 7)],
                                             AF.Exp, bias=T[:, TB(i, 4):TB(i, 5)],
                                             scale=1.0).then_inc(ss.h, ss.tick())
                            block.scalar(_)

                        def _(v):
                            v.wait_ge(ss.h, ss.n)
                            for i in (a, b):
                                v.memset(T[:, TB(i, 13):TB(i, 14)], 0.0)
                                v.memset(T[:, TB(i, 11):TB(i, 12)], 0.0)
                            v.sem_inc(vb.h, vb.tick())
                        block.vector(_)

                        def seg_list(i):
                            sE, sS = sEs[i % 2], sSs[i % 2]
                            return [
                                lambda v: v.scalar_tensor_tensor(sS[:], sE[:], T[:, TB(i, 10):TB(i, 11)],
                                                                 sE[:], AL.is_lt, AL.mult),
                                lambda v: v.tensor_reduce(T[:, TB(i, 12):TB(i, 13)], sS[:], AX.X, AL.max),
                                lambda v: v.tensor_scalar(jkb[:], sE[:], T[:, TB(i, 12):TB(i, 13)], None,
                                                          AL.is_ge, AL.add,
                                                          accum_out=T[:, TB(i, 7):TB(i, 8)]),
                                lambda v: (v.tensor_scalar(T[:, TB(i, 8):TB(i, 9)], T[:, TB(i, 7):TB(i, 8)],
                                                           TOPK - 0.5, None, AL.is_ge, AL.bypass),
                                           v.tensor_scalar(T[:, TB(i, 14):TB(i, 15)],
                                                           T[:, TB(i, 11):TB(i, 12)],
                                                           0.5, None, AL.is_lt, AL.bypass))[-1],
                                lambda v: v.tensor_tensor(T[:, TB(i, 14):TB(i, 15)],
                                                          T[:, TB(i, 14):TB(i, 15)],
                                                          T[:, TB(i, 8):TB(i, 9)], AL.mult),
                                lambda v: (v.scalar_tensor_tensor(T[:, TB(i, 13):TB(i, 14)],
                                                                  T[:, TB(i, 12):TB(i, 13)],
                                                                  T[:, TB(i, 14):TB(i, 15)],
                                                                  T[:, TB(i, 13):TB(i, 14)], AL.mult, AL.add),
                                           v.tensor_tensor(T[:, TB(i, 11):TB(i, 12)],
                                                           T[:, TB(i, 11):TB(i, 12)],
                                                           T[:, TB(i, 8):TB(i, 9)], AL.max),
                                           v.tensor_copy(T[:, TB(i, 10):TB(i, 11)],
                                                         T[:, TB(i, 12):TB(i, 13)]))[-1],
                                lambda v: v.tensor_scalar(T[:, TB(i, 14):TB(i, 15)],
                                                          T[:, TB(i, 11):TB(i, 12)],
                                                          0.5, None, AL.is_lt, AL.bypass),
                                lambda v: v.scalar_tensor_tensor(T[:, TB(i, 13):TB(i, 14)],
                                                                 T[:, TB(i, 9):TB(i, 10)],
                                                                 T[:, TB(i, 14):TB(i, 15)],
                                                                 T[:, TB(i, 13):TB(i, 14)], AL.mult, AL.add),
                                lambda v: v.scalar_tensor_tensor(sS[:], sE[:], T[:, TB(i, 13):TB(i, 14)],
                                                                 sE[:], AL.is_ge, AL.mult,
                                                                 accum_out=T[:, TB(i, 14):TB(i, 15)]),
                                lambda v: v.reciprocal(T[:, TB(i, 15):TB(i, 16)],
                                                       T[:, TB(i, 14):TB(i, 15)]),
                                lambda v: v.tensor_scalar(sE[:], sS[:], T[:, TB(i, 15):TB(i, 16)], None,
                                                          AL.mult, AL.bypass),
                            ]
                        assert NDESC == 1
                        segs = {a: seg_list(a), b: seg_list(b)}
                        lastw = {a: vb.n, b: vb.n}
                        nseg = len(segs[a])
                        for k in range(nseg):
                            for i in (a, b):
                                def _(v, i=i, k=k, w=lastw[i]):
                                    v.wait_ge(vb.h, w)
                                    segs[i][k](v).then_inc(vb.h, vb.tick())
                                block.vector(_)
                                lastw[i] = vb.n
                        def _(v, w=lastw[b]):
                            v.wait_ge(vb.h, w)
                            v.sem_inc(sv.h, sv.tick())
                            v.sem_inc(sv.h, sv.tick())
                        block.vector(_)

                    def emit_pv(i):
                        p = i % 2
                        sE = sEs[p]
                        for grp in range(8):
                            ph = (grp % 2) * 512
                            def _t(te, grp=grp, ph=ph, sE=sE):
                                te.wait_ge(sv.h, sv.n)
                                for j in range(4):
                                    ck = grp * 4 + j
                                    ins = te.transpose(pT[:, ph + j * 128: ph + (j + 1) * 128],
                                                       sE[:, ck * 128:(ck + 1) * 128], sid[:])
                                ins.then_inc(st.h, st.tick())
                            block.tensor(_t)

                            def _c(v, grp=grp, ph=ph):
                                v.wait_ge(st.h, st.n)
                                v.tensor_copy(spt[:, ph:ph + 512],
                                              pT[:, ph:ph + 512]).then_inc(sv.h, sv.tick())
                            block.vector(_c)

                            def _m(te, grp=grp, ph=ph):
                                te.wait_ge(sv.h, sv.n)
                                for j in range(4):
                                    ck = grp * 4 + j
                                    ins = te.matmul(pCT[:], vtok[:, ck * 128:(ck + 1) * 128],
                                                    spt[:, ph + j * 128: ph + (j + 1) * 128],
                                                    start=(ck == 0), stop=(ck == 31))
                                ins.then_inc(st.h, st.tick())
                            block.tensor(_m)
                        snap_pv_mm[i] = st.n

                    def emit_ctx_evac(i):
                        h, qt = divmod(i, 4)
                        def _(v, h=h, qt=qt, w=snap_pv_mm[i]):
                            v.wait_ge(st.h, w)
                            v.tensor_copy(ctxT[:, h * QG + qt * 128: h * QG + (qt + 1) * 128],
                                          pCT[:]).then_inc(sv.h, sv.tick())
                        block.vector(_)

                    for pr in range(NSUB // 2):
                        a, b = 2 * pr, 2 * pr + 1
                        h, qta = divmod(a, 4)
                        if qta == 0:
                            emit_head_loads(h)
                            if a > 0:
                                emit_pv(a - 2)
                                emit_ctx_evac(a - 2)
                                emit_pv(a - 1)
                                emit_ctx_evac(a - 1)
                            emit_vtrans(h)
                        else:
                            emit_pv(a - 2)
                            emit_ctx_evac(a - 2)
                            emit_pv(a - 1)
                            emit_ctx_evac(a - 1)
                        emit_scores(a)
                        snap_pmm_free = sv.n
                        emit_scores(b)
                        snap_pmm_free = sv.n
                        emit_bisect_pair(a, b)
                        emit_desc_mask_pair(a, b)
                        snap_P[a] = sv.n
                        snap_P[b] = sv.n
                    emit_pv(NSUB - 2)
                    emit_ctx_evac(NSUB - 2)
                    emit_pv(NSUB - 1)
                    emit_ctx_evac(NSUB - 1)

                with ExitStack() as po:
                    swo = sbuf(f"swo_{qg}", [128, 32 * 512], st=po)
                    sbo = sbuf(f"sbo_{qg}", [128, 512], st=po)
                    sxr = sbuf(f"sxr_{qg}", [128, 512], st=po)
                    sev2 = sbuf(f"sev2_{qg}", [128, 512], st=po)

                    for dk in range(8):
                        def _(sy, dk=dk):
                            sy.wait_ge(st.h, st.n)
                            src3 = woT[:, dk * 512:(dk + 1) * 512].rearrange(
                                "(c p) n -> p c n", p=128)
                            dst3 = swo[:].rearrange("p (c n) -> p c n", c=32)
                            for grp in range(8):
                                sy.dma_start(out=dst3[:, grp * 4:(grp + 1) * 4, :],
                                             in_=src3[:, grp * 4:(grp + 1) * 4, :]).then_inc(dl.h, 16)
                                dl.inc(16)
                            sy.dma_start(out=sbo[:], in_=bo_rep[:, dk * 512:(dk + 1) * 512]).then_inc(dl.h, 16)
                            dl.inc(16)
                        block.sync(_)

                        for qt in range(QG // 128):
                            pp = (qt % 2) * 512
                            def _(sy, dk=dk, qt=qt, qg=qg):
                                sy.wait_ge(sv.h, sv.n)
                                row0 = qg * QG + qt * 128
                                sy.dma_start(out=sxr[:],
                                             in_=x_tok[row0:row0 + 128,
                                                       dk * 512:(dk + 1) * 512]).then_inc(dl.h, 16)
                                dl.inc(16)
                            block.sync(_)

                            def _(te, dk=dk, qt=qt, pp=pp):
                                te.wait_ge(dl.h, dl.n)
                                te.wait_ge(sv.h, sv.n)
                                for dc in range(32):
                                    ins = te.matmul(pMM[:, pp:pp + 512],
                                                    ctxT[:, dc * QG + qt * 128: dc * QG + (qt + 1) * 128],
                                                    swo[:, dc * 512:(dc + 1) * 512],
                                                    start=(dc == 0), stop=(dc == 31))
                                ins.then_inc(st.h, st.tick())
                            block.tensor(_)

                            def _(v, dk=dk, qt=qt, pp=pp):
                                v.wait_ge(st.h, st.n)
                                v.wait_ge(dg.h, dg.n)
                                v.tensor_tensor(sev2[:], sbo[:], sxr[:], AL.add)
                                v.tensor_tensor(sev2[:], sev2[:], pMM[:, pp:pp + 512],
                                                AL.add).then_inc(sv.h, sv.tick())
                            block.vector(_)

                            def _(g, dk=dk, qt=qt, qg=qg):
                                g.wait_ge(sv.h, sv.n)
                                row0 = qg * QG + qt * 128
                                g.dma_start(out=x2_sc[row0:row0 + 128, dk * 512:(dk + 1) * 512],
                                            in_=sev2[:]).then_inc(dg.h, 16)
                                dg.inc(16)
                            block.gpsimd(_)

        with ExitStack() as pc:
            sx = sbuf("sxc", [128, D], st=pc)
            ssq = sbuf("ssqc", [128, D], st=pc)
            sq1 = sbuf("sq1", [128, D], st=pc)
            sq2 = sbuf("sq2", [128, D], st=pc)
            sf8 = sbuf("sf8", [128, D], FP8, st=pc)
            smk = sbuf("smk", [128, D], I32, st=pc)
            sqb = sbuf("sqb", [128, D], BF16, st=pc)
            sqb2 = sbuf("sqb2", [128, 512], BF16, st=pc)
            sasrow = sbuf("sasrow", [1, 1024], st=pc)
            bc_as = sbuf("bc_as", [128, 1024], st=pc)
            xqT = sbuf("xqT", [128, 32 * R], BF16, st=pc)

            for tt in range(8):
                def _(sy, tt=tt):
                    sy.wait_ge(st.h, st.n)
                    sy.wait_ge(dg.h, dg.n)
                    sy.dma_start(out=sx[:], in_=x2_sc[tt * 128:(tt + 1) * 128, :]).then_inc(dl.h, 16)
                    dl.inc(16)
                block.sync(_)

                def _(v):
                    v.wait_ge(dl.h, dl.n)
                    v.tensor_reduce(T[:, 0:1], sx[:], AX.X, AL.add).then_inc(vb.h, vb.tick())
                    v.wait_ge(vb.h, vb.n)
                    v.tensor_scalar(T[:, 0:1], T[:, 0:1], 1.0 / D, None, AL.mult,
                                    AL.bypass).then_inc(sv.h, sv.tick())
                block.vector(_)

                def _(s):
                    s.wait_ge(sv.h, sv.n)
                    s.activation(ssq[:], sx[:], AF.Square, bias=0.0, scale=1.0,
                                 accum_out=T[:, 1:2]).then_inc(ss.h, ss.tick())
                block.scalar(_)

                def _(v):
                    v.wait_ge(ss.h, ss.n)
                    v.tensor_scalar(T[:, 1:2], T[:, 1:2], 1.0 / D, None, AL.mult, AL.bypass)
                    v.tensor_tensor(T[:, 2:3], T[:, 0:1], T[:, 0:1], AL.mult).then_inc(vb.h, vb.tick())
                    v.wait_ge(vb.h, vb.n)
                    v.tensor_tensor(T[:, 1:2], T[:, 1:2], T[:, 2:3], AL.subtract).then_inc(vb.h, vb.tick())
                    v.wait_ge(vb.h, vb.n)
                    v.tensor_scalar(T[:, 1:2], T[:, 1:2], EPS_LN, None, AL.add,
                                    AL.bypass).then_inc(sv.h, sv.tick())
                block.vector(_)

                def _(s):
                    s.wait_ge(sv.h, sv.n)
                    s.activation(T[:, 2:3], T[:, 1:2], AF.Sqrt, bias=0.0,
                                 scale=1.0).then_inc(ss.h, ss.tick())
                block.scalar(_)

                def _(v, tt=tt):
                    v.wait_ge(ss.h, ss.n)
                    v.reciprocal(T[:, 3:4], T[:, 2:3]).then_inc(vb.h, vb.tick())
                    v.wait_ge(vb.h, vb.n)
                    v.tensor_scalar(ssq[:], sx[:], T[:, 0:1], T[:, 3:4], AL.subtract, AL.mult)
                    v.tensor_reduce(T[:, 5:6], ssq[:], AX.X, AL.max,
                                    apply_absolute_value=True).then_inc(vb.h, vb.tick())
                    v.wait_ge(vb.h, vb.n)
                    v.tensor_scalar(T[:, 5:6], T[:, 5:6], INV448, 1e-6, AL.mult,
                                    AL.max).then_inc(vb.h, vb.tick())
                    v.wait_ge(vb.h, vb.n)
                    v.tensor_copy(sasr[:, tt:tt + 1], T[:, 5:6])
                    v.reciprocal(T[:, 6:7], T[:, 5:6]).then_inc(vb.h, vb.tick())
                    v.wait_ge(vb.h, vb.n)
                    v.tensor_scalar(sq1[:], ssq[:], T[:, 6:7], -448.0, AL.mult, AL.max)
                    v.tensor_scalar(sq1[:], sq1[:], 448.0, None, AL.min, AL.bypass)
                    v.tensor_scalar(sq2[:], sq1[:], -240.0, 240.0, AL.max, AL.min)
                    v.tensor_copy(sf8[:], sq2[:]).then_inc(vb.h, vb.tick())
                    v.wait_ge(vb.h, vb.n)
                    v.tensor_copy(sq2[:], sf8[:])
                    v.tensor_tensor(ssq[:], sq1[:], sq1[:], AL.mult).then_inc(vb.h, vb.tick())
                    v.wait_ge(vb.h, vb.n)
                    v.tensor_scalar(smk[:], ssq[:], 224.0 * 224.0, None, AL.is_gt, AL.bypass)
                    v.tensor_scalar(ssq[:], sq1[:], DEK, None, AL.mult, AL.bypass)
                    v.tensor_tensor(sx[:], ssq[:], sq1[:], AL.subtract)
                    v.tensor_tensor(ssq[:], ssq[:], sx[:], AL.subtract).then_inc(vb.h, vb.tick())
                    v.wait_ge(vb.h, vb.n)
                    v.select(sq1[:], smk[:], ssq[:], sq2[:]).then_inc(vb.h, vb.tick())
                    v.wait_ge(vb.h, vb.n)
                    v.tensor_copy(sqb[:], sq1[:]).then_inc(sv.h, sv.tick())
                block.vector(_)

                for b8 in range(8):
                    def _(te, tt=tt, b8=b8):
                        te.wait_ge(sv.h, sv.n)
                        for j in range(4):
                            c = b8 * 4 + j
                            ins = te.transpose(pTb[:, j * 128:(j + 1) * 128],
                                               sqb[:, c * 128:(c + 1) * 128], sidb[:])
                        ins.then_inc(st.h, st.tick())
                    block.tensor(_)

                    def _(v, tt=tt, b8=b8):
                        v.wait_ge(st.h, st.n)
                        for j in range(4):
                            c = b8 * 4 + j
                            ins = v.tensor_copy(
                                xqT[:, c * R + tt * 128: c * R + (tt + 1) * 128],
                                pTb[:, j * 128:(j + 1) * 128])
                        ins.then_inc(sv.h, sv.tick())
                    block.vector(_)

                def _(te, tt=tt):
                    te.wait_ge(sv.h, sv.n)
                    te.transpose(pT[0:1, 0:128], sasr[:, tt:tt + 1], sid[:]).then_inc(st.h, st.tick())
                block.tensor(_)

                def _(v, tt=tt):
                    v.wait_ge(st.h, st.n)
                    v.tensor_copy(sasrow[0:1, tt * 128:(tt + 1) * 128],
                                  pT[0:1, 0:128]).then_inc(sv.h, sv.tick())
                block.vector(_)

            def _(te):
                te.wait_ge(sv.h, sv.n)
                for half in range(2):
                    ins = te.matmul(pMM[:, half * 512:(half + 1) * 512], sones[:],
                                    sasrow[0:1, half * 512:(half + 1) * 512],
                                    start=True, stop=True)
                ins.then_inc(st.h, st.tick())
            block.tensor(_)

            def _(v):
                v.wait_ge(st.h, st.n)
                v.tensor_copy(bc_as[:, 0:1024], pMM[:, 0:1024]).then_inc(sv.h, sv.tick())
            block.vector(_)

            for wt in range(32):
                def _(sy, wt=wt):
                    sy.wait_ge(st.h, st.n)
                    sy.dma_start(out=sx[:], in_=fw[wt * 128:(wt + 1) * 128, :]).then_inc(dl.h, 16)
                    dl.inc(16)
                block.sync(_)

                def _(v, wt=wt):
                    v.wait_ge(dl.h, dl.n)
                    v.tensor_reduce(T[:, 0:1], sx[:], AX.X, AL.max,
                                    apply_absolute_value=True).then_inc(vb.h, vb.tick())
                    v.wait_ge(vb.h, vb.n)
                    v.tensor_scalar(T[:, 0:1], T[:, 0:1], INV448, 1e-6, AL.mult,
                                    AL.max).then_inc(vb.h, vb.tick())
                    v.wait_ge(vb.h, vb.n)
                    v.tensor_copy(sws16[:, wt:wt + 1], T[:, 0:1]).then_inc(vb.h, vb.tick())
                    v.wait_ge(vb.h, vb.n)
                    v.tensor_copy(sws[:, wt:wt + 1], sws16[:, wt:wt + 1]).then_inc(vb.h, vb.tick())
                    v.wait_ge(vb.h, vb.n)
                    v.reciprocal(swsr[:, wt:wt + 1], sws[:, wt:wt + 1]).then_inc(vb.h, vb.tick())
                    v.wait_ge(vb.h, vb.n)
                    v.tensor_scalar(sq1[:], sx[:], swsr[:, wt:wt + 1], -448.0, AL.mult, AL.max)
                    v.tensor_scalar(sq1[:], sq1[:], 448.0, None, AL.min, AL.bypass)
                    v.tensor_scalar(sq2[:], sq1[:], -240.0, 240.0, AL.max, AL.min)
                    v.tensor_copy(sf8[:], sq2[:]).then_inc(vb.h, vb.tick())
                    v.wait_ge(vb.h, vb.n)
                    v.tensor_copy(sq2[:], sf8[:])
                    v.tensor_tensor(ssq[:], sq1[:], sq1[:], AL.mult).then_inc(vb.h, vb.tick())
                    v.wait_ge(vb.h, vb.n)
                    v.tensor_scalar(smk[:], ssq[:], 224.0 * 224.0, None, AL.is_gt, AL.bypass)
                    v.tensor_scalar(ssq[:], sq1[:], DEK, None, AL.mult, AL.bypass)
                    v.tensor_tensor(sx[:], ssq[:], sq1[:], AL.subtract)
                    v.tensor_tensor(ssq[:], ssq[:], sx[:], AL.subtract).then_inc(vb.h, vb.tick())
                    v.wait_ge(vb.h, vb.n)
                    v.select(sq1[:], smk[:], ssq[:], sq2[:]).then_inc(vb.h, vb.tick())
                    v.wait_ge(vb.h, vb.n)
                    v.tensor_copy(sqb[:], sq1[:]).then_inc(sv.h, sv.tick())
                block.vector(_)

                for b8 in range(8):
                    def _(te, wt=wt, b8=b8):
                        te.wait_ge(sv.h, sv.n)
                        te.wait_ge(dg.h, dg.n)
                        for j in range(4):
                            c = b8 * 4 + j
                            ins = te.transpose(pTb[:, j * 128:(j + 1) * 128],
                                               sqb[:, c * 128:(c + 1) * 128], sidb[:])
                        ins.then_inc(st.h, st.tick())
                    block.tensor(_)

                    def _(v, wt=wt, b8=b8):
                        v.wait_ge(st.h, st.n)
                        v.tensor_copy(sqb2[:], pTb[:]).then_inc(sv.h, sv.tick())
                    block.vector(_)

                    def _(g, wt=wt, b8=b8):
                        g.wait_ge(sv.h, sv.n)
                        for j in range(4):
                            c = b8 * 4 + j
                            g.dma_start(out=w8t_sc[c * 128:(c + 1) * 128, wt * 128:(wt + 1) * 128],
                                        in_=sqb2[:, j * 128:(j + 1) * 128]).then_inc(dg.h, 16)
                            dg.inc(16)
                    block.gpsimd(_)

            with ExitStack() as pd:
                pd2 = ExitStack()
                sw8 = sbuf("sw8", [128, 32 * 128], BF16, st=pd2)
                sw8b = sbuf("sw8b", [128, 32 * 128], BF16, st=pd2)
                t2f = sbuf("t2f", [128, 1024], st=pd2)
                o16 = sbuf("o16", [128, 1024], BF16, st=pd2)

                for do in range(32):
                    pp = (do % 2) * 1024
                    swb = sw8 if do % 2 == 0 else sw8b
                    def _(sy, do=do, swb=swb):
                        if do == 0:
                            sy.wait_ge(dg.h, dg.n)
                        sy.wait_ge(st.h, st.n if do == 0 else st.n - 1)
                        src3 = w8t_sc[:, do * 128:(do + 1) * 128].rearrange(
                            "(c p) n -> p c n", p=128)
                        dst3 = swb[:].rearrange("p (c n) -> p c n", c=32)
                        for grp in range(4):
                            sy.dma_start(out=dst3[:, grp * 8:(grp + 1) * 8, :],
                                         in_=src3[:, grp * 8:(grp + 1) * 8, :]).then_inc(dl.h, 16)
                            dl.inc(16)
                    block.sync(_)

                    def _(te, do=do, pp=pp, swb=swb):
                        te.wait_ge(dl.h, dl.n)
                        te.wait_ge(sv.h, sv.n if do < 2 else sv.n - 1)
                        for hf in range(2):
                            for di in range(32):
                                ins = te.matmul(pMM[:, pp + hf * 512:pp + hf * 512 + 512],
                                                swb[:, di * 128:(di + 1) * 128],
                                                xqT[:, di * R + hf * 512:di * R + hf * 512 + 512],
                                                start=(di == 0), stop=(di == 31))
                        ins.then_inc(st.h, st.tick())
                    block.tensor(_)

                    def _(v, do=do, pp=pp):
                        v.wait_ge(st.h, st.n)
                        v.wait_ge(ss.h, ss.n)
                        v.scalar_tensor_tensor(t2f[:], pMM[:, pp:pp + 1024], sws[:, do:do + 1],
                                               bc_as[:, 0:1024], AL.mult,
                                               AL.mult).then_inc(sv.h, sv.tick())
                    block.vector(_)

                    def _(s, do=do):
                        s.wait_ge(sv.h, sv.n)
                        s.wait_ge(dg.h, dg.n)
                        s.activation(o16[:], t2f[:], AF.Identity, bias=sfb[:, do:do + 1],
                                     scale=1.0).then_inc(ss.h, ss.tick())
                    block.scalar(_)

                    def _(g, do=do):
                        g.wait_ge(ss.h, ss.n)
                        g.dma_start(out=ot_sc[do * 128:(do + 1) * 128, :], in_=o16[:]).then_inc(dg.h, 16)
                        dg.inc(16)
                    block.gpsimd(_)

                pd2.close()
                sot = sbuf("sot", [128, D], BF16, st=pd)
                sy_ = sbuf("sy", [128, D], st=pd)

                for tt in range(8):
                    def _(sy, tt=tt):
                        sy.wait_ge(dg.h, dg.n)
                        sy.wait_ge(sv.h, sv.n)
                        sy.dma_start(out=sx[:], in_=x2_sc[tt * 128:(tt + 1) * 128, :]).then_inc(dl.h, 16)
                        dl.inc(16)
                        sy.dma_start(out=sot[:], in_=ot_sc[:, tt * 128:(tt + 1) * 128],
                                     transpose=True).then_inc(dl.h, 16)
                        dl.inc(16)
                    block.sync(_)

                    def _(v, tt=tt):
                        v.wait_ge(dl.h, dl.n)
                        v.tensor_tensor(sy_[:], sx[:], sot[:], AL.add).then_inc(sv.h, sv.tick())
                    block.vector(_)

                    def _(g, tt=tt):
                        g.wait_ge(sv.h, sv.n)
                        g.dma_start(out=y_out[tt * 128:(tt + 1) * 128, :], in_=sy_[:]).then_inc(dg.h, 16)
                        dg.inc(16)
                        if tt == 7 and not DEBUG_DUMPS:
                            g.wait_ge(dg.h, dg.n)
                    block.gpsimd(_)

                if DEBUG_DUMPS:
                    dumps = [("d_qt", qt_sc, [D, R], F32), ("d_kin", kin, [D, R], F32),
                             ("d_vin", vin, [D, R], F32), ("d_kall", kall, [4, D, R], F32),
                             ("d_x2", x2_sc, [R, D], F32), ("d_ot", ot_sc, [D, R], BF16),
                             ("d_w8t", w8t_sc, [D, D], BF16)]
                    def _(g):
                        for nm, srcap, shp, dt in dumps:
                            dout = nc.dram_tensor(nm, shp, dt, kind="ExternalOutput").ap()
                            flat_in = srcap.rearrange("... -> (...)") if len(shp) > 2 else srcap
                            nchunk = 8
                            if len(shp) == 3:
                                for j in range(shp[0]):
                                    g.dma_start(out=dout[j], in_=srcap[j]).then_inc(dg.h, 16)
                                    dg.inc(16)
                            else:
                                rows = shp[0]
                                for j in range(nchunk):
                                    r0, r1 = rows * j // nchunk, rows * (j + 1) // nchunk
                                    g.dma_start(out=dout[r0:r1], in_=srcap[r0:r1]).then_inc(dg.h, 16)
                                    dg.inc(16)
                        g.wait_ge(dg.h, dg.n)
                    block.gpsimd(_)

    return nc


_NC_CACHE = [None]


def kernel(**inputs):
    x = np.asarray(inputs["L0_attn_hidden_states"], np.float32)
    wq = np.asarray(inputs["L0_attn_wq"], np.float32)
    wk = np.asarray(inputs["L0_attn_wk"], np.float32)
    wv = np.asarray(inputs["L0_attn_wv"], np.float32)
    wo = np.asarray(inputs["L0_attn_wo"], np.float32)
    bq = np.asarray(inputs["L0_attn_bq"], np.float32)
    bk = np.asarray(inputs["L0_attn_bk"], np.float32)
    bv = np.asarray(inputs["L0_attn_bv"], np.float32)
    bo = np.asarray(inputs["L0_attn_bo"], np.float32)
    fw_ = np.asarray(inputs["L0_ffn_weight"], np.float32)
    fb_ = np.asarray(inputs["L0_ffn_bias"], np.float32)

    wqT = np.ascontiguousarray(wq.T)
    wkT = np.ascontiguousarray(wk.T)
    wvT = np.ascontiguousarray(wv.T)
    woT = np.ascontiguousarray(wo.T)
    boR = np.ascontiguousarray(np.broadcast_to(bo, (128, D)))

    if _NC_CACHE[0] is None:
        _NC_CACHE[0] = build_program()
    nc = _NC_CACHE[0]

    in_maps = []
    for c in range(8):
        b = c // 4
        r0 = (c % 4) * R
        in_maps.append({
            "x_tok": np.ascontiguousarray(x[b, r0:r0 + R]),
            "wqT": wqT, "wkT": wkT, "wvT": wvT, "woT": woT,
            "bq": bq, "bk": bk, "bv": bv, "bo_rep": boR,
            "fw": fw_, "fb": fb_,
        })
    res = run_bass_kernel_spmd(nc, in_maps, core_ids=list(range(8)))
    out = np.empty((2, SEQ, D), np.float32)
    for c in range(8):
        b = c // 4
        r0 = (c % 4) * R
        out[b, r0:r0 + R] = res.results[c]["y"]
    return out
